# version 14
# speedup vs baseline: 1.5047x; 1.0301x over previous
import numpy as np
import contextlib
import concourse.bass as bass
import concourse.mybir as mybir
from concourse.bass_utils import run_bass_kernel_spmd

F32 = mybir.dt.float32
BF16 = mybir.dt.bfloat16
I32 = mybir.dt.int32
AF = mybir.ActivationFunctionType
ALU = mybir.AluOpType
AXX = mybir.AxisListType.X

D = 1024
NG = D // 128
HID = 3072
NHG = HID // 128
FFD = 4096
NFG = FFD // 128
NPAIR = 8
EPS = 1e-5
NEG = -1.0e9

DEBUG_TAP = None


class Reg:
    __slots__ = ("name", "w", "readers")
    def __init__(self, name):
        self.name = name
        self.w = None
        self.readers = []


class B:
    ENG_ATTR = {"v": "vector", "s": "scalar", "t": "tensor",
                "g": "gpsimd", "q": "sync"}
    ENG_SEM = {"v": "sv", "s": "ss", "t": "st", "g": "sg", "q": None}

    def __init__(self, nc, block, sems, mksem):
        self.nc = nc
        self.block = block
        self.sems = sems
        self.mksem = mksem
        self.cnt = {k: 0 for k in sems}
        self.wm = {}
        self.maxwait = {}
        self.n_inst = 0
        self.buf = {e: [] for e in self.ENG_ATTR}

    def flush(self):
        for e in self.ENG_ATTR:
            bodies = self.buf[e]
            if not bodies:
                continue
            self.buf[e] = []
            def section(engobj, bodies=bodies):
                for bd in bodies:
                    bd(engobj)
            getattr(self.block, self.ENG_ATTR[e])(section)

    def dma_sem(self, key):
        key = "D_" + key
        if key not in self.sems:
            self.sems[key] = self.mksem(key)
            self.cnt[key] = 0
        return key

    def emit(self, eng, fn, r=(), w=(), dma=None, inc=None):
        needs = {}
        for reg_ in r:
            if reg_.w is not None:
                s, c = reg_.w
                if needs.get(s, 0) < c:
                    needs[s] = c
        for reg_ in w:
            if reg_.w is not None:
                s, c = reg_.w
                if needs.get(s, 0) < c:
                    needs[s] = c
            for (s, c) in reg_.readers:
                if needs.get(s, 0) < c:
                    needs[s] = c
        if dma is not None:
            sem_name = self.dma_sem(dma)
            inc_v = 16 if inc is None else inc
            mw = self.maxwait.get(sem_name, 0)
            if mw > 0 and needs.get(sem_name, 0) < mw:
                needs[sem_name] = mw
        else:
            sem_name = self.ENG_SEM[eng]
            inc_v = 1
        for s_, v_ in needs.items():
            if self.maxwait.get(s_, 0) < v_:
                self.maxwait[s_] = v_
        def body(engobj):
            for s in sorted(needs):
                v = needs[s]
                key = (eng, s)
                if self.wm.get(key, -1) < v:
                    engobj.wait_ge(self.sems[s], v)
                    self.wm[key] = v
            instr = fn(engobj)
            instr.then_inc(self.sems[sem_name], inc_v)
        self.buf[eng].append(body)
        self.n_inst += 1
        self.cnt[sem_name] += inc_v
        tag = (sem_name, self.cnt[sem_name])
        for reg_ in w:
            reg_.w = tag
            reg_.readers = []
        for reg_ in r:
            reg_.readers.append(tag)
        return tag

    def barrier(self, engs=("v", "s", "t", "g", "q")):
        snapshot = dict(self.cnt)
        for e in engs:
            def body(engobj, e=e):
                for s, v in sorted(snapshot.items()):
                    if v > 0 and self.wm.get((e, s), -1) < v:
                        engobj.wait_ge(self.sems[s], v)
                        self.wm[(e, s)] = v
            self.buf[e].append(body)
        for s, v in snapshot.items():
            if self.maxwait.get(s, 0) < v:
                self.maxwait[s] = v
        self.flush()


def build(T=2048, tap=None, cores=8):
    TC = min(512, T)
    NTC = T // TC
    NT2 = T // 128
    TCB = min(512, T)
    NTCB = T // TCB
    WQC = TC
    NW = D // WQC
    MXB = 40 + NG * NW

    nc = bass.Bass("TRN2", target_bir_lowering=False, debug=False)

    h = {}
    def din(name, shape, dtype=F32):
        h[name] = nc.dram_tensor(name, shape, dtype, kind="ExternalInput").ap()
        return h[name]

    din("xT", [D, T])
    din("wrT", [D, D]); din("wkT", [D, D]); din("wvT", [D, D]); din("woT", [D, D])
    din("wfkT", [FFD, D]); din("wfvT", [D, FFD]); din("wfrT", [D, D])
    din("wgT", [HID, D], I32); din("wuT", [HID, D], I32); din("wdT", [D, HID], I32)
    din("q_T", [D, D]); din("k_T", [D, D]); din("v_T", [D, D]); din("o_T", [D, D])
    din("wT", [D, D])
    for nm in ("ln1w", "ln1b", "ln2w", "ln2b", "decay", "u", "gamma", "beta"):
        din(nm, [128, NG])
    din("maskc", [128, (TC // 128) * TC])

    yT = nc.dram_tensor("yT", [D, T], F32, kind="ExternalOutput").ap()
    dbg = None
    if tap is not None:
        dbg = nc.dram_tensor("dbg", [D, T], F32, kind="ExternalOutput").ap()
    sp_g = nc.dram_tensor("sp_g", [D, T], BF16, kind="Internal").ap()
    sp_rw = nc.dram_tensor("sp_rw", [D, T], BF16, kind="Internal").ap()
    sp_xm = nc.dram_tensor("sp_xm", [D, T], BF16, kind="Internal").ap()
    sp_xn = nc.dram_tensor("sp_xn", [D, T], BF16, kind="Internal").ap()
    cc_in = nc.dram_tensor("cc_in", [1, 8], F32, kind="Internal").ap()
    cc_out = nc.dram_tensor("cc_out", [1, 8], F32, kind="Internal").ap()

    es = contextlib.ExitStack()
    sb = lambda name, shape, dtype=F32: es.enter_context(
        nc.sbuf_tensor(name, shape, dtype))
    mkpsum = lambda name, shape, dtype=F32: es.enter_context(
        nc.psum_tensor(name, shape, dtype))
    mksem = lambda name: es.enter_context(nc.semaphore(name))

    SLOT_OFF = [0, 1024, 2048, 3072, 4096, 7168]
    SLOT_W = [1024, 1024, 1024, 1024, 3072, 4096]
    wslot = sb("wslot", [128, 11264], BF16)
    s0b = sb("s0b", [128, T], BF16)
    s1b = sb("s1b", [128, T], BF16)
    m0b = sb("m0b", [128, T], BF16)
    rows = sb("rows", [1, 2 * TC])
    rowx = sb("rowx", [1, TC])
    mub = sb("mub", [128, T], BF16)
    wbcol = sb("wbcol", [128, 48])
    wcolf = sb("wcolf", [128, NG])
    bcolb = sb("bcolb", [128, NG], BF16)
    onescol = sb("onescol", [128, 2], BF16)
    onescolf = sb("onescolf", [1, 128])
    xf = sb("xf", [128, 2 * TC])
    x2b = sb("x2b", [128, 2 * TC], BF16)
    ev1 = sb("ev1", [128, 2 * TC])
    ev2 = sb("ev2", [128, 2 * TC])
    cols = {}
    for nm in ("ln1w", "ln1b", "ln2w", "ln2b", "decay", "u", "gamma", "beta"):
        cols[nm] = sb("c_" + nm, [128, NG])
    ew = sb("c_ew", [128, NG])
    eu = sb("c_eu", [128, NG])
    mxc = sb("mxc", [128, 40 + 2 * NG * NW])
    scol = sb("scol", [128, NT2])

    PS = [mkpsum(f"ps{i}", [128, 512]) for i in range(6)]
    PSS = mkpsum("pss", [128, 512])
    PSB = mkpsum("psb", [128, 512])

    sems = {}
    for s_ in ("sv", "ss", "st", "sg"):
        sems[s_] = mksem(s_)

    blk_ctx = nc.Block()
    block = blk_ctx.__enter__()
    b = B(nc, block, sems, mksem)
    E = b.emit
    R = {}
    def reg(name):
        if name not in R:
            R[name] = Reg(name)
        return R[name]

    rps = [reg(f"ps.{i}") for i in range(6)]
    rpss = reg("pss"); rpsb = reg("psb")
    rrows = reg("rows"); rrowx = reg("rowx")
    rwbcol = reg("wbcol"); rbcolb = reg("bcolb")
    rmub = reg("mub")
    rs0b = reg("s0b"); rs1b = reg("s1b"); rm0b = reg("m0b")
    rcols = reg("cols")
    rxf = [reg("xf.0"), reg("xf.1")]
    rx2b = [reg("x2b.0"), reg("x2b.1")]
    rev1 = [reg("ev1.0"), reg("ev1.1")]
    rev2 = [reg("ev2.0"), reg("ev2.1")]
    rslot = [reg(f"wslot.{i}") for i in range(6)]
    rones = reg("ones")
    rmask = reg("masks")
    rmxc = reg("mxc"); rscol = reg("scol")
    rcc = reg("ccdone")
    rhbm_g = reg("hbm_g"); rhbm_rw = reg("hbm_rw"); rhbm_xm = reg("hbm_xm")

    def slot(i, k=None, kw=128):
        o = SLOT_OFF[i]
        if k is None:
            return wslot[:, o:o + SLOT_W[i]]
        return wslot[:, o + k * kw:o + (k + 1) * kw]

    for nm in ("ln1w", "ln1b", "ln2w", "ln2b", "decay", "u", "gamma", "beta"):
        E("q", lambda q, nm=nm: q.dma_start(
            out=cols[nm][:, :], in_=h[nm][:, :]), w=[rcols], dma="pre")
    E("v", lambda v: v.memset(onescol[:, :], 1.0), w=[rones])
    E("v", lambda v: v.memset(onescolf[:, :], 1.0), w=[rones])
    E("s", lambda s: s.activation(out=ew[:, :], in_=cols["decay"][:, :],
                                  func=AF.Exp), r=[rcols], w=[rcols])
    E("s", lambda s: s.activation(out=ew[:, :], in_=ew[:, :],
                                  func=AF.Exp, scale=-1.0), r=[rcols], w=[rcols])
    E("s", lambda s: s.activation(out=eu[:, :], in_=cols["u"][:, :],
                                  func=AF.Exp), r=[rcols], w=[rcols])

    def S1(): return rows[0:1, 0:TC]
    def S2(): return rows[0:1, TC:2 * TC]
    def RX(): return rowx[0:1, 0:TC]

    def bcast_tc(src, sreg, dst, rdst, tc):
        E("t", lambda t: t.matmul(
            out=PSB[:, 0:TC], lhsT=onescolf[0:1, :], rhs=src,
            start=True, stop=True), r=[sreg, rones], w=[rpsb])
        E("v", lambda v: v.tensor_copy(
            out=dst[:, tc * TC:(tc + 1) * TC], in_=PSB[:, 0:TC]),
          r=[rpsb], w=[rdst])

    def ln_rows_tc(tc, mode):
        C = float(D)
        E("v", lambda v: v.tensor_scalar(
            out=S1(), in0=S1(), scalar1=1.0 / C, scalar2=None,
            op0=ALU.mult), r=[rrows], w=[rrows])
        bcast_tc(S1(), rrows, mub, rmub, tc)
        E("v", lambda v: v.tensor_tensor(
            out=RX(), in0=S1(), in1=S1(), op=ALU.mult),
          r=[rrows], w=[rrowx])
        E("v", lambda v: v.scalar_tensor_tensor(
            out=S2(), in0=S2(), scalar=1.0 / C, in1=RX(),
            op0=ALU.mult, op1=ALU.subtract), r=[rrows, rrowx], w=[rrows])
        E("v", lambda v: v.tensor_scalar(
            out=S2(), in0=S2(), scalar1=EPS, scalar2=None, op0=ALU.add),
          r=[rrows], w=[rrows])
        E("s", lambda s: s.activation(out=RX(), in_=S2(), func=AF.Ln),
          r=[rrows], w=[rrowx])
        E("s", lambda s: s.activation(out=RX(), in_=RX(), func=AF.Exp,
                                      scale=-0.5), r=[rrowx], w=[rrowx])
        if mode == "ln2":
            bcast_tc(RX(), rrowx, s1b, rs1b, tc)
            return
        bcast_tc(RX(), rrowx, s0b, rs0b, tc)
        if mode == "l1":
            nj = TC // 128
            for j in range(nj):
                E("g", lambda g_, j=j: g_.dma_start(
                    out=scol[:, tc * nj + j:tc * nj + j + 1],
                    in_=rowx[0:1, j * 128:(j + 1) * 128],
                    allow_slow_non_contiguous=True),
                  r=[rrowx], w=[rscol], dma="scol")
        if mode == "l0":
            E("v", lambda v: v.tensor_tensor(
                out=S1(), in0=S1(), in1=RX(), op=ALU.mult),
              r=[rrows, rrowx], w=[rrows])
            bcast_tc(S1(), rrows, m0b, rm0b, tc)
        E("v", lambda v: v.tensor_tensor(
            out=S2(), in0=S2(), in1=RX(), op=ALU.mult),
          r=[rrows, rrowx], w=[rrows])
        E("v", lambda v: v.tensor_tensor(
            out=S2(), in0=S2(), in1=RX(), op=ALU.mult),
          r=[rrows, rrowx], w=[rrows])
        E("v", lambda v: v.tensor_scalar(
            out=S2(), in0=S2(), scalar1=EPS, scalar2=None, op0=ALU.add),
          r=[rrows], w=[rrows])
        E("s", lambda s: s.activation(out=S2(), in_=S2(), func=AF.Ln),
          r=[rrows], w=[rrows])
        E("s", lambda s: s.activation(out=S2(), in_=S2(), func=AF.Exp,
                                      scale=-0.5), r=[rrows], w=[rrows])
        E("v", lambda v: v.tensor_tensor(
            out=RX(), in0=RX(), in1=S2(), op=ALU.mult),
          r=[rrows], w=[rrowx])
        bcast_tc(RX(), rrowx, s1b, rs1b, tc)
        if mode == "l1":
            E("v", lambda v: v.tensor_tensor(
                out=S1(), in0=S1(), in1=RX(), op=ALU.mult),
              r=[rrowx], w=[rrows])
            bcast_tc(S1(), rrows, m0b, rm0b, tc)

    def stream_w(wap, mi, nk, si):
        E("g", lambda g_: g_.dma_start(
            out=slot(si)[:, 0:nk * 128],
            in_=wap[mi * 128:(mi + 1) * 128, 0:nk * 128]),
          w=[rslot[si]], dma=f"w{si}")

    rslot5h = [reg("wslot.5a"), reg("wslot.5b")]
    def stream_w5(wap, mi, nk):
        hk = nk // 2
        E("g", lambda g_, hk=hk: g_.dma_start(
            out=slot(5)[:, 0:hk * 128],
            in_=wap[mi * 128:(mi + 1) * 128, 0:hk * 128]),
          w=[rslot5h[0]], dma="w5a")
        E("g", lambda g_, hk=hk: g_.dma_start(
            out=slot(5)[:, hk * 128:nk * 128],
            in_=wap[mi * 128:(mi + 1) * 128, hk * 128:nk * 128]),
          w=[rslot5h[1]], dma="w5b")


    def wb_column(si, nk, ci, ps=None, rp=None):
        ps = PSB if ps is None else ps
        rp = rpsb if rp is None else rp
        for k in range(nk):
            E("t", lambda t, k=k, ps=ps: t.matmul(
                out=ps[:, 0:1], lhsT=slot(si, k), rhs=bcolb[:, k:k + 1],
                start=(k == 0), stop=(k == nk - 1)),
              r=[rslot[si], rbcolb], w=[rp])
        E("v", lambda v, ps=ps: v.tensor_copy(
            out=wbcol[:, ci:ci + 1], in_=ps[:, 0:1]), r=[rp], w=[rwbcol])

    def scale_slot(si, nk):
        for k in range(nk):
            E("v", lambda v, k=k: v.tensor_scalar(
                out=slot(si, k), in0=slot(si, k), scalar1=wcolf[:, k:k + 1],
                scalar2=None, op0=ALU.mult),
              r=[rwbcol, rslot[si]], w=[rslot[si]])

    p0 = contextlib.ExitStack()
    sb0 = lambda name, shape, dtype=F32: p0.enter_context(
        nc.sbuf_tensor(name, shape, dtype))
    xbuf = sb0("xbuf", [128, NG * T], BF16)
    scA = sb0("scA", [128, T])
    scB = sb0("scB", [128, T])
    scC = sb0("scC", [128, T + 8])
    scD = sb0("scD", [128, T + 8])
    scE = sb0("scE", [128, T + 8])
    scF = sb0("scF", [128, T + 8])
    sigr = sb0("sigr", [128, T])
    d1t = sb0("d1t", [128, T])
    hbuf = sb0("hbuf", [128, NHG * TCB], BF16)
    gtile = sb0("gtile", [128, 2 * TCB], BF16)
    rwt = sb0("rwt", [128, T], BF16)

    rxb = [reg(f"xb.{g}") for g in range(NG)]
    rsc = {nm: reg(nm) for nm in ("scA", "scB", "scC", "scD", "scE", "scF")}
    rsigr = reg("sigr"); rd1 = reg("d1t")
    rhb = reg("hbuf")
    rgt = [reg("gtile.0"), reg("gtile.1")]
    rrwt = reg("rwt")

    for tc in range(NTC):
        for g in range(NG):
            pp = g % 2
            xfs = xf[:, pp * TC:(pp + 1) * TC]
            E("q", lambda q, g=g, tc=tc, xfs=xfs: q.dma_start(
                out=xfs, in_=h["xT"][g * 128:(g + 1) * 128,
                                     tc * TC:(tc + 1) * TC]),
              w=[rxf[pp]], dma=f"xf{pp}")
            E("s", lambda s, g=g, tc=tc, xfs=xfs: s.activation(
                out=xbuf[:, g * T + tc * TC:g * T + (tc + 1) * TC],
                in_=xfs, func=AF.Copy), r=[rxf[pp]], w=[rxb[g]])
            E("s", lambda s, xfs=xfs, pp=pp: s.activation(
                out=x2b[:, pp * TC:(pp + 1) * TC], in_=xfs, func=AF.Square),
              r=[rxf[pp]], w=[rx2b[pp]])
            E("t", lambda t, g=g, tc=tc: t.matmul(
                out=PSS[0:1, 0:TC], lhsT=onescol[:, 0:1],
                rhs=xbuf[:, g * T + tc * TC:g * T + (tc + 1) * TC],
                start=(g == 0), stop=(g == NG - 1)),
              r=[rxb[g], rones], w=[rpss])
            E("t", lambda t, g=g, pp=pp: t.matmul(
                out=PSB[0:1, 0:TC], lhsT=onescol[:, 0:1],
                rhs=x2b[:, pp * TC:(pp + 1) * TC],
                start=(g == 0), stop=(g == NG - 1)),
              r=[rx2b[pp], rones], w=[rpsb])
        E("v", lambda v: v.tensor_copy(out=S1(), in_=PSS[0:1, 0:TC]),
          r=[rpss], w=[rrows])
        E("v", lambda v: v.tensor_copy(out=S2(), in_=PSB[0:1, 0:TC]),
          r=[rpsb], w=[rrows])
        ln_rows_tc(tc, "l0")

    for g in range(NG):
        E("v", lambda v, g=g: v.tensor_tensor(
            out=xbuf[:, g * T:(g + 1) * T], in0=xbuf[:, g * T:(g + 1) * T],
            in1=mub[:, 0:T], op=ALU.subtract), r=[rmub], w=[rxb[g]])
    E("v", lambda v: v.tensor_copy(out=wcolf[:, :], in_=cols["ln1w"][:, :]),
      r=[rcols], w=[rwbcol])
    E("v", lambda v: v.tensor_copy(out=bcolb[:, :], in_=cols["ln1b"][:, :]),
      r=[rcols], w=[rbcolb])
    rhbm_xn = reg("hbm_xn")
    def xn_chunks():
        for g in range(NG):
            for tc2 in range(NTC):
                def xs(g=g, tc2=tc2):
                    pp = (g * NTC + tc2) % 2
                    E("v", lambda v, g=g, tc2=tc2, pp=pp: v.tensor_tensor(
                        out=x2b[:, pp * TC:(pp + 1) * TC],
                        in0=xbuf[:, g * T + tc2 * TC:g * T + (tc2 + 1) * TC],
                        in1=s0b[:, tc2 * TC:(tc2 + 1) * TC], op=ALU.mult),
                      r=[rxb[g], rs0b], w=[rx2b[pp]])
                    E("q", lambda q, g=g, tc2=tc2, pp=pp: q.dma_start(
                        out=sp_xn[g * 128:(g + 1) * 128,
                                  tc2 * TC:(tc2 + 1) * TC],
                        in_=x2b[:, pp * TC:(pp + 1) * TC]),
                      r=[rx2b[pp]], w=[rhbm_xn], dma="hxn")
                yield xs

    def bb_chunks():
        for tcb in range(NTCB):
            sl = slice(tcb * TCB, (tcb + 1) * TCB)
            for m in range(NHG):
                def mb(tcb=tcb, m=m, sl=sl):
                    for wi, (wap, si, bank) in enumerate(
                            ((h["wgT"], 3, 2), (h["wuT"], 4, 3))):
                        stream_w(wap, m, NG, si)
                        for k in range(NG):
                            E("t", lambda t, k=k, si=si, bank=bank, sl=sl:
                                t.matmul(
                                    out=PS[bank][:, 0:TCB], lhsT=slot(si, k),
                                    rhs=xbuf[:, k * T + sl.start:
                                             k * T + sl.stop],
                                    start=(k == 0), stop=(k == NG - 1)),
                              r=[rslot[si], rxb[k]], w=[rps[bank]])
                    pp = m % 2
                    E("v", lambda v, sl=sl, pp=pp: v.scalar_tensor_tensor(
                        out=ev1[:, pp * TC:pp * TC + TCB],
                        in0=PS[2][:, 0:TCB], scalar=1.0 / 127.0,
                        in1=s0b[:, sl], op0=ALU.mult, op1=ALU.mult),
                      r=[rps[2], rs0b], w=[rev1[pp]])
                    E("v", lambda v, sl=sl, pp=pp: v.scalar_tensor_tensor(
                        out=ev2[:, pp * TC:pp * TC + TCB],
                        in0=PS[3][:, 0:TCB], scalar=1.0 / 127.0,
                        in1=s0b[:, sl], op0=ALU.mult, op1=ALU.mult),
                      r=[rps[3], rs0b], w=[rev2[pp]])
                    E("s", lambda s, pp=pp: s.activation(
                        out=x2b[:, pp * TC:pp * TC + TCB],
                        in_=ev1[:, pp * TC:pp * TC + TCB],
                        func=AF.Tanh, scale=0.5), r=[rev1[pp]], w=[rx2b[pp]])
                    E("v", lambda v, pp=pp: v.scalar_tensor_tensor(
                        out=ev1[:, pp * TC:pp * TC + TCB],
                        in0=x2b[:, pp * TC:pp * TC + TCB], scalar=1.0,
                        in1=ev1[:, pp * TC:pp * TC + TCB],
                        op0=ALU.add, op1=ALU.mult),
                      r=[rx2b[pp], rev1[pp]], w=[rev1[pp]])
                    E("v", lambda v, m=m, pp=pp: v.tensor_tensor(
                        out=hbuf[:, m * TCB:(m + 1) * TCB],
                        in0=ev1[:, pp * TC:pp * TC + TCB],
                        in1=ev2[:, pp * TC:pp * TC + TCB], op=ALU.mult),
                      r=[rev1[pp], rev2[pp]], w=[rhb])
                yield mb
            for mo in range(NG):
                def dn(tcb=tcb, mo=mo, sl=sl):
                    stream_w5(h["wdT"], mo, NHG)
                    bank = 4 + (mo % 2)
                    for k in range(NHG):
                        E("t", lambda t, k=k, bank=bank: t.matmul(
                            out=PS[bank][:, 0:TCB], lhsT=slot(5, k),
                            rhs=hbuf[:, k * TCB:(k + 1) * TCB],
                            start=(k == 0), stop=(k == NHG - 1)),
                          r=[rslot5h[2 * k // NHG], rhb], w=[rps[bank]])
                    pp = mo % 2
                    E("s", lambda s, bank=bank, pp=pp: s.activation(
                        out=x2b[:, pp * TC:pp * TC + TCB],
                        in_=PS[bank][:, 0:TCB], func=AF.Tanh,
                        scale=0.5 / 254.0), r=[rps[bank]], w=[rx2b[pp]])
                    E("v", lambda v, bank=bank, pp=pp: v.scalar_tensor_tensor(
                        out=gtile[:, pp * TCB:(pp + 1) * TCB],
                        in0=x2b[:, pp * TC:pp * TC + TCB], scalar=1.0,
                        in1=PS[bank][:, 0:TCB], op0=ALU.add, op1=ALU.mult),
                      r=[rx2b[pp], rps[bank]], w=[rgt[pp]])
                    E("q", lambda q, mo=mo, sl=sl, pp=pp: q.dma_start(
                        out=sp_g[mo * 128:(mo + 1) * 128, sl],
                        in_=gtile[:, pp * TCB:(pp + 1) * TCB]),
                      r=[rgt[pp]], w=[rhbm_g], dma="hg")
                yield dn

    import itertools
    bbgen = itertools.chain(xn_chunks(), bb_chunks())
    bb_left = [NG * NTC + NTCB * (NHG + NG)]
    def bb_pull(n):
        for _ in range(n):
            if bb_left[0] <= 0:
                return
            next(bbgen)()
            bb_left[0] -= 1

    def rkv_group(gm):
        stream_w(h["wkT"], gm, NG, 0)
        stream_w(h["wvT"], gm, NG, 1)
        stream_w(h["wrT"], gm, NG, 2)
        for si, ci in ((0, 0), (1, 1), (2, 2)):
            wb_column(si, NG, ci)
            scale_slot(si, NG)

        def proj(si, ci, dst, rdst, bias_add=True):
            for tc in range(NTC):
                bank = tc % 2
                for k in range(NG):
                    E("t", lambda t, k=k, si=si, tc=tc, bank=bank: t.matmul(
                        out=PS[bank][:, 0:TC], lhsT=slot(si, k),
                        rhs=xbuf[:, k * T + tc * TC:k * T + (tc + 1) * TC],
                        start=(k == 0), stop=(k == NG - 1)),
                      r=[rslot[si], rxb[k]], w=[rps[bank]])
                E("v", lambda v, tc=tc, bank=bank: v.tensor_tensor(
                    out=dst[:, tc * TC:(tc + 1) * TC], in0=PS[bank][:, 0:TC],
                    in1=s1b[:, tc * TC:(tc + 1) * TC], op=ALU.mult),
                  r=[rps[bank], rs1b], w=[rdst])
                if bias_add:
                    E("v", lambda v, tc=tc, ci=ci: v.tensor_scalar(
                        out=dst[:, tc * TC:(tc + 1) * TC],
                        in0=dst[:, tc * TC:(tc + 1) * TC],
                        scalar1=wbcol[:, ci:ci + 1], scalar2=None,
                        op0=ALU.add), r=[rwbcol], w=[rdst])

        proj(0, 0, scB, rsc["scB"], bias_add=False)
        if tap == "l0_k":
            E("v", lambda v: v.tensor_scalar(
                out=scB[:, :], in0=scB[:, :], scalar1=wbcol[:, 0:1],
                scalar2=None, op0=ALU.add), r=[rwbcol], w=[rsc["scB"]])
            E("q", lambda q: q.dma_start(
                out=dbg[gm * 128:(gm + 1) * 128, :], in_=scB[:, :]),
              r=[rsc["scB"]], dma="out")
            E("s", lambda s: s.activation(out=scB[:, :], in_=scB[:, :],
                                          func=AF.Exp),
              r=[rsc["scB"]], w=[rsc["scB"]])
        else:
            E("s", lambda s: s.activation(out=scB[:, :], in_=scB[:, :],
                                          func=AF.Exp, bias=wbcol[:, 0:1]),
              r=[rsc["scB"], rwbcol], w=[rsc["scB"]])
        proj(1, 1, scA, rsc["scA"], bias_add=False)
        E("v", lambda v: v.scalar_tensor_tensor(
            out=scA[:, :], in0=scA[:, :], scalar=wbcol[:, 1:2],
            in1=scB[:, :], op0=ALU.add, op1=ALU.mult),
          r=[rsc["scB"], rwbcol], w=[rsc["scA"]])
        proj(2, 2, sigr, rsigr, bias_add=False)
        E("v", lambda v: v.tensor_scalar(
            out=wbcol[:, 3:4], in0=wbcol[:, 2:3], scalar1=0.5, scalar2=None,
            op0=ALU.mult), r=[rwbcol], w=[rwbcol])
        E("s", lambda s: s.activation(out=sigr[:, :], in_=sigr[:, :],
                                      func=AF.Tanh, scale=0.5,
                                      bias=wbcol[:, 3:4]),
          r=[rsigr, rwbcol], w=[rsigr])

        th = []
        A = th.append
        A(lambda: (E("v", lambda v: v.memset(d1t[:, :], 1.0), w=[rd1]),
                   E("v", lambda v: v.tensor_scalar(
                       out=d1t[:, :], in0=d1t[:, :], scalar1=ew[:, gm:gm + 1],
                       scalar2=None, op0=ALU.mult), r=[rcols], w=[rd1])))
        A(lambda: E("v", lambda v: v.tensor_copy(out=scE[:, 1:T + 1],
                                                 in_=scA[:, ::-1]),
                    r=[rsc["scA"]], w=[rsc["scE"]]))
        A(lambda: E("v", lambda v: v.tensor_copy(out=scF[:, 1:T + 1],
                                                 in_=scB[:, ::-1]),
                    r=[rsc["scB"]], w=[rsc["scF"]]))
        def _zc():
            for nm, tt in (("scC", scC), ("scD", scD), ("scE", scE),
                           ("scF", scF)):
                E("v", lambda v, tt=tt: v.memset(tt[:, 0:1], 0.0),
                  w=[rsc[nm]])
        A(_zc)
        A(lambda: E("v", lambda v: v.tensor_tensor_scan(
            out=scC[:, 1:T + 1], data0=d1t[:, :], data1=scA[:, :],
            initial=0.0, op0=ALU.mult, op1=ALU.add),
          r=[rd1, rsc["scA"]], w=[rsc["scC"]]))
        A(lambda: E("v", lambda v: v.tensor_tensor_scan(
            out=scD[:, 1:T + 1], data0=d1t[:, :], data1=scB[:, :],
            initial=0.0, op0=ALU.mult, op1=ALU.add),
          r=[rd1, rsc["scB"]], w=[rsc["scD"]]))
        A(lambda: E("v", lambda v: v.tensor_tensor_scan(
            out=scE[:, 1:T + 1], data0=d1t[:, :], data1=scE[:, 1:T + 1],
            initial=0.0, op0=ALU.mult, op1=ALU.add),
          r=[rd1], w=[rsc["scE"]]))
        A(lambda: E("v", lambda v: v.tensor_tensor_scan(
            out=scF[:, 1:T + 1], data0=d1t[:, :], data1=scF[:, 1:T + 1],
            initial=0.0, op0=ALU.mult, op1=ALU.add),
          r=[rd1], w=[rsc["scF"]]))
        A(lambda: E("v", lambda v: v.tensor_scalar(
            out=scA[:, :], in0=scA[:, :], scalar1=eu[:, gm:gm + 1],
            scalar2=None, op0=ALU.mult), r=[rcols], w=[rsc["scA"]]))
        A(lambda: E("v", lambda v: v.tensor_scalar(
            out=scB[:, :], in0=scB[:, :], scalar1=eu[:, gm:gm + 1],
            scalar2=None, op0=ALU.mult), r=[rcols], w=[rsc["scB"]]))
        A(lambda: E("v", lambda v: v.tensor_tensor(
            out=scC[:, 0:T], in0=scC[:, 0:T], in1=scA[:, :], op=ALU.add),
          r=[rsc["scA"]], w=[rsc["scC"]]))
        A(lambda: E("v", lambda v: v.tensor_tensor(
            out=scD[:, 0:T], in0=scD[:, 0:T], in1=scB[:, :], op=ALU.add),
          r=[rsc["scB"]], w=[rsc["scD"]]))
        A(lambda: E("v", lambda v: v.tensor_tensor(
            out=scA[:, :], in0=scA[:, :], in1=scE[:, 0:T][:, ::-1],
            op=ALU.add), r=[rsc["scE"]], w=[rsc["scA"]]))
        A(lambda: E("v", lambda v: v.tensor_tensor(
            out=scB[:, :], in0=scB[:, :], in1=scF[:, 0:T][:, ::-1],
            op=ALU.add), r=[rsc["scF"]], w=[rsc["scB"]]))
        A(lambda: E("v", lambda v: v.tensor_tensor(
            out=scA[:, :], in0=scA[:, :], in1=scD[:, 0:T], op=ALU.mult),
          r=[rsc["scD"]], w=[rsc["scA"]]))
        A(lambda: E("v", lambda v: v.tensor_tensor(
            out=scC[:, 0:T], in0=scC[:, 0:T], in1=scB[:, :], op=ALU.mult),
          r=[rsc["scB"]], w=[rsc["scC"]]))
        A(lambda: E("v", lambda v: v.tensor_tensor(
            out=scC[:, 0:T], in0=scC[:, 0:T], in1=scA[:, :], op=ALU.add),
          r=[rsc["scA"]], w=[rsc["scC"]]))
        A(lambda: E("v", lambda v: v.tensor_tensor(
            out=scD[:, 0:T], in0=scD[:, 0:T], in1=scB[:, :], op=ALU.mult),
          r=[rsc["scB"]], w=[rsc["scD"]]))
        A(lambda: E("s", lambda s: s.activation(
            out=scD[:, 0:T], in_=scD[:, 0:T], func=AF.Ln), w=[rsc["scD"]]))
        A(lambda: E("s", lambda s: s.activation(
            out=scD[:, 0:T], in_=scD[:, 0:T], func=AF.Exp, scale=-1.0),
          w=[rsc["scD"]]))
        A(lambda: E("v", lambda v: v.tensor_tensor(
            out=scC[:, 0:T], in0=scC[:, 0:T], in1=scD[:, 0:T], op=ALU.mult),
          r=[rsc["scD"]], w=[rsc["scC"]]))
        if tap == "l0_wkv":
            def _dump():
                E("v", lambda v: v.tensor_scalar(
                    out=scC[:, 0:T], in0=scC[:, 0:T], scalar1=0.5,
                    scalar2=None, op0=ALU.mult), r=[rsc["scC"]],
                  w=[rsc["scC"]])
                E("q", lambda q: q.dma_start(
                    out=dbg[gm * 128:(gm + 1) * 128, :], in_=scC[:, 0:T]),
                  r=[rsc["scC"]], dma="out")
            A(_dump)
            return th
        def _rw():
            E("v", lambda v: v.scalar_tensor_tensor(
                out=rwt[:, :], in0=sigr[:, :], scalar=1.0, in1=scC[:, 0:T],
                op0=ALU.add, op1=ALU.mult), r=[rsigr, rsc["scC"]], w=[rrwt])
            E("q", lambda q: q.dma_start(
                out=sp_rw[gm * 128:(gm + 1) * 128, :], in_=rwt[:, :]),
              r=[rrwt], w=[rhbm_rw], dma="hrw")
        A(_rw)
        return th

    BBTOT = NG * NTC + NTCB * (NHG + NG)
    pulled = 0
    for gm in range(NG):
        th = rkv_group(gm)
        quota = (gm + 1) * BBTOT // NG - pulled
        n = max(quota, len(th))
        ti = 0
        pi = 0
        for i in range(n):
            while pi * n < quota * (i + 1) and pi < quota:
                bb_pull(1)
                pi += 1
                pulled += 1
            while ti * n <= i * len(th) and ti < len(th):
                th[ti]()
                ti += 1
        while ti < len(th):
            th[ti]()
            ti += 1
    bb_pull(10 ** 9)

    b.barrier()
    p0.close()

    if tap in ("l0_k", "l0_wkv"):
        blk_ctx.__exit__(None, None, None)
        es.close()
        return nc, b

    p1 = contextlib.ExitStack()
    sb1 = lambda name, shape, dtype=F32: p1.enter_context(
        nc.sbuf_tensor(name, shape, dtype))
    x1 = sb1("x1", [128, NG * T])
    xb1 = sb1("xb1", [128, NG * T], BF16)
    rxb1 = [reg(f"xb1.{g}") for g in range(NG)]
    rx1t = [[reg(f"x1.{g}.{t_}") for t_ in range(NTC)] for g in range(NG)]
    rx1 = [reg(f"x1.{g}") for g in range(NG)]

    p1b = contextlib.ExitStack()
    sb1b = lambda name, shape, dtype=F32: p1b.enter_context(
        nc.sbuf_tensor(name, shape, dtype))
    kkb = sb1b("kkb", [128, NFG * TC], BF16)
    rwring = sb1b("rwring", [128, 2 * NG * TC], BF16)
    xmring = rwring
    g1ring = sb1b("g1ring", [128, 2 * TC], BF16)
    xcring = wslot[:, 3072:3072 + NG * TC]

    rkk = [reg(f"kk.{i}") for i in range(NFG)]
    rrwr = [reg("rwring.0"), reg("rwring.1")]
    rxmr = rrwr
    rg1 = [reg("g1ring.0"), reg("g1ring.1")]
    rxcr = reg("xcring")

    for tc in range(NTC):
        ppr = tc % 2
        E("q", lambda q, tc=tc, ppr=ppr: q.dma_start(
            out=rwring[:, ppr * NG * TC:(ppr + 1) * NG * TC].rearrange(
                "p (k c) -> p k c", k=NG),
            in_=sp_rw[:, tc * TC:(tc + 1) * TC].rearrange(
                "(k p) c -> p k c", p=128)), r=[rhbm_rw], w=[rrwr[ppr]], dma=f"rr{ppr}")
        for mo in range(NG):
            si = mo % 2
            stream_w(h["woT"], mo, NG, si)
            bank = 2 + (mo % 2)
            for k in range(NG):
                E("t", lambda t, k=k, si=si, bank=bank, ppr=ppr: t.matmul(
                    out=PS[bank][:, 0:TC], lhsT=slot(si, k),
                    rhs=rwring[:, ppr * NG * TC + k * TC:
                               ppr * NG * TC + (k + 1) * TC],
                    start=(k == 0), stop=(k == NG - 1)),
                  r=[rslot[si], rrwr[ppr]], w=[rps[bank]])
            pp = mo % 2
            E("q", lambda q, mo=mo, tc=tc, pp=pp: q.dma_start(
                out=g1ring[:, pp * TC:(pp + 1) * TC],
                in_=sp_xn[mo * 128:(mo + 1) * 128, tc * TC:(tc + 1) * TC]),
              r=[rhbm_xn], w=[rg1[pp]], dma=f"g1{pp}")
            E("v", lambda v, pp=pp, bank=bank: v.scalar_tensor_tensor(
                out=ev1[:, pp * TC:(pp + 1) * TC], in0=PS[bank][:, 0:TC],
                scalar=0.25, in1=g1ring[:, pp * TC:(pp + 1) * TC],
                op0=ALU.mult, op1=ALU.add),
              r=[rps[bank], rg1[pp]], w=[rev1[pp]])
            E("s", lambda s, pp=pp: s.activation(
                out=x2b[:, pp * TC:(pp + 1) * TC],
                in_=ev1[:, pp * TC:(pp + 1) * TC], func=AF.Copy),
              r=[rev1[pp]], w=[rx2b[pp]])
            E("q", lambda q, mo=mo, tc=tc, pp=pp: q.dma_start(
                out=sp_xm[mo * 128:(mo + 1) * 128, tc * TC:(tc + 1) * TC],
                in_=x2b[:, pp * TC:(pp + 1) * TC]),
              r=[rx2b[pp]], w=[rhbm_xm], dma="hxm")
            E("t", lambda t, mo=mo, pp=pp: t.matmul(
                out=PSS[0:1, 0:TC], lhsT=onescol[:, 0:1],
                rhs=x2b[:, pp * TC:(pp + 1) * TC],
                start=(mo == 0), stop=(mo == NG - 1)),
              r=[rx2b[pp], rones], w=[rpss])
            E("s", lambda s, pp=pp: s.activation(
                out=x2b[:, pp * TC:(pp + 1) * TC],
                in_=ev1[:, pp * TC:(pp + 1) * TC], func=AF.Square),
              r=[rev1[pp], rx2b[pp]], w=[rx2b[pp]])
            E("t", lambda t, mo=mo, pp=pp: t.matmul(
                out=PSB[0:1, 0:TC], lhsT=onescol[:, 0:1],
                rhs=x2b[:, pp * TC:(pp + 1) * TC],
                start=(mo == 0), stop=(mo == NG - 1)),
              r=[rx2b[pp], rones], w=[rpsb])
        E("v", lambda v: v.tensor_copy(out=S1(), in_=PSS[0:1, 0:TC]),
          r=[rpss], w=[rrows])
        E("v", lambda v: v.tensor_copy(out=S2(), in_=PSB[0:1, 0:TC]),
          r=[rpsb], w=[rrows])
        ln_rows_tc(tc, "ln2")

    if tap == "l0_g":
        for g in range(NG):
            for tc in range(NTC):
                pp = (g * NTC + tc) % 2
                E("g", lambda q, g=g, tc=tc, pp=pp: q.dma_start(
                    out=ev1[:, pp * TC:(pp + 1) * TC],
                    in_=sp_g[g * 128:(g + 1) * 128, tc * TC:(tc + 1) * TC]),
                  r=[reg("hbm_g")], w=[rev1[pp]], dma=f"dbr{pp}")
                E("v", lambda v, pp=pp: v.tensor_copy(
                    out=ev2[:, pp * TC:(pp + 1) * TC],
                    in_=ev1[:, pp * TC:(pp + 1) * TC]),
                  r=[rev1[pp]], w=[rev2[pp]])
                E("q", lambda q, g=g, tc=tc, pp=pp: q.dma_start(
                    out=dbg[g * 128:(g + 1) * 128, tc * TC:(tc + 1) * TC],
                    in_=ev2[:, pp * TC:(pp + 1) * TC]),
                  r=[rev2[pp]], dma=f"dbw{pp}")

    if tap == "l0_xmid":
        for g in range(NG):
            for tc in range(NTC):
                pp = (g * NTC + tc) % 2
                E("g", lambda q, g=g, tc=tc, pp=pp: q.dma_start(
                    out=ev1[:, pp * TC:(pp + 1) * TC],
                    in_=sp_xm[g * 128:(g + 1) * 128, tc * TC:(tc + 1) * TC]),
                  r=[rhbm_xm], w=[rev1[pp]], dma=f"dbr{pp}")
                E("v", lambda v, pp=pp: v.tensor_copy(
                    out=ev2[:, pp * TC:(pp + 1) * TC],
                    in_=ev1[:, pp * TC:(pp + 1) * TC]),
                  r=[rev1[pp]], w=[rev2[pp]])
                E("q", lambda q, g=g, tc=tc, pp=pp: q.dma_start(
                    out=dbg[g * 128:(g + 1) * 128, tc * TC:(tc + 1) * TC],
                    in_=ev2[:, pp * TC:(pp + 1) * TC]),
                  r=[rev2[pp]], dma=f"dbw{pp}")

    E("v", lambda v: v.tensor_copy(out=wcolf[:, :], in_=cols["ln2w"][:, :]),
      r=[rcols], w=[rwbcol])
    E("v", lambda v: v.tensor_copy(out=bcolb[:, :], in_=cols["ln2b"][:, :]),
      r=[rcols], w=[rbcolb])

    for tc in range(NTC):
        ppx = tc % 2
        E("q", lambda q, tc=tc, ppx=ppx: q.dma_start(
            out=xmring[:, ppx * NG * TC:(ppx + 1) * NG * TC].rearrange(
                "p (k c) -> p k c", k=NG),
            in_=sp_xm[:, tc * TC:(tc + 1) * TC].rearrange(
                "(k p) c -> p k c", p=128)), r=[rhbm_xm], w=[rxmr[ppx]], dma=f"xm{ppx}")
        def xm(k, ppx=ppx):
            return xmring[:, ppx * NG * TC + k * TC:ppx * NG * TC + (k + 1) * TC]
        def xc(k):
            return xcring[:, k * TC:(k + 1) * TC]
        for k in range(NG):
            E("v", lambda v, a=ev1[:, 0:TC], b_=xm(k),
                   c=mub[:, tc * TC:(tc + 1) * TC]:
                v.scalar_tensor_tensor(out=a, in0=c, scalar=-1.0, in1=b_,
                                       op0=ALU.mult, op1=ALU.add),
              r=[rxmr[ppx], rmub], w=[rev1[0]])
            E("v", lambda v, tc=tc: v.tensor_tensor(
                out=ev1[:, 0:TC], in0=ev1[:, 0:TC],
                in1=s1b[:, tc * TC:(tc + 1) * TC], op=ALU.mult),
              r=[rs1b], w=[rev1[0]])
            E("v", lambda v, a=xc(k), k=k: v.tensor_scalar(
                out=a, in0=ev1[:, 0:TC], scalar1=wcolf[:, k:k + 1],
                scalar2=None, op0=ALU.mult), r=[rev1[0], rwbcol], w=[rxcr])
        for mf in range(NFG):
            si = mf % 2
            stream_w(h["wfkT"], mf, NG, si)
            if tc == 0:
                wb_column(si, NG, 8 + mf, ps=PSS, rp=rpss)
            bank = mf % 2
            for k in range(NG):
                E("t", lambda t, k=k, si=si, bank=bank, a=xc(k): t.matmul(
                    out=PS[bank][:, 0:TC], lhsT=slot(si, k), rhs=a,
                    start=(k == 0), stop=(k == NG - 1)),
                  r=[rslot[si], rxcr], w=[rps[bank]])
            pp = mf % 2
            E("v", lambda v, pp=pp, mf=mf, bank=bank: v.tensor_scalar(
                out=ev1[:, pp * TC:(pp + 1) * TC],
                in0=PS[bank][:, 0:TC],
                scalar1=wbcol[:, 8 + mf:9 + mf],
                scalar2=0.0, op0=ALU.add, op1=ALU.max),
              r=[rps[bank], rwbcol], w=[rev1[pp]])
            E("s", lambda s, mf=mf, pp=pp: s.activation(
                out=kkb[:, mf * TC:(mf + 1) * TC],
                in_=ev1[:, pp * TC:(pp + 1) * TC], func=AF.Square),
              r=[rev1[pp]], w=[rkk[mf]])
        for mo in range(NG):
            si = 2
            stream_w(h["wfrT"], mo, NG, si)
            if tc == 0:
                wb_column(si, NG, 40 + mo, ps=PSS, rp=rpss)
                E("v", lambda v, mo=mo: v.tensor_scalar(
                    out=wbcol[:, 40 + mo:41 + mo],
                    in0=wbcol[:, 40 + mo:41 + mo], scalar1=0.5,
                    scalar2=None, op0=ALU.mult), r=[rwbcol], w=[rwbcol])
            bank = 2
            for k in range(NG):
                E("t", lambda t, k=k, si=si, bank=bank, a=xc(k): t.matmul(
                    out=PS[bank][:, 0:TC], lhsT=slot(si, k), rhs=a,
                    start=(k == 0), stop=(k == NG - 1)),
                  r=[rslot[si], rxcr], w=[rps[bank]])
            pp2 = mo % 2
            E("s", lambda s, pp2=pp2, mo=mo, bank=bank: s.activation(
                out=ev2[:, pp2 * TC:(pp2 + 1) * TC],
                in_=PS[bank][:, 0:TC], func=AF.Tanh,
                scale=0.5, bias=wbcol[:, 40 + mo:41 + mo]),
              r=[rps[bank], rwbcol], w=[rev2[pp2]])
            stream_w5(h["wfvT"], mo, NFG)
            bank2 = 4 + (mo % 2)
            for k in range(NFG):
                E("t", lambda t, k=k, bank2=bank2: t.matmul(
                    out=PS[bank2][:, 0:TC], lhsT=slot(5, k),
                    rhs=kkb[:, k * TC:(k + 1) * TC],
                    start=(k == 0), stop=(k == NFG - 1)),
                  r=[rslot5h[2 * k // NFG], rkk[k]], w=[rps[bank2]])
            pp = mo % 2
            xfs = xf[:, pp * TC:(pp + 1) * TC]
            E("q", lambda q, mo=mo, tc=tc, xfs=xfs: q.dma_start(
                out=xfs, in_=h["xT"][mo * 128:(mo + 1) * 128,
                                     tc * TC:(tc + 1) * TC]),
              w=[rxf[pp]], dma=f"xf{pp}")
            E("q", lambda q, mo=mo, tc=tc, pp=pp: q.dma_start(
                out=g1ring[:, pp * TC:(pp + 1) * TC],
                in_=sp_g[mo * 128:(mo + 1) * 128, tc * TC:(tc + 1) * TC]),
              r=[rhbm_g], w=[rg1[pp]], dma=f"g1{pp}")
            E("v", lambda v, pp2=pp2, bank2=bank2: v.scalar_tensor_tensor(
                out=ev2[:, pp2 * TC:(pp2 + 1) * TC],
                in0=ev2[:, pp2 * TC:(pp2 + 1) * TC], scalar=1.0,
                in1=PS[bank2][:, 0:TC], op0=ALU.add, op1=ALU.mult),
              r=[rps[bank2]], w=[rev2[pp2]])
            E("v", lambda v, pp2=pp2, a=xm(mo): v.scalar_tensor_tensor(
                out=ev2[:, pp2 * TC:(pp2 + 1) * TC],
                in0=ev2[:, pp2 * TC:(pp2 + 1) * TC], scalar=0.5,
                in1=a, op0=ALU.mult, op1=ALU.add),
              r=[rxmr[ppx]], w=[rev2[pp2]])
            E("v", lambda v, pp2=pp2, pp=pp: v.tensor_tensor(
                out=ev2[:, pp2 * TC:(pp2 + 1) * TC],
                in0=ev2[:, pp2 * TC:(pp2 + 1) * TC],
                in1=g1ring[:, pp * TC:(pp + 1) * TC], op=ALU.mult),
              r=[rg1[pp]], w=[rev2[pp2]])
            E("v", lambda v, pp2=pp2, mo=mo, tc=tc, pp=pp:
                v.scalar_tensor_tensor(
                    out=x1[:, mo * T + tc * TC:mo * T + (tc + 1) * TC],
                    in0=ev2[:, pp2 * TC:(pp2 + 1) * TC], scalar=1.0 / 508.0,
                    in1=xf[:, pp * TC:(pp + 1) * TC], op0=ALU.mult,
                    op1=ALU.add), r=[rev2[pp2], rxf[pp]], w=[rx1t[mo][tc]])
            E("s", lambda s, mo=mo, tc=tc, pp=pp: s.activation(
                out=x2b[:, pp * TC:(pp + 1) * TC],
                in_=x1[:, mo * T + tc * TC:mo * T + (tc + 1) * TC],
                func=AF.Copy), r=[rx1t[mo][tc]], w=[rx2b[pp]])
            E("t", lambda t, mo=mo, pp=pp: t.matmul(
                out=PSB[0:1, 0:TC], lhsT=onescol[:, 0:1],
                rhs=x2b[:, pp * TC:(pp + 1) * TC],
                start=(mo == 0), stop=(mo == NG - 1)),
              r=[rx2b[pp], rones], w=[rpsb])
            E("s", lambda s, mo=mo, tc=tc, pp=pp: s.activation(
                out=x2b[:, pp * TC:(pp + 1) * TC],
                in_=x1[:, mo * T + tc * TC:mo * T + (tc + 1) * TC],
                func=AF.Square), r=[rx1t[mo][tc]], w=[rx2b[pp]])
            E("t", lambda t, mo=mo, pp=pp: t.matmul(
                out=PS[3][0:1, 0:TC], lhsT=onescol[:, 0:1],
                rhs=x2b[:, pp * TC:(pp + 1) * TC],
                start=(mo == 0), stop=(mo == NG - 1)),
              r=[rx2b[pp], rones], w=[rps[3]])
        E("v", lambda v: v.tensor_copy(out=S1(), in_=PSB[0:1, 0:TC]),
          r=[rpsb], w=[rrows])
        E("v", lambda v: v.tensor_copy(out=S2(), in_=PS[3][0:1, 0:TC]),
          r=[rps[3]], w=[rrows])
        ln_rows_tc(tc, "l1")
        for mo in range(NG):
            E("v", lambda v, mo=mo, tc=tc: v.tensor_tensor(
                out=xb1[:, mo * T + tc * TC:mo * T + (tc + 1) * TC],
                in0=x1[:, mo * T + tc * TC:mo * T + (tc + 1) * TC],
                in1=mub[:, tc * TC:(tc + 1) * TC], op=ALU.subtract),
              r=[rx1t[mo][tc], rmub], w=[rxb1[mo]])

    if tap == "l0_out":
        for g in range(NG):
            E("q", lambda q, g=g: q.dma_start(
                out=dbg[g * 128:(g + 1) * 128, :],
                in_=x1[:, g * T:(g + 1) * T]), r=list(rx1t[g]), dma="out")

    b.barrier()
    p1b.close()

    p2 = contextlib.ExitStack()
    sb2 = lambda name, shape, dtype=F32: p2.enter_context(
        nc.sbuf_tensor(name, shape, dtype))
    qpair = sb2("qpair", [128, T], BF16)
    kpair = sb2("kpair", [128, T], BF16)
    vpair = sb2("vpair", [128, NT2 * 130], BF16)
    attw = sb2("attw", [128, 4 * 512], BF16)
    p64 = sb2("p64", [65, 512])
    p64b = sb2("p64b", [65, 512], BF16)
    ones64 = sb2("ones64", [65, 64], BF16)
    stg = sb2("stg", [64, 2 * 2 * TC], BF16)
    rtile = sb2("rtile", [64, 512], BF16)
    masks = sb2("masksb", [128, (TC // 128) * TC], BF16)
    E("g", lambda g_: g_.dma_start(
        out=masks[:, :], in_=h["maskc"][:, :]), w=[rmask], dma="prem")

    rqp = reg("qpair"); rkp = reg("kpair"); rvp = reg("vpair")
    ratt = [reg(f"attw.{i}") for i in range(4)]
    rp64 = reg("p64"); rp64b = reg("p64b"); ro64 = reg("ones64")
    rstg = [reg("stg.0"), reg("stg.1")]
    rrt = reg("rtile")

    E("v", lambda v: v.memset(ones64[:, :], 1.0), w=[ro64])

    def prework():
            for g in range(NG):
                for tc in range(NTC):
                    pp = tc % 2
                    E("v", lambda v, g=g, tc=tc, pp=pp: v.tensor_tensor(
                        out=ev1[:, pp * TC:(pp + 1) * TC],
                        in0=x1[:, g * T + tc * TC:g * T + (tc + 1) * TC],
                        in1=s1b[:, tc * TC:(tc + 1) * TC], op=ALU.mult),
                      r=[rx1t[g][tc], rs1b], w=[rev1[pp]])
                    E("v", lambda v, tc=tc, pp=pp: v.tensor_tensor(
                        out=ev1[:, pp * TC:(pp + 1) * TC],
                        in0=ev1[:, pp * TC:(pp + 1) * TC],
                        in1=m0b[:, tc * TC:(tc + 1) * TC], op=ALU.subtract),
                      r=[rm0b], w=[rev1[pp]])
                    E("v", lambda v, g=g, pp=pp: v.tensor_scalar(
                        out=ev1[:, pp * TC:(pp + 1) * TC],
                        in0=ev1[:, pp * TC:(pp + 1) * TC],
                        scalar1=cols["gamma"][:, g:g + 1],
                        scalar2=cols["beta"][:, g:g + 1], op0=ALU.mult, op1=ALU.add),
                      r=[rcols], w=[rev1[pp]])
                    E("v", lambda v, g=g, tc=tc, pp=pp: v.tensor_reduce(
                        out=mxc[:, g * NTC + tc:g * NTC + tc + 1],
                        in_=ev1[:, pp * TC:(pp + 1) * TC], axis=AXX,
                        op=ALU.max, apply_absolute_value=True),
                      r=[rev1[pp]], w=[rmxc])
                    yield
            E("v", lambda v: v.tensor_reduce(
                out=mxc[:, 33:34], in_=mxc[:, 0:NG * NTC], axis=AXX,
                op=ALU.max, apply_absolute_value=False), r=[rmxc], w=[rmxc])
            E("g", lambda g_: g_.dma_start(out=p64[0:1, 0:128], in_=mxc[:, 33:34], allow_slow_non_contiguous=True),
              r=[rmxc], w=[rp64], dma="p64s")
            E("v", lambda v: v.tensor_reduce(
                out=p64[0:1, 128:129], in_=p64[0:1, 0:128], axis=AXX,
                op=ALU.max, apply_absolute_value=False), r=[rp64], w=[rp64])
            E("g", lambda g_: g_.dma_start(out=cc_in[0:1, 0:1],
                                           in_=p64[0:1, 128:129]),
              r=[rp64], w=[reg("hbm_cc")], dma="p64s")
            E("g", lambda g_: g_.collective_compute(
                "AllReduce", ALU.max, replica_groups=[list(range(cores))],
                ins=[cc_in[0:1, 0:1]], outs=[cc_out[0:1, 0:1]]),
              r=[reg("hbm_cc")], w=[rcc], dma="cc", inc=1)
            E("g", lambda g_: g_.dma_start(out=p64[0:1, 129:130],
                                           in_=cc_out[0:1, 0:1]),
              r=[rcc], w=[rp64], dma="p64s")
            E("v", lambda v: v.tensor_scalar(
                out=p64[0:1, 130:131], in0=p64[0:1, 129:130], scalar1=1.0 / 127.0,
                scalar2=None, op0=ALU.mult), r=[rp64], w=[rp64])
            E("v", lambda v: v.reciprocal(out=p64[0:1, 131:132],
                                          in_=p64[0:1, 130:131]), r=[rp64], w=[rp64])
            E("t", lambda t: t.matmul(
                out=PSB[:, 0:2], lhsT=onescolf[0:1, :], rhs=p64[0:1, 130:132],
                start=True, stop=True), r=[rp64, rones], w=[rpsb])
            E("v", lambda v: v.tensor_copy(out=mxc[:, 34:36], in_=PSB[:, 0:2]),
              r=[rpsb], w=[rmxc])

            for kq in range(NG):
                for c2 in range(NW):
                    pp = (kq * NW + c2) % 2
                    E("q", lambda q, kq=kq, c2=c2, pp=pp: q.dma_start(
                        out=xf[:, pp * TC:pp * TC + WQC],
                        in_=h["wT"][kq * 128:(kq + 1) * 128, c2 * WQC:(c2 + 1) * WQC]),
                      w=[rxf[pp]], dma=f"xf{pp}")
                    E("v", lambda v, kq=kq, c2=c2, pp=pp: v.tensor_reduce(
                        out=mxc[:, 40 + kq * NW + c2:41 + kq * NW + c2],
                        in_=xf[:, pp * TC:pp * TC + WQC], axis=AXX,
                        op=ALU.add, apply_absolute_value=True),
                      r=[rxf[pp]], w=[rmxc])
                    yield
            E("v", lambda v: v.tensor_reduce(
                out=mxc[:, 38:39], in_=mxc[:, 40:40 + NG * NW], axis=AXX,
                op=ALU.add), r=[rmxc], w=[rmxc])
            E("g", lambda g_: g_.dma_start(out=p64[0:1, 0:128], in_=mxc[:, 38:39], allow_slow_non_contiguous=True),
              r=[rmxc], w=[rp64], dma="p64s")
            E("v", lambda v: v.tensor_reduce(
                out=p64[0:1, 132:133], in_=p64[0:1, 0:128], axis=AXX, op=ALU.add),
              r=[rp64], w=[rp64])
            E("v", lambda v: v.tensor_scalar(
                out=p64[0:1, 133:134], in0=p64[0:1, 132:133],
                scalar1=0.7 / float(D * D), scalar2=None, op0=ALU.mult),
              r=[rp64], w=[rp64])
            E("t", lambda t: t.matmul(
                out=PSB[:, 0:1], lhsT=onescolf[0:1, :], rhs=p64[0:1, 133:134],
                start=True, stop=True), r=[rp64, rones], w=[rpsb])
            E("v", lambda v: v.tensor_copy(out=mxc[:, 36:37], in_=PSB[:, 0:1]),
              r=[rpsb], w=[rmxc])
            for kq in range(NG):
                for c2 in range(NW):
                    pp = (kq * NW + c2) % 2
                    E("q", lambda q, kq=kq, c2=c2, pp=pp: q.dma_start(
                        out=xf[:, pp * TC:pp * TC + WQC],
                        in_=h["wT"][kq * 128:(kq + 1) * 128, c2 * WQC:(c2 + 1) * WQC]),
                      w=[rxf[pp]], dma=f"xf{pp}")
                    E("s", lambda s, pp=pp: s.activation(
                        out=ev1[:, pp * TC:pp * TC + WQC],
                        in_=xf[:, pp * TC:pp * TC + WQC], func=AF.Abs),
                      r=[rxf[pp]], w=[rev1[pp]])
                    E("v", lambda v, pp=pp: v.tensor_scalar(
                        out=ev2[:, pp * TC:pp * TC + WQC],
                        in0=ev1[:, pp * TC:pp * TC + WQC], scalar1=mxc[:, 36:37],
                        scalar2=None, op0=ALU.is_gt), r=[rev1[pp], rmxc],
                      w=[rev2[pp]])
                    E("v", lambda v, kq=kq, c2=c2, pp=pp: v.tensor_reduce(
                        out=mxc[:, 40 + kq * NW + c2:41 + kq * NW + c2],
                        in_=ev2[:, pp * TC:pp * TC + WQC], axis=AXX, op=ALU.add),
                      r=[rev2[pp]], w=[rmxc])
                    E("v", lambda v, pp=pp: v.tensor_tensor(
                        out=ev1[:, pp * TC:pp * TC + WQC],
                        in0=ev1[:, pp * TC:pp * TC + WQC],
                        in1=ev2[:, pp * TC:pp * TC + WQC], op=ALU.mult),
                      r=[rev2[pp]], w=[rev1[pp]])
                    E("v", lambda v, kq=kq, c2=c2, pp=pp: v.tensor_reduce(
                        out=mxc[:, MXB + kq * NW + c2:MXB + 1 + kq * NW + c2],
                        in_=ev1[:, pp * TC:pp * TC + WQC], axis=AXX, op=ALU.add),
                      r=[rev1[pp]], w=[rmxc])
                    yield
            E("v", lambda v: v.tensor_reduce(
                out=mxc[:, 38:39], in_=mxc[:, 40:40 + NG * NW], axis=AXX,
                op=ALU.add), r=[rmxc], w=[rmxc])
            E("v", lambda v: v.tensor_reduce(
                out=mxc[:, 39:40], in_=mxc[:, MXB:MXB + NG * NW], axis=AXX,
                op=ALU.add), r=[rmxc], w=[rmxc])
            E("g", lambda g_: g_.dma_start(out=p64[0:1, 0:128], in_=mxc[:, 38:39], allow_slow_non_contiguous=True),
              r=[rmxc], w=[rp64], dma="p64s")
            E("g", lambda g_: g_.dma_start(out=p64[0:1, 128:256], in_=mxc[:, 39:40], allow_slow_non_contiguous=True),
              r=[rmxc], w=[rp64], dma="p64s")
            E("v", lambda v: v.tensor_reduce(
                out=p64[0:1, 256:257], in_=p64[0:1, 0:128], axis=AXX, op=ALU.add),
              r=[rp64], w=[rp64])
            E("v", lambda v: v.tensor_reduce(
                out=p64[0:1, 257:258], in_=p64[0:1, 128:256], axis=AXX, op=ALU.add),
              r=[rp64], w=[rp64])
            E("v", lambda v: v.tensor_scalar(
                out=p64[0:1, 256:257], in0=p64[0:1, 256:257], scalar1=1.0,
                scalar2=None, op0=ALU.max), r=[rp64], w=[rp64])
            E("v", lambda v: v.reciprocal(out=p64[0:1, 256:257],
                                          in_=p64[0:1, 256:257]), r=[rp64], w=[rp64])
            E("v", lambda v: v.tensor_tensor(
                out=p64[0:1, 257:258], in0=p64[0:1, 257:258], in1=p64[0:1, 256:257],
                op=ALU.mult), r=[rp64], w=[rp64])
            E("t", lambda t: t.matmul(
                out=PSB[:, 0:1], lhsT=onescolf[0:1, :], rhs=p64[0:1, 257:258],
                start=True, stop=True), r=[rp64, rones], w=[rpsb])
            E("v", lambda v: v.tensor_copy(out=mxc[:, 37:38], in_=PSB[:, 0:1]),
              r=[rpsb], w=[rmxc])
            yield
    _pw = prework()
    _pw_done = [False]
    def pw_pull(n):
        for _ in range(n):
            if _pw_done[0]:
                return
            try:
                next(_pw)
            except StopIteration:
                _pw_done[0] = True
                return

    for pair in range(NPAIR):
        pw_pull(12)
        stream_w(h["q_T"], pair, NG, 0)
        stream_w(h["k_T"], pair, NG, 1)
        for tc in range(NTC):
            for si, dst, rdst in ((0, qpair, rqp), (1, kpair, rkp)):
                bank = si
                for k in range(NG):
                    E("t", lambda t, k=k, si=si, tc=tc, bank=bank: t.matmul(
                        out=PS[bank][:, 0:TC], lhsT=slot(si, k),
                        rhs=xb1[:, k * T + tc * TC:k * T + (tc + 1) * TC],
                        start=(k == 0), stop=(k == NG - 1)),
                      r=[rslot[si], rxb1[k]], w=[rps[bank]])
                E("v", lambda v, tc=tc, bank=bank, dst=dst, rdst=rdst:
                    v.tensor_tensor(
                        out=dst[:, tc * TC:(tc + 1) * TC],
                        in0=PS[bank][:, 0:TC],
                        in1=s0b[:, tc * TC:(tc + 1) * TC], op=ALU.mult),
                  r=[rps[bank], rs0b], w=[rdst])
        stream_w(h["v_T"], pair, NG, 2)
        for t2 in range(NT2):
            for k in range(NG):
                E("t", lambda t, k=k, t2=t2: t.matmul(
                    out=PS[4][:, 0:128],
                    lhsT=xb1[:, k * T + t2 * 128:k * T + (t2 + 1) * 128],
                    rhs=slot(2, k), start=(k == 0), stop=(k == NG - 1)),
                  r=[rslot[2], rxb1[k]], w=[rps[4]])
            for half in range(2):
                E("v", lambda v, t2=t2, half=half: v.tensor_scalar(
                    out=vpair[:, t2 * 130 + half * 65:
                              t2 * 130 + half * 65 + 64],
                    in0=PS[4][:, half * 64:(half + 1) * 64],
                    scalar1=scol[:, t2:t2 + 1], scalar2=None, op0=ALU.mult),
                  r=[rps[4], rscol], w=[rvp])
            E("v", lambda v, t2=t2: v.memset(
                vpair[:, t2 * 130 + 64:t2 * 130 + 65], 1.0), w=[rvp])
            E("v", lambda v, t2=t2: v.memset(
                vpair[:, t2 * 130 + 129:t2 * 130 + 130], 1.0), w=[rvp])
        for tc in range(NTC):
            ppg = tc % 2
            nk2 = (tc + 1) * (TC // 128)
            for k2 in range(nk2):
                for half in range(2):
                    hb = half * 64
                    sbank = half
                    E("t", lambda t, k2=k2, hb=hb, tc=tc, sbank=sbank:
                        t.matmul(
                            out=PS[sbank][:, 0:TC],
                            lhsT=kpair[hb:hb + 64, k2 * 128:(k2 + 1) * 128],
                            rhs=qpair[hb:hb + 64, tc * TC:(tc + 1) * TC],
                            start=True, stop=True),
                      r=[rkp, rqp], w=[rps[sbank]])
                    dj = k2 * 128 - tc * TC
                    if dj >= 0:
                        ji = dj // 128
                        E("v", lambda v, ji=ji, sbank=sbank: v.tensor_tensor(
                            out=PS[sbank][:, 0:TC], in0=PS[sbank][:, 0:TC],
                            in1=masks[:, ji * TC:(ji + 1) * TC], op=ALU.add),
                          r=[rmask], w=[rps[sbank]])
                    slot_a = (k2 % 2) * 2 + half
                    E("s", lambda s, sbank=sbank, slot_a=slot_a: s.activation(
                        out=attw[:, slot_a * 512:slot_a * 512 + TC],
                        in_=PS[sbank][:, 0:TC], func=AF.Exp, scale=0.125),
                      r=[rps[sbank]], w=[ratt[slot_a]])
                    E("t", lambda t, k2=k2, half=half, slot_a=slot_a,
                           nk2=nk2: t.matmul(
                            out=PS[4 + half][0:65, 0:TC],
                            lhsT=vpair[:, k2 * 130 + half * 65:
                                       k2 * 130 + half * 65 + 65],
                            rhs=attw[:, slot_a * 512:slot_a * 512 + TC],
                            start=(k2 == 0), stop=(k2 == nk2 - 1)),
                      r=[rvp, ratt[slot_a]], w=[rps[4 + half]])
            for half in range(2):
                E("s", lambda s, half=half: s.activation(
                    out=p64[64:65, half * 256 if False else 0:TC]
                    if False else p64[64:65, 0:TC],
                    in_=PS[4 + half][64:65, 0:TC], func=AF.Ln),
                  r=[rps[4 + half]], w=[rp64])
                E("s", lambda s: s.activation(
                    out=p64b[64:65, 0:TC], in_=p64[64:65, 0:TC],
                    func=AF.Exp, scale=-1.0), r=[rp64], w=[rp64b])
                E("t", lambda t, half=half: t.matmul(
                    out=PS[2 + half][0:64, 0:TC], lhsT=ones64[64:65, 0:64],
                    rhs=p64b[64:65, 0:TC], start=True, stop=True),
                  r=[rp64b, ro64], w=[rps[2 + half]])
                E("v", lambda v, half=half: v.tensor_copy(
                    out=rtile[0:64, 0:TC], in_=PS[2 + half][0:64, 0:TC]),
                  r=[rps[2 + half]], w=[rrt])
                E("v", lambda v, half=half, ppg=ppg: v.tensor_tensor(
                    out=stg[0:64, ppg * 2 * TC + half * TC:
                            ppg * 2 * TC + half * TC + TC],
                    in0=PS[4 + half][0:64, 0:TC], in1=rtile[0:64, 0:TC],
                    op=ALU.mult), r=[rps[4 + half], rrt], w=[rstg[ppg]])
            E("q", lambda q, pair=pair, tc=tc, ppg=ppg: q.dma_start(
                out=sp_rw[pair * 128:(pair + 1) * 128,
                          tc * TC:(tc + 1) * TC].rearrange(
                    "(hh p) c -> p hh c", hh=2),
                in_=stg[0:64, ppg * 2 * TC:ppg * 2 * TC + 2 * TC].rearrange(
                    "p (hh q) -> p hh q", hh=2)),
              r=[rstg[ppg]], w=[rhbm_rw], dma="hrw")

    pw_pull(10 ** 9)

    if tap == "l1_att":
        for g in range(NG):
            for tc in range(NTC):
                pp = (g * NTC + tc) % 2
                E("g", lambda q, g=g, tc=tc, pp=pp: q.dma_start(
                    out=ev1[:, pp * TC:(pp + 1) * TC],
                    in_=sp_rw[g * 128:(g + 1) * 128, tc * TC:(tc + 1) * TC]),
                  r=[rhbm_rw], w=[rev1[pp]], dma=f"dbr{pp}")
                E("v", lambda v, pp=pp: v.tensor_copy(
                    out=ev2[:, pp * TC:(pp + 1) * TC],
                    in_=ev1[:, pp * TC:(pp + 1) * TC]),
                  r=[rev1[pp]], w=[rev2[pp]])
                E("q", lambda q, g=g, tc=tc, pp=pp: q.dma_start(
                    out=dbg[g * 128:(g + 1) * 128, tc * TC:(tc + 1) * TC],
                    in_=ev2[:, pp * TC:(pp + 1) * TC]),
                  r=[rev2[pp]], dma=f"dbw{pp}")

    b.barrier()
    p2.close()

    p3 = contextlib.ExitStack()
    sb3 = lambda name, shape, dtype=F32: p3.enter_context(
        nc.sbuf_tensor(name, shape, dtype))
    wqb = sb3("wqb", [128, NG * D], BF16)
    xqr = sb3("xqr", [128, 2 * NG * TC], BF16)
    attr = sb3("attr", [128, 2 * NG * TC], BF16)
    qint = sb3("qint", [128, TC], I32)
    xev = xf

    rwq = reg("wqb")
    rxq = [reg("xqr.0"), reg("xqr.1")]
    rqi = reg("qint")
    rxe = rxf
    ratr = [reg("attr.0"), reg("attr.1")]

    for kq in range(NG):
        for c2 in range(NW):
            pp = (kq * NW + c2) % 2
            E("q", lambda q, kq=kq, c2=c2, pp=pp: q.dma_start(
                out=xf[:, pp * TC:pp * TC + WQC],
                in_=h["wT"][kq * 128:(kq + 1) * 128, c2 * WQC:(c2 + 1) * WQC]),
              w=[rxf[pp]], dma=f"xf{pp}")
            E("s", lambda s, pp=pp: s.activation(
                out=ev1[:, pp * TC:pp * TC + WQC],
                in_=xf[:, pp * TC:pp * TC + WQC], func=AF.Abs),
              r=[rxf[pp]], w=[rev1[pp]])
            E("v", lambda v, pp=pp: v.tensor_scalar(
                out=ev1[:, pp * TC:pp * TC + WQC],
                in0=ev1[:, pp * TC:pp * TC + WQC], scalar1=mxc[:, 36:37],
                scalar2=None, op0=ALU.is_gt), r=[rmxc], w=[rev1[pp]])
            E("s", lambda s, pp=pp: s.sign(
                out=ev2[:, pp * TC:pp * TC + WQC],
                in_=xf[:, pp * TC:pp * TC + WQC]), r=[rxf[pp]], w=[rev2[pp]])
            E("v", lambda v, pp=pp: v.tensor_tensor(
                out=ev1[:, pp * TC:pp * TC + WQC],
                in0=ev1[:, pp * TC:pp * TC + WQC],
                in1=ev2[:, pp * TC:pp * TC + WQC], op=ALU.mult),
              r=[rev2[pp]], w=[rev1[pp]])
            E("v", lambda v, kq=kq, c2=c2, pp=pp: v.tensor_scalar(
                out=wqb[:, kq * D + c2 * WQC:kq * D + (c2 + 1) * WQC],
                in0=ev1[:, pp * TC:pp * TC + WQC], scalar1=mxc[:, 37:38],
                scalar2=None, op0=ALU.mult), r=[rev1[pp], rmxc], w=[rwq])

    def xq_chunk(tc, k):
        ppx = tc % 2
        pp = k % 2
        E("v", lambda v, k=k, tc=tc, pp=pp: v.tensor_tensor(
            out=xev[:, pp * TC:(pp + 1) * TC],
            in0=x1[:, k * T + tc * TC:k * T + (tc + 1) * TC],
            in1=s1b[:, tc * TC:(tc + 1) * TC], op=ALU.mult),
          r=[rx1t[k][tc], rs1b], w=[rxe[pp]])
        E("v", lambda v, tc=tc, pp=pp: v.tensor_tensor(
            out=xev[:, pp * TC:(pp + 1) * TC],
            in0=xev[:, pp * TC:(pp + 1) * TC],
            in1=m0b[:, tc * TC:(tc + 1) * TC], op=ALU.subtract),
          r=[rm0b], w=[rxe[pp]])
        E("v", lambda v, k=k, pp=pp: v.tensor_scalar(
            out=xev[:, pp * TC:(pp + 1) * TC],
            in0=xev[:, pp * TC:(pp + 1) * TC],
            scalar1=cols["gamma"][:, k:k + 1],
            scalar2=cols["beta"][:, k:k + 1], op0=ALU.mult, op1=ALU.add),
          r=[rcols], w=[rxe[pp]])
        E("v", lambda v, pp=pp: v.tensor_scalar(
            out=xev[:, pp * TC:(pp + 1) * TC],
            in0=xev[:, pp * TC:(pp + 1) * TC], scalar1=mxc[:, 35:36],
            scalar2=None, op0=ALU.mult), r=[rmxc], w=[rxe[pp]])
        E("v", lambda v, pp=pp: v.tensor_scalar(
            out=xev[:, pp * TC:(pp + 1) * TC],
            in0=xev[:, pp * TC:(pp + 1) * TC], scalar1=-127.0,
            scalar2=127.0, op0=ALU.max, op1=ALU.min), w=[rxe[pp]])
        E("s", lambda s, pp=pp: s.sign(
            out=qint[:, 0:TC].bitcast(F32) if False else ev2[:, 0:TC],
            in_=xev[:, pp * TC:(pp + 1) * TC]),
          r=[rxe[pp]], w=[rev2[0]])
        E("v", lambda v, pp=pp: v.scalar_tensor_tensor(
            out=xev[:, pp * TC:(pp + 1) * TC],
            in0=ev2[:, 0:TC], scalar=0.5,
            in1=xev[:, pp * TC:(pp + 1) * TC], op0=ALU.mult,
            op1=ALU.add), r=[rev2[0]], w=[rxe[pp]])
        E("v", lambda v, pp=pp: v.tensor_copy(
            out=qint[:, 0:TC], in_=xev[:, pp * TC:(pp + 1) * TC]),
          r=[rxe[pp]], w=[rqi])
        E("v", lambda v, k=k, tc=tc, ppx=ppx: v.tensor_scalar(
            out=xqr[:, ppx * NG * TC + k * TC:
                    ppx * NG * TC + (k + 1) * TC],
            in0=qint[:, 0:TC], scalar1=mxc[:, 34:35],
            scalar2=None, op0=ALU.mult), r=[rqi, rmxc],
          w=[rxq[ppx]])

    def attr_dma(tc):
        ppx = tc % 2
        E("q", lambda q, tc=tc, ppx=ppx: q.dma_start(
            out=attr[:, ppx * NG * TC:(ppx + 1) * NG * TC].rearrange(
                "p (k c) -> p k c", k=NG),
            in_=sp_rw[:, tc * TC:(tc + 1) * TC].rearrange(
                "(k p) c -> p k c", p=128)),
          r=[rhbm_rw], w=[ratr[ppx]], dma=f"at{ppx}")

    attr_dma(0)
    for k in range(NG):
        xq_chunk(0, k)
    for tc in range(NTC):
        ppx = tc % 2
        if tc + 1 < NTC:
            attr_dma(tc + 1)
        for mo in range(NG):
            if tc + 1 < NTC:
                xq_chunk(tc + 1, mo)
            si = mo % 2
            stream_w(h["o_T"], mo, NG, si)
            bank = 2 + (mo % 2)
            for k in range(NG):
                E("t", lambda t, k=k, si=si, bank=bank, ppx=ppx: t.matmul(
                    out=PS[bank][:, 0:TC], lhsT=slot(si, k),
                    rhs=attr[:, ppx * NG * TC + k * TC:
                             ppx * NG * TC + (k + 1) * TC],
                    start=(k == 0), stop=(k == NG - 1)),
                  r=[rslot[si], ratr[ppx]], w=[rps[bank]])
            bank2 = 4 + (mo % 2)
            for k in range(NG):
                E("t", lambda t, k=k, mo=mo, bank2=bank2, ppx=ppx: t.matmul(
                    out=PS[bank2][:, 0:TC],
                    lhsT=wqb[:, k * D + mo * 128:k * D + (mo + 1) * 128],
                    rhs=xqr[:, ppx * NG * TC + k * TC:
                            ppx * NG * TC + (k + 1) * TC],
                    start=(k == 0), stop=(k == NG - 1)),
                  r=[rwq, rxq[ppx]], w=[rps[bank2]])
            pp = mo % 2
            E("s", lambda s, bank2=bank2, pp=pp: s.activation(
                out=x2b[:, pp * TC:(pp + 1) * TC], in_=PS[bank2][:, 0:TC],
                func=AF.Tanh, scale=0.5), r=[rps[bank2]], w=[rx2b[pp]])
            E("v", lambda v, bank2=bank2, pp=pp: v.scalar_tensor_tensor(
                out=ev1[:, pp * TC:(pp + 1) * TC],
                in0=x2b[:, pp * TC:(pp + 1) * TC], scalar=1.0,
                in1=PS[bank2][:, 0:TC], op0=ALU.add, op1=ALU.mult),
              r=[rx2b[pp], rps[bank2]], w=[rev1[pp]])
            E("v", lambda v, bank=bank, pp=pp: v.tensor_tensor(
                out=ev1[:, pp * TC:(pp + 1) * TC],
                in0=ev1[:, pp * TC:(pp + 1) * TC], in1=PS[bank][:, 0:TC],
                op=ALU.mult), r=[rps[bank]], w=[rev1[pp]])
            E("v", lambda v, mo=mo, tc=tc, pp=pp: v.scalar_tensor_tensor(
                out=x1[:, mo * T + tc * TC:mo * T + (tc + 1) * TC],
                in0=ev1[:, pp * TC:(pp + 1) * TC], scalar=0.5,
                in1=x1[:, mo * T + tc * TC:mo * T + (tc + 1) * TC],
                op0=ALU.mult, op1=ALU.add), r=[rev1[pp]], w=[rx1t[mo][tc]])
            E("q", lambda q, mo=mo, tc=tc: q.dma_start(
                out=yT[mo * 128:(mo + 1) * 128, tc * TC:(tc + 1) * TC],
                in_=x1[:, mo * T + tc * TC:mo * T + (tc + 1) * TC]),
              r=[rx1t[mo][tc]], dma="out")

    b.barrier(engs=("q",))
    p3.close()
    p1.close()

    blk_ctx.__exit__(None, None, None)
    es.close()
    return nc, b



def make_in_maps(inputs, T=2048):
    tr = lambda a: np.ascontiguousarray(np.asarray(a).T)
    _cv = lambda a: np.ascontiguousarray(np.asarray(a).reshape(8, 128).T)
    def tl(a):
        a = np.asarray(a)
        no, ni = a.shape
        nm, nk = no // 128, ni // 128
        return np.ascontiguousarray(
            a.T.reshape(nk, 128, nm, 128).transpose(2, 1, 0, 3)
            .reshape(nm * 128, nk * 128))
    TC = min(512, T)
    maskc = np.zeros((TC // 128, 128, TC), np.float32)
    for j in range(TC // 128):
        for p in range(128):
            maskc[j, p, :min(p + j * 128, TC)] = NEG
    maskc = np.ascontiguousarray(maskc.transpose(1, 0, 2).reshape(128, -1))
    common = {
        "wrT": tl(inputs["L0_attn_Wr"]), "wkT": tl(inputs["L0_attn_Wk"]),
        "wvT": tl(inputs["L0_attn_Wv"]), "woT": tl(inputs["L0_attn_Wo"]),
        "wfkT": tl(inputs["L0_attn_Wfk"]), "wfvT": tl(inputs["L0_attn_Wfv"]),
        "wfrT": tl(inputs["L0_attn_Wfr"]),
        "wgT": tl(inputs["L0_ffn_w_gate"]), "wuT": tl(inputs["L0_ffn_w_up"]),
        "wdT": tl(inputs["L0_ffn_w_down"]),
        "q_T": tl(inputs["L1_attn_q_proj_weight"]),
        "k_T": tl(inputs["L1_attn_k_proj_weight"]),
        "v_T": tl(inputs["L1_attn_v_proj_weight"]),
        "o_T": tl(inputs["L1_attn_o_proj_weight"]),
        "wT": tr(inputs["L1_ffn_weight"]),
        "ln1w": _cv(inputs["L0_attn_ln1_w"]),
        "ln1b": _cv(inputs["L0_attn_ln1_b"]),
        "ln2w": _cv(inputs["L0_attn_ln2_w"]),
        "ln2b": _cv(inputs["L0_attn_ln2_b"]),
        "decay": _cv(inputs["L0_attn_decay"]),
        "u": _cv(inputs["L0_attn_u"]),
        "gamma": _cv(inputs["L1_ffn_gamma"]),
        "beta": _cv(inputs["L1_ffn_beta"]),
        "maskc": maskc,
    }
    x = np.asarray(inputs["L0_attn_x"])
    in_maps = []
    for ci in range(x.shape[0]):
        m = dict(common)
        m["xT"] = tr(x[ci])
        in_maps.append(m)
    return in_maps


def kernel(**inputs):
    T = 2048
    nc, b = build(T=T, tap=DEBUG_TAP, cores=8)
    in_maps = make_in_maps(inputs, T=T)
    res = run_bass_kernel_spmd(nc, in_maps, core_ids=list(range(8)))
    out = np.stack([np.asarray(r["yT"]).T for r in res.results], axis=0)
    return out.astype(np.float32)


# version 17
# speedup vs baseline: 2.1447x; 1.4254x over previous
import numpy as np
import contextlib
import concourse.bass as bass
import concourse.mybir as mybir
from concourse.bass_utils import run_bass_kernel_spmd

F32 = mybir.dt.float32
BF16 = mybir.dt.bfloat16
I32 = mybir.dt.int32
AF = mybir.ActivationFunctionType
ALU = mybir.AluOpType
AXX = mybir.AxisListType.X

D = 1024
NG = D // 128
HID = 3072
NHG = HID // 128
FFD = 4096
NFG = FFD // 128
NPAIR = 8
EPS = 1e-5
NEG = -1.0e9

DEBUG_TAP = None
SAFE_SYNC = False


class Reg:
    __slots__ = ("name", "w", "readers")
    def __init__(self, name):
        self.name = name
        self.w = None
        self.readers = []


class B:
    ENG_ATTR = {"v": "vector", "s": "scalar", "t": "tensor",
                "g": "gpsimd", "q": "sync"}
    ENG_SEM = {"v": "sv", "s": "ss", "t": "st", "g": "sg", "q": None}

    def __init__(self, nc, block, sems, mksem):
        self.nc = nc
        self.block = block
        self.sems = sems
        self.mksem = mksem
        self.cnt = {k: 0 for k in sems}
        self.wm = {}
        self.maxwait = {}
        self.n_inst = 0
        self.buf = {e: [] for e in self.ENG_ATTR}

    def flush(self):
        for e in self.ENG_ATTR:
            bodies = self.buf[e]
            if not bodies:
                continue
            self.buf[e] = []
            def section(engobj, bodies=bodies):
                for bd in bodies:
                    bd(engobj)
            getattr(self.block, self.ENG_ATTR[e])(section)

    def dma_sem(self, key):
        key = "D_" + key
        if key not in self.sems:
            self.sems[key] = self.mksem(key)
            self.cnt[key] = 0
        return key

    def emit(self, eng, fn, r=(), w=(), dma=None, inc=None):
        needs = {}
        for reg_ in r:
            if reg_.w is not None:
                s, c = reg_.w
                if needs.get(s, 0) < c:
                    needs[s] = c
        for reg_ in w:
            if reg_.w is not None:
                s, c = reg_.w
                if needs.get(s, 0) < c:
                    needs[s] = c
            for (s, c) in reg_.readers:
                if needs.get(s, 0) < c:
                    needs[s] = c
        if dma is not None:
            sem_name = self.dma_sem(dma)
            inc_v = 16 if inc is None else inc
            mw = self.maxwait.get(sem_name, 0)
            if mw > 0 and needs.get(sem_name, 0) < mw:
                needs[sem_name] = mw
        else:
            sem_name = self.ENG_SEM[eng]
            inc_v = 1
        own = self.ENG_SEM[eng]
        if not SAFE_SYNC and eng in ("t",) and own in needs:
            del needs[own]
        for s_, v_ in needs.items():
            if self.maxwait.get(s_, 0) < v_:
                self.maxwait[s_] = v_
        def body(engobj):
            for s in sorted(needs):
                v = needs[s]
                key = (eng, s)
                if self.wm.get(key, -1) < v:
                    engobj.wait_ge(self.sems[s], v)
                    self.wm[key] = v
            instr = fn(engobj)
            instr.then_inc(self.sems[sem_name], inc_v)
        self.buf[eng].append(body)
        self.n_inst += 1
        self.cnt[sem_name] += inc_v
        tag = (sem_name, self.cnt[sem_name])
        for reg_ in w:
            reg_.w = tag
            reg_.readers = []
        for reg_ in r:
            reg_.readers.append(tag)
        return tag

    def barrier(self, engs=("v", "s", "t", "g", "q")):
        snapshot = dict(self.cnt)
        for e in engs:
            def body(engobj, e=e):
                for s, v in sorted(snapshot.items()):
                    if v > 0 and self.wm.get((e, s), -1) < v:
                        engobj.wait_ge(self.sems[s], v)
                        self.wm[(e, s)] = v
            self.buf[e].append(body)
        for s, v in snapshot.items():
            if self.maxwait.get(s, 0) < v:
                self.maxwait[s] = v
        self.flush()


def build(T=2048, tap=None, cores=8):
    TC = min(512, T)
    NTC = T // TC
    NT2 = T // 128
    TCB = min(512, T)
    NTCB = T // TCB
    WQC = TC
    NW = D // WQC
    MXB = 40 + NG * NW

    nc = bass.Bass("TRN2", target_bir_lowering=False, debug=False,
                   detect_race_conditions=SAFE_SYNC)

    h = {}
    def din(name, shape, dtype=F32):
        h[name] = nc.dram_tensor(name, shape, dtype, kind="ExternalInput").ap()
        return h[name]

    din("xT", [D, T])
    din("wrT", [D, D]); din("wkT", [D, D]); din("wvT", [D, D]); din("woT", [D, D])
    din("wfkT", [FFD, D]); din("wfvT", [D, FFD]); din("wfrT", [D, D])
    din("wgT", [HID, D], I32); din("wuT", [HID, D], I32); din("wdT", [D, HID], I32)
    din("q_T", [D, D]); din("k_T", [D, D]); din("v_T", [D, D]); din("o_T", [D, D])
    din("wT", [D, D])
    for nm in ("ln1w", "ln1b", "ln2w", "ln2b", "decay", "u", "gamma", "beta"):
        din(nm, [128, NG])
    din("maskc", [128, (TC // 128) * TC])

    yT = nc.dram_tensor("yT", [D, T], F32, kind="ExternalOutput").ap()
    dbg = None
    if tap is not None:
        dbg = nc.dram_tensor("dbg", [D, T], F32, kind="ExternalOutput").ap()
    sp_g = nc.dram_tensor("sp_g", [D, T], BF16, kind="Internal").ap()
    sp_rw = nc.dram_tensor("sp_rw", [D, T], BF16, kind="Internal").ap()
    sp_xm = nc.dram_tensor("sp_xm", [D, T], BF16, kind="Internal").ap()
    sp_xn = nc.dram_tensor("sp_xn", [D, T], BF16, kind="Internal").ap()
    cc_in = nc.dram_tensor("cc_in", [1, 8], F32, kind="Internal").ap()
    cc_out = nc.dram_tensor("cc_out", [1, 8], F32, kind="Internal").ap()

    es = contextlib.ExitStack()
    sb = lambda name, shape, dtype=F32: es.enter_context(
        nc.sbuf_tensor(name, shape, dtype))
    mkpsum = lambda name, shape, dtype=F32: es.enter_context(
        nc.psum_tensor(name, shape, dtype))
    mksem = lambda name: es.enter_context(nc.semaphore(name))

    SLOT_OFF = [0, 1024, 2048, 3072, 4096, 7168]
    SLOT_W = [1024, 1024, 1024, 1024, 3072, 4096]
    wslot = sb("wslot", [128, 11264], BF16)
    s0b = sb("s0b", [128, T], BF16)
    s1b = sb("s1b", [128, T], BF16)
    m0b = sb("m0b", [128, T], BF16)
    rows = sb("rows", [1, 2 * TC])
    rowx = sb("rowx", [1, TC])
    mub = sb("mub", [128, T], BF16)
    wbcol = sb("wbcol", [128, 48])
    wcolf = sb("wcolf", [128, NG])
    bcolb = sb("bcolb", [128, NG], BF16)
    onescol = sb("onescol", [128, 2], BF16)
    onescolf = sb("onescolf", [1, 128])
    xf = sb("xf", [128, 2 * TC])
    x2b = sb("x2b", [128, 2 * TC], BF16)
    ev1 = sb("ev1", [128, 2 * TC])
    ev2 = sb("ev2", [128, 2 * TC])
    cols = {}
    for nm in ("ln1w", "ln1b", "ln2w", "ln2b", "decay", "u", "gamma", "beta"):
        cols[nm] = sb("c_" + nm, [128, NG])
    ew = sb("c_ew", [128, NG])
    eu = sb("c_eu", [128, NG])
    mxc = sb("mxc", [128, 40 + 2 * NG * NW])
    scol = sb("scol", [128, NT2])

    PS = [mkpsum(f"ps{i}", [128, 512]) for i in range(6)]
    PSS = mkpsum("pss", [128, 512])
    PSB = mkpsum("psb", [128, 512])

    sems = {}
    for s_ in ("sv", "ss", "st", "sg"):
        sems[s_] = mksem(s_)

    blk_ctx = nc.Block()
    block = blk_ctx.__enter__()
    b = B(nc, block, sems, mksem)
    E = b.emit
    R = {}
    def reg(name):
        if name not in R:
            R[name] = Reg(name)
        return R[name]

    rps = [reg(f"ps.{i}") for i in range(6)]
    rpss = reg("pss"); rpsb = reg("psb")
    rrows = reg("rows"); rrowx = reg("rowx")
    rwbcol = reg("wbcol"); rbcolb = reg("bcolb")
    rmub = reg("mub")
    rs0b = reg("s0b"); rs1b = reg("s1b"); rm0b = reg("m0b")
    rcols = reg("cols")
    rxf = [reg("xf.0"), reg("xf.1")]
    rx2b = [reg("x2b.0"), reg("x2b.1")]
    rev1 = [reg("ev1.0"), reg("ev1.1")]
    rev2 = [reg("ev2.0"), reg("ev2.1")]
    rslot = [reg(f"wslot.{i}") for i in range(6)]
    rones = reg("ones")
    rmask = reg("masks")
    rmxc = reg("mxc"); rscol = reg("scol")
    rcc = reg("ccdone")
    rhbm_g = reg("hbm_g"); rhbm_rw = reg("hbm_rw"); rhbm_xm = reg("hbm_xm")

    def slot(i, k=None, kw=128):
        o = SLOT_OFF[i]
        if k is None:
            return wslot[:, o:o + SLOT_W[i]]
        return wslot[:, o + k * kw:o + (k + 1) * kw]

    for nm in ("ln1w", "ln1b", "ln2w", "ln2b", "decay", "u", "gamma", "beta"):
        E("q", lambda q, nm=nm: q.dma_start(
            out=cols[nm][:, :], in_=h[nm][:, :]), w=[rcols], dma="pre")
    E("v", lambda v: v.memset(onescol[:, :], 1.0), w=[rones])
    E("v", lambda v: v.memset(onescolf[:, :], 1.0), w=[rones])
    E("s", lambda s: s.activation(out=ew[:, :], in_=cols["decay"][:, :],
                                  func=AF.Exp), r=[rcols], w=[rcols])
    E("s", lambda s: s.activation(out=ew[:, :], in_=ew[:, :],
                                  func=AF.Exp, scale=-1.0), r=[rcols], w=[rcols])
    E("s", lambda s: s.activation(out=eu[:, :], in_=cols["u"][:, :],
                                  func=AF.Exp), r=[rcols], w=[rcols])

    def S1(): return rows[0:1, 0:TC]
    def S2(): return rows[0:1, TC:2 * TC]
    def RX(): return rowx[0:1, 0:TC]

    def bcast_tc(src, sreg, dst, rdst, tc):
        E("t", lambda t: t.matmul(
            out=PSB[:, 0:TC], lhsT=onescolf[0:1, :], rhs=src,
            start=True, stop=True), r=[sreg, rones], w=[rpsb])
        E("v", lambda v: v.tensor_copy(
            out=dst[:, tc * TC:(tc + 1) * TC], in_=PSB[:, 0:TC]),
          r=[rpsb], w=[rdst])

    def ln_rows_tc(tc, mode):
        C = float(D)
        E("v", lambda v: v.tensor_scalar(
            out=S1(), in0=S1(), scalar1=1.0 / C, scalar2=None,
            op0=ALU.mult), r=[rrows], w=[rrows])
        bcast_tc(S1(), rrows, mub, rmub, tc)
        E("v", lambda v: v.tensor_tensor(
            out=RX(), in0=S1(), in1=S1(), op=ALU.mult),
          r=[rrows], w=[rrowx])
        E("v", lambda v: v.scalar_tensor_tensor(
            out=S2(), in0=S2(), scalar=1.0 / C, in1=RX(),
            op0=ALU.mult, op1=ALU.subtract), r=[rrows, rrowx], w=[rrows])
        E("v", lambda v: v.tensor_scalar(
            out=S2(), in0=S2(), scalar1=EPS, scalar2=None, op0=ALU.add),
          r=[rrows], w=[rrows])
        E("s", lambda s: s.activation(out=RX(), in_=S2(), func=AF.Ln),
          r=[rrows], w=[rrowx])
        E("s", lambda s: s.activation(out=RX(), in_=RX(), func=AF.Exp,
                                      scale=-0.5), r=[rrowx], w=[rrowx])
        if mode == "ln2":
            bcast_tc(RX(), rrowx, s1b, rs1b, tc)
            return
        bcast_tc(RX(), rrowx, s0b, rs0b, tc)
        if mode == "l1":
            nj = TC // 128
            for j in range(nj):
                E("g", lambda g_, j=j: g_.dma_start(
                    out=scol[:, tc * nj + j:tc * nj + j + 1],
                    in_=rowx[0:1, j * 128:(j + 1) * 128],
                    allow_slow_non_contiguous=True),
                  r=[rrowx], w=[rscol], dma="scol")
        if mode == "l0":
            E("v", lambda v: v.tensor_tensor(
                out=S1(), in0=S1(), in1=RX(), op=ALU.mult),
              r=[rrows, rrowx], w=[rrows])
            bcast_tc(S1(), rrows, m0b, rm0b, tc)
        E("v", lambda v: v.tensor_tensor(
            out=S2(), in0=S2(), in1=RX(), op=ALU.mult),
          r=[rrows, rrowx], w=[rrows])
        E("v", lambda v: v.tensor_tensor(
            out=S2(), in0=S2(), in1=RX(), op=ALU.mult),
          r=[rrows, rrowx], w=[rrows])
        E("v", lambda v: v.tensor_scalar(
            out=S2(), in0=S2(), scalar1=EPS, scalar2=None, op0=ALU.add),
          r=[rrows], w=[rrows])
        E("s", lambda s: s.activation(out=S2(), in_=S2(), func=AF.Ln),
          r=[rrows], w=[rrows])
        E("s", lambda s: s.activation(out=S2(), in_=S2(), func=AF.Exp,
                                      scale=-0.5), r=[rrows], w=[rrows])
        E("v", lambda v: v.tensor_tensor(
            out=RX(), in0=RX(), in1=S2(), op=ALU.mult),
          r=[rrows], w=[rrowx])
        bcast_tc(RX(), rrowx, s1b, rs1b, tc)
        if mode == "l1":
            E("v", lambda v: v.tensor_tensor(
                out=S1(), in0=S1(), in1=RX(), op=ALU.mult),
              r=[rrowx], w=[rrows])
            bcast_tc(S1(), rrows, m0b, rm0b, tc)

    def stream_w(wap, mi, nk, si):
        E("g", lambda g_: g_.dma_start(
            out=slot(si)[:, 0:nk * 128],
            in_=wap[mi * 128:(mi + 1) * 128, 0:nk * 128]),
          w=[rslot[si]], dma=f"w{si}")

    rslot5h = [reg("wslot.5a"), reg("wslot.5b")]
    def stream_w5(wap, mi, nk):
        hk = nk // 2
        E("g", lambda g_, hk=hk: g_.dma_start(
            out=slot(5)[:, 0:hk * 128],
            in_=wap[mi * 128:(mi + 1) * 128, 0:hk * 128]),
          w=[rslot5h[0]], dma="w5a")
        E("g", lambda g_, hk=hk: g_.dma_start(
            out=slot(5)[:, hk * 128:nk * 128],
            in_=wap[mi * 128:(mi + 1) * 128, hk * 128:nk * 128]),
          w=[rslot5h[1]], dma="w5b")


    def wb_column(si, nk, ci, ps=None, rp=None):
        ps = PSB if ps is None else ps
        rp = rpsb if rp is None else rp
        for k in range(nk):
            E("t", lambda t, k=k, ps=ps: t.matmul(
                out=ps[:, 0:1], lhsT=slot(si, k), rhs=bcolb[:, k:k + 1],
                start=(k == 0), stop=(k == nk - 1)),
              r=[rslot[si], rbcolb], w=[rp])
        E("v", lambda v, ps=ps: v.tensor_copy(
            out=wbcol[:, ci:ci + 1], in_=ps[:, 0:1]), r=[rp], w=[rwbcol])

    def scale_slot(si, nk):
        for k in range(nk):
            E("v", lambda v, k=k: v.tensor_scalar(
                out=slot(si, k), in0=slot(si, k), scalar1=wcolf[:, k:k + 1],
                scalar2=None, op0=ALU.mult),
              r=[rwbcol, rslot[si]], w=[rslot[si]])

    p0 = contextlib.ExitStack()
    sb0 = lambda name, shape, dtype=F32: p0.enter_context(
        nc.sbuf_tensor(name, shape, dtype))
    xbuf = sb0("xbuf", [128, NG * T], BF16)
    scA = sb0("scA", [128, T])
    scB = sb0("scB", [128, T])
    scC = sb0("scC", [128, T + 8])
    scD = sb0("scD", [128, T + 8])
    scE = sb0("scE", [128, T + 8])
    scF = sb0("scF", [128, T + 8])
    sigr = sb0("sigr", [128, T])
    d1t = sb0("d1t", [128, T])
    hbuf = sb0("hbuf", [128, NHG * TCB], BF16)
    gtile = sb0("gtile", [128, 2 * TCB], BF16)
    rwt = sb0("rwt", [128, T], BF16)

    rxb = [reg(f"xb.{g}") for g in range(NG)]
    rsc = {nm: reg(nm) for nm in ("scA", "scB", "scC", "scD", "scE", "scF")}
    rsigr = reg("sigr"); rd1 = reg("d1t")
    rhb = reg("hbuf")
    rgt = [reg("gtile.0"), reg("gtile.1")]
    rrwt = reg("rwt")

    for tc in range(NTC):
        for g in range(NG):
            pp = g % 2
            xfs = xf[:, pp * TC:(pp + 1) * TC]
            E("q", lambda q, g=g, tc=tc, xfs=xfs: q.dma_start(
                out=xfs, in_=h["xT"][g * 128:(g + 1) * 128,
                                     tc * TC:(tc + 1) * TC]),
              w=[rxf[pp]], dma=f"xf{pp}")
            E("s", lambda s, g=g, tc=tc, xfs=xfs: s.activation(
                out=xbuf[:, g * T + tc * TC:g * T + (tc + 1) * TC],
                in_=xfs, func=AF.Copy), r=[rxf[pp]], w=[rxb[g]])
            E("s", lambda s, xfs=xfs, pp=pp: s.activation(
                out=x2b[:, pp * TC:(pp + 1) * TC], in_=xfs, func=AF.Square),
              r=[rxf[pp]], w=[rx2b[pp]])
            E("t", lambda t, g=g, tc=tc: t.matmul(
                out=PSS[0:1, 0:TC], lhsT=onescol[:, 0:1],
                rhs=xbuf[:, g * T + tc * TC:g * T + (tc + 1) * TC],
                start=(g == 0), stop=(g == NG - 1)),
              r=[rxb[g], rones], w=[rpss])
            E("t", lambda t, g=g, pp=pp: t.matmul(
                out=PSB[0:1, 0:TC], lhsT=onescol[:, 0:1],
                rhs=x2b[:, pp * TC:(pp + 1) * TC],
                start=(g == 0), stop=(g == NG - 1)),
              r=[rx2b[pp], rones], w=[rpsb])
        E("v", lambda v: v.tensor_copy(out=S1(), in_=PSS[0:1, 0:TC]),
          r=[rpss], w=[rrows])
        E("v", lambda v: v.tensor_copy(out=S2(), in_=PSB[0:1, 0:TC]),
          r=[rpsb], w=[rrows])
        ln_rows_tc(tc, "l0")

    for g in range(NG):
        E("v", lambda v, g=g: v.tensor_tensor(
            out=xbuf[:, g * T:(g + 1) * T], in0=xbuf[:, g * T:(g + 1) * T],
            in1=mub[:, 0:T], op=ALU.subtract), r=[rmub], w=[rxb[g]])
    E("v", lambda v: v.tensor_copy(out=wcolf[:, :], in_=cols["ln1w"][:, :]),
      r=[rcols], w=[rwbcol])
    E("v", lambda v: v.tensor_copy(out=bcolb[:, :], in_=cols["ln1b"][:, :]),
      r=[rcols], w=[rbcolb])
    rhbm_xn = reg("hbm_xn")
    def xn_chunks():
        for g in range(NG):
            for tc2 in range(NTC):
                def xs(g=g, tc2=tc2):
                    pp = (g * NTC + tc2) % 2
                    E("v", lambda v, g=g, tc2=tc2, pp=pp: v.tensor_tensor(
                        out=x2b[:, pp * TC:(pp + 1) * TC],
                        in0=xbuf[:, g * T + tc2 * TC:g * T + (tc2 + 1) * TC],
                        in1=s0b[:, tc2 * TC:(tc2 + 1) * TC], op=ALU.mult),
                      r=[rxb[g], rs0b], w=[rx2b[pp]])
                    E("q", lambda q, g=g, tc2=tc2, pp=pp: q.dma_start(
                        out=sp_xn[g * 128:(g + 1) * 128,
                                  tc2 * TC:(tc2 + 1) * TC],
                        in_=x2b[:, pp * TC:(pp + 1) * TC]),
                      r=[rx2b[pp]], w=[rhbm_xn], dma="hxn")
                yield xs

    def bb_chunks():
        for tcb in range(NTCB):
            sl = slice(tcb * TCB, (tcb + 1) * TCB)
            for m in range(NHG):
                def mb(tcb=tcb, m=m, sl=sl):
                    for wi, (wap, si, bank) in enumerate(
                            ((h["wgT"], 3, 2), (h["wuT"], 4, 3))):
                        stream_w(wap, m, NG, si)
                        for k in range(NG):
                            E("t", lambda t, k=k, si=si, bank=bank, sl=sl:
                                t.matmul(
                                    out=PS[bank][:, 0:TCB], lhsT=slot(si, k),
                                    rhs=xbuf[:, k * T + sl.start:
                                             k * T + sl.stop],
                                    start=(k == 0), stop=(k == NG - 1)),
                              r=[rslot[si], rxb[k]], w=[rps[bank]])
                    pp = m % 2
                    E("v", lambda v, sl=sl, pp=pp: v.scalar_tensor_tensor(
                        out=ev1[:, pp * TC:pp * TC + TCB],
                        in0=PS[2][:, 0:TCB], scalar=1.0 / 127.0,
                        in1=s0b[:, sl], op0=ALU.mult, op1=ALU.mult),
                      r=[rps[2], rs0b], w=[rev1[pp]])
                    E("v", lambda v, sl=sl, pp=pp: v.scalar_tensor_tensor(
                        out=ev2[:, pp * TC:pp * TC + TCB],
                        in0=PS[3][:, 0:TCB], scalar=1.0 / 127.0,
                        in1=s0b[:, sl], op0=ALU.mult, op1=ALU.mult),
                      r=[rps[3], rs0b], w=[rev2[pp]])
                    E("s", lambda s, pp=pp: s.activation(
                        out=x2b[:, pp * TC:pp * TC + TCB],
                        in_=ev1[:, pp * TC:pp * TC + TCB],
                        func=AF.Tanh, scale=0.5), r=[rev1[pp]], w=[rx2b[pp]])
                    E("v", lambda v, pp=pp: v.scalar_tensor_tensor(
                        out=ev1[:, pp * TC:pp * TC + TCB],
                        in0=x2b[:, pp * TC:pp * TC + TCB], scalar=1.0,
                        in1=ev1[:, pp * TC:pp * TC + TCB],
                        op0=ALU.add, op1=ALU.mult),
                      r=[rx2b[pp], rev1[pp]], w=[rev1[pp]])
                    E("v", lambda v, m=m, pp=pp: v.tensor_tensor(
                        out=hbuf[:, m * TCB:(m + 1) * TCB],
                        in0=ev1[:, pp * TC:pp * TC + TCB],
                        in1=ev2[:, pp * TC:pp * TC + TCB], op=ALU.mult),
                      r=[rev1[pp], rev2[pp]], w=[rhb])
                yield mb
            for mo in range(NG):
                def dn(tcb=tcb, mo=mo, sl=sl):
                    stream_w5(h["wdT"], mo, NHG)
                    bank = 4 + (mo % 2)
                    for k in range(NHG):
                        E("t", lambda t, k=k, bank=bank: t.matmul(
                            out=PS[bank][:, 0:TCB], lhsT=slot(5, k),
                            rhs=hbuf[:, k * TCB:(k + 1) * TCB],
                            start=(k == 0), stop=(k == NHG - 1)),
                          r=[rslot5h[2 * k // NHG], rhb], w=[rps[bank]])
                    pp = mo % 2
                    E("s", lambda s, bank=bank, pp=pp: s.activation(
                        out=x2b[:, pp * TC:pp * TC + TCB],
                        in_=PS[bank][:, 0:TCB], func=AF.Tanh,
                        scale=0.5 / 254.0), r=[rps[bank]], w=[rx2b[pp]])
                    E("v", lambda v, bank=bank, pp=pp: v.scalar_tensor_tensor(
                        out=gtile[:, pp * TCB:(pp + 1) * TCB],
                        in0=x2b[:, pp * TC:pp * TC + TCB], scalar=1.0,
                        in1=PS[bank][:, 0:TCB], op0=ALU.add, op1=ALU.mult),
                      r=[rx2b[pp], rps[bank]], w=[rgt[pp]])
                    E("q", lambda q, mo=mo, sl=sl, pp=pp: q.dma_start(
                        out=sp_g[mo * 128:(mo + 1) * 128, sl],
                        in_=gtile[:, pp * TCB:(pp + 1) * TCB]),
                      r=[rgt[pp]], w=[rhbm_g], dma="hg")
                yield dn

    import itertools
    bbgen = itertools.chain(xn_chunks(), bb_chunks())
    bb_left = [NG * NTC + NTCB * (NHG + NG)]
    def bb_pull(n):
        for _ in range(n):
            if bb_left[0] <= 0:
                return
            next(bbgen)()
            bb_left[0] -= 1

    def rkv_group(gm):
        stream_w(h["wkT"], gm, NG, 0)
        stream_w(h["wvT"], gm, NG, 1)
        stream_w(h["wrT"], gm, NG, 2)
        for si, ci in ((0, 0), (1, 1), (2, 2)):
            wb_column(si, NG, ci)
            scale_slot(si, NG)

        def proj(si, ci, dst, rdst, bias_add=True):
            for tc in range(NTC):
                bank = tc % 2
                for k in range(NG):
                    E("t", lambda t, k=k, si=si, tc=tc, bank=bank: t.matmul(
                        out=PS[bank][:, 0:TC], lhsT=slot(si, k),
                        rhs=xbuf[:, k * T + tc * TC:k * T + (tc + 1) * TC],
                        start=(k == 0), stop=(k == NG - 1)),
                      r=[rslot[si], rxb[k]], w=[rps[bank]])
                E("v", lambda v, tc=tc, bank=bank: v.tensor_tensor(
                    out=dst[:, tc * TC:(tc + 1) * TC], in0=PS[bank][:, 0:TC],
                    in1=s1b[:, tc * TC:(tc + 1) * TC], op=ALU.mult),
                  r=[rps[bank], rs1b], w=[rdst])
                if bias_add:
                    E("v", lambda v, tc=tc, ci=ci: v.tensor_scalar(
                        out=dst[:, tc * TC:(tc + 1) * TC],
                        in0=dst[:, tc * TC:(tc + 1) * TC],
                        scalar1=wbcol[:, ci:ci + 1], scalar2=None,
                        op0=ALU.add), r=[rwbcol], w=[rdst])

        proj(0, 0, scB, rsc["scB"], bias_add=False)
        if tap == "l0_k":
            E("v", lambda v: v.tensor_scalar(
                out=scB[:, :], in0=scB[:, :], scalar1=wbcol[:, 0:1],
                scalar2=None, op0=ALU.add), r=[rwbcol], w=[rsc["scB"]])
            E("q", lambda q: q.dma_start(
                out=dbg[gm * 128:(gm + 1) * 128, :], in_=scB[:, :]),
              r=[rsc["scB"]], dma="out")
            E("s", lambda s: s.activation(out=scB[:, :], in_=scB[:, :],
                                          func=AF.Exp),
              r=[rsc["scB"]], w=[rsc["scB"]])
        else:
            E("s", lambda s: s.activation(out=scB[:, :], in_=scB[:, :],
                                          func=AF.Exp, bias=wbcol[:, 0:1]),
              r=[rsc["scB"], rwbcol], w=[rsc["scB"]])
        proj(1, 1, scA, rsc["scA"], bias_add=False)
        E("v", lambda v: v.scalar_tensor_tensor(
            out=scA[:, :], in0=scA[:, :], scalar=wbcol[:, 1:2],
            in1=scB[:, :], op0=ALU.add, op1=ALU.mult),
          r=[rsc["scB"], rwbcol], w=[rsc["scA"]])
        proj(2, 2, sigr, rsigr, bias_add=False)
        E("v", lambda v: v.tensor_scalar(
            out=wbcol[:, 3:4], in0=wbcol[:, 2:3], scalar1=0.5, scalar2=None,
            op0=ALU.mult), r=[rwbcol], w=[rwbcol])
        E("s", lambda s: s.activation(out=sigr[:, :], in_=sigr[:, :],
                                      func=AF.Tanh, scale=0.5,
                                      bias=wbcol[:, 3:4]),
          r=[rsigr, rwbcol], w=[rsigr])

        th = []
        A = th.append
        A(lambda: (E("v", lambda v: v.memset(d1t[:, :], 1.0), w=[rd1]),
                   E("v", lambda v: v.tensor_scalar(
                       out=d1t[:, :], in0=d1t[:, :], scalar1=ew[:, gm:gm + 1],
                       scalar2=None, op0=ALU.mult), r=[rcols], w=[rd1])))
        A(lambda: E("v", lambda v: v.tensor_copy(out=scE[:, 1:T + 1],
                                                 in_=scA[:, ::-1]),
                    r=[rsc["scA"]], w=[rsc["scE"]]))
        A(lambda: E("v", lambda v: v.tensor_copy(out=scF[:, 1:T + 1],
                                                 in_=scB[:, ::-1]),
                    r=[rsc["scB"]], w=[rsc["scF"]]))
        def _zc():
            for nm, tt in (("scC", scC), ("scD", scD), ("scE", scE),
                           ("scF", scF)):
                E("v", lambda v, tt=tt: v.memset(tt[:, 0:1], 0.0),
                  w=[rsc[nm]])
        A(_zc)
        A(lambda: E("v", lambda v: v.tensor_tensor_scan(
            out=scC[:, 1:T + 1], data0=d1t[:, :], data1=scA[:, :],
            initial=0.0, op0=ALU.mult, op1=ALU.add),
          r=[rd1, rsc["scA"]], w=[rsc["scC"]]))
        A(lambda: E("v", lambda v: v.tensor_tensor_scan(
            out=scD[:, 1:T + 1], data0=d1t[:, :], data1=scB[:, :],
            initial=0.0, op0=ALU.mult, op1=ALU.add),
          r=[rd1, rsc["scB"]], w=[rsc["scD"]]))
        A(lambda: E("v", lambda v: v.tensor_tensor_scan(
            out=scE[:, 1:T + 1], data0=d1t[:, :], data1=scE[:, 1:T + 1],
            initial=0.0, op0=ALU.mult, op1=ALU.add),
          r=[rd1], w=[rsc["scE"]]))
        A(lambda: E("v", lambda v: v.tensor_tensor_scan(
            out=scF[:, 1:T + 1], data0=d1t[:, :], data1=scF[:, 1:T + 1],
            initial=0.0, op0=ALU.mult, op1=ALU.add),
          r=[rd1], w=[rsc["scF"]]))
        A(lambda: E("v", lambda v: v.tensor_scalar(
            out=scA[:, :], in0=scA[:, :], scalar1=eu[:, gm:gm + 1],
            scalar2=None, op0=ALU.mult), r=[rcols], w=[rsc["scA"]]))
        A(lambda: E("v", lambda v: v.tensor_scalar(
            out=scB[:, :], in0=scB[:, :], scalar1=eu[:, gm:gm + 1],
            scalar2=None, op0=ALU.mult), r=[rcols], w=[rsc["scB"]]))
        A(lambda: E("v", lambda v: v.tensor_tensor(
            out=scC[:, 0:T], in0=scC[:, 0:T], in1=scA[:, :], op=ALU.add),
          r=[rsc["scA"]], w=[rsc["scC"]]))
        A(lambda: E("v", lambda v: v.tensor_tensor(
            out=scD[:, 0:T], in0=scD[:, 0:T], in1=scB[:, :], op=ALU.add),
          r=[rsc["scB"]], w=[rsc["scD"]]))
        A(lambda: E("v", lambda v: v.tensor_tensor(
            out=scA[:, :], in0=scA[:, :], in1=scE[:, 0:T][:, ::-1],
            op=ALU.add), r=[rsc["scE"]], w=[rsc["scA"]]))
        A(lambda: E("v", lambda v: v.tensor_tensor(
            out=scB[:, :], in0=scB[:, :], in1=scF[:, 0:T][:, ::-1],
            op=ALU.add), r=[rsc["scF"]], w=[rsc["scB"]]))
        A(lambda: E("v", lambda v: v.tensor_tensor(
            out=scA[:, :], in0=scA[:, :], in1=scD[:, 0:T], op=ALU.mult),
          r=[rsc["scD"]], w=[rsc["scA"]]))
        A(lambda: E("v", lambda v: v.tensor_tensor(
            out=scC[:, 0:T], in0=scC[:, 0:T], in1=scB[:, :], op=ALU.mult),
          r=[rsc["scB"]], w=[rsc["scC"]]))
        A(lambda: E("v", lambda v: v.tensor_tensor(
            out=scC[:, 0:T], in0=scC[:, 0:T], in1=scA[:, :], op=ALU.add),
          r=[rsc["scA"]], w=[rsc["scC"]]))
        A(lambda: E("v", lambda v: v.tensor_tensor(
            out=scD[:, 0:T], in0=scD[:, 0:T], in1=scB[:, :], op=ALU.mult),
          r=[rsc["scB"]], w=[rsc["scD"]]))
        A(lambda: E("s", lambda s: s.activation(
            out=scD[:, 0:T], in_=scD[:, 0:T], func=AF.Ln), w=[rsc["scD"]]))
        A(lambda: E("s", lambda s: s.activation(
            out=scD[:, 0:T], in_=scD[:, 0:T], func=AF.Exp, scale=-1.0),
          w=[rsc["scD"]]))
        A(lambda: E("v", lambda v: v.tensor_tensor(
            out=scC[:, 0:T], in0=scC[:, 0:T], in1=scD[:, 0:T], op=ALU.mult),
          r=[rsc["scD"]], w=[rsc["scC"]]))
        if tap == "l0_wkv":
            def _dump():
                E("v", lambda v: v.tensor_scalar(
                    out=scC[:, 0:T], in0=scC[:, 0:T], scalar1=0.5,
                    scalar2=None, op0=ALU.mult), r=[rsc["scC"]],
                  w=[rsc["scC"]])
                E("q", lambda q: q.dma_start(
                    out=dbg[gm * 128:(gm + 1) * 128, :], in_=scC[:, 0:T]),
                  r=[rsc["scC"]], dma="out")
            A(_dump)
            return th
        def _rw():
            E("v", lambda v: v.scalar_tensor_tensor(
                out=rwt[:, :], in0=sigr[:, :], scalar=1.0, in1=scC[:, 0:T],
                op0=ALU.add, op1=ALU.mult), r=[rsigr, rsc["scC"]], w=[rrwt])
            E("q", lambda q: q.dma_start(
                out=sp_rw[gm * 128:(gm + 1) * 128, :], in_=rwt[:, :]),
              r=[rrwt], w=[rhbm_rw], dma="hrw")
        A(_rw)
        return th

    BBTOT = NG * NTC + NTCB * (NHG + NG)
    pulled = 0
    for gm in range(NG):
        th = rkv_group(gm)
        quota = (gm + 1) * BBTOT // NG - pulled
        n = max(quota, len(th))
        ti = 0
        pi = 0
        for i in range(n):
            while pi * n < quota * (i + 1) and pi < quota:
                bb_pull(1)
                pi += 1
                pulled += 1
            while ti * n <= i * len(th) and ti < len(th):
                th[ti]()
                ti += 1
        while ti < len(th):
            th[ti]()
            ti += 1
    bb_pull(10 ** 9)

    b.barrier()
    p0.close()

    if tap in ("l0_k", "l0_wkv"):
        blk_ctx.__exit__(None, None, None)
        es.close()
        return nc, b

    p1 = contextlib.ExitStack()
    sb1 = lambda name, shape, dtype=F32: p1.enter_context(
        nc.sbuf_tensor(name, shape, dtype))
    x1 = sb1("x1", [128, NG * T])
    xb1 = sb1("xb1", [128, NG * T], BF16)
    rxb1 = [reg(f"xb1.{g}") for g in range(NG)]
    rx1t = [[reg(f"x1.{g}.{t_}") for t_ in range(NTC)] for g in range(NG)]
    rx1 = [reg(f"x1.{g}") for g in range(NG)]

    p1b = contextlib.ExitStack()
    sb1b = lambda name, shape, dtype=F32: p1b.enter_context(
        nc.sbuf_tensor(name, shape, dtype))
    kkb = sb1b("kkb", [128, NFG * TC], BF16)
    rwring = sb1b("rwring", [128, 2 * NG * TC], BF16)
    xmring = rwring
    g1ring = sb1b("g1ring", [128, 2 * TC], BF16)
    xcring = wslot[:, 3072:3072 + NG * TC]

    rkk = [reg(f"kk.{i}") for i in range(NFG)]
    rrwr = [reg("rwring.0"), reg("rwring.1")]
    rxmr = rrwr
    rg1 = [reg("g1ring.0"), reg("g1ring.1")]
    rxcr = reg("xcring")

    for tc in range(NTC):
        ppr = tc % 2
        E("q", lambda q, tc=tc, ppr=ppr: q.dma_start(
            out=rwring[:, ppr * NG * TC:(ppr + 1) * NG * TC].rearrange(
                "p (k c) -> p k c", k=NG),
            in_=sp_rw[:, tc * TC:(tc + 1) * TC].rearrange(
                "(k p) c -> p k c", p=128)), r=[rhbm_rw], w=[rrwr[ppr]], dma=f"rr{ppr}")
        for mo in range(NG):
            si = mo % 2
            stream_w(h["woT"], mo, NG, si)
            bank = 2 + (mo % 2)
            for k in range(NG):
                E("t", lambda t, k=k, si=si, bank=bank, ppr=ppr: t.matmul(
                    out=PS[bank][:, 0:TC], lhsT=slot(si, k),
                    rhs=rwring[:, ppr * NG * TC + k * TC:
                               ppr * NG * TC + (k + 1) * TC],
                    start=(k == 0), stop=(k == NG - 1)),
                  r=[rslot[si], rrwr[ppr]], w=[rps[bank]])
            pp = mo % 2
            E("q", lambda q, mo=mo, tc=tc, pp=pp: q.dma_start(
                out=g1ring[:, pp * TC:(pp + 1) * TC],
                in_=sp_xn[mo * 128:(mo + 1) * 128, tc * TC:(tc + 1) * TC]),
              r=[rhbm_xn], w=[rg1[pp]], dma=f"g1{pp}")
            E("v", lambda v, pp=pp, bank=bank: v.scalar_tensor_tensor(
                out=ev1[:, pp * TC:(pp + 1) * TC], in0=PS[bank][:, 0:TC],
                scalar=0.25, in1=g1ring[:, pp * TC:(pp + 1) * TC],
                op0=ALU.mult, op1=ALU.add),
              r=[rps[bank], rg1[pp]], w=[rev1[pp]])
            E("s", lambda s, pp=pp: s.activation(
                out=x2b[:, pp * TC:(pp + 1) * TC],
                in_=ev1[:, pp * TC:(pp + 1) * TC], func=AF.Copy),
              r=[rev1[pp]], w=[rx2b[pp]])
            E("q", lambda q, mo=mo, tc=tc, pp=pp: q.dma_start(
                out=sp_xm[mo * 128:(mo + 1) * 128, tc * TC:(tc + 1) * TC],
                in_=x2b[:, pp * TC:(pp + 1) * TC]),
              r=[rx2b[pp]], w=[rhbm_xm], dma="hxm")
            E("t", lambda t, mo=mo, pp=pp: t.matmul(
                out=PSS[0:1, 0:TC], lhsT=onescol[:, 0:1],
                rhs=x2b[:, pp * TC:(pp + 1) * TC],
                start=(mo == 0), stop=(mo == NG - 1)),
              r=[rx2b[pp], rones], w=[rpss])
            E("s", lambda s, pp=pp: s.activation(
                out=x2b[:, pp * TC:(pp + 1) * TC],
                in_=ev1[:, pp * TC:(pp + 1) * TC], func=AF.Square),
              r=[rev1[pp], rx2b[pp]], w=[rx2b[pp]])
            E("t", lambda t, mo=mo, pp=pp: t.matmul(
                out=PSB[0:1, 0:TC], lhsT=onescol[:, 0:1],
                rhs=x2b[:, pp * TC:(pp + 1) * TC],
                start=(mo == 0), stop=(mo == NG - 1)),
              r=[rx2b[pp], rones], w=[rpsb])
        E("v", lambda v: v.tensor_copy(out=S1(), in_=PSS[0:1, 0:TC]),
          r=[rpss], w=[rrows])
        E("v", lambda v: v.tensor_copy(out=S2(), in_=PSB[0:1, 0:TC]),
          r=[rpsb], w=[rrows])
        ln_rows_tc(tc, "ln2")

    if tap == "l0_g":
        for g in range(NG):
            for tc in range(NTC):
                pp = (g * NTC + tc) % 2
                E("g", lambda q, g=g, tc=tc, pp=pp: q.dma_start(
                    out=ev1[:, pp * TC:(pp + 1) * TC],
                    in_=sp_g[g * 128:(g + 1) * 128, tc * TC:(tc + 1) * TC]),
                  r=[reg("hbm_g")], w=[rev1[pp]], dma=f"dbr{pp}")
                E("v", lambda v, pp=pp: v.tensor_copy(
                    out=ev2[:, pp * TC:(pp + 1) * TC],
                    in_=ev1[:, pp * TC:(pp + 1) * TC]),
                  r=[rev1[pp]], w=[rev2[pp]])
                E("q", lambda q, g=g, tc=tc, pp=pp: q.dma_start(
                    out=dbg[g * 128:(g + 1) * 128, tc * TC:(tc + 1) * TC],
                    in_=ev2[:, pp * TC:(pp + 1) * TC]),
                  r=[rev2[pp]], dma=f"dbw{pp}")

    if tap == "l0_xmid":
        for g in range(NG):
            for tc in range(NTC):
                pp = (g * NTC + tc) % 2
                E("g", lambda q, g=g, tc=tc, pp=pp: q.dma_start(
                    out=ev1[:, pp * TC:(pp + 1) * TC],
                    in_=sp_xm[g * 128:(g + 1) * 128, tc * TC:(tc + 1) * TC]),
                  r=[rhbm_xm], w=[rev1[pp]], dma=f"dbr{pp}")
                E("v", lambda v, pp=pp: v.tensor_copy(
                    out=ev2[:, pp * TC:(pp + 1) * TC],
                    in_=ev1[:, pp * TC:(pp + 1) * TC]),
                  r=[rev1[pp]], w=[rev2[pp]])
                E("q", lambda q, g=g, tc=tc, pp=pp: q.dma_start(
                    out=dbg[g * 128:(g + 1) * 128, tc * TC:(tc + 1) * TC],
                    in_=ev2[:, pp * TC:(pp + 1) * TC]),
                  r=[rev2[pp]], dma=f"dbw{pp}")

    E("v", lambda v: v.tensor_copy(out=wcolf[:, :], in_=cols["ln2w"][:, :]),
      r=[rcols], w=[rwbcol])
    E("v", lambda v: v.tensor_copy(out=bcolb[:, :], in_=cols["ln2b"][:, :]),
      r=[rcols], w=[rbcolb])

    for tc in range(NTC):
        ppx = tc % 2
        E("q", lambda q, tc=tc, ppx=ppx: q.dma_start(
            out=xmring[:, ppx * NG * TC:(ppx + 1) * NG * TC].rearrange(
                "p (k c) -> p k c", k=NG),
            in_=sp_xm[:, tc * TC:(tc + 1) * TC].rearrange(
                "(k p) c -> p k c", p=128)), r=[rhbm_xm], w=[rxmr[ppx]], dma=f"xm{ppx}")
        def xm(k, ppx=ppx):
            return xmring[:, ppx * NG * TC + k * TC:ppx * NG * TC + (k + 1) * TC]
        def xc(k):
            return xcring[:, k * TC:(k + 1) * TC]
        for k in range(NG):
            E("v", lambda v, a=ev1[:, 0:TC], b_=xm(k),
                   c=mub[:, tc * TC:(tc + 1) * TC]:
                v.scalar_tensor_tensor(out=a, in0=c, scalar=-1.0, in1=b_,
                                       op0=ALU.mult, op1=ALU.add),
              r=[rxmr[ppx], rmub], w=[rev1[0]])
            E("v", lambda v, tc=tc: v.tensor_tensor(
                out=ev1[:, 0:TC], in0=ev1[:, 0:TC],
                in1=s1b[:, tc * TC:(tc + 1) * TC], op=ALU.mult),
              r=[rs1b], w=[rev1[0]])
            E("v", lambda v, a=xc(k), k=k: v.tensor_scalar(
                out=a, in0=ev1[:, 0:TC], scalar1=wcolf[:, k:k + 1],
                scalar2=None, op0=ALU.mult), r=[rev1[0], rwbcol], w=[rxcr])
        for mf in range(NFG):
            si = mf % 2
            stream_w(h["wfkT"], mf, NG, si)
            if tc == 0:
                wb_column(si, NG, 8 + mf, ps=PSS, rp=rpss)
            bank = mf % 2
            for k in range(NG):
                E("t", lambda t, k=k, si=si, bank=bank, a=xc(k): t.matmul(
                    out=PS[bank][:, 0:TC], lhsT=slot(si, k), rhs=a,
                    start=(k == 0), stop=(k == NG - 1)),
                  r=[rslot[si], rxcr], w=[rps[bank]])
            pp = mf % 2
            E("v", lambda v, pp=pp, mf=mf, bank=bank: v.tensor_scalar(
                out=ev1[:, pp * TC:(pp + 1) * TC],
                in0=PS[bank][:, 0:TC],
                scalar1=wbcol[:, 8 + mf:9 + mf],
                scalar2=0.0, op0=ALU.add, op1=ALU.max),
              r=[rps[bank], rwbcol], w=[rev1[pp]])
            E("s", lambda s, mf=mf, pp=pp: s.activation(
                out=kkb[:, mf * TC:(mf + 1) * TC],
                in_=ev1[:, pp * TC:(pp + 1) * TC], func=AF.Square),
              r=[rev1[pp]], w=[rkk[mf]])
        for mo in range(NG):
            si = 2
            stream_w(h["wfrT"], mo, NG, si)
            if tc == 0:
                wb_column(si, NG, 40 + mo, ps=PSS, rp=rpss)
                E("v", lambda v, mo=mo: v.tensor_scalar(
                    out=wbcol[:, 40 + mo:41 + mo],
                    in0=wbcol[:, 40 + mo:41 + mo], scalar1=0.5,
                    scalar2=None, op0=ALU.mult), r=[rwbcol], w=[rwbcol])
            bank = 2
            for k in range(NG):
                E("t", lambda t, k=k, si=si, bank=bank, a=xc(k): t.matmul(
                    out=PS[bank][:, 0:TC], lhsT=slot(si, k), rhs=a,
                    start=(k == 0), stop=(k == NG - 1)),
                  r=[rslot[si], rxcr], w=[rps[bank]])
            pp2 = mo % 2
            E("s", lambda s, pp2=pp2, mo=mo, bank=bank: s.activation(
                out=ev2[:, pp2 * TC:(pp2 + 1) * TC],
                in_=PS[bank][:, 0:TC], func=AF.Tanh,
                scale=0.5, bias=wbcol[:, 40 + mo:41 + mo]),
              r=[rps[bank], rwbcol], w=[rev2[pp2]])
            stream_w5(h["wfvT"], mo, NFG)
            bank2 = 4 + (mo % 2)
            for k in range(NFG):
                E("t", lambda t, k=k, bank2=bank2: t.matmul(
                    out=PS[bank2][:, 0:TC], lhsT=slot(5, k),
                    rhs=kkb[:, k * TC:(k + 1) * TC],
                    start=(k == 0), stop=(k == NFG - 1)),
                  r=[rslot5h[2 * k // NFG], rkk[k]], w=[rps[bank2]])
            pp = mo % 2
            xfs = xf[:, pp * TC:(pp + 1) * TC]
            E("q", lambda q, mo=mo, tc=tc, xfs=xfs: q.dma_start(
                out=xfs, in_=h["xT"][mo * 128:(mo + 1) * 128,
                                     tc * TC:(tc + 1) * TC]),
              w=[rxf[pp]], dma=f"xf{pp}")
            E("q", lambda q, mo=mo, tc=tc, pp=pp: q.dma_start(
                out=g1ring[:, pp * TC:(pp + 1) * TC],
                in_=sp_g[mo * 128:(mo + 1) * 128, tc * TC:(tc + 1) * TC]),
              r=[rhbm_g], w=[rg1[pp]], dma=f"g1{pp}")
            E("v", lambda v, pp2=pp2, bank2=bank2: v.scalar_tensor_tensor(
                out=ev2[:, pp2 * TC:(pp2 + 1) * TC],
                in0=ev2[:, pp2 * TC:(pp2 + 1) * TC], scalar=1.0,
                in1=PS[bank2][:, 0:TC], op0=ALU.add, op1=ALU.mult),
              r=[rps[bank2]], w=[rev2[pp2]])
            E("v", lambda v, pp2=pp2, a=xm(mo): v.scalar_tensor_tensor(
                out=ev2[:, pp2 * TC:(pp2 + 1) * TC],
                in0=ev2[:, pp2 * TC:(pp2 + 1) * TC], scalar=0.5,
                in1=a, op0=ALU.mult, op1=ALU.add),
              r=[rxmr[ppx]], w=[rev2[pp2]])
            E("v", lambda v, pp2=pp2, pp=pp: v.tensor_tensor(
                out=ev2[:, pp2 * TC:(pp2 + 1) * TC],
                in0=ev2[:, pp2 * TC:(pp2 + 1) * TC],
                in1=g1ring[:, pp * TC:(pp + 1) * TC], op=ALU.mult),
              r=[rg1[pp]], w=[rev2[pp2]])
            E("v", lambda v, pp2=pp2, mo=mo, tc=tc, pp=pp:
                v.scalar_tensor_tensor(
                    out=x1[:, mo * T + tc * TC:mo * T + (tc + 1) * TC],
                    in0=ev2[:, pp2 * TC:(pp2 + 1) * TC], scalar=1.0 / 508.0,
                    in1=xf[:, pp * TC:(pp + 1) * TC], op0=ALU.mult,
                    op1=ALU.add), r=[rev2[pp2], rxf[pp]], w=[rx1t[mo][tc]])
            E("s", lambda s, mo=mo, tc=tc, pp=pp: s.activation(
                out=x2b[:, pp * TC:(pp + 1) * TC],
                in_=x1[:, mo * T + tc * TC:mo * T + (tc + 1) * TC],
                func=AF.Copy), r=[rx1t[mo][tc]], w=[rx2b[pp]])
            E("t", lambda t, mo=mo, pp=pp: t.matmul(
                out=PSB[0:1, 0:TC], lhsT=onescol[:, 0:1],
                rhs=x2b[:, pp * TC:(pp + 1) * TC],
                start=(mo == 0), stop=(mo == NG - 1)),
              r=[rx2b[pp], rones], w=[rpsb])
            E("s", lambda s, mo=mo, tc=tc, pp=pp: s.activation(
                out=x2b[:, pp * TC:(pp + 1) * TC],
                in_=x1[:, mo * T + tc * TC:mo * T + (tc + 1) * TC],
                func=AF.Square), r=[rx1t[mo][tc]], w=[rx2b[pp]])
            E("t", lambda t, mo=mo, pp=pp: t.matmul(
                out=PS[3][0:1, 0:TC], lhsT=onescol[:, 0:1],
                rhs=x2b[:, pp * TC:(pp + 1) * TC],
                start=(mo == 0), stop=(mo == NG - 1)),
              r=[rx2b[pp], rones], w=[rps[3]])
        E("v", lambda v: v.tensor_copy(out=S1(), in_=PSB[0:1, 0:TC]),
          r=[rpsb], w=[rrows])
        E("v", lambda v: v.tensor_copy(out=S2(), in_=PS[3][0:1, 0:TC]),
          r=[rps[3]], w=[rrows])
        ln_rows_tc(tc, "l1")
        for mo in range(NG):
            E("v", lambda v, mo=mo, tc=tc: v.tensor_tensor(
                out=xb1[:, mo * T + tc * TC:mo * T + (tc + 1) * TC],
                in0=x1[:, mo * T + tc * TC:mo * T + (tc + 1) * TC],
                in1=mub[:, tc * TC:(tc + 1) * TC], op=ALU.subtract),
              r=[rx1t[mo][tc], rmub], w=[rxb1[mo]])

    if tap == "l0_out":
        for g in range(NG):
            E("q", lambda q, g=g: q.dma_start(
                out=dbg[g * 128:(g + 1) * 128, :],
                in_=x1[:, g * T:(g + 1) * T]), r=list(rx1t[g]), dma="out")

    b.barrier()
    p1b.close()

    p2 = contextlib.ExitStack()
    sb2 = lambda name, shape, dtype=F32: p2.enter_context(
        nc.sbuf_tensor(name, shape, dtype))
    qpair = sb2("qpair", [128, T], BF16)
    kpair = sb2("kpair", [128, T], BF16)
    vpair = sb2("vpair", [128, NT2 * 130], BF16)
    attw = sb2("attw", [128, 4 * 512], BF16)
    p64 = sb2("p64", [65, 512])
    p64b = sb2("p64b", [65, 512], BF16)
    ones64 = sb2("ones64", [65, 64], BF16)
    stg = sb2("stg", [64, 2 * 2 * TC], BF16)
    rtile = sb2("rtile", [64, 512], BF16)
    masks = sb2("masksb", [128, (TC // 128) * TC], BF16)
    E("g", lambda g_: g_.dma_start(
        out=masks[:, :], in_=h["maskc"][:, :]), w=[rmask], dma="prem")

    rqp = reg("qpair"); rkp = reg("kpair"); rvp = reg("vpair")
    ratt = [reg(f"attw.{i}") for i in range(4)]
    rp64 = reg("p64"); rp64b = reg("p64b"); ro64 = reg("ones64")
    rstg = [reg("stg.0"), reg("stg.1")]
    rrt = reg("rtile")

    E("v", lambda v: v.memset(ones64[:, :], 1.0), w=[ro64])

    def prework():
            for g in range(NG):
                for tc in range(NTC):
                    pp = tc % 2
                    E("v", lambda v, g=g, tc=tc, pp=pp: v.tensor_tensor(
                        out=ev1[:, pp * TC:(pp + 1) * TC],
                        in0=x1[:, g * T + tc * TC:g * T + (tc + 1) * TC],
                        in1=s1b[:, tc * TC:(tc + 1) * TC], op=ALU.mult),
                      r=[rx1t[g][tc], rs1b], w=[rev1[pp]])
                    E("v", lambda v, tc=tc, pp=pp: v.tensor_tensor(
                        out=ev1[:, pp * TC:(pp + 1) * TC],
                        in0=ev1[:, pp * TC:(pp + 1) * TC],
                        in1=m0b[:, tc * TC:(tc + 1) * TC], op=ALU.subtract),
                      r=[rm0b], w=[rev1[pp]])
                    E("v", lambda v, g=g, pp=pp: v.tensor_scalar(
                        out=ev1[:, pp * TC:(pp + 1) * TC],
                        in0=ev1[:, pp * TC:(pp + 1) * TC],
                        scalar1=cols["gamma"][:, g:g + 1],
                        scalar2=cols["beta"][:, g:g + 1], op0=ALU.mult, op1=ALU.add),
                      r=[rcols], w=[rev1[pp]])
                    E("v", lambda v, g=g, tc=tc, pp=pp: v.tensor_reduce(
                        out=mxc[:, g * NTC + tc:g * NTC + tc + 1],
                        in_=ev1[:, pp * TC:(pp + 1) * TC], axis=AXX,
                        op=ALU.max, apply_absolute_value=True),
                      r=[rev1[pp]], w=[rmxc])
                    yield
            E("v", lambda v: v.tensor_reduce(
                out=mxc[:, 33:34], in_=mxc[:, 0:NG * NTC], axis=AXX,
                op=ALU.max, apply_absolute_value=False), r=[rmxc], w=[rmxc])
            E("g", lambda g_: g_.dma_start(out=p64[0:1, 0:128], in_=mxc[:, 33:34], allow_slow_non_contiguous=True),
              r=[rmxc], w=[rp64], dma="p64s")
            E("v", lambda v: v.tensor_reduce(
                out=p64[0:1, 128:129], in_=p64[0:1, 0:128], axis=AXX,
                op=ALU.max, apply_absolute_value=False), r=[rp64], w=[rp64])
            E("g", lambda g_: g_.dma_start(out=cc_in[0:1, 0:1],
                                           in_=p64[0:1, 128:129]),
              r=[rp64], w=[reg("hbm_cc")], dma="p64s")
            E("g", lambda g_: g_.collective_compute(
                "AllReduce", ALU.max, replica_groups=[list(range(cores))],
                ins=[cc_in[0:1, 0:1]], outs=[cc_out[0:1, 0:1]]),
              r=[reg("hbm_cc")], w=[rcc], dma="cc", inc=1)
            E("g", lambda g_: g_.dma_start(out=p64[0:1, 129:130],
                                           in_=cc_out[0:1, 0:1]),
              r=[rcc], w=[rp64], dma="p64s")
            E("v", lambda v: v.tensor_scalar(
                out=p64[0:1, 130:131], in0=p64[0:1, 129:130], scalar1=1.0 / 127.0,
                scalar2=None, op0=ALU.mult), r=[rp64], w=[rp64])
            E("v", lambda v: v.reciprocal(out=p64[0:1, 131:132],
                                          in_=p64[0:1, 130:131]), r=[rp64], w=[rp64])
            E("t", lambda t: t.matmul(
                out=PSB[:, 0:2], lhsT=onescolf[0:1, :], rhs=p64[0:1, 130:132],
                start=True, stop=True), r=[rp64, rones], w=[rpsb])
            E("v", lambda v: v.tensor_copy(out=mxc[:, 34:36], in_=PSB[:, 0:2]),
              r=[rpsb], w=[rmxc])

            for kq in range(NG):
                for c2 in range(NW):
                    pp = (kq * NW + c2) % 2
                    E("q", lambda q, kq=kq, c2=c2, pp=pp: q.dma_start(
                        out=xf[:, pp * TC:pp * TC + WQC],
                        in_=h["wT"][kq * 128:(kq + 1) * 128, c2 * WQC:(c2 + 1) * WQC]),
                      w=[rxf[pp]], dma=f"xf{pp}")
                    E("v", lambda v, kq=kq, c2=c2, pp=pp: v.tensor_reduce(
                        out=mxc[:, 40 + kq * NW + c2:41 + kq * NW + c2],
                        in_=xf[:, pp * TC:pp * TC + WQC], axis=AXX,
                        op=ALU.add, apply_absolute_value=True),
                      r=[rxf[pp]], w=[rmxc])
                    yield
            E("v", lambda v: v.tensor_reduce(
                out=mxc[:, 38:39], in_=mxc[:, 40:40 + NG * NW], axis=AXX,
                op=ALU.add), r=[rmxc], w=[rmxc])
            E("g", lambda g_: g_.dma_start(out=p64[0:1, 0:128], in_=mxc[:, 38:39], allow_slow_non_contiguous=True),
              r=[rmxc], w=[rp64], dma="p64s")
            E("v", lambda v: v.tensor_reduce(
                out=p64[0:1, 132:133], in_=p64[0:1, 0:128], axis=AXX, op=ALU.add),
              r=[rp64], w=[rp64])
            E("v", lambda v: v.tensor_scalar(
                out=p64[0:1, 133:134], in0=p64[0:1, 132:133],
                scalar1=0.7 / float(D * D), scalar2=None, op0=ALU.mult),
              r=[rp64], w=[rp64])
            E("t", lambda t: t.matmul(
                out=PSB[:, 0:1], lhsT=onescolf[0:1, :], rhs=p64[0:1, 133:134],
                start=True, stop=True), r=[rp64, rones], w=[rpsb])
            E("v", lambda v: v.tensor_copy(out=mxc[:, 36:37], in_=PSB[:, 0:1]),
              r=[rpsb], w=[rmxc])
            for kq in range(NG):
                for c2 in range(NW):
                    pp = (kq * NW + c2) % 2
                    E("q", lambda q, kq=kq, c2=c2, pp=pp: q.dma_start(
                        out=xf[:, pp * TC:pp * TC + WQC],
                        in_=h["wT"][kq * 128:(kq + 1) * 128, c2 * WQC:(c2 + 1) * WQC]),
                      w=[rxf[pp]], dma=f"xf{pp}")
                    E("s", lambda s, pp=pp: s.activation(
                        out=ev1[:, pp * TC:pp * TC + WQC],
                        in_=xf[:, pp * TC:pp * TC + WQC], func=AF.Abs),
                      r=[rxf[pp]], w=[rev1[pp]])
                    E("v", lambda v, pp=pp: v.tensor_scalar(
                        out=ev2[:, pp * TC:pp * TC + WQC],
                        in0=ev1[:, pp * TC:pp * TC + WQC], scalar1=mxc[:, 36:37],
                        scalar2=None, op0=ALU.is_gt), r=[rev1[pp], rmxc],
                      w=[rev2[pp]])
                    E("v", lambda v, kq=kq, c2=c2, pp=pp: v.tensor_reduce(
                        out=mxc[:, 40 + kq * NW + c2:41 + kq * NW + c2],
                        in_=ev2[:, pp * TC:pp * TC + WQC], axis=AXX, op=ALU.add),
                      r=[rev2[pp]], w=[rmxc])
                    E("v", lambda v, pp=pp: v.tensor_tensor(
                        out=ev1[:, pp * TC:pp * TC + WQC],
                        in0=ev1[:, pp * TC:pp * TC + WQC],
                        in1=ev2[:, pp * TC:pp * TC + WQC], op=ALU.mult),
                      r=[rev2[pp]], w=[rev1[pp]])
                    E("v", lambda v, kq=kq, c2=c2, pp=pp: v.tensor_reduce(
                        out=mxc[:, MXB + kq * NW + c2:MXB + 1 + kq * NW + c2],
                        in_=ev1[:, pp * TC:pp * TC + WQC], axis=AXX, op=ALU.add),
                      r=[rev1[pp]], w=[rmxc])
                    yield
            E("v", lambda v: v.tensor_reduce(
                out=mxc[:, 38:39], in_=mxc[:, 40:40 + NG * NW], axis=AXX,
                op=ALU.add), r=[rmxc], w=[rmxc])
            E("v", lambda v: v.tensor_reduce(
                out=mxc[:, 39:40], in_=mxc[:, MXB:MXB + NG * NW], axis=AXX,
                op=ALU.add), r=[rmxc], w=[rmxc])
            E("g", lambda g_: g_.dma_start(out=p64[0:1, 0:128], in_=mxc[:, 38:39], allow_slow_non_contiguous=True),
              r=[rmxc], w=[rp64], dma="p64s")
            E("g", lambda g_: g_.dma_start(out=p64[0:1, 128:256], in_=mxc[:, 39:40], allow_slow_non_contiguous=True),
              r=[rmxc], w=[rp64], dma="p64s")
            E("v", lambda v: v.tensor_reduce(
                out=p64[0:1, 256:257], in_=p64[0:1, 0:128], axis=AXX, op=ALU.add),
              r=[rp64], w=[rp64])
            E("v", lambda v: v.tensor_reduce(
                out=p64[0:1, 257:258], in_=p64[0:1, 128:256], axis=AXX, op=ALU.add),
              r=[rp64], w=[rp64])
            E("v", lambda v: v.tensor_scalar(
                out=p64[0:1, 256:257], in0=p64[0:1, 256:257], scalar1=1.0,
                scalar2=None, op0=ALU.max), r=[rp64], w=[rp64])
            E("v", lambda v: v.reciprocal(out=p64[0:1, 256:257],
                                          in_=p64[0:1, 256:257]), r=[rp64], w=[rp64])
            E("v", lambda v: v.tensor_tensor(
                out=p64[0:1, 257:258], in0=p64[0:1, 257:258], in1=p64[0:1, 256:257],
                op=ALU.mult), r=[rp64], w=[rp64])
            E("t", lambda t: t.matmul(
                out=PSB[:, 0:1], lhsT=onescolf[0:1, :], rhs=p64[0:1, 257:258],
                start=True, stop=True), r=[rp64, rones], w=[rpsb])
            E("v", lambda v: v.tensor_copy(out=mxc[:, 37:38], in_=PSB[:, 0:1]),
              r=[rpsb], w=[rmxc])
            yield
    _pw = prework()
    _pw_done = [False]
    def pw_pull(n):
        for _ in range(n):
            if _pw_done[0]:
                return
            try:
                next(_pw)
            except StopIteration:
                _pw_done[0] = True
                return

    for pair in range(NPAIR):
        pw_pull(12)
        stream_w(h["q_T"], pair, NG, 0)
        stream_w(h["k_T"], pair, NG, 1)
        for tc in range(NTC):
            for si, dst, rdst in ((0, qpair, rqp), (1, kpair, rkp)):
                bank = si
                for k in range(NG):
                    E("t", lambda t, k=k, si=si, tc=tc, bank=bank: t.matmul(
                        out=PS[bank][:, 0:TC], lhsT=slot(si, k),
                        rhs=xb1[:, k * T + tc * TC:k * T + (tc + 1) * TC],
                        start=(k == 0), stop=(k == NG - 1)),
                      r=[rslot[si], rxb1[k]], w=[rps[bank]])
                E("v", lambda v, tc=tc, bank=bank, dst=dst, rdst=rdst:
                    v.tensor_tensor(
                        out=dst[:, tc * TC:(tc + 1) * TC],
                        in0=PS[bank][:, 0:TC],
                        in1=s0b[:, tc * TC:(tc + 1) * TC], op=ALU.mult),
                  r=[rps[bank], rs0b], w=[rdst])
        stream_w(h["v_T"], pair, NG, 2)
        for t2 in range(NT2):
            for k in range(NG):
                E("t", lambda t, k=k, t2=t2: t.matmul(
                    out=PS[4][:, 0:128],
                    lhsT=xb1[:, k * T + t2 * 128:k * T + (t2 + 1) * 128],
                    rhs=slot(2, k), start=(k == 0), stop=(k == NG - 1)),
                  r=[rslot[2], rxb1[k]], w=[rps[4]])
            for half in range(2):
                E("v", lambda v, t2=t2, half=half: v.tensor_scalar(
                    out=vpair[:, t2 * 130 + half * 65:
                              t2 * 130 + half * 65 + 64],
                    in0=PS[4][:, half * 64:(half + 1) * 64],
                    scalar1=scol[:, t2:t2 + 1], scalar2=None, op0=ALU.mult),
                  r=[rps[4], rscol], w=[rvp])
            E("v", lambda v, t2=t2: v.memset(
                vpair[:, t2 * 130 + 64:t2 * 130 + 65], 1.0), w=[rvp])
            E("v", lambda v, t2=t2: v.memset(
                vpair[:, t2 * 130 + 129:t2 * 130 + 130], 1.0), w=[rvp])
        for tc in range(NTC):
            ppg = tc % 2
            nk2 = (tc + 1) * (TC // 128)
            for k2 in range(nk2):
                for half in range(2):
                    hb = half * 64
                    sbank = half
                    E("t", lambda t, k2=k2, hb=hb, tc=tc, sbank=sbank:
                        t.matmul(
                            out=PS[sbank][:, 0:TC],
                            lhsT=kpair[hb:hb + 64, k2 * 128:(k2 + 1) * 128],
                            rhs=qpair[hb:hb + 64, tc * TC:(tc + 1) * TC],
                            start=True, stop=True),
                      r=[rkp, rqp], w=[rps[sbank]])
                    dj = k2 * 128 - tc * TC
                    if dj >= 0:
                        ji = dj // 128
                        E("v", lambda v, ji=ji, sbank=sbank: v.tensor_tensor(
                            out=PS[sbank][:, 0:TC], in0=PS[sbank][:, 0:TC],
                            in1=masks[:, ji * TC:(ji + 1) * TC], op=ALU.add),
                          r=[rmask], w=[rps[sbank]])
                    slot_a = (k2 % 2) * 2 + half
                    E("s", lambda s, sbank=sbank, slot_a=slot_a: s.activation(
                        out=attw[:, slot_a * 512:slot_a * 512 + TC],
                        in_=PS[sbank][:, 0:TC], func=AF.Exp, scale=0.125),
                      r=[rps[sbank]], w=[ratt[slot_a]])
                    E("t", lambda t, k2=k2, half=half, slot_a=slot_a,
                           nk2=nk2: t.matmul(
                            out=PS[4 + half][0:65, 0:TC],
                            lhsT=vpair[:, k2 * 130 + half * 65:
                                       k2 * 130 + half * 65 + 65],
                            rhs=attw[:, slot_a * 512:slot_a * 512 + TC],
                            start=(k2 == 0), stop=(k2 == nk2 - 1)),
                      r=[rvp, ratt[slot_a]], w=[rps[4 + half]])
            for half in range(2):
                E("s", lambda s, half=half: s.activation(
                    out=p64[64:65, half * 256 if False else 0:TC]
                    if False else p64[64:65, 0:TC],
                    in_=PS[4 + half][64:65, 0:TC], func=AF.Ln),
                  r=[rps[4 + half]], w=[rp64])
                E("s", lambda s: s.activation(
                    out=p64b[64:65, 0:TC], in_=p64[64:65, 0:TC],
                    func=AF.Exp, scale=-1.0), r=[rp64], w=[rp64b])
                E("t", lambda t, half=half: t.matmul(
                    out=PS[2 + half][0:64, 0:TC], lhsT=ones64[64:65, 0:64],
                    rhs=p64b[64:65, 0:TC], start=True, stop=True),
                  r=[rp64b, ro64], w=[rps[2 + half]])
                E("v", lambda v, half=half: v.tensor_copy(
                    out=rtile[0:64, 0:TC], in_=PS[2 + half][0:64, 0:TC]),
                  r=[rps[2 + half]], w=[rrt])
                E("v", lambda v, half=half, ppg=ppg: v.tensor_tensor(
                    out=stg[0:64, ppg * 2 * TC + half * TC:
                            ppg * 2 * TC + half * TC + TC],
                    in0=PS[4 + half][0:64, 0:TC], in1=rtile[0:64, 0:TC],
                    op=ALU.mult), r=[rps[4 + half], rrt], w=[rstg[ppg]])
            E("q", lambda q, pair=pair, tc=tc, ppg=ppg: q.dma_start(
                out=sp_rw[pair * 128:(pair + 1) * 128,
                          tc * TC:(tc + 1) * TC].rearrange(
                    "(hh p) c -> p hh c", hh=2),
                in_=stg[0:64, ppg * 2 * TC:ppg * 2 * TC + 2 * TC].rearrange(
                    "p (hh q) -> p hh q", hh=2)),
              r=[rstg[ppg]], w=[rhbm_rw], dma="hrw")

    pw_pull(10 ** 9)

    if tap == "l1_att":
        for g in range(NG):
            for tc in range(NTC):
                pp = (g * NTC + tc) % 2
                E("g", lambda q, g=g, tc=tc, pp=pp: q.dma_start(
                    out=ev1[:, pp * TC:(pp + 1) * TC],
                    in_=sp_rw[g * 128:(g + 1) * 128, tc * TC:(tc + 1) * TC]),
                  r=[rhbm_rw], w=[rev1[pp]], dma=f"dbr{pp}")
                E("v", lambda v, pp=pp: v.tensor_copy(
                    out=ev2[:, pp * TC:(pp + 1) * TC],
                    in_=ev1[:, pp * TC:(pp + 1) * TC]),
                  r=[rev1[pp]], w=[rev2[pp]])
                E("q", lambda q, g=g, tc=tc, pp=pp: q.dma_start(
                    out=dbg[g * 128:(g + 1) * 128, tc * TC:(tc + 1) * TC],
                    in_=ev2[:, pp * TC:(pp + 1) * TC]),
                  r=[rev2[pp]], dma=f"dbw{pp}")

    b.barrier()
    p2.close()

    p3 = contextlib.ExitStack()
    sb3 = lambda name, shape, dtype=F32: p3.enter_context(
        nc.sbuf_tensor(name, shape, dtype))
    wqb = sb3("wqb", [128, NG * D], BF16)
    xqr = sb3("xqr", [128, 2 * NG * TC], BF16)
    attr = sb3("attr", [128, 2 * NG * TC], BF16)
    qint = sb3("qint", [128, TC], I32)
    xev = xf

    rwq = reg("wqb")
    rxq = [reg("xqr.0"), reg("xqr.1")]
    rqi = reg("qint")
    rxe = rxf
    ratr = [reg("attr.0"), reg("attr.1")]

    for kq in range(NG):
        for c2 in range(NW):
            pp = (kq * NW + c2) % 2
            E("q", lambda q, kq=kq, c2=c2, pp=pp: q.dma_start(
                out=xf[:, pp * TC:pp * TC + WQC],
                in_=h["wT"][kq * 128:(kq + 1) * 128, c2 * WQC:(c2 + 1) * WQC]),
              w=[rxf[pp]], dma=f"xf{pp}")
            E("s", lambda s, pp=pp: s.activation(
                out=ev1[:, pp * TC:pp * TC + WQC],
                in_=xf[:, pp * TC:pp * TC + WQC], func=AF.Abs),
              r=[rxf[pp]], w=[rev1[pp]])
            E("v", lambda v, pp=pp: v.tensor_scalar(
                out=ev1[:, pp * TC:pp * TC + WQC],
                in0=ev1[:, pp * TC:pp * TC + WQC], scalar1=mxc[:, 36:37],
                scalar2=None, op0=ALU.is_gt), r=[rmxc], w=[rev1[pp]])
            E("s", lambda s, pp=pp: s.sign(
                out=ev2[:, pp * TC:pp * TC + WQC],
                in_=xf[:, pp * TC:pp * TC + WQC]), r=[rxf[pp]], w=[rev2[pp]])
            E("v", lambda v, pp=pp: v.tensor_tensor(
                out=ev1[:, pp * TC:pp * TC + WQC],
                in0=ev1[:, pp * TC:pp * TC + WQC],
                in1=ev2[:, pp * TC:pp * TC + WQC], op=ALU.mult),
              r=[rev2[pp]], w=[rev1[pp]])
            E("v", lambda v, kq=kq, c2=c2, pp=pp: v.tensor_scalar(
                out=wqb[:, kq * D + c2 * WQC:kq * D + (c2 + 1) * WQC],
                in0=ev1[:, pp * TC:pp * TC + WQC], scalar1=mxc[:, 37:38],
                scalar2=None, op0=ALU.mult), r=[rev1[pp], rmxc], w=[rwq])

    def xq_chunk(tc, k):
        ppx = tc % 2
        pp = k % 2
        E("v", lambda v, k=k, tc=tc, pp=pp: v.tensor_tensor(
            out=xev[:, pp * TC:(pp + 1) * TC],
            in0=x1[:, k * T + tc * TC:k * T + (tc + 1) * TC],
            in1=s1b[:, tc * TC:(tc + 1) * TC], op=ALU.mult),
          r=[rx1t[k][tc], rs1b], w=[rxe[pp]])
        E("v", lambda v, tc=tc, pp=pp: v.tensor_tensor(
            out=xev[:, pp * TC:(pp + 1) * TC],
            in0=xev[:, pp * TC:(pp + 1) * TC],
            in1=m0b[:, tc * TC:(tc + 1) * TC], op=ALU.subtract),
          r=[rm0b], w=[rxe[pp]])
        E("v", lambda v, k=k, pp=pp: v.tensor_scalar(
            out=xev[:, pp * TC:(pp + 1) * TC],
            in0=xev[:, pp * TC:(pp + 1) * TC],
            scalar1=cols["gamma"][:, k:k + 1],
            scalar2=cols["beta"][:, k:k + 1], op0=ALU.mult, op1=ALU.add),
          r=[rcols], w=[rxe[pp]])
        E("v", lambda v, pp=pp: v.tensor_scalar(
            out=xev[:, pp * TC:(pp + 1) * TC],
            in0=xev[:, pp * TC:(pp + 1) * TC], scalar1=mxc[:, 35:36],
            scalar2=None, op0=ALU.mult), r=[rmxc], w=[rxe[pp]])
        E("v", lambda v, pp=pp: v.tensor_scalar(
            out=xev[:, pp * TC:(pp + 1) * TC],
            in0=xev[:, pp * TC:(pp + 1) * TC], scalar1=-127.0,
            scalar2=127.0, op0=ALU.max, op1=ALU.min), w=[rxe[pp]])
        E("s", lambda s, pp=pp: s.sign(
            out=qint[:, 0:TC].bitcast(F32) if False else ev2[:, 0:TC],
            in_=xev[:, pp * TC:(pp + 1) * TC]),
          r=[rxe[pp]], w=[rev2[0]])
        E("v", lambda v, pp=pp: v.scalar_tensor_tensor(
            out=xev[:, pp * TC:(pp + 1) * TC],
            in0=ev2[:, 0:TC], scalar=0.5,
            in1=xev[:, pp * TC:(pp + 1) * TC], op0=ALU.mult,
            op1=ALU.add), r=[rev2[0]], w=[rxe[pp]])
        E("v", lambda v, pp=pp: v.tensor_copy(
            out=qint[:, 0:TC], in_=xev[:, pp * TC:(pp + 1) * TC]),
          r=[rxe[pp]], w=[rqi])
        E("v", lambda v, k=k, tc=tc, ppx=ppx: v.tensor_scalar(
            out=xqr[:, ppx * NG * TC + k * TC:
                    ppx * NG * TC + (k + 1) * TC],
            in0=qint[:, 0:TC], scalar1=mxc[:, 34:35],
            scalar2=None, op0=ALU.mult), r=[rqi, rmxc],
          w=[rxq[ppx]])

    def attr_dma(tc):
        ppx = tc % 2
        E("q", lambda q, tc=tc, ppx=ppx: q.dma_start(
            out=attr[:, ppx * NG * TC:(ppx + 1) * NG * TC].rearrange(
                "p (k c) -> p k c", k=NG),
            in_=sp_rw[:, tc * TC:(tc + 1) * TC].rearrange(
                "(k p) c -> p k c", p=128)),
          r=[rhbm_rw], w=[ratr[ppx]], dma=f"at{ppx}")

    attr_dma(0)
    for k in range(NG):
        xq_chunk(0, k)
    for tc in range(NTC):
        ppx = tc % 2
        if tc + 1 < NTC:
            attr_dma(tc + 1)
        for mo in range(NG):
            if tc + 1 < NTC:
                xq_chunk(tc + 1, mo)
            si = mo % 2
            stream_w(h["o_T"], mo, NG, si)
            bank = 2 + (mo % 2)
            for k in range(NG):
                E("t", lambda t, k=k, si=si, bank=bank, ppx=ppx: t.matmul(
                    out=PS[bank][:, 0:TC], lhsT=slot(si, k),
                    rhs=attr[:, ppx * NG * TC + k * TC:
                             ppx * NG * TC + (k + 1) * TC],
                    start=(k == 0), stop=(k == NG - 1)),
                  r=[rslot[si], ratr[ppx]], w=[rps[bank]])
            bank2 = 4 + (mo % 2)
            for k in range(NG):
                E("t", lambda t, k=k, mo=mo, bank2=bank2, ppx=ppx: t.matmul(
                    out=PS[bank2][:, 0:TC],
                    lhsT=wqb[:, k * D + mo * 128:k * D + (mo + 1) * 128],
                    rhs=xqr[:, ppx * NG * TC + k * TC:
                            ppx * NG * TC + (k + 1) * TC],
                    start=(k == 0), stop=(k == NG - 1)),
                  r=[rwq, rxq[ppx]], w=[rps[bank2]])
            pp = mo % 2
            E("s", lambda s, bank2=bank2, pp=pp: s.activation(
                out=x2b[:, pp * TC:(pp + 1) * TC], in_=PS[bank2][:, 0:TC],
                func=AF.Tanh, scale=0.5), r=[rps[bank2]], w=[rx2b[pp]])
            E("v", lambda v, bank2=bank2, pp=pp: v.scalar_tensor_tensor(
                out=ev1[:, pp * TC:(pp + 1) * TC],
                in0=x2b[:, pp * TC:(pp + 1) * TC], scalar=1.0,
                in1=PS[bank2][:, 0:TC], op0=ALU.add, op1=ALU.mult),
              r=[rx2b[pp], rps[bank2]], w=[rev1[pp]])
            E("v", lambda v, bank=bank, pp=pp: v.tensor_tensor(
                out=ev1[:, pp * TC:(pp + 1) * TC],
                in0=ev1[:, pp * TC:(pp + 1) * TC], in1=PS[bank][:, 0:TC],
                op=ALU.mult), r=[rps[bank]], w=[rev1[pp]])
            E("v", lambda v, mo=mo, tc=tc, pp=pp: v.scalar_tensor_tensor(
                out=x1[:, mo * T + tc * TC:mo * T + (tc + 1) * TC],
                in0=ev1[:, pp * TC:(pp + 1) * TC], scalar=0.5,
                in1=x1[:, mo * T + tc * TC:mo * T + (tc + 1) * TC],
                op0=ALU.mult, op1=ALU.add), r=[rev1[pp]], w=[rx1t[mo][tc]])
            E("q", lambda q, mo=mo, tc=tc: q.dma_start(
                out=yT[mo * 128:(mo + 1) * 128, tc * TC:(tc + 1) * TC],
                in_=x1[:, mo * T + tc * TC:mo * T + (tc + 1) * TC]),
              r=[rx1t[mo][tc]], dma="out")

    b.barrier(engs=("q",))
    p3.close()
    p1.close()

    blk_ctx.__exit__(None, None, None)
    es.close()
    return nc, b



def make_in_maps(inputs, T=2048):
    tr = lambda a: np.ascontiguousarray(np.asarray(a).T)
    _cv = lambda a: np.ascontiguousarray(np.asarray(a).reshape(8, 128).T)
    def tl(a):
        a = np.asarray(a)
        no, ni = a.shape
        nm, nk = no // 128, ni // 128
        return np.ascontiguousarray(
            a.T.reshape(nk, 128, nm, 128).transpose(2, 1, 0, 3)
            .reshape(nm * 128, nk * 128))
    TC = min(512, T)
    maskc = np.zeros((TC // 128, 128, TC), np.float32)
    for j in range(TC // 128):
        for p in range(128):
            maskc[j, p, :min(p + j * 128, TC)] = NEG
    maskc = np.ascontiguousarray(maskc.transpose(1, 0, 2).reshape(128, -1))
    common = {
        "wrT": tl(inputs["L0_attn_Wr"]), "wkT": tl(inputs["L0_attn_Wk"]),
        "wvT": tl(inputs["L0_attn_Wv"]), "woT": tl(inputs["L0_attn_Wo"]),
        "wfkT": tl(inputs["L0_attn_Wfk"]), "wfvT": tl(inputs["L0_attn_Wfv"]),
        "wfrT": tl(inputs["L0_attn_Wfr"]),
        "wgT": tl(inputs["L0_ffn_w_gate"]), "wuT": tl(inputs["L0_ffn_w_up"]),
        "wdT": tl(inputs["L0_ffn_w_down"]),
        "q_T": tl(inputs["L1_attn_q_proj_weight"]),
        "k_T": tl(inputs["L1_attn_k_proj_weight"]),
        "v_T": tl(inputs["L1_attn_v_proj_weight"]),
        "o_T": tl(inputs["L1_attn_o_proj_weight"]),
        "wT": tr(inputs["L1_ffn_weight"]),
        "ln1w": _cv(inputs["L0_attn_ln1_w"]),
        "ln1b": _cv(inputs["L0_attn_ln1_b"]),
        "ln2w": _cv(inputs["L0_attn_ln2_w"]),
        "ln2b": _cv(inputs["L0_attn_ln2_b"]),
        "decay": _cv(inputs["L0_attn_decay"]),
        "u": _cv(inputs["L0_attn_u"]),
        "gamma": _cv(inputs["L1_ffn_gamma"]),
        "beta": _cv(inputs["L1_ffn_beta"]),
        "maskc": maskc,
    }
    x = np.asarray(inputs["L0_attn_x"])
    in_maps = []
    for ci in range(x.shape[0]):
        m = dict(common)
        m["xT"] = tr(x[ci])
        in_maps.append(m)
    return in_maps


def kernel(**inputs):
    T = 2048
    nc, b = build(T=T, tap=DEBUG_TAP, cores=8)
    in_maps = make_in_maps(inputs, T=T)
    res = run_bass_kernel_spmd(nc, in_maps, core_ids=list(range(8)))
    out = np.stack([np.asarray(r["yT"]).T for r in res.results], axis=0)
    return out.astype(np.float32)


# version 19
# speedup vs baseline: 2.1612x; 1.0077x over previous
import numpy as np
import contextlib
import concourse.bass as bass
import concourse.mybir as mybir
from concourse.bass_utils import run_bass_kernel_spmd

F32 = mybir.dt.float32
BF16 = mybir.dt.bfloat16
I32 = mybir.dt.int32
AF = mybir.ActivationFunctionType
ALU = mybir.AluOpType
AXX = mybir.AxisListType.X

D = 1024
NG = D // 128
HID = 3072
NHG = HID // 128
FFD = 4096
NFG = FFD // 128
NPAIR = 8
EPS = 1e-5
NEG = -1.0e9

DEBUG_TAP = None
SAFE_SYNC = False


class Reg:
    __slots__ = ("name", "w", "readers")
    def __init__(self, name):
        self.name = name
        self.w = None
        self.readers = []


class B:
    ENG_ATTR = {"v": "vector", "s": "scalar", "t": "tensor",
                "g": "gpsimd", "q": "sync"}
    ENG_SEM = {"v": "sv", "s": "ss", "t": "st", "g": "sg", "q": None}

    def __init__(self, nc, block, sems, mksem):
        self.nc = nc
        self.block = block
        self.sems = sems
        self.mksem = mksem
        self.cnt = {k: 0 for k in sems}
        self.wm = {}
        self.maxwait = {}
        self.n_inst = 0
        self.buf = {e: [] for e in self.ENG_ATTR}

    def flush(self):
        for e in self.ENG_ATTR:
            bodies = self.buf[e]
            if not bodies:
                continue
            self.buf[e] = []
            def section(engobj, bodies=bodies):
                for bd in bodies:
                    bd(engobj)
            getattr(self.block, self.ENG_ATTR[e])(section)

    def dma_sem(self, key):
        key = "D_" + key
        if key not in self.sems:
            self.sems[key] = self.mksem(key)
            self.cnt[key] = 0
        return key

    def emit(self, eng, fn, r=(), w=(), dma=None, inc=None):
        needs = {}
        for reg_ in r:
            if reg_.w is not None:
                s, c = reg_.w
                if needs.get(s, 0) < c:
                    needs[s] = c
        for reg_ in w:
            if reg_.w is not None:
                s, c = reg_.w
                if needs.get(s, 0) < c:
                    needs[s] = c
            for (s, c) in reg_.readers:
                if needs.get(s, 0) < c:
                    needs[s] = c
        if dma is not None:
            sem_name = self.dma_sem(dma)
            inc_v = 16 if inc is None else inc
            mw = self.maxwait.get(sem_name, 0)
            if mw > 0 and needs.get(sem_name, 0) < mw:
                needs[sem_name] = mw
        else:
            sem_name = self.ENG_SEM[eng]
            inc_v = 1
        own = self.ENG_SEM[eng]
        if not SAFE_SYNC and eng in ("t", "s") and own in needs:
            del needs[own]
        for s_, v_ in needs.items():
            if self.maxwait.get(s_, 0) < v_:
                self.maxwait[s_] = v_
        def body(engobj):
            for s in sorted(needs):
                v = needs[s]
                key = (eng, s)
                if self.wm.get(key, -1) < v:
                    engobj.wait_ge(self.sems[s], v)
                    self.wm[key] = v
            instr = fn(engobj)
            instr.then_inc(self.sems[sem_name], inc_v)
        self.buf[eng].append(body)
        self.n_inst += 1
        self.cnt[sem_name] += inc_v
        tag = (sem_name, self.cnt[sem_name])
        for reg_ in w:
            reg_.w = tag
            reg_.readers = []
        for reg_ in r:
            reg_.readers.append(tag)
        return tag

    def barrier(self, engs=("v", "s", "t", "g", "q")):
        snapshot = dict(self.cnt)
        for e in engs:
            def body(engobj, e=e):
                for s, v in sorted(snapshot.items()):
                    if v > 0 and self.wm.get((e, s), -1) < v:
                        engobj.wait_ge(self.sems[s], v)
                        self.wm[(e, s)] = v
            self.buf[e].append(body)
        for s, v in snapshot.items():
            if self.maxwait.get(s, 0) < v:
                self.maxwait[s] = v
        self.flush()


def build(T=2048, tap=None, cores=8):
    TC = min(512, T)
    NTC = T // TC
    NT2 = T // 128
    TCB = min(512, T)
    NTCB = T // TCB
    WQC = TC
    NW = D // WQC
    MXB = 40 + NG * NW

    nc = bass.Bass("TRN2", target_bir_lowering=False, debug=False,
                   detect_race_conditions=SAFE_SYNC)

    h = {}
    def din(name, shape, dtype=F32):
        h[name] = nc.dram_tensor(name, shape, dtype, kind="ExternalInput").ap()
        return h[name]

    din("xT", [D, T])
    din("wrT", [D, D]); din("wkT", [D, D]); din("wvT", [D, D]); din("woT", [D, D])
    din("wfkT", [FFD, D]); din("wfvT", [D, FFD]); din("wfrT", [D, D])
    din("wgT", [HID, D], I32); din("wuT", [HID, D], I32); din("wdT", [D, HID], I32)
    din("q_T", [D, D]); din("k_T", [D, D]); din("v_T", [D, D]); din("o_T", [D, D])
    din("wT", [D, D])
    for nm in ("ln1w", "ln1b", "ln2w", "ln2b", "decay", "u", "gamma", "beta"):
        din(nm, [128, NG])
    din("maskc", [128, (TC // 128) * TC])

    yT = nc.dram_tensor("yT", [D, T], F32, kind="ExternalOutput").ap()
    dbg = None
    if tap is not None:
        dbg = nc.dram_tensor("dbg", [D, T], F32, kind="ExternalOutput").ap()
    sp_g = nc.dram_tensor("sp_g", [D, T], BF16, kind="Internal").ap()
    sp_rw = nc.dram_tensor("sp_rw", [D, T], BF16, kind="Internal").ap()
    sp_xm = nc.dram_tensor("sp_xm", [D, T], BF16, kind="Internal").ap()
    sp_xn = nc.dram_tensor("sp_xn", [D, T], BF16, kind="Internal").ap()
    cc_in = nc.dram_tensor("cc_in", [1, 8], F32, kind="Internal").ap()
    cc_out = nc.dram_tensor("cc_out", [1, 8], F32, kind="Internal").ap()

    es = contextlib.ExitStack()
    sb = lambda name, shape, dtype=F32: es.enter_context(
        nc.sbuf_tensor(name, shape, dtype))
    mkpsum = lambda name, shape, dtype=F32: es.enter_context(
        nc.psum_tensor(name, shape, dtype))
    mksem = lambda name: es.enter_context(nc.semaphore(name))

    SLOT_OFF = [0, 1024, 2048, 3072, 4096, 7168]
    SLOT_W = [1024, 1024, 1024, 1024, 3072, 4096]
    wslot = sb("wslot", [128, 11264], BF16)
    s0b = sb("s0b", [128, T], BF16)
    s1b = sb("s1b", [128, T], BF16)
    m0b = sb("m0b", [128, T], BF16)
    rows = sb("rows", [1, 2 * TC])
    rowx = sb("rowx", [1, TC])
    mub = sb("mub", [128, T], BF16)
    wbcol = sb("wbcol", [128, 48])
    wcolf = sb("wcolf", [128, NG])
    bcolb = sb("bcolb", [128, NG], BF16)
    onescol = sb("onescol", [128, 2], BF16)
    onescolf = sb("onescolf", [1, 128])
    xf = sb("xf", [128, 2 * TC])
    x2b = sb("x2b", [128, 2 * TC], BF16)
    ev1 = sb("ev1", [128, 2 * TC])
    ev2 = sb("ev2", [128, 2 * TC])
    cols = {}
    for nm in ("ln1w", "ln1b", "ln2w", "ln2b", "decay", "u", "gamma", "beta"):
        cols[nm] = sb("c_" + nm, [128, NG])
    ew = sb("c_ew", [128, NG])
    eu = sb("c_eu", [128, NG])
    mxc = sb("mxc", [128, 40 + 2 * NG * NW])
    scol = sb("scol", [128, NT2])

    PS = [mkpsum(f"ps{i}", [128, 512]) for i in range(6)]
    PSS = mkpsum("pss", [128, 512])
    PSB = mkpsum("psb", [128, 512])

    sems = {}
    for s_ in ("sv", "ss", "st", "sg"):
        sems[s_] = mksem(s_)

    blk_ctx = nc.Block()
    block = blk_ctx.__enter__()
    b = B(nc, block, sems, mksem)
    E = b.emit
    R = {}
    def reg(name):
        if name not in R:
            R[name] = Reg(name)
        return R[name]

    rps = [reg(f"ps.{i}") for i in range(6)]
    rpss = reg("pss"); rpsb = reg("psb")
    rrows = reg("rows"); rrowx = reg("rowx")
    rwbcol = reg("wbcol"); rbcolb = reg("bcolb")
    rmub = reg("mub")
    rs0b = reg("s0b"); rs1b = reg("s1b"); rm0b = reg("m0b")
    rcols = reg("cols")
    rxf = [reg("xf.0"), reg("xf.1")]
    rx2b = [reg("x2b.0"), reg("x2b.1")]
    rev1 = [reg("ev1.0"), reg("ev1.1")]
    rev2 = [reg("ev2.0"), reg("ev2.1")]
    rslot = [reg(f"wslot.{i}") for i in range(6)]
    rones = reg("ones")
    rmask = reg("masks")
    rmxc = reg("mxc"); rscol = reg("scol")
    rcc = reg("ccdone")
    rhbm_g = reg("hbm_g"); rhbm_rw = reg("hbm_rw"); rhbm_xm = reg("hbm_xm")

    def slot(i, k=None, kw=128):
        o = SLOT_OFF[i]
        if k is None:
            return wslot[:, o:o + SLOT_W[i]]
        return wslot[:, o + k * kw:o + (k + 1) * kw]

    for nm in ("ln1w", "ln1b", "ln2w", "ln2b", "decay", "u", "gamma", "beta"):
        E("q", lambda q, nm=nm: q.dma_start(
            out=cols[nm][:, :], in_=h[nm][:, :]), w=[rcols], dma="pre")
    E("v", lambda v: v.memset(onescol[:, :], 1.0), w=[rones])
    E("v", lambda v: v.memset(onescolf[:, :], 1.0), w=[rones])
    E("s", lambda s: s.activation(out=ew[:, :], in_=cols["decay"][:, :],
                                  func=AF.Exp), r=[rcols], w=[rcols])
    E("s", lambda s: s.activation(out=ew[:, :], in_=ew[:, :],
                                  func=AF.Exp, scale=-1.0), r=[rcols], w=[rcols])
    E("s", lambda s: s.activation(out=eu[:, :], in_=cols["u"][:, :],
                                  func=AF.Exp), r=[rcols], w=[rcols])

    def S1(): return rows[0:1, 0:TC]
    def S2(): return rows[0:1, TC:2 * TC]
    def RX(): return rowx[0:1, 0:TC]

    def bcast_tc(src, sreg, dst, rdst, tc):
        E("t", lambda t: t.matmul(
            out=PSB[:, 0:TC], lhsT=onescolf[0:1, :], rhs=src,
            start=True, stop=True), r=[sreg, rones], w=[rpsb])
        E("v", lambda v: v.tensor_copy(
            out=dst[:, tc * TC:(tc + 1) * TC], in_=PSB[:, 0:TC]),
          r=[rpsb], w=[rdst])

    def ln_rows_tc(tc, mode):
        C = float(D)
        E("v", lambda v: v.tensor_scalar(
            out=S1(), in0=S1(), scalar1=1.0 / C, scalar2=None,
            op0=ALU.mult), r=[rrows], w=[rrows])
        bcast_tc(S1(), rrows, mub, rmub, tc)
        E("v", lambda v: v.tensor_tensor(
            out=RX(), in0=S1(), in1=S1(), op=ALU.mult),
          r=[rrows], w=[rrowx])
        E("v", lambda v: v.scalar_tensor_tensor(
            out=S2(), in0=S2(), scalar=1.0 / C, in1=RX(),
            op0=ALU.mult, op1=ALU.subtract), r=[rrows, rrowx], w=[rrows])
        E("v", lambda v: v.tensor_scalar(
            out=S2(), in0=S2(), scalar1=EPS, scalar2=None, op0=ALU.add),
          r=[rrows], w=[rrows])
        E("s", lambda s: s.activation(out=RX(), in_=S2(), func=AF.Ln),
          r=[rrows], w=[rrowx])
        E("s", lambda s: s.activation(out=RX(), in_=RX(), func=AF.Exp,
                                      scale=-0.5), r=[rrowx], w=[rrowx])
        if mode == "ln2":
            bcast_tc(RX(), rrowx, s1b, rs1b, tc)
            return
        bcast_tc(RX(), rrowx, s0b, rs0b, tc)
        if mode == "l1":
            nj = TC // 128
            for j in range(nj):
                E("g", lambda g_, j=j: g_.dma_start(
                    out=scol[:, tc * nj + j:tc * nj + j + 1],
                    in_=rowx[0:1, j * 128:(j + 1) * 128],
                    allow_slow_non_contiguous=True),
                  r=[rrowx], w=[rscol], dma="scol")
        if mode == "l0":
            E("v", lambda v: v.tensor_tensor(
                out=S1(), in0=S1(), in1=RX(), op=ALU.mult),
              r=[rrows, rrowx], w=[rrows])
            bcast_tc(S1(), rrows, m0b, rm0b, tc)
        E("v", lambda v: v.tensor_tensor(
            out=S2(), in0=S2(), in1=RX(), op=ALU.mult),
          r=[rrows, rrowx], w=[rrows])
        E("v", lambda v: v.tensor_tensor(
            out=S2(), in0=S2(), in1=RX(), op=ALU.mult),
          r=[rrows, rrowx], w=[rrows])
        E("v", lambda v: v.tensor_scalar(
            out=S2(), in0=S2(), scalar1=EPS, scalar2=None, op0=ALU.add),
          r=[rrows], w=[rrows])
        E("s", lambda s: s.activation(out=S2(), in_=S2(), func=AF.Ln),
          r=[rrows], w=[rrows])
        E("s", lambda s: s.activation(out=S2(), in_=S2(), func=AF.Exp,
                                      scale=-0.5), r=[rrows], w=[rrows])
        E("v", lambda v: v.tensor_tensor(
            out=RX(), in0=RX(), in1=S2(), op=ALU.mult),
          r=[rrows], w=[rrowx])
        bcast_tc(RX(), rrowx, s1b, rs1b, tc)
        if mode == "l1":
            E("v", lambda v: v.tensor_tensor(
                out=S1(), in0=S1(), in1=RX(), op=ALU.mult),
              r=[rrowx], w=[rrows])
            bcast_tc(S1(), rrows, m0b, rm0b, tc)

    def stream_w(wap, mi, nk, si):
        E("g", lambda g_: g_.dma_start(
            out=slot(si)[:, 0:nk * 128],
            in_=wap[mi * 128:(mi + 1) * 128, 0:nk * 128]),
          w=[rslot[si]], dma=f"w{si}")

    rslot5h = [reg("wslot.5a"), reg("wslot.5b")]
    def stream_w5(wap, mi, nk):
        hk = nk // 2
        E("g", lambda g_, hk=hk: g_.dma_start(
            out=slot(5)[:, 0:hk * 128],
            in_=wap[mi * 128:(mi + 1) * 128, 0:hk * 128]),
          w=[rslot5h[0]], dma="w5a")
        E("g", lambda g_, hk=hk: g_.dma_start(
            out=slot(5)[:, hk * 128:nk * 128],
            in_=wap[mi * 128:(mi + 1) * 128, hk * 128:nk * 128]),
          w=[rslot5h[1]], dma="w5b")


    def wb_column(si, nk, ci, ps=None, rp=None):
        ps = PSB if ps is None else ps
        rp = rpsb if rp is None else rp
        for k in range(nk):
            E("t", lambda t, k=k, ps=ps: t.matmul(
                out=ps[:, 0:1], lhsT=slot(si, k), rhs=bcolb[:, k:k + 1],
                start=(k == 0), stop=(k == nk - 1)),
              r=[rslot[si], rbcolb], w=[rp])
        E("v", lambda v, ps=ps: v.tensor_copy(
            out=wbcol[:, ci:ci + 1], in_=ps[:, 0:1]), r=[rp], w=[rwbcol])

    def scale_slot(si, nk):
        for k in range(nk):
            E("v", lambda v, k=k: v.tensor_scalar(
                out=slot(si, k), in0=slot(si, k), scalar1=wcolf[:, k:k + 1],
                scalar2=None, op0=ALU.mult),
              r=[rwbcol, rslot[si]], w=[rslot[si]])

    p0 = contextlib.ExitStack()
    sb0 = lambda name, shape, dtype=F32: p0.enter_context(
        nc.sbuf_tensor(name, shape, dtype))
    xbuf = sb0("xbuf", [128, NG * T], BF16)
    scA = sb0("scA", [128, T])
    scB = sb0("scB", [128, T])
    scC = sb0("scC", [128, T + 8])
    scD = sb0("scD", [128, T + 8])
    scE = sb0("scE", [128, T + 8])
    scF = sb0("scF", [128, T + 8])
    sigr = sb0("sigr", [128, T])
    d1t = sb0("d1t", [128, T])
    hbuf = sb0("hbuf", [128, NHG * TCB], BF16)
    gtile = sb0("gtile", [128, 2 * TCB], BF16)
    rwt = sb0("rwt", [128, T], BF16)

    rxb = [reg(f"xb.{g}") for g in range(NG)]
    rsc = {nm: reg(nm) for nm in ("scA", "scB", "scC", "scD", "scE", "scF")}
    rsigr = reg("sigr"); rd1 = reg("d1t")
    rhb = reg("hbuf")
    rgt = [reg("gtile.0"), reg("gtile.1")]
    rrwt = reg("rwt")

    for tc in range(NTC):
        for g in range(NG):
            pp = g % 2
            xfs = xf[:, pp * TC:(pp + 1) * TC]
            E("q", lambda q, g=g, tc=tc, xfs=xfs: q.dma_start(
                out=xfs, in_=h["xT"][g * 128:(g + 1) * 128,
                                     tc * TC:(tc + 1) * TC]),
              w=[rxf[pp]], dma=f"xf{pp}")
            E("s", lambda s, g=g, tc=tc, xfs=xfs: s.activation(
                out=xbuf[:, g * T + tc * TC:g * T + (tc + 1) * TC],
                in_=xfs, func=AF.Copy), r=[rxf[pp]], w=[rxb[g]])
            E("s", lambda s, xfs=xfs, pp=pp: s.activation(
                out=x2b[:, pp * TC:(pp + 1) * TC], in_=xfs, func=AF.Square),
              r=[rxf[pp]], w=[rx2b[pp]])
            E("t", lambda t, g=g, tc=tc: t.matmul(
                out=PSS[0:1, 0:TC], lhsT=onescol[:, 0:1],
                rhs=xbuf[:, g * T + tc * TC:g * T + (tc + 1) * TC],
                start=(g == 0), stop=(g == NG - 1)),
              r=[rxb[g], rones], w=[rpss])
            E("t", lambda t, g=g, pp=pp: t.matmul(
                out=PSB[0:1, 0:TC], lhsT=onescol[:, 0:1],
                rhs=x2b[:, pp * TC:(pp + 1) * TC],
                start=(g == 0), stop=(g == NG - 1)),
              r=[rx2b[pp], rones], w=[rpsb])
        E("v", lambda v: v.tensor_copy(out=S1(), in_=PSS[0:1, 0:TC]),
          r=[rpss], w=[rrows])
        E("v", lambda v: v.tensor_copy(out=S2(), in_=PSB[0:1, 0:TC]),
          r=[rpsb], w=[rrows])
        ln_rows_tc(tc, "l0")

    for g in range(NG):
        E("v", lambda v, g=g: v.tensor_tensor(
            out=xbuf[:, g * T:(g + 1) * T], in0=xbuf[:, g * T:(g + 1) * T],
            in1=mub[:, 0:T], op=ALU.subtract), r=[rmub], w=[rxb[g]])
    E("v", lambda v: v.tensor_copy(out=wcolf[:, :], in_=cols["ln1w"][:, :]),
      r=[rcols], w=[rwbcol])
    E("v", lambda v: v.tensor_copy(out=bcolb[:, :], in_=cols["ln1b"][:, :]),
      r=[rcols], w=[rbcolb])
    rhbm_xn = reg("hbm_xn")
    def xn_chunks():
        for g in range(NG):
            for tc2 in range(NTC):
                def xs(g=g, tc2=tc2):
                    pp = (g * NTC + tc2) % 2
                    E("v", lambda v, g=g, tc2=tc2, pp=pp: v.tensor_tensor(
                        out=x2b[:, pp * TC:(pp + 1) * TC],
                        in0=xbuf[:, g * T + tc2 * TC:g * T + (tc2 + 1) * TC],
                        in1=s0b[:, tc2 * TC:(tc2 + 1) * TC], op=ALU.mult),
                      r=[rxb[g], rs0b], w=[rx2b[pp]])
                    E("q", lambda q, g=g, tc2=tc2, pp=pp: q.dma_start(
                        out=sp_xn[g * 128:(g + 1) * 128,
                                  tc2 * TC:(tc2 + 1) * TC],
                        in_=x2b[:, pp * TC:(pp + 1) * TC]),
                      r=[rx2b[pp]], w=[rhbm_xn], dma="hxn")
                yield xs

    def bb_chunks():
        for tcb in range(NTCB):
            sl = slice(tcb * TCB, (tcb + 1) * TCB)
            for m in range(NHG):
                def mb(tcb=tcb, m=m, sl=sl):
                    for wi, (wap, si, bank) in enumerate(
                            ((h["wgT"], 3, 2), (h["wuT"], 4, 3))):
                        stream_w(wap, m, NG, si)
                        for k in range(NG):
                            E("t", lambda t, k=k, si=si, bank=bank, sl=sl:
                                t.matmul(
                                    out=PS[bank][:, 0:TCB], lhsT=slot(si, k),
                                    rhs=xbuf[:, k * T + sl.start:
                                             k * T + sl.stop],
                                    start=(k == 0), stop=(k == NG - 1)),
                              r=[rslot[si], rxb[k]], w=[rps[bank]])
                    pp = m % 2
                    E("v", lambda v, sl=sl, pp=pp: v.scalar_tensor_tensor(
                        out=ev1[:, pp * TC:pp * TC + TCB],
                        in0=PS[2][:, 0:TCB], scalar=1.0 / 127.0,
                        in1=s0b[:, sl], op0=ALU.mult, op1=ALU.mult),
                      r=[rps[2], rs0b], w=[rev1[pp]])
                    E("v", lambda v, sl=sl, pp=pp: v.scalar_tensor_tensor(
                        out=ev2[:, pp * TC:pp * TC + TCB],
                        in0=PS[3][:, 0:TCB], scalar=1.0 / 127.0,
                        in1=s0b[:, sl], op0=ALU.mult, op1=ALU.mult),
                      r=[rps[3], rs0b], w=[rev2[pp]])
                    E("s", lambda s, pp=pp: s.activation(
                        out=x2b[:, pp * TC:pp * TC + TCB],
                        in_=ev1[:, pp * TC:pp * TC + TCB],
                        func=AF.Tanh, scale=0.5), r=[rev1[pp]], w=[rx2b[pp]])
                    E("v", lambda v, pp=pp: v.scalar_tensor_tensor(
                        out=ev1[:, pp * TC:pp * TC + TCB],
                        in0=x2b[:, pp * TC:pp * TC + TCB], scalar=1.0,
                        in1=ev1[:, pp * TC:pp * TC + TCB],
                        op0=ALU.add, op1=ALU.mult),
                      r=[rx2b[pp], rev1[pp]], w=[rev1[pp]])
                    E("v", lambda v, m=m, pp=pp: v.tensor_tensor(
                        out=hbuf[:, m * TCB:(m + 1) * TCB],
                        in0=ev1[:, pp * TC:pp * TC + TCB],
                        in1=ev2[:, pp * TC:pp * TC + TCB], op=ALU.mult),
                      r=[rev1[pp], rev2[pp]], w=[rhb])
                yield mb
            for mo in range(NG):
                def dn(tcb=tcb, mo=mo, sl=sl):
                    stream_w5(h["wdT"], mo, NHG)
                    bank = 4 + (mo % 2)
                    for k in range(NHG):
                        E("t", lambda t, k=k, bank=bank: t.matmul(
                            out=PS[bank][:, 0:TCB], lhsT=slot(5, k),
                            rhs=hbuf[:, k * TCB:(k + 1) * TCB],
                            start=(k == 0), stop=(k == NHG - 1)),
                          r=[rslot5h[2 * k // NHG], rhb], w=[rps[bank]])
                    pp = mo % 2
                    E("s", lambda s, bank=bank, pp=pp: s.activation(
                        out=x2b[:, pp * TC:pp * TC + TCB],
                        in_=PS[bank][:, 0:TCB], func=AF.Tanh,
                        scale=0.5 / 254.0), r=[rps[bank]], w=[rx2b[pp]])
                    E("v", lambda v, bank=bank, pp=pp: v.scalar_tensor_tensor(
                        out=gtile[:, pp * TCB:(pp + 1) * TCB],
                        in0=x2b[:, pp * TC:pp * TC + TCB], scalar=1.0,
                        in1=PS[bank][:, 0:TCB], op0=ALU.add, op1=ALU.mult),
                      r=[rx2b[pp], rps[bank]], w=[rgt[pp]])
                    E("q", lambda q, mo=mo, sl=sl, pp=pp: q.dma_start(
                        out=sp_g[mo * 128:(mo + 1) * 128, sl],
                        in_=gtile[:, pp * TCB:(pp + 1) * TCB]),
                      r=[rgt[pp]], w=[rhbm_g], dma="hg")
                yield dn

    import itertools
    bbgen = itertools.chain(xn_chunks(), bb_chunks())
    bb_left = [NG * NTC + NTCB * (NHG + NG)]
    def bb_pull(n):
        for _ in range(n):
            if bb_left[0] <= 0:
                return
            next(bbgen)()
            bb_left[0] -= 1

    def rkv_group(gm):
        stream_w(h["wkT"], gm, NG, 0)
        stream_w(h["wvT"], gm, NG, 1)
        stream_w(h["wrT"], gm, NG, 2)
        for si, ci in ((0, 0), (1, 1), (2, 2)):
            wb_column(si, NG, ci)
            scale_slot(si, NG)

        def proj(si, ci, dst, rdst, bias_add=True):
            for tc in range(NTC):
                bank = tc % 2
                for k in range(NG):
                    E("t", lambda t, k=k, si=si, tc=tc, bank=bank: t.matmul(
                        out=PS[bank][:, 0:TC], lhsT=slot(si, k),
                        rhs=xbuf[:, k * T + tc * TC:k * T + (tc + 1) * TC],
                        start=(k == 0), stop=(k == NG - 1)),
                      r=[rslot[si], rxb[k]], w=[rps[bank]])
                E("v", lambda v, tc=tc, bank=bank: v.tensor_tensor(
                    out=dst[:, tc * TC:(tc + 1) * TC], in0=PS[bank][:, 0:TC],
                    in1=s1b[:, tc * TC:(tc + 1) * TC], op=ALU.mult),
                  r=[rps[bank], rs1b], w=[rdst])
                if bias_add:
                    E("v", lambda v, tc=tc, ci=ci: v.tensor_scalar(
                        out=dst[:, tc * TC:(tc + 1) * TC],
                        in0=dst[:, tc * TC:(tc + 1) * TC],
                        scalar1=wbcol[:, ci:ci + 1], scalar2=None,
                        op0=ALU.add), r=[rwbcol], w=[rdst])

        proj(0, 0, scB, rsc["scB"], bias_add=False)
        if tap == "l0_k":
            E("v", lambda v: v.tensor_scalar(
                out=scB[:, :], in0=scB[:, :], scalar1=wbcol[:, 0:1],
                scalar2=None, op0=ALU.add), r=[rwbcol], w=[rsc["scB"]])
            E("q", lambda q: q.dma_start(
                out=dbg[gm * 128:(gm + 1) * 128, :], in_=scB[:, :]),
              r=[rsc["scB"]], dma="out")
            E("s", lambda s: s.activation(out=scB[:, :], in_=scB[:, :],
                                          func=AF.Exp),
              r=[rsc["scB"]], w=[rsc["scB"]])
        else:
            E("s", lambda s: s.activation(out=scB[:, :], in_=scB[:, :],
                                          func=AF.Exp, bias=wbcol[:, 0:1]),
              r=[rsc["scB"], rwbcol], w=[rsc["scB"]])
        proj(1, 1, scA, rsc["scA"], bias_add=False)
        E("v", lambda v: v.scalar_tensor_tensor(
            out=scA[:, :], in0=scA[:, :], scalar=wbcol[:, 1:2],
            in1=scB[:, :], op0=ALU.add, op1=ALU.mult),
          r=[rsc["scB"], rwbcol], w=[rsc["scA"]])
        proj(2, 2, sigr, rsigr, bias_add=False)
        E("v", lambda v: v.tensor_scalar(
            out=wbcol[:, 3:4], in0=wbcol[:, 2:3], scalar1=0.5, scalar2=None,
            op0=ALU.mult), r=[rwbcol], w=[rwbcol])
        E("s", lambda s: s.activation(out=sigr[:, :], in_=sigr[:, :],
                                      func=AF.Tanh, scale=0.5,
                                      bias=wbcol[:, 3:4]),
          r=[rsigr, rwbcol], w=[rsigr])

        th = []
        A = th.append
        A(lambda: (E("v", lambda v: v.memset(d1t[:, :], 1.0), w=[rd1]),
                   E("v", lambda v: v.tensor_scalar(
                       out=d1t[:, :], in0=d1t[:, :], scalar1=ew[:, gm:gm + 1],
                       scalar2=None, op0=ALU.mult), r=[rcols], w=[rd1])))
        A(lambda: E("v", lambda v: v.tensor_copy(out=scE[:, 1:T + 1],
                                                 in_=scA[:, ::-1]),
                    r=[rsc["scA"]], w=[rsc["scE"]]))
        A(lambda: E("v", lambda v: v.tensor_copy(out=scF[:, 1:T + 1],
                                                 in_=scB[:, ::-1]),
                    r=[rsc["scB"]], w=[rsc["scF"]]))
        def _zc():
            for nm, tt in (("scC", scC), ("scD", scD), ("scE", scE),
                           ("scF", scF)):
                E("v", lambda v, tt=tt: v.memset(tt[:, 0:1], 0.0),
                  w=[rsc[nm]])
        A(_zc)
        A(lambda: E("v", lambda v: v.tensor_tensor_scan(
            out=scC[:, 1:T + 1], data0=d1t[:, :], data1=scA[:, :],
            initial=0.0, op0=ALU.mult, op1=ALU.add),
          r=[rd1, rsc["scA"]], w=[rsc["scC"]]))
        A(lambda: E("v", lambda v: v.tensor_tensor_scan(
            out=scD[:, 1:T + 1], data0=d1t[:, :], data1=scB[:, :],
            initial=0.0, op0=ALU.mult, op1=ALU.add),
          r=[rd1, rsc["scB"]], w=[rsc["scD"]]))
        A(lambda: E("v", lambda v: v.tensor_tensor_scan(
            out=scE[:, 1:T + 1], data0=d1t[:, :], data1=scE[:, 1:T + 1],
            initial=0.0, op0=ALU.mult, op1=ALU.add),
          r=[rd1], w=[rsc["scE"]]))
        A(lambda: E("v", lambda v: v.tensor_tensor_scan(
            out=scF[:, 1:T + 1], data0=d1t[:, :], data1=scF[:, 1:T + 1],
            initial=0.0, op0=ALU.mult, op1=ALU.add),
          r=[rd1], w=[rsc["scF"]]))
        A(lambda: E("v", lambda v: v.tensor_scalar(
            out=scA[:, :], in0=scA[:, :], scalar1=eu[:, gm:gm + 1],
            scalar2=None, op0=ALU.mult), r=[rcols], w=[rsc["scA"]]))
        A(lambda: E("v", lambda v: v.tensor_scalar(
            out=scB[:, :], in0=scB[:, :], scalar1=eu[:, gm:gm + 1],
            scalar2=None, op0=ALU.mult), r=[rcols], w=[rsc["scB"]]))
        A(lambda: E("v", lambda v: v.tensor_tensor(
            out=scC[:, 0:T], in0=scC[:, 0:T], in1=scA[:, :], op=ALU.add),
          r=[rsc["scA"]], w=[rsc["scC"]]))
        A(lambda: E("v", lambda v: v.tensor_tensor(
            out=scD[:, 0:T], in0=scD[:, 0:T], in1=scB[:, :], op=ALU.add),
          r=[rsc["scB"]], w=[rsc["scD"]]))
        A(lambda: E("v", lambda v: v.tensor_tensor(
            out=scA[:, :], in0=scA[:, :], in1=scE[:, 0:T][:, ::-1],
            op=ALU.add), r=[rsc["scE"]], w=[rsc["scA"]]))
        A(lambda: E("v", lambda v: v.tensor_tensor(
            out=scB[:, :], in0=scB[:, :], in1=scF[:, 0:T][:, ::-1],
            op=ALU.add), r=[rsc["scF"]], w=[rsc["scB"]]))
        A(lambda: E("v", lambda v: v.tensor_tensor(
            out=scA[:, :], in0=scA[:, :], in1=scD[:, 0:T], op=ALU.mult),
          r=[rsc["scD"]], w=[rsc["scA"]]))
        A(lambda: E("v", lambda v: v.tensor_tensor(
            out=scC[:, 0:T], in0=scC[:, 0:T], in1=scB[:, :], op=ALU.mult),
          r=[rsc["scB"]], w=[rsc["scC"]]))
        A(lambda: E("v", lambda v: v.tensor_tensor(
            out=scC[:, 0:T], in0=scC[:, 0:T], in1=scA[:, :], op=ALU.add),
          r=[rsc["scA"]], w=[rsc["scC"]]))
        A(lambda: E("v", lambda v: v.tensor_tensor(
            out=scD[:, 0:T], in0=scD[:, 0:T], in1=scB[:, :], op=ALU.mult),
          r=[rsc["scB"]], w=[rsc["scD"]]))
        A(lambda: E("s", lambda s: s.activation(
            out=scD[:, 0:T], in_=scD[:, 0:T], func=AF.Ln), w=[rsc["scD"]]))
        A(lambda: E("s", lambda s: s.activation(
            out=scD[:, 0:T], in_=scD[:, 0:T], func=AF.Exp, scale=-1.0),
          w=[rsc["scD"]]))
        A(lambda: E("v", lambda v: v.tensor_tensor(
            out=scC[:, 0:T], in0=scC[:, 0:T], in1=scD[:, 0:T], op=ALU.mult),
          r=[rsc["scD"]], w=[rsc["scC"]]))
        if tap == "l0_wkv":
            def _dump():
                E("v", lambda v: v.tensor_scalar(
                    out=scC[:, 0:T], in0=scC[:, 0:T], scalar1=0.5,
                    scalar2=None, op0=ALU.mult), r=[rsc["scC"]],
                  w=[rsc["scC"]])
                E("q", lambda q: q.dma_start(
                    out=dbg[gm * 128:(gm + 1) * 128, :], in_=scC[:, 0:T]),
                  r=[rsc["scC"]], dma="out")
            A(_dump)
            return th
        def _rw():
            E("v", lambda v: v.scalar_tensor_tensor(
                out=rwt[:, :], in0=sigr[:, :], scalar=1.0, in1=scC[:, 0:T],
                op0=ALU.add, op1=ALU.mult), r=[rsigr, rsc["scC"]], w=[rrwt])
            E("q", lambda q: q.dma_start(
                out=sp_rw[gm * 128:(gm + 1) * 128, :], in_=rwt[:, :]),
              r=[rrwt], w=[rhbm_rw], dma="hrw")
        A(_rw)
        return th

    BBTOT = NG * NTC + NTCB * (NHG + NG)
    pulled = 0
    for gm in range(NG):
        th = rkv_group(gm)
        quota = (gm + 1) * BBTOT // NG - pulled
        n = max(quota, len(th))
        ti = 0
        pi = 0
        for i in range(n):
            while pi * n < quota * (i + 1) and pi < quota:
                bb_pull(1)
                pi += 1
                pulled += 1
            while ti * n <= i * len(th) and ti < len(th):
                th[ti]()
                ti += 1
        while ti < len(th):
            th[ti]()
            ti += 1
    bb_pull(10 ** 9)

    b.barrier()
    p0.close()

    if tap in ("l0_k", "l0_wkv"):
        blk_ctx.__exit__(None, None, None)
        es.close()
        return nc, b

    p1 = contextlib.ExitStack()
    sb1 = lambda name, shape, dtype=F32: p1.enter_context(
        nc.sbuf_tensor(name, shape, dtype))
    x1 = sb1("x1", [128, NG * T])
    xb1 = sb1("xb1", [128, NG * T], BF16)
    rxb1 = [reg(f"xb1.{g}") for g in range(NG)]
    rx1t = [[reg(f"x1.{g}.{t_}") for t_ in range(NTC)] for g in range(NG)]
    rx1 = [reg(f"x1.{g}") for g in range(NG)]

    p1b = contextlib.ExitStack()
    sb1b = lambda name, shape, dtype=F32: p1b.enter_context(
        nc.sbuf_tensor(name, shape, dtype))
    kkb = sb1b("kkb", [128, NFG * TC], BF16)
    rwring = sb1b("rwring", [128, 2 * NG * TC], BF16)
    xmring = rwring
    g1ring = sb1b("g1ring", [128, 2 * TC], BF16)
    xcring = wslot[:, 3072:3072 + NG * TC]

    rkk = [reg(f"kk.{i}") for i in range(NFG)]
    rrwr = [reg("rwring.0"), reg("rwring.1")]
    rxmr = rrwr
    rg1 = [reg("g1ring.0"), reg("g1ring.1")]
    rxcr = reg("xcring")

    for tc in range(NTC):
        ppr = tc % 2
        E("q", lambda q, tc=tc, ppr=ppr: q.dma_start(
            out=rwring[:, ppr * NG * TC:(ppr + 1) * NG * TC].rearrange(
                "p (k c) -> p k c", k=NG),
            in_=sp_rw[:, tc * TC:(tc + 1) * TC].rearrange(
                "(k p) c -> p k c", p=128)), r=[rhbm_rw], w=[rrwr[ppr]], dma=f"rr{ppr}")
        for mo in range(NG):
            si = mo % 2
            stream_w(h["woT"], mo, NG, si)
            bank = 2 + (mo % 2)
            for k in range(NG):
                E("t", lambda t, k=k, si=si, bank=bank, ppr=ppr: t.matmul(
                    out=PS[bank][:, 0:TC], lhsT=slot(si, k),
                    rhs=rwring[:, ppr * NG * TC + k * TC:
                               ppr * NG * TC + (k + 1) * TC],
                    start=(k == 0), stop=(k == NG - 1)),
                  r=[rslot[si], rrwr[ppr]], w=[rps[bank]])
            pp = mo % 2
            E("q", lambda q, mo=mo, tc=tc, pp=pp: q.dma_start(
                out=g1ring[:, pp * TC:(pp + 1) * TC],
                in_=sp_xn[mo * 128:(mo + 1) * 128, tc * TC:(tc + 1) * TC]),
              r=[rhbm_xn], w=[rg1[pp]], dma=f"g1{pp}")
            E("v", lambda v, pp=pp, bank=bank: v.scalar_tensor_tensor(
                out=ev1[:, pp * TC:(pp + 1) * TC], in0=PS[bank][:, 0:TC],
                scalar=0.25, in1=g1ring[:, pp * TC:(pp + 1) * TC],
                op0=ALU.mult, op1=ALU.add),
              r=[rps[bank], rg1[pp]], w=[rev1[pp]])
            E("s", lambda s, pp=pp: s.activation(
                out=x2b[:, pp * TC:(pp + 1) * TC],
                in_=ev1[:, pp * TC:(pp + 1) * TC], func=AF.Copy),
              r=[rev1[pp]], w=[rx2b[pp]])
            E("q", lambda q, mo=mo, tc=tc, pp=pp: q.dma_start(
                out=sp_xm[mo * 128:(mo + 1) * 128, tc * TC:(tc + 1) * TC],
                in_=x2b[:, pp * TC:(pp + 1) * TC]),
              r=[rx2b[pp]], w=[rhbm_xm], dma="hxm")
            E("t", lambda t, mo=mo, pp=pp: t.matmul(
                out=PSS[0:1, 0:TC], lhsT=onescol[:, 0:1],
                rhs=x2b[:, pp * TC:(pp + 1) * TC],
                start=(mo == 0), stop=(mo == NG - 1)),
              r=[rx2b[pp], rones], w=[rpss])
            E("s", lambda s, pp=pp: s.activation(
                out=x2b[:, pp * TC:(pp + 1) * TC],
                in_=ev1[:, pp * TC:(pp + 1) * TC], func=AF.Square),
              r=[rev1[pp], rx2b[pp]], w=[rx2b[pp]])
            E("t", lambda t, mo=mo, pp=pp: t.matmul(
                out=PSB[0:1, 0:TC], lhsT=onescol[:, 0:1],
                rhs=x2b[:, pp * TC:(pp + 1) * TC],
                start=(mo == 0), stop=(mo == NG - 1)),
              r=[rx2b[pp], rones], w=[rpsb])
        E("v", lambda v: v.tensor_copy(out=S1(), in_=PSS[0:1, 0:TC]),
          r=[rpss], w=[rrows])
        E("v", lambda v: v.tensor_copy(out=S2(), in_=PSB[0:1, 0:TC]),
          r=[rpsb], w=[rrows])
        ln_rows_tc(tc, "ln2")

    if tap == "l0_g":
        for g in range(NG):
            for tc in range(NTC):
                pp = (g * NTC + tc) % 2
                E("g", lambda q, g=g, tc=tc, pp=pp: q.dma_start(
                    out=ev1[:, pp * TC:(pp + 1) * TC],
                    in_=sp_g[g * 128:(g + 1) * 128, tc * TC:(tc + 1) * TC]),
                  r=[reg("hbm_g")], w=[rev1[pp]], dma=f"dbr{pp}")
                E("v", lambda v, pp=pp: v.tensor_copy(
                    out=ev2[:, pp * TC:(pp + 1) * TC],
                    in_=ev1[:, pp * TC:(pp + 1) * TC]),
                  r=[rev1[pp]], w=[rev2[pp]])
                E("q", lambda q, g=g, tc=tc, pp=pp: q.dma_start(
                    out=dbg[g * 128:(g + 1) * 128, tc * TC:(tc + 1) * TC],
                    in_=ev2[:, pp * TC:(pp + 1) * TC]),
                  r=[rev2[pp]], dma=f"dbw{pp}")

    if tap == "l0_xmid":
        for g in range(NG):
            for tc in range(NTC):
                pp = (g * NTC + tc) % 2
                E("g", lambda q, g=g, tc=tc, pp=pp: q.dma_start(
                    out=ev1[:, pp * TC:(pp + 1) * TC],
                    in_=sp_xm[g * 128:(g + 1) * 128, tc * TC:(tc + 1) * TC]),
                  r=[rhbm_xm], w=[rev1[pp]], dma=f"dbr{pp}")
                E("v", lambda v, pp=pp: v.tensor_copy(
                    out=ev2[:, pp * TC:(pp + 1) * TC],
                    in_=ev1[:, pp * TC:(pp + 1) * TC]),
                  r=[rev1[pp]], w=[rev2[pp]])
                E("q", lambda q, g=g, tc=tc, pp=pp: q.dma_start(
                    out=dbg[g * 128:(g + 1) * 128, tc * TC:(tc + 1) * TC],
                    in_=ev2[:, pp * TC:(pp + 1) * TC]),
                  r=[rev2[pp]], dma=f"dbw{pp}")

    E("v", lambda v: v.tensor_copy(out=wcolf[:, :], in_=cols["ln2w"][:, :]),
      r=[rcols], w=[rwbcol])
    E("v", lambda v: v.tensor_copy(out=bcolb[:, :], in_=cols["ln2b"][:, :]),
      r=[rcols], w=[rbcolb])

    for tc in range(NTC):
        ppx = tc % 2
        E("q", lambda q, tc=tc, ppx=ppx: q.dma_start(
            out=xmring[:, ppx * NG * TC:(ppx + 1) * NG * TC].rearrange(
                "p (k c) -> p k c", k=NG),
            in_=sp_xm[:, tc * TC:(tc + 1) * TC].rearrange(
                "(k p) c -> p k c", p=128)), r=[rhbm_xm], w=[rxmr[ppx]], dma=f"xm{ppx}")
        def xm(k, ppx=ppx):
            return xmring[:, ppx * NG * TC + k * TC:ppx * NG * TC + (k + 1) * TC]
        def xc(k):
            return xcring[:, k * TC:(k + 1) * TC]
        for k in range(NG):
            E("v", lambda v, a=ev1[:, 0:TC], b_=xm(k),
                   c=mub[:, tc * TC:(tc + 1) * TC]:
                v.scalar_tensor_tensor(out=a, in0=c, scalar=-1.0, in1=b_,
                                       op0=ALU.mult, op1=ALU.add),
              r=[rxmr[ppx], rmub], w=[rev1[0]])
            E("v", lambda v, tc=tc: v.tensor_tensor(
                out=ev1[:, 0:TC], in0=ev1[:, 0:TC],
                in1=s1b[:, tc * TC:(tc + 1) * TC], op=ALU.mult),
              r=[rs1b], w=[rev1[0]])
            E("v", lambda v, a=xc(k), k=k: v.tensor_scalar(
                out=a, in0=ev1[:, 0:TC], scalar1=wcolf[:, k:k + 1],
                scalar2=None, op0=ALU.mult), r=[rev1[0], rwbcol], w=[rxcr])
        for mf in range(NFG):
            si = mf % 2
            stream_w(h["wfkT"], mf, NG, si)
            if tc == 0:
                wb_column(si, NG, 8 + mf, ps=PSS, rp=rpss)
            bank = mf % 2
            for k in range(NG):
                E("t", lambda t, k=k, si=si, bank=bank, a=xc(k): t.matmul(
                    out=PS[bank][:, 0:TC], lhsT=slot(si, k), rhs=a,
                    start=(k == 0), stop=(k == NG - 1)),
                  r=[rslot[si], rxcr], w=[rps[bank]])
            pp = mf % 2
            E("v", lambda v, pp=pp, mf=mf, bank=bank: v.tensor_scalar(
                out=ev1[:, pp * TC:(pp + 1) * TC],
                in0=PS[bank][:, 0:TC],
                scalar1=wbcol[:, 8 + mf:9 + mf],
                scalar2=0.0, op0=ALU.add, op1=ALU.max),
              r=[rps[bank], rwbcol], w=[rev1[pp]])
            E("s", lambda s, mf=mf, pp=pp: s.activation(
                out=kkb[:, mf * TC:(mf + 1) * TC],
                in_=ev1[:, pp * TC:(pp + 1) * TC], func=AF.Square),
              r=[rev1[pp]], w=[rkk[mf]])
        for mo in range(NG):
            si = 2
            stream_w(h["wfrT"], mo, NG, si)
            if tc == 0:
                wb_column(si, NG, 40 + mo, ps=PSS, rp=rpss)
                E("v", lambda v, mo=mo: v.tensor_scalar(
                    out=wbcol[:, 40 + mo:41 + mo],
                    in0=wbcol[:, 40 + mo:41 + mo], scalar1=0.5,
                    scalar2=None, op0=ALU.mult), r=[rwbcol], w=[rwbcol])
            bank = 2
            for k in range(NG):
                E("t", lambda t, k=k, si=si, bank=bank, a=xc(k): t.matmul(
                    out=PS[bank][:, 0:TC], lhsT=slot(si, k), rhs=a,
                    start=(k == 0), stop=(k == NG - 1)),
                  r=[rslot[si], rxcr], w=[rps[bank]])
            pp2 = mo % 2
            E("s", lambda s, pp2=pp2, mo=mo, bank=bank: s.activation(
                out=ev2[:, pp2 * TC:(pp2 + 1) * TC],
                in_=PS[bank][:, 0:TC], func=AF.Tanh,
                scale=0.5, bias=wbcol[:, 40 + mo:41 + mo]),
              r=[rps[bank], rwbcol], w=[rev2[pp2]])
            stream_w5(h["wfvT"], mo, NFG)
            bank2 = 4 + (mo % 2)
            for k in range(NFG):
                E("t", lambda t, k=k, bank2=bank2: t.matmul(
                    out=PS[bank2][:, 0:TC], lhsT=slot(5, k),
                    rhs=kkb[:, k * TC:(k + 1) * TC],
                    start=(k == 0), stop=(k == NFG - 1)),
                  r=[rslot5h[2 * k // NFG], rkk[k]], w=[rps[bank2]])
            pp = mo % 2
            xfs = xf[:, pp * TC:(pp + 1) * TC]
            E("q", lambda q, mo=mo, tc=tc, xfs=xfs: q.dma_start(
                out=xfs, in_=h["xT"][mo * 128:(mo + 1) * 128,
                                     tc * TC:(tc + 1) * TC]),
              w=[rxf[pp]], dma=f"xf{pp}")
            E("q", lambda q, mo=mo, tc=tc, pp=pp: q.dma_start(
                out=g1ring[:, pp * TC:(pp + 1) * TC],
                in_=sp_g[mo * 128:(mo + 1) * 128, tc * TC:(tc + 1) * TC]),
              r=[rhbm_g], w=[rg1[pp]], dma=f"g1{pp}")
            E("v", lambda v, pp2=pp2, bank2=bank2: v.scalar_tensor_tensor(
                out=ev2[:, pp2 * TC:(pp2 + 1) * TC],
                in0=ev2[:, pp2 * TC:(pp2 + 1) * TC], scalar=1.0,
                in1=PS[bank2][:, 0:TC], op0=ALU.add, op1=ALU.mult),
              r=[rps[bank2]], w=[rev2[pp2]])
            E("v", lambda v, pp2=pp2, a=xm(mo): v.scalar_tensor_tensor(
                out=ev2[:, pp2 * TC:(pp2 + 1) * TC],
                in0=ev2[:, pp2 * TC:(pp2 + 1) * TC], scalar=0.5,
                in1=a, op0=ALU.mult, op1=ALU.add),
              r=[rxmr[ppx]], w=[rev2[pp2]])
            E("v", lambda v, pp2=pp2, pp=pp: v.tensor_tensor(
                out=ev2[:, pp2 * TC:(pp2 + 1) * TC],
                in0=ev2[:, pp2 * TC:(pp2 + 1) * TC],
                in1=g1ring[:, pp * TC:(pp + 1) * TC], op=ALU.mult),
              r=[rg1[pp]], w=[rev2[pp2]])
            E("v", lambda v, pp2=pp2, mo=mo, tc=tc, pp=pp:
                v.scalar_tensor_tensor(
                    out=x1[:, mo * T + tc * TC:mo * T + (tc + 1) * TC],
                    in0=ev2[:, pp2 * TC:(pp2 + 1) * TC], scalar=1.0 / 508.0,
                    in1=xf[:, pp * TC:(pp + 1) * TC], op0=ALU.mult,
                    op1=ALU.add), r=[rev2[pp2], rxf[pp]], w=[rx1t[mo][tc]])
            E("s", lambda s, mo=mo, tc=tc, pp=pp: s.activation(
                out=x2b[:, pp * TC:(pp + 1) * TC],
                in_=x1[:, mo * T + tc * TC:mo * T + (tc + 1) * TC],
                func=AF.Copy), r=[rx1t[mo][tc]], w=[rx2b[pp]])
            E("t", lambda t, mo=mo, pp=pp: t.matmul(
                out=PSB[0:1, 0:TC], lhsT=onescol[:, 0:1],
                rhs=x2b[:, pp * TC:(pp + 1) * TC],
                start=(mo == 0), stop=(mo == NG - 1)),
              r=[rx2b[pp], rones], w=[rpsb])
            E("s", lambda s, mo=mo, tc=tc, pp=pp: s.activation(
                out=x2b[:, pp * TC:(pp + 1) * TC],
                in_=x1[:, mo * T + tc * TC:mo * T + (tc + 1) * TC],
                func=AF.Square), r=[rx1t[mo][tc]], w=[rx2b[pp]])
            E("t", lambda t, mo=mo, pp=pp: t.matmul(
                out=PS[3][0:1, 0:TC], lhsT=onescol[:, 0:1],
                rhs=x2b[:, pp * TC:(pp + 1) * TC],
                start=(mo == 0), stop=(mo == NG - 1)),
              r=[rx2b[pp], rones], w=[rps[3]])
        E("v", lambda v: v.tensor_copy(out=S1(), in_=PSB[0:1, 0:TC]),
          r=[rpsb], w=[rrows])
        E("v", lambda v: v.tensor_copy(out=S2(), in_=PS[3][0:1, 0:TC]),
          r=[rps[3]], w=[rrows])
        ln_rows_tc(tc, "l1")
        for mo in range(NG):
            E("v", lambda v, mo=mo, tc=tc: v.tensor_tensor(
                out=xb1[:, mo * T + tc * TC:mo * T + (tc + 1) * TC],
                in0=x1[:, mo * T + tc * TC:mo * T + (tc + 1) * TC],
                in1=mub[:, tc * TC:(tc + 1) * TC], op=ALU.subtract),
              r=[rx1t[mo][tc], rmub], w=[rxb1[mo]])

    if tap == "l0_out":
        for g in range(NG):
            E("q", lambda q, g=g: q.dma_start(
                out=dbg[g * 128:(g + 1) * 128, :],
                in_=x1[:, g * T:(g + 1) * T]), r=list(rx1t[g]), dma="out")

    b.barrier()
    p1b.close()

    p2 = contextlib.ExitStack()
    sb2 = lambda name, shape, dtype=F32: p2.enter_context(
        nc.sbuf_tensor(name, shape, dtype))
    qpair = sb2("qpair", [128, T], BF16)
    kpair = sb2("kpair", [128, T], BF16)
    vpair = sb2("vpair", [128, NT2 * 130], BF16)
    attw = sb2("attw", [128, 4 * 512], BF16)
    p64 = sb2("p64", [65, 512])
    p64b = sb2("p64b", [65, 512], BF16)
    ones64 = sb2("ones64", [65, 64], BF16)
    stg = sb2("stg", [64, 2 * 2 * TC], BF16)
    rtile = sb2("rtile", [64, 512], BF16)
    masks = sb2("masksb", [128, (TC // 128) * TC], BF16)
    E("g", lambda g_: g_.dma_start(
        out=masks[:, :], in_=h["maskc"][:, :]), w=[rmask], dma="prem")

    rqp = reg("qpair"); rkp = reg("kpair"); rvp = reg("vpair")
    ratt = [reg(f"attw.{i}") for i in range(4)]
    rp64 = reg("p64"); rp64b = reg("p64b"); ro64 = reg("ones64")
    rstg = [reg("stg.0"), reg("stg.1")]
    rrt = reg("rtile")

    E("v", lambda v: v.memset(ones64[:, :], 1.0), w=[ro64])

    def prework():
            for g in range(NG):
                for tc in range(NTC):
                    pp = tc % 2
                    E("v", lambda v, g=g, tc=tc, pp=pp: v.tensor_tensor(
                        out=ev1[:, pp * TC:(pp + 1) * TC],
                        in0=x1[:, g * T + tc * TC:g * T + (tc + 1) * TC],
                        in1=s1b[:, tc * TC:(tc + 1) * TC], op=ALU.mult),
                      r=[rx1t[g][tc], rs1b], w=[rev1[pp]])
                    E("v", lambda v, tc=tc, pp=pp: v.tensor_tensor(
                        out=ev1[:, pp * TC:(pp + 1) * TC],
                        in0=ev1[:, pp * TC:(pp + 1) * TC],
                        in1=m0b[:, tc * TC:(tc + 1) * TC], op=ALU.subtract),
                      r=[rm0b], w=[rev1[pp]])
                    E("v", lambda v, g=g, pp=pp: v.tensor_scalar(
                        out=ev1[:, pp * TC:(pp + 1) * TC],
                        in0=ev1[:, pp * TC:(pp + 1) * TC],
                        scalar1=cols["gamma"][:, g:g + 1],
                        scalar2=cols["beta"][:, g:g + 1], op0=ALU.mult, op1=ALU.add),
                      r=[rcols], w=[rev1[pp]])
                    E("v", lambda v, g=g, tc=tc, pp=pp: v.tensor_reduce(
                        out=mxc[:, g * NTC + tc:g * NTC + tc + 1],
                        in_=ev1[:, pp * TC:(pp + 1) * TC], axis=AXX,
                        op=ALU.max, apply_absolute_value=True),
                      r=[rev1[pp]], w=[rmxc])
                    yield
            E("v", lambda v: v.tensor_reduce(
                out=mxc[:, 33:34], in_=mxc[:, 0:NG * NTC], axis=AXX,
                op=ALU.max, apply_absolute_value=False), r=[rmxc], w=[rmxc])
            E("g", lambda g_: g_.dma_start(out=p64[0:1, 0:128], in_=mxc[:, 33:34], allow_slow_non_contiguous=True),
              r=[rmxc], w=[rp64], dma="p64s")
            E("v", lambda v: v.tensor_reduce(
                out=p64[0:1, 128:129], in_=p64[0:1, 0:128], axis=AXX,
                op=ALU.max, apply_absolute_value=False), r=[rp64], w=[rp64])
            E("g", lambda g_: g_.dma_start(out=cc_in[0:1, 0:1],
                                           in_=p64[0:1, 128:129]),
              r=[rp64], w=[reg("hbm_cc")], dma="p64s")
            E("g", lambda g_: g_.collective_compute(
                "AllReduce", ALU.max, replica_groups=[list(range(cores))],
                ins=[cc_in[0:1, 0:1]], outs=[cc_out[0:1, 0:1]]),
              r=[reg("hbm_cc")], w=[rcc], dma="cc", inc=1)
            E("g", lambda g_: g_.dma_start(out=p64[0:1, 129:130],
                                           in_=cc_out[0:1, 0:1]),
              r=[rcc], w=[rp64], dma="p64s")
            E("v", lambda v: v.tensor_scalar(
                out=p64[0:1, 130:131], in0=p64[0:1, 129:130], scalar1=1.0 / 127.0,
                scalar2=None, op0=ALU.mult), r=[rp64], w=[rp64])
            E("v", lambda v: v.reciprocal(out=p64[0:1, 131:132],
                                          in_=p64[0:1, 130:131]), r=[rp64], w=[rp64])
            E("t", lambda t: t.matmul(
                out=PSB[:, 0:2], lhsT=onescolf[0:1, :], rhs=p64[0:1, 130:132],
                start=True, stop=True), r=[rp64, rones], w=[rpsb])
            E("v", lambda v: v.tensor_copy(out=mxc[:, 34:36], in_=PSB[:, 0:2]),
              r=[rpsb], w=[rmxc])

            for kq in range(NG):
                for c2 in range(NW):
                    pp = (kq * NW + c2) % 2
                    E("q", lambda q, kq=kq, c2=c2, pp=pp: q.dma_start(
                        out=xf[:, pp * TC:pp * TC + WQC],
                        in_=h["wT"][kq * 128:(kq + 1) * 128, c2 * WQC:(c2 + 1) * WQC]),
                      w=[rxf[pp]], dma=f"xf{pp}")
                    E("v", lambda v, kq=kq, c2=c2, pp=pp: v.tensor_reduce(
                        out=mxc[:, 40 + kq * NW + c2:41 + kq * NW + c2],
                        in_=xf[:, pp * TC:pp * TC + WQC], axis=AXX,
                        op=ALU.add, apply_absolute_value=True),
                      r=[rxf[pp]], w=[rmxc])
                    yield
            E("v", lambda v: v.tensor_reduce(
                out=mxc[:, 38:39], in_=mxc[:, 40:40 + NG * NW], axis=AXX,
                op=ALU.add), r=[rmxc], w=[rmxc])
            E("g", lambda g_: g_.dma_start(out=p64[0:1, 0:128], in_=mxc[:, 38:39], allow_slow_non_contiguous=True),
              r=[rmxc], w=[rp64], dma="p64s")
            E("v", lambda v: v.tensor_reduce(
                out=p64[0:1, 132:133], in_=p64[0:1, 0:128], axis=AXX, op=ALU.add),
              r=[rp64], w=[rp64])
            E("v", lambda v: v.tensor_scalar(
                out=p64[0:1, 133:134], in0=p64[0:1, 132:133],
                scalar1=0.7 / float(D * D), scalar2=None, op0=ALU.mult),
              r=[rp64], w=[rp64])
            E("t", lambda t: t.matmul(
                out=PSB[:, 0:1], lhsT=onescolf[0:1, :], rhs=p64[0:1, 133:134],
                start=True, stop=True), r=[rp64, rones], w=[rpsb])
            E("v", lambda v: v.tensor_copy(out=mxc[:, 36:37], in_=PSB[:, 0:1]),
              r=[rpsb], w=[rmxc])
            for kq in range(NG):
                for c2 in range(NW):
                    pp = (kq * NW + c2) % 2
                    E("q", lambda q, kq=kq, c2=c2, pp=pp: q.dma_start(
                        out=xf[:, pp * TC:pp * TC + WQC],
                        in_=h["wT"][kq * 128:(kq + 1) * 128, c2 * WQC:(c2 + 1) * WQC]),
                      w=[rxf[pp]], dma=f"xf{pp}")
                    E("s", lambda s, pp=pp: s.activation(
                        out=ev1[:, pp * TC:pp * TC + WQC],
                        in_=xf[:, pp * TC:pp * TC + WQC], func=AF.Abs),
                      r=[rxf[pp]], w=[rev1[pp]])
                    E("v", lambda v, pp=pp: v.tensor_scalar(
                        out=ev2[:, pp * TC:pp * TC + WQC],
                        in0=ev1[:, pp * TC:pp * TC + WQC], scalar1=mxc[:, 36:37],
                        scalar2=None, op0=ALU.is_gt), r=[rev1[pp], rmxc],
                      w=[rev2[pp]])
                    E("v", lambda v, kq=kq, c2=c2, pp=pp: v.tensor_reduce(
                        out=mxc[:, 40 + kq * NW + c2:41 + kq * NW + c2],
                        in_=ev2[:, pp * TC:pp * TC + WQC], axis=AXX, op=ALU.add),
                      r=[rev2[pp]], w=[rmxc])
                    E("v", lambda v, pp=pp: v.tensor_tensor(
                        out=ev1[:, pp * TC:pp * TC + WQC],
                        in0=ev1[:, pp * TC:pp * TC + WQC],
                        in1=ev2[:, pp * TC:pp * TC + WQC], op=ALU.mult),
                      r=[rev2[pp]], w=[rev1[pp]])
                    E("v", lambda v, kq=kq, c2=c2, pp=pp: v.tensor_reduce(
                        out=mxc[:, MXB + kq * NW + c2:MXB + 1 + kq * NW + c2],
                        in_=ev1[:, pp * TC:pp * TC + WQC], axis=AXX, op=ALU.add),
                      r=[rev1[pp]], w=[rmxc])
                    yield
            E("v", lambda v: v.tensor_reduce(
                out=mxc[:, 38:39], in_=mxc[:, 40:40 + NG * NW], axis=AXX,
                op=ALU.add), r=[rmxc], w=[rmxc])
            E("v", lambda v: v.tensor_reduce(
                out=mxc[:, 39:40], in_=mxc[:, MXB:MXB + NG * NW], axis=AXX,
                op=ALU.add), r=[rmxc], w=[rmxc])
            E("g", lambda g_: g_.dma_start(out=p64[0:1, 0:128], in_=mxc[:, 38:39], allow_slow_non_contiguous=True),
              r=[rmxc], w=[rp64], dma="p64s")
            E("g", lambda g_: g_.dma_start(out=p64[0:1, 128:256], in_=mxc[:, 39:40], allow_slow_non_contiguous=True),
              r=[rmxc], w=[rp64], dma="p64s")
            E("v", lambda v: v.tensor_reduce(
                out=p64[0:1, 256:257], in_=p64[0:1, 0:128], axis=AXX, op=ALU.add),
              r=[rp64], w=[rp64])
            E("v", lambda v: v.tensor_reduce(
                out=p64[0:1, 257:258], in_=p64[0:1, 128:256], axis=AXX, op=ALU.add),
              r=[rp64], w=[rp64])
            E("v", lambda v: v.tensor_scalar(
                out=p64[0:1, 256:257], in0=p64[0:1, 256:257], scalar1=1.0,
                scalar2=None, op0=ALU.max), r=[rp64], w=[rp64])
            E("v", lambda v: v.reciprocal(out=p64[0:1, 256:257],
                                          in_=p64[0:1, 256:257]), r=[rp64], w=[rp64])
            E("v", lambda v: v.tensor_tensor(
                out=p64[0:1, 257:258], in0=p64[0:1, 257:258], in1=p64[0:1, 256:257],
                op=ALU.mult), r=[rp64], w=[rp64])
            E("t", lambda t: t.matmul(
                out=PSB[:, 0:1], lhsT=onescolf[0:1, :], rhs=p64[0:1, 257:258],
                start=True, stop=True), r=[rp64, rones], w=[rpsb])
            E("v", lambda v: v.tensor_copy(out=mxc[:, 37:38], in_=PSB[:, 0:1]),
              r=[rpsb], w=[rmxc])
            yield
    _pw = prework()
    _pw_done = [False]
    def pw_pull(n):
        for _ in range(n):
            if _pw_done[0]:
                return
            try:
                next(_pw)
            except StopIteration:
                _pw_done[0] = True
                return

    for pair in range(NPAIR):
        pw_pull(12)
        stream_w(h["q_T"], pair, NG, 0)
        stream_w(h["k_T"], pair, NG, 1)
        for tc in range(NTC):
            for si, dst, rdst in ((0, qpair, rqp), (1, kpair, rkp)):
                bank = si
                for k in range(NG):
                    E("t", lambda t, k=k, si=si, tc=tc, bank=bank: t.matmul(
                        out=PS[bank][:, 0:TC], lhsT=slot(si, k),
                        rhs=xb1[:, k * T + tc * TC:k * T + (tc + 1) * TC],
                        start=(k == 0), stop=(k == NG - 1)),
                      r=[rslot[si], rxb1[k]], w=[rps[bank]])
                E("v", lambda v, tc=tc, bank=bank, dst=dst, rdst=rdst:
                    v.tensor_tensor(
                        out=dst[:, tc * TC:(tc + 1) * TC],
                        in0=PS[bank][:, 0:TC],
                        in1=s0b[:, tc * TC:(tc + 1) * TC], op=ALU.mult),
                  r=[rps[bank], rs0b], w=[rdst])
        stream_w(h["v_T"], pair, NG, 2)
        for t2 in range(NT2):
            for k in range(NG):
                E("t", lambda t, k=k, t2=t2: t.matmul(
                    out=PS[4][:, 0:128],
                    lhsT=xb1[:, k * T + t2 * 128:k * T + (t2 + 1) * 128],
                    rhs=slot(2, k), start=(k == 0), stop=(k == NG - 1)),
                  r=[rslot[2], rxb1[k]], w=[rps[4]])
            for half in range(2):
                E("v", lambda v, t2=t2, half=half: v.tensor_scalar(
                    out=vpair[:, t2 * 130 + half * 65:
                              t2 * 130 + half * 65 + 64],
                    in0=PS[4][:, half * 64:(half + 1) * 64],
                    scalar1=scol[:, t2:t2 + 1], scalar2=None, op0=ALU.mult),
                  r=[rps[4], rscol], w=[rvp])
            E("v", lambda v, t2=t2: v.memset(
                vpair[:, t2 * 130 + 64:t2 * 130 + 65], 1.0), w=[rvp])
            E("v", lambda v, t2=t2: v.memset(
                vpair[:, t2 * 130 + 129:t2 * 130 + 130], 1.0), w=[rvp])
        for tc in range(NTC):
            ppg = tc % 2
            nk2 = (tc + 1) * (TC // 128)
            for k2 in range(nk2):
                for half in range(2):
                    hb = half * 64
                    sbank = half
                    E("t", lambda t, k2=k2, hb=hb, tc=tc, sbank=sbank:
                        t.matmul(
                            out=PS[sbank][:, 0:TC],
                            lhsT=kpair[hb:hb + 64, k2 * 128:(k2 + 1) * 128],
                            rhs=qpair[hb:hb + 64, tc * TC:(tc + 1) * TC],
                            start=True, stop=True),
                      r=[rkp, rqp], w=[rps[sbank]])
                    dj = k2 * 128 - tc * TC
                    if dj >= 0:
                        ji = dj // 128
                        E("v", lambda v, ji=ji, sbank=sbank: v.tensor_tensor(
                            out=PS[sbank][:, 0:TC], in0=PS[sbank][:, 0:TC],
                            in1=masks[:, ji * TC:(ji + 1) * TC], op=ALU.add),
                          r=[rmask], w=[rps[sbank]])
                    slot_a = (k2 % 2) * 2 + half
                    E("s", lambda s, sbank=sbank, slot_a=slot_a: s.activation(
                        out=attw[:, slot_a * 512:slot_a * 512 + TC],
                        in_=PS[sbank][:, 0:TC], func=AF.Exp, scale=0.125),
                      r=[rps[sbank]], w=[ratt[slot_a]])
                    E("t", lambda t, k2=k2, half=half, slot_a=slot_a,
                           nk2=nk2: t.matmul(
                            out=PS[4 + half][0:65, 0:TC],
                            lhsT=vpair[:, k2 * 130 + half * 65:
                                       k2 * 130 + half * 65 + 65],
                            rhs=attw[:, slot_a * 512:slot_a * 512 + TC],
                            start=(k2 == 0), stop=(k2 == nk2 - 1)),
                      r=[rvp, ratt[slot_a]], w=[rps[4 + half]])
            for half in range(2):
                E("s", lambda s, half=half: s.activation(
                    out=p64[64:65, half * 256 if False else 0:TC]
                    if False else p64[64:65, 0:TC],
                    in_=PS[4 + half][64:65, 0:TC], func=AF.Ln),
                  r=[rps[4 + half]], w=[rp64])
                E("s", lambda s: s.activation(
                    out=p64b[64:65, 0:TC], in_=p64[64:65, 0:TC],
                    func=AF.Exp, scale=-1.0), r=[rp64], w=[rp64b])
                E("t", lambda t, half=half: t.matmul(
                    out=PS[2 + half][0:64, 0:TC], lhsT=ones64[64:65, 0:64],
                    rhs=p64b[64:65, 0:TC], start=True, stop=True),
                  r=[rp64b, ro64], w=[rps[2 + half]])
                E("v", lambda v, half=half: v.tensor_copy(
                    out=rtile[0:64, 0:TC], in_=PS[2 + half][0:64, 0:TC]),
                  r=[rps[2 + half]], w=[rrt])
                E("v", lambda v, half=half, ppg=ppg: v.tensor_tensor(
                    out=stg[0:64, ppg * 2 * TC + half * TC:
                            ppg * 2 * TC + half * TC + TC],
                    in0=PS[4 + half][0:64, 0:TC], in1=rtile[0:64, 0:TC],
                    op=ALU.mult), r=[rps[4 + half], rrt], w=[rstg[ppg]])
            E("q", lambda q, pair=pair, tc=tc, ppg=ppg: q.dma_start(
                out=sp_rw[pair * 128:(pair + 1) * 128,
                          tc * TC:(tc + 1) * TC].rearrange(
                    "(hh p) c -> p hh c", hh=2),
                in_=stg[0:64, ppg * 2 * TC:ppg * 2 * TC + 2 * TC].rearrange(
                    "p (hh q) -> p hh q", hh=2)),
              r=[rstg[ppg]], w=[rhbm_rw], dma="hrw")

    pw_pull(10 ** 9)

    if tap == "l1_att":
        for g in range(NG):
            for tc in range(NTC):
                pp = (g * NTC + tc) % 2
                E("g", lambda q, g=g, tc=tc, pp=pp: q.dma_start(
                    out=ev1[:, pp * TC:(pp + 1) * TC],
                    in_=sp_rw[g * 128:(g + 1) * 128, tc * TC:(tc + 1) * TC]),
                  r=[rhbm_rw], w=[rev1[pp]], dma=f"dbr{pp}")
                E("v", lambda v, pp=pp: v.tensor_copy(
                    out=ev2[:, pp * TC:(pp + 1) * TC],
                    in_=ev1[:, pp * TC:(pp + 1) * TC]),
                  r=[rev1[pp]], w=[rev2[pp]])
                E("q", lambda q, g=g, tc=tc, pp=pp: q.dma_start(
                    out=dbg[g * 128:(g + 1) * 128, tc * TC:(tc + 1) * TC],
                    in_=ev2[:, pp * TC:(pp + 1) * TC]),
                  r=[rev2[pp]], dma=f"dbw{pp}")

    b.barrier()
    p2.close()

    p3 = contextlib.ExitStack()
    sb3 = lambda name, shape, dtype=F32: p3.enter_context(
        nc.sbuf_tensor(name, shape, dtype))
    wqb = sb3("wqb", [128, NG * D], BF16)
    xqr = sb3("xqr", [128, 2 * NG * TC], BF16)
    attr = sb3("attr", [128, 2 * NG * TC], BF16)
    qint = sb3("qint", [128, TC], I32)
    xev = xf

    rwq = reg("wqb")
    rxq = [reg("xqr.0"), reg("xqr.1")]
    rqi = reg("qint")
    rxe = rxf
    ratr = [reg("attr.0"), reg("attr.1")]

    for kq in range(NG):
        for c2 in range(NW):
            pp = (kq * NW + c2) % 2
            E("q", lambda q, kq=kq, c2=c2, pp=pp: q.dma_start(
                out=xf[:, pp * TC:pp * TC + WQC],
                in_=h["wT"][kq * 128:(kq + 1) * 128, c2 * WQC:(c2 + 1) * WQC]),
              w=[rxf[pp]], dma=f"xf{pp}")
            E("s", lambda s, pp=pp: s.activation(
                out=ev1[:, pp * TC:pp * TC + WQC],
                in_=xf[:, pp * TC:pp * TC + WQC], func=AF.Abs),
              r=[rxf[pp]], w=[rev1[pp]])
            E("v", lambda v, pp=pp: v.tensor_scalar(
                out=ev1[:, pp * TC:pp * TC + WQC],
                in0=ev1[:, pp * TC:pp * TC + WQC], scalar1=mxc[:, 36:37],
                scalar2=None, op0=ALU.is_gt), r=[rmxc], w=[rev1[pp]])
            E("s", lambda s, pp=pp: s.sign(
                out=ev2[:, pp * TC:pp * TC + WQC],
                in_=xf[:, pp * TC:pp * TC + WQC]), r=[rxf[pp]], w=[rev2[pp]])
            E("v", lambda v, pp=pp: v.tensor_tensor(
                out=ev1[:, pp * TC:pp * TC + WQC],
                in0=ev1[:, pp * TC:pp * TC + WQC],
                in1=ev2[:, pp * TC:pp * TC + WQC], op=ALU.mult),
              r=[rev2[pp]], w=[rev1[pp]])
            E("v", lambda v, kq=kq, c2=c2, pp=pp: v.tensor_scalar(
                out=wqb[:, kq * D + c2 * WQC:kq * D + (c2 + 1) * WQC],
                in0=ev1[:, pp * TC:pp * TC + WQC], scalar1=mxc[:, 37:38],
                scalar2=None, op0=ALU.mult), r=[rev1[pp], rmxc], w=[rwq])

    def xq_chunk(tc, k):
        ppx = tc % 2
        pp = k % 2
        E("v", lambda v, k=k, tc=tc, pp=pp: v.tensor_tensor(
            out=xev[:, pp * TC:(pp + 1) * TC],
            in0=x1[:, k * T + tc * TC:k * T + (tc + 1) * TC],
            in1=s1b[:, tc * TC:(tc + 1) * TC], op=ALU.mult),
          r=[rx1t[k][tc], rs1b], w=[rxe[pp]])
        E("v", lambda v, tc=tc, pp=pp: v.tensor_tensor(
            out=xev[:, pp * TC:(pp + 1) * TC],
            in0=xev[:, pp * TC:(pp + 1) * TC],
            in1=m0b[:, tc * TC:(tc + 1) * TC], op=ALU.subtract),
          r=[rm0b], w=[rxe[pp]])
        E("v", lambda v, k=k, pp=pp: v.tensor_scalar(
            out=xev[:, pp * TC:(pp + 1) * TC],
            in0=xev[:, pp * TC:(pp + 1) * TC],
            scalar1=cols["gamma"][:, k:k + 1],
            scalar2=cols["beta"][:, k:k + 1], op0=ALU.mult, op1=ALU.add),
          r=[rcols], w=[rxe[pp]])
        E("v", lambda v, pp=pp: v.tensor_scalar(
            out=xev[:, pp * TC:(pp + 1) * TC],
            in0=xev[:, pp * TC:(pp + 1) * TC], scalar1=mxc[:, 35:36],
            scalar2=None, op0=ALU.mult), r=[rmxc], w=[rxe[pp]])
        E("v", lambda v, pp=pp: v.tensor_scalar(
            out=xev[:, pp * TC:(pp + 1) * TC],
            in0=xev[:, pp * TC:(pp + 1) * TC], scalar1=-127.0,
            scalar2=127.0, op0=ALU.max, op1=ALU.min), w=[rxe[pp]])
        E("s", lambda s, pp=pp: s.sign(
            out=qint[:, 0:TC].bitcast(F32) if False else ev2[:, 0:TC],
            in_=xev[:, pp * TC:(pp + 1) * TC]),
          r=[rxe[pp]], w=[rev2[0]])
        E("v", lambda v, pp=pp: v.scalar_tensor_tensor(
            out=xev[:, pp * TC:(pp + 1) * TC],
            in0=ev2[:, 0:TC], scalar=0.5,
            in1=xev[:, pp * TC:(pp + 1) * TC], op0=ALU.mult,
            op1=ALU.add), r=[rev2[0]], w=[rxe[pp]])
        E("v", lambda v, pp=pp: v.tensor_copy(
            out=qint[:, 0:TC], in_=xev[:, pp * TC:(pp + 1) * TC]),
          r=[rxe[pp]], w=[rqi])
        E("v", lambda v, k=k, tc=tc, ppx=ppx: v.tensor_scalar(
            out=xqr[:, ppx * NG * TC + k * TC:
                    ppx * NG * TC + (k + 1) * TC],
            in0=qint[:, 0:TC], scalar1=mxc[:, 34:35],
            scalar2=None, op0=ALU.mult), r=[rqi, rmxc],
          w=[rxq[ppx]])

    def attr_dma(tc):
        ppx = tc % 2
        E("q", lambda q, tc=tc, ppx=ppx: q.dma_start(
            out=attr[:, ppx * NG * TC:(ppx + 1) * NG * TC].rearrange(
                "p (k c) -> p k c", k=NG),
            in_=sp_rw[:, tc * TC:(tc + 1) * TC].rearrange(
                "(k p) c -> p k c", p=128)),
          r=[rhbm_rw], w=[ratr[ppx]], dma=f"at{ppx}")

    attr_dma(0)
    for k in range(NG):
        xq_chunk(0, k)
    for tc in range(NTC):
        ppx = tc % 2
        if tc + 1 < NTC:
            attr_dma(tc + 1)
        for mo in range(NG):
            if tc + 1 < NTC:
                xq_chunk(tc + 1, mo)
            si = mo % 2
            stream_w(h["o_T"], mo, NG, si)
            bank = 2 + (mo % 2)
            for k in range(NG):
                E("t", lambda t, k=k, si=si, bank=bank, ppx=ppx: t.matmul(
                    out=PS[bank][:, 0:TC], lhsT=slot(si, k),
                    rhs=attr[:, ppx * NG * TC + k * TC:
                             ppx * NG * TC + (k + 1) * TC],
                    start=(k == 0), stop=(k == NG - 1)),
                  r=[rslot[si], ratr[ppx]], w=[rps[bank]])
            bank2 = 4 + (mo % 2)
            for k in range(NG):
                E("t", lambda t, k=k, mo=mo, bank2=bank2, ppx=ppx: t.matmul(
                    out=PS[bank2][:, 0:TC],
                    lhsT=wqb[:, k * D + mo * 128:k * D + (mo + 1) * 128],
                    rhs=xqr[:, ppx * NG * TC + k * TC:
                            ppx * NG * TC + (k + 1) * TC],
                    start=(k == 0), stop=(k == NG - 1)),
                  r=[rwq, rxq[ppx]], w=[rps[bank2]])
            pp = mo % 2
            E("s", lambda s, bank2=bank2, pp=pp: s.activation(
                out=x2b[:, pp * TC:(pp + 1) * TC], in_=PS[bank2][:, 0:TC],
                func=AF.Tanh, scale=0.5), r=[rps[bank2]], w=[rx2b[pp]])
            E("v", lambda v, bank2=bank2, pp=pp: v.scalar_tensor_tensor(
                out=ev1[:, pp * TC:(pp + 1) * TC],
                in0=x2b[:, pp * TC:(pp + 1) * TC], scalar=1.0,
                in1=PS[bank2][:, 0:TC], op0=ALU.add, op1=ALU.mult),
              r=[rx2b[pp], rps[bank2]], w=[rev1[pp]])
            E("v", lambda v, bank=bank, pp=pp: v.tensor_tensor(
                out=ev1[:, pp * TC:(pp + 1) * TC],
                in0=ev1[:, pp * TC:(pp + 1) * TC], in1=PS[bank][:, 0:TC],
                op=ALU.mult), r=[rps[bank]], w=[rev1[pp]])
            E("v", lambda v, mo=mo, tc=tc, pp=pp: v.scalar_tensor_tensor(
                out=x1[:, mo * T + tc * TC:mo * T + (tc + 1) * TC],
                in0=ev1[:, pp * TC:(pp + 1) * TC], scalar=0.5,
                in1=x1[:, mo * T + tc * TC:mo * T + (tc + 1) * TC],
                op0=ALU.mult, op1=ALU.add), r=[rev1[pp]], w=[rx1t[mo][tc]])
            E("q", lambda q, mo=mo, tc=tc: q.dma_start(
                out=yT[mo * 128:(mo + 1) * 128, tc * TC:(tc + 1) * TC],
                in_=x1[:, mo * T + tc * TC:mo * T + (tc + 1) * TC]),
              r=[rx1t[mo][tc]], dma="out")

    b.barrier(engs=("q",))
    p3.close()
    p1.close()

    blk_ctx.__exit__(None, None, None)
    es.close()
    return nc, b



def make_in_maps(inputs, T=2048):
    tr = lambda a: np.ascontiguousarray(np.asarray(a).T)
    _cv = lambda a: np.ascontiguousarray(np.asarray(a).reshape(8, 128).T)
    def tl(a):
        a = np.asarray(a)
        no, ni = a.shape
        nm, nk = no // 128, ni // 128
        return np.ascontiguousarray(
            a.T.reshape(nk, 128, nm, 128).transpose(2, 1, 0, 3)
            .reshape(nm * 128, nk * 128))
    TC = min(512, T)
    maskc = np.zeros((TC // 128, 128, TC), np.float32)
    for j in range(TC // 128):
        for p in range(128):
            maskc[j, p, :min(p + j * 128, TC)] = NEG
    maskc = np.ascontiguousarray(maskc.transpose(1, 0, 2).reshape(128, -1))
    common = {
        "wrT": tl(inputs["L0_attn_Wr"]), "wkT": tl(inputs["L0_attn_Wk"]),
        "wvT": tl(inputs["L0_attn_Wv"]), "woT": tl(inputs["L0_attn_Wo"]),
        "wfkT": tl(inputs["L0_attn_Wfk"]), "wfvT": tl(inputs["L0_attn_Wfv"]),
        "wfrT": tl(inputs["L0_attn_Wfr"]),
        "wgT": tl(inputs["L0_ffn_w_gate"]), "wuT": tl(inputs["L0_ffn_w_up"]),
        "wdT": tl(inputs["L0_ffn_w_down"]),
        "q_T": tl(inputs["L1_attn_q_proj_weight"]),
        "k_T": tl(inputs["L1_attn_k_proj_weight"]),
        "v_T": tl(inputs["L1_attn_v_proj_weight"]),
        "o_T": tl(inputs["L1_attn_o_proj_weight"]),
        "wT": tr(inputs["L1_ffn_weight"]),
        "ln1w": _cv(inputs["L0_attn_ln1_w"]),
        "ln1b": _cv(inputs["L0_attn_ln1_b"]),
        "ln2w": _cv(inputs["L0_attn_ln2_w"]),
        "ln2b": _cv(inputs["L0_attn_ln2_b"]),
        "decay": _cv(inputs["L0_attn_decay"]),
        "u": _cv(inputs["L0_attn_u"]),
        "gamma": _cv(inputs["L1_ffn_gamma"]),
        "beta": _cv(inputs["L1_ffn_beta"]),
        "maskc": maskc,
    }
    x = np.asarray(inputs["L0_attn_x"])
    in_maps = []
    for ci in range(x.shape[0]):
        m = dict(common)
        m["xT"] = tr(x[ci])
        in_maps.append(m)
    return in_maps


def kernel(**inputs):
    T = 2048
    nc, b = build(T=T, tap=DEBUG_TAP, cores=8)
    in_maps = make_in_maps(inputs, T=T)
    res = run_bass_kernel_spmd(nc, in_maps, core_ids=list(range(8)))
    out = np.stack([np.asarray(r["yT"]).T for r in res.results], axis=0)
    return out.astype(np.float32)


# version 20
# speedup vs baseline: 2.1701x; 1.0041x over previous
import numpy as np
import contextlib
import concourse.bass as bass
import concourse.mybir as mybir
from concourse.bass_utils import run_bass_kernel_spmd

F32 = mybir.dt.float32
BF16 = mybir.dt.bfloat16
I32 = mybir.dt.int32
AF = mybir.ActivationFunctionType
ALU = mybir.AluOpType
AXX = mybir.AxisListType.X

D = 1024
NG = D // 128
HID = 3072
NHG = HID // 128
FFD = 4096
NFG = FFD // 128
NPAIR = 8
EPS = 1e-5
NEG = -1.0e9

DEBUG_TAP = None
SAFE_SYNC = False


class Reg:
    __slots__ = ("name", "w", "readers")
    def __init__(self, name):
        self.name = name
        self.w = None
        self.readers = []


class B:
    ENG_ATTR = {"v": "vector", "s": "scalar", "t": "tensor",
                "g": "gpsimd", "q": "sync"}
    ENG_SEM = {"v": "sv", "s": "ss", "t": "st", "g": "sg", "q": None}

    def __init__(self, nc, block, sems, mksem):
        self.nc = nc
        self.block = block
        self.sems = sems
        self.mksem = mksem
        self.cnt = {k: 0 for k in sems}
        self.wm = {}
        self.maxwait = {}
        self.n_inst = 0
        self.buf = {e: [] for e in self.ENG_ATTR}

    def flush(self):
        for e in self.ENG_ATTR:
            bodies = self.buf[e]
            if not bodies:
                continue
            self.buf[e] = []
            def section(engobj, bodies=bodies):
                for bd in bodies:
                    bd(engobj)
            getattr(self.block, self.ENG_ATTR[e])(section)

    def dma_sem(self, key):
        key = "D_" + key
        if key not in self.sems:
            self.sems[key] = self.mksem(key)
            self.cnt[key] = 0
        return key

    def emit(self, eng, fn, r=(), w=(), dma=None, inc=None):
        needs = {}
        for reg_ in r:
            if reg_.w is not None:
                s, c = reg_.w
                if needs.get(s, 0) < c:
                    needs[s] = c
        for reg_ in w:
            if reg_.w is not None:
                s, c = reg_.w
                if needs.get(s, 0) < c:
                    needs[s] = c
            for (s, c) in reg_.readers:
                if needs.get(s, 0) < c:
                    needs[s] = c
        if dma is not None:
            sem_name = self.dma_sem(dma)
            inc_v = 16 if inc is None else inc
            mw = self.maxwait.get(sem_name, 0)
            if mw > 0 and needs.get(sem_name, 0) < mw:
                needs[sem_name] = mw
        else:
            sem_name = self.ENG_SEM[eng]
            inc_v = 1
        own = self.ENG_SEM[eng]
        if not SAFE_SYNC and eng in ("t", "s") and own in needs:
            del needs[own]
        for s_, v_ in needs.items():
            if self.maxwait.get(s_, 0) < v_:
                self.maxwait[s_] = v_
        def body(engobj):
            for s in sorted(needs):
                v = needs[s]
                key = (eng, s)
                if self.wm.get(key, -1) < v:
                    engobj.wait_ge(self.sems[s], v)
                    self.wm[key] = v
            instr = fn(engobj)
            instr.then_inc(self.sems[sem_name], inc_v)
        self.buf[eng].append(body)
        self.n_inst += 1
        self.cnt[sem_name] += inc_v
        tag = (sem_name, self.cnt[sem_name])
        for reg_ in w:
            reg_.w = tag
            reg_.readers = []
        for reg_ in r:
            reg_.readers.append(tag)
        return tag

    def barrier(self, engs=("v", "s", "t", "g", "q")):
        snapshot = dict(self.cnt)
        for e in engs:
            def body(engobj, e=e):
                for s, v in sorted(snapshot.items()):
                    if v > 0 and self.wm.get((e, s), -1) < v:
                        engobj.wait_ge(self.sems[s], v)
                        self.wm[(e, s)] = v
            self.buf[e].append(body)
        for s, v in snapshot.items():
            if self.maxwait.get(s, 0) < v:
                self.maxwait[s] = v
        self.flush()


def build(T=2048, tap=None, cores=8):
    TC = min(512, T)
    NTC = T // TC
    NT2 = T // 128
    TCB = min(512, T)
    NTCB = T // TCB
    WQC = TC
    NW = D // WQC
    MXB = 40 + NG * NW

    nc = bass.Bass("TRN2", target_bir_lowering=False, debug=False,
                   detect_race_conditions=SAFE_SYNC)

    h = {}
    def din(name, shape, dtype=F32):
        h[name] = nc.dram_tensor(name, shape, dtype, kind="ExternalInput").ap()
        return h[name]

    din("xT", [D, T])
    din("wrT", [D, D]); din("wkT", [D, D]); din("wvT", [D, D]); din("woT", [D, D])
    din("wfkT", [FFD, D]); din("wfvT", [D, FFD]); din("wfrT", [D, D])
    din("wgT", [HID, D], I32); din("wuT", [HID, D], I32); din("wdT", [D, HID], I32)
    din("q_T", [D, D]); din("k_T", [D, D]); din("v_T", [D, D]); din("o_T", [D, D])
    din("wT", [D, D])
    for nm in ("ln1w", "ln1b", "ln2w", "ln2b", "decay", "u", "gamma", "beta"):
        din(nm, [128, NG])
    din("maskc", [128, (TC // 128) * TC])

    yT = nc.dram_tensor("yT", [D, T], F32, kind="ExternalOutput").ap()
    dbg = None
    if tap is not None:
        dbg = nc.dram_tensor("dbg", [D, T], F32, kind="ExternalOutput").ap()
    sp_g = nc.dram_tensor("sp_g", [D, T], BF16, kind="Internal").ap()
    sp_rw = nc.dram_tensor("sp_rw", [D, T], BF16, kind="Internal").ap()
    sp_xm = nc.dram_tensor("sp_xm", [D, T], BF16, kind="Internal").ap()
    sp_xn = nc.dram_tensor("sp_xn", [D, T], BF16, kind="Internal").ap()
    cc_in = nc.dram_tensor("cc_in", [1, 8], F32, kind="Internal").ap()
    cc_out = nc.dram_tensor("cc_out", [1, 8], F32, kind="Internal").ap()

    es = contextlib.ExitStack()
    sb = lambda name, shape, dtype=F32: es.enter_context(
        nc.sbuf_tensor(name, shape, dtype))
    mkpsum = lambda name, shape, dtype=F32: es.enter_context(
        nc.psum_tensor(name, shape, dtype))
    mksem = lambda name: es.enter_context(nc.semaphore(name))

    SLOT_OFF = [0, 1024, 2048, 3072, 4096, 7168]
    SLOT_W = [1024, 1024, 1024, 1024, 3072, 4096]
    wslot = sb("wslot", [128, 11264], BF16)
    s0b = sb("s0b", [128, T], BF16)
    s1b = sb("s1b", [128, T], BF16)
    m0b = sb("m0b", [128, T], BF16)
    rows = sb("rows", [1, 2 * TC])
    rowx = sb("rowx", [1, TC])
    mub = sb("mub", [128, T], BF16)
    wbcol = sb("wbcol", [128, 48])
    wcolf = sb("wcolf", [128, NG])
    bcolb = sb("bcolb", [128, NG], BF16)
    onescol = sb("onescol", [128, 2], BF16)
    onescolf = sb("onescolf", [1, 128])
    xf = sb("xf", [128, 2 * TC])
    x2b = sb("x2b", [128, 2 * TC], BF16)
    ev1 = sb("ev1", [128, 2 * TC])
    ev2 = sb("ev2", [128, 2 * TC])
    cols = {}
    for nm in ("ln1w", "ln1b", "ln2w", "ln2b", "decay", "u", "gamma", "beta"):
        cols[nm] = sb("c_" + nm, [128, NG])
    ew = sb("c_ew", [128, NG])
    eu = sb("c_eu", [128, NG])
    mxc = sb("mxc", [128, 40 + 2 * NG * NW])
    scol = sb("scol", [128, NT2])

    PS = [mkpsum(f"ps{i}", [128, 512]) for i in range(6)]
    PSS = mkpsum("pss", [128, 512])
    PSB = mkpsum("psb", [128, 512])

    sems = {}
    for s_ in ("sv", "ss", "st", "sg"):
        sems[s_] = mksem(s_)

    blk_ctx = nc.Block()
    block = blk_ctx.__enter__()
    b = B(nc, block, sems, mksem)
    E = b.emit
    R = {}
    def reg(name):
        if name not in R:
            R[name] = Reg(name)
        return R[name]

    rps = [reg(f"ps.{i}") for i in range(6)]
    rpss = reg("pss"); rpsb = reg("psb")
    rrows = reg("rows"); rrowx = reg("rowx")
    rwbcol = reg("wbcol"); rbcolb = reg("bcolb")
    rmub = reg("mub")
    rs0b = reg("s0b"); rs1b = reg("s1b"); rm0b = reg("m0b")
    rcols = reg("cols")
    rxf = [reg("xf.0"), reg("xf.1")]
    rx2b = [reg("x2b.0"), reg("x2b.1")]
    rev1 = [reg("ev1.0"), reg("ev1.1")]
    rev2 = [reg("ev2.0"), reg("ev2.1")]
    rslot = [reg(f"wslot.{i}") for i in range(6)]
    rones = reg("ones")
    rmask = reg("masks")
    rmxc = reg("mxc"); rscol = reg("scol")
    rcc = reg("ccdone")
    rhbm_g = reg("hbm_g"); rhbm_rw = reg("hbm_rw"); rhbm_xm = reg("hbm_xm")

    def slot(i, k=None, kw=128):
        o = SLOT_OFF[i]
        if k is None:
            return wslot[:, o:o + SLOT_W[i]]
        return wslot[:, o + k * kw:o + (k + 1) * kw]

    for nm in ("ln1w", "ln1b", "ln2w", "ln2b", "decay", "u", "gamma", "beta"):
        E("q", lambda q, nm=nm: q.dma_start(
            out=cols[nm][:, :], in_=h[nm][:, :]), w=[rcols], dma="pre")
    E("v", lambda v: v.memset(onescol[:, :], 1.0), w=[rones])
    E("v", lambda v: v.memset(onescolf[:, :], 1.0), w=[rones])
    E("s", lambda s: s.activation(out=ew[:, :], in_=cols["decay"][:, :],
                                  func=AF.Exp), r=[rcols], w=[rcols])
    E("s", lambda s: s.activation(out=ew[:, :], in_=ew[:, :],
                                  func=AF.Exp, scale=-1.0), r=[rcols], w=[rcols])
    E("s", lambda s: s.activation(out=eu[:, :], in_=cols["u"][:, :],
                                  func=AF.Exp), r=[rcols], w=[rcols])

    def S1(): return rows[0:1, 0:TC]
    def S2(): return rows[0:1, TC:2 * TC]
    def RX(): return rowx[0:1, 0:TC]

    def bcast_tc(src, sreg, dst, rdst, tc):
        E("t", lambda t: t.matmul(
            out=PSB[:, 0:TC], lhsT=onescolf[0:1, :], rhs=src,
            start=True, stop=True), r=[sreg, rones], w=[rpsb])
        E("v", lambda v: v.tensor_copy(
            out=dst[:, tc * TC:(tc + 1) * TC], in_=PSB[:, 0:TC]),
          r=[rpsb], w=[rdst])

    def ln_rows_tc(tc, mode):
        C = float(D)
        E("v", lambda v: v.tensor_scalar(
            out=S1(), in0=S1(), scalar1=1.0 / C, scalar2=None,
            op0=ALU.mult), r=[rrows], w=[rrows])
        bcast_tc(S1(), rrows, mub, rmub, tc)
        E("v", lambda v: v.tensor_tensor(
            out=RX(), in0=S1(), in1=S1(), op=ALU.mult),
          r=[rrows], w=[rrowx])
        E("v", lambda v: v.scalar_tensor_tensor(
            out=S2(), in0=S2(), scalar=1.0 / C, in1=RX(),
            op0=ALU.mult, op1=ALU.subtract), r=[rrows, rrowx], w=[rrows])
        E("v", lambda v: v.tensor_scalar(
            out=S2(), in0=S2(), scalar1=EPS, scalar2=None, op0=ALU.add),
          r=[rrows], w=[rrows])
        E("s", lambda s: s.activation(out=RX(), in_=S2(), func=AF.Ln),
          r=[rrows], w=[rrowx])
        E("s", lambda s: s.activation(out=RX(), in_=RX(), func=AF.Exp,
                                      scale=-0.5), r=[rrowx], w=[rrowx])
        if mode == "ln2":
            bcast_tc(RX(), rrowx, s1b, rs1b, tc)
            return
        bcast_tc(RX(), rrowx, s0b, rs0b, tc)
        if mode == "l1":
            nj = TC // 128
            for j in range(nj):
                E("g", lambda g_, j=j: g_.dma_start(
                    out=scol[:, tc * nj + j:tc * nj + j + 1],
                    in_=rowx[0:1, j * 128:(j + 1) * 128],
                    allow_slow_non_contiguous=True),
                  r=[rrowx], w=[rscol], dma="scol")
        if mode == "l0":
            E("v", lambda v: v.tensor_tensor(
                out=S1(), in0=S1(), in1=RX(), op=ALU.mult),
              r=[rrows, rrowx], w=[rrows])
            bcast_tc(S1(), rrows, m0b, rm0b, tc)
        E("v", lambda v: v.tensor_tensor(
            out=S2(), in0=S2(), in1=RX(), op=ALU.mult),
          r=[rrows, rrowx], w=[rrows])
        E("v", lambda v: v.tensor_tensor(
            out=S2(), in0=S2(), in1=RX(), op=ALU.mult),
          r=[rrows, rrowx], w=[rrows])
        E("v", lambda v: v.tensor_scalar(
            out=S2(), in0=S2(), scalar1=EPS, scalar2=None, op0=ALU.add),
          r=[rrows], w=[rrows])
        E("s", lambda s: s.activation(out=S2(), in_=S2(), func=AF.Ln),
          r=[rrows], w=[rrows])
        E("s", lambda s: s.activation(out=S2(), in_=S2(), func=AF.Exp,
                                      scale=-0.5), r=[rrows], w=[rrows])
        E("v", lambda v: v.tensor_tensor(
            out=RX(), in0=RX(), in1=S2(), op=ALU.mult),
          r=[rrows], w=[rrowx])
        bcast_tc(RX(), rrowx, s1b, rs1b, tc)
        if mode == "l1":
            E("v", lambda v: v.tensor_tensor(
                out=S1(), in0=S1(), in1=RX(), op=ALU.mult),
              r=[rrowx], w=[rrows])
            bcast_tc(S1(), rrows, m0b, rm0b, tc)

    def stream_w(wap, mi, nk, si):
        E("g", lambda g_: g_.dma_start(
            out=slot(si)[:, 0:nk * 128],
            in_=wap[mi * 128:(mi + 1) * 128, 0:nk * 128]),
          w=[rslot[si]], dma=f"w{si}")

    rslot5h = [reg("wslot.5a"), reg("wslot.5b")]
    def stream_w5(wap, mi, nk):
        hk = nk // 2
        E("g", lambda g_, hk=hk: g_.dma_start(
            out=slot(5)[:, 0:hk * 128],
            in_=wap[mi * 128:(mi + 1) * 128, 0:hk * 128]),
          w=[rslot5h[0]], dma="w5a")
        E("g", lambda g_, hk=hk: g_.dma_start(
            out=slot(5)[:, hk * 128:nk * 128],
            in_=wap[mi * 128:(mi + 1) * 128, hk * 128:nk * 128]),
          w=[rslot5h[1]], dma="w5b")


    def wb_column(si, nk, ci, ps=None, rp=None):
        ps = PSB if ps is None else ps
        rp = rpsb if rp is None else rp
        for k in range(nk):
            E("t", lambda t, k=k, ps=ps: t.matmul(
                out=ps[:, 0:1], lhsT=slot(si, k), rhs=bcolb[:, k:k + 1],
                start=(k == 0), stop=(k == nk - 1)),
              r=[rslot[si], rbcolb], w=[rp])
        E("v", lambda v, ps=ps: v.tensor_copy(
            out=wbcol[:, ci:ci + 1], in_=ps[:, 0:1]), r=[rp], w=[rwbcol])

    def scale_slot(si, nk):
        for k in range(nk):
            E("v", lambda v, k=k: v.tensor_scalar(
                out=slot(si, k), in0=slot(si, k), scalar1=wcolf[:, k:k + 1],
                scalar2=None, op0=ALU.mult),
              r=[rwbcol, rslot[si]], w=[rslot[si]])

    p0 = contextlib.ExitStack()
    sb0 = lambda name, shape, dtype=F32: p0.enter_context(
        nc.sbuf_tensor(name, shape, dtype))
    xbuf = sb0("xbuf", [128, NG * T], BF16)
    scA = sb0("scA", [128, T])
    scB = sb0("scB", [128, T])
    scC = sb0("scC", [128, T + 8])
    scD = sb0("scD", [128, T + 8])
    scE = sb0("scE", [128, T + 8])
    scF = sb0("scF", [128, T + 8])
    sigr = sb0("sigr", [128, T])
    d1t = sb0("d1t", [128, T])
    hbuf = sb0("hbuf", [128, NHG * TCB], BF16)
    gtile = sb0("gtile", [128, 2 * TCB], BF16)
    rwt = sb0("rwt", [128, T], BF16)

    rxb = [reg(f"xb.{g}") for g in range(NG)]
    rsc = {nm: reg(nm) for nm in ("scA", "scB", "scC", "scD", "scE", "scF")}
    rsigr = reg("sigr"); rd1 = reg("d1t")
    rhb = reg("hbuf")
    rgt = [reg("gtile.0"), reg("gtile.1")]
    rrwt = reg("rwt")

    for tc in range(NTC):
        for g in range(NG):
            pp = g % 2
            xfs = xf[:, pp * TC:(pp + 1) * TC]
            E("q", lambda q, g=g, tc=tc, xfs=xfs: q.dma_start(
                out=xfs, in_=h["xT"][g * 128:(g + 1) * 128,
                                     tc * TC:(tc + 1) * TC]),
              w=[rxf[pp]], dma=f"xf{pp}")
            E("s", lambda s, g=g, tc=tc, xfs=xfs: s.activation(
                out=xbuf[:, g * T + tc * TC:g * T + (tc + 1) * TC],
                in_=xfs, func=AF.Copy), r=[rxf[pp]], w=[rxb[g]])
            E("s", lambda s, xfs=xfs, pp=pp: s.activation(
                out=x2b[:, pp * TC:(pp + 1) * TC], in_=xfs, func=AF.Square),
              r=[rxf[pp]], w=[rx2b[pp]])
            E("t", lambda t, g=g, tc=tc: t.matmul(
                out=PSS[0:1, 0:TC], lhsT=onescol[:, 0:1],
                rhs=xbuf[:, g * T + tc * TC:g * T + (tc + 1) * TC],
                start=(g == 0), stop=(g == NG - 1)),
              r=[rxb[g], rones], w=[rpss])
            E("t", lambda t, g=g, pp=pp: t.matmul(
                out=PSB[0:1, 0:TC], lhsT=onescol[:, 0:1],
                rhs=x2b[:, pp * TC:(pp + 1) * TC],
                start=(g == 0), stop=(g == NG - 1)),
              r=[rx2b[pp], rones], w=[rpsb])
        E("v", lambda v: v.tensor_copy(out=S1(), in_=PSS[0:1, 0:TC]),
          r=[rpss], w=[rrows])
        E("v", lambda v: v.tensor_copy(out=S2(), in_=PSB[0:1, 0:TC]),
          r=[rpsb], w=[rrows])
        ln_rows_tc(tc, "l0")

    for g in range(NG):
        E("v", lambda v, g=g: v.tensor_tensor(
            out=xbuf[:, g * T:(g + 1) * T], in0=xbuf[:, g * T:(g + 1) * T],
            in1=mub[:, 0:T], op=ALU.subtract), r=[rmub], w=[rxb[g]])
    E("v", lambda v: v.tensor_copy(out=wcolf[:, :], in_=cols["ln1w"][:, :]),
      r=[rcols], w=[rwbcol])
    E("v", lambda v: v.tensor_copy(out=bcolb[:, :], in_=cols["ln1b"][:, :]),
      r=[rcols], w=[rbcolb])
    rhbm_xn = reg("hbm_xn")
    def xn_chunks():
        for g in range(NG):
            for tc2 in range(NTC):
                def xs(g=g, tc2=tc2):
                    pp = (g * NTC + tc2) % 2
                    E("v", lambda v, g=g, tc2=tc2, pp=pp: v.tensor_tensor(
                        out=x2b[:, pp * TC:(pp + 1) * TC],
                        in0=xbuf[:, g * T + tc2 * TC:g * T + (tc2 + 1) * TC],
                        in1=s0b[:, tc2 * TC:(tc2 + 1) * TC], op=ALU.mult),
                      r=[rxb[g], rs0b], w=[rx2b[pp]])
                    E("q", lambda q, g=g, tc2=tc2, pp=pp: q.dma_start(
                        out=sp_xn[g * 128:(g + 1) * 128,
                                  tc2 * TC:(tc2 + 1) * TC],
                        in_=x2b[:, pp * TC:(pp + 1) * TC]),
                      r=[rx2b[pp]], w=[rhbm_xn], dma="hxn")
                yield xs

    def bb_chunks():
        for tcb in range(NTCB):
            sl = slice(tcb * TCB, (tcb + 1) * TCB)
            for m in range(NHG):
                def mb(tcb=tcb, m=m, sl=sl):
                    for wi, (wap, si, bank) in enumerate(
                            ((h["wgT"], 3, 2), (h["wuT"], 4, 3))):
                        stream_w(wap, m, NG, si)
                        for k in range(NG):
                            E("t", lambda t, k=k, si=si, bank=bank, sl=sl:
                                t.matmul(
                                    out=PS[bank][:, 0:TCB], lhsT=slot(si, k),
                                    rhs=xbuf[:, k * T + sl.start:
                                             k * T + sl.stop],
                                    start=(k == 0), stop=(k == NG - 1)),
                              r=[rslot[si], rxb[k]], w=[rps[bank]])
                    pp = m % 2
                    E("v", lambda v, sl=sl, pp=pp: v.scalar_tensor_tensor(
                        out=ev1[:, pp * TC:pp * TC + TCB],
                        in0=PS[2][:, 0:TCB], scalar=1.0 / 127.0,
                        in1=s0b[:, sl], op0=ALU.mult, op1=ALU.mult),
                      r=[rps[2], rs0b], w=[rev1[pp]])
                    E("v", lambda v, sl=sl, pp=pp: v.scalar_tensor_tensor(
                        out=ev2[:, pp * TC:pp * TC + TCB],
                        in0=PS[3][:, 0:TCB], scalar=1.0 / 127.0,
                        in1=s0b[:, sl], op0=ALU.mult, op1=ALU.mult),
                      r=[rps[3], rs0b], w=[rev2[pp]])
                    E("s", lambda s, pp=pp: s.activation(
                        out=x2b[:, pp * TC:pp * TC + TCB],
                        in_=ev1[:, pp * TC:pp * TC + TCB],
                        func=AF.Tanh, scale=0.5), r=[rev1[pp]], w=[rx2b[pp]])
                    E("v", lambda v, pp=pp: v.scalar_tensor_tensor(
                        out=ev1[:, pp * TC:pp * TC + TCB],
                        in0=x2b[:, pp * TC:pp * TC + TCB], scalar=1.0,
                        in1=ev1[:, pp * TC:pp * TC + TCB],
                        op0=ALU.add, op1=ALU.mult),
                      r=[rx2b[pp], rev1[pp]], w=[rev1[pp]])
                    E("v", lambda v, m=m, pp=pp: v.tensor_tensor(
                        out=hbuf[:, m * TCB:(m + 1) * TCB],
                        in0=ev1[:, pp * TC:pp * TC + TCB],
                        in1=ev2[:, pp * TC:pp * TC + TCB], op=ALU.mult),
                      r=[rev1[pp], rev2[pp]], w=[rhb])
                yield mb
            for mo in range(NG):
                def dn(tcb=tcb, mo=mo, sl=sl):
                    stream_w5(h["wdT"], mo, NHG)
                    bank = 4 + (mo % 2)
                    for k in range(NHG):
                        E("t", lambda t, k=k, bank=bank: t.matmul(
                            out=PS[bank][:, 0:TCB], lhsT=slot(5, k),
                            rhs=hbuf[:, k * TCB:(k + 1) * TCB],
                            start=(k == 0), stop=(k == NHG - 1)),
                          r=[rslot5h[2 * k // NHG], rhb], w=[rps[bank]])
                    pp = mo % 2
                    E("s", lambda s, bank=bank, pp=pp: s.activation(
                        out=x2b[:, pp * TC:pp * TC + TCB],
                        in_=PS[bank][:, 0:TCB], func=AF.Tanh,
                        scale=0.5 / 254.0), r=[rps[bank]], w=[rx2b[pp]])
                    E("v", lambda v, bank=bank, pp=pp: v.scalar_tensor_tensor(
                        out=gtile[:, pp * TCB:(pp + 1) * TCB],
                        in0=x2b[:, pp * TC:pp * TC + TCB], scalar=1.0,
                        in1=PS[bank][:, 0:TCB], op0=ALU.add, op1=ALU.mult),
                      r=[rx2b[pp], rps[bank]], w=[rgt[pp]])
                    E("q", lambda q, mo=mo, sl=sl, pp=pp: q.dma_start(
                        out=sp_g[mo * 128:(mo + 1) * 128, sl],
                        in_=gtile[:, pp * TCB:(pp + 1) * TCB]),
                      r=[rgt[pp]], w=[rhbm_g], dma="hg")
                yield dn

    import itertools
    bbgen = itertools.chain(xn_chunks(), bb_chunks())
    bb_left = [NG * NTC + NTCB * (NHG + NG)]
    def bb_pull(n):
        for _ in range(n):
            if bb_left[0] <= 0:
                return
            next(bbgen)()
            bb_left[0] -= 1

    def rkv_group(gm):
        stream_w(h["wkT"], gm, NG, 0)
        stream_w(h["wvT"], gm, NG, 1)
        stream_w(h["wrT"], gm, NG, 2)
        for si, ci in ((0, 0), (1, 1), (2, 2)):
            wb_column(si, NG, ci)
            scale_slot(si, NG)

        def proj(si, ci, dst, rdst, bias_add=True):
            for tc in range(NTC):
                bank = tc % 2
                for k in range(NG):
                    E("t", lambda t, k=k, si=si, tc=tc, bank=bank: t.matmul(
                        out=PS[bank][:, 0:TC], lhsT=slot(si, k),
                        rhs=xbuf[:, k * T + tc * TC:k * T + (tc + 1) * TC],
                        start=(k == 0), stop=(k == NG - 1)),
                      r=[rslot[si], rxb[k]], w=[rps[bank]])
                E("v", lambda v, tc=tc, bank=bank: v.tensor_tensor(
                    out=dst[:, tc * TC:(tc + 1) * TC], in0=PS[bank][:, 0:TC],
                    in1=s1b[:, tc * TC:(tc + 1) * TC], op=ALU.mult),
                  r=[rps[bank], rs1b], w=[rdst])
                if bias_add:
                    E("v", lambda v, tc=tc, ci=ci: v.tensor_scalar(
                        out=dst[:, tc * TC:(tc + 1) * TC],
                        in0=dst[:, tc * TC:(tc + 1) * TC],
                        scalar1=wbcol[:, ci:ci + 1], scalar2=None,
                        op0=ALU.add), r=[rwbcol], w=[rdst])

        proj(0, 0, scB, rsc["scB"], bias_add=False)
        if tap == "l0_k":
            E("v", lambda v: v.tensor_scalar(
                out=scB[:, :], in0=scB[:, :], scalar1=wbcol[:, 0:1],
                scalar2=None, op0=ALU.add), r=[rwbcol], w=[rsc["scB"]])
            E("q", lambda q: q.dma_start(
                out=dbg[gm * 128:(gm + 1) * 128, :], in_=scB[:, :]),
              r=[rsc["scB"]], dma="out")
            E("s", lambda s: s.activation(out=scB[:, :], in_=scB[:, :],
                                          func=AF.Exp),
              r=[rsc["scB"]], w=[rsc["scB"]])
        else:
            E("s", lambda s: s.activation(out=scB[:, :], in_=scB[:, :],
                                          func=AF.Exp, bias=wbcol[:, 0:1]),
              r=[rsc["scB"], rwbcol], w=[rsc["scB"]])
        proj(1, 1, scA, rsc["scA"], bias_add=False)
        E("v", lambda v: v.scalar_tensor_tensor(
            out=scA[:, :], in0=scA[:, :], scalar=wbcol[:, 1:2],
            in1=scB[:, :], op0=ALU.add, op1=ALU.mult),
          r=[rsc["scB"], rwbcol], w=[rsc["scA"]])
        proj(2, 2, sigr, rsigr, bias_add=False)
        E("v", lambda v: v.tensor_scalar(
            out=wbcol[:, 3:4], in0=wbcol[:, 2:3], scalar1=0.5, scalar2=None,
            op0=ALU.mult), r=[rwbcol], w=[rwbcol])
        E("s", lambda s: s.activation(out=sigr[:, :], in_=sigr[:, :],
                                      func=AF.Tanh, scale=0.5,
                                      bias=wbcol[:, 3:4]),
          r=[rsigr, rwbcol], w=[rsigr])

        th = []
        A = th.append
        A(lambda: (E("v", lambda v: v.memset(d1t[:, :], 1.0), w=[rd1]),
                   E("v", lambda v: v.tensor_scalar(
                       out=d1t[:, :], in0=d1t[:, :], scalar1=ew[:, gm:gm + 1],
                       scalar2=None, op0=ALU.mult), r=[rcols], w=[rd1])))
        A(lambda: E("v", lambda v: v.tensor_copy(out=scE[:, 1:T + 1],
                                                 in_=scA[:, ::-1]),
                    r=[rsc["scA"]], w=[rsc["scE"]]))
        A(lambda: E("v", lambda v: v.tensor_copy(out=scF[:, 1:T + 1],
                                                 in_=scB[:, ::-1]),
                    r=[rsc["scB"]], w=[rsc["scF"]]))
        def _zc():
            for nm, tt in (("scC", scC), ("scD", scD), ("scE", scE),
                           ("scF", scF)):
                E("v", lambda v, tt=tt: v.memset(tt[:, 0:1], 0.0),
                  w=[rsc[nm]])
        A(_zc)
        A(lambda: E("v", lambda v: v.tensor_tensor_scan(
            out=scC[:, 1:T + 1], data0=d1t[:, :], data1=scA[:, :],
            initial=0.0, op0=ALU.mult, op1=ALU.add),
          r=[rd1, rsc["scA"]], w=[rsc["scC"]]))
        A(lambda: E("v", lambda v: v.tensor_tensor_scan(
            out=scD[:, 1:T + 1], data0=d1t[:, :], data1=scB[:, :],
            initial=0.0, op0=ALU.mult, op1=ALU.add),
          r=[rd1, rsc["scB"]], w=[rsc["scD"]]))
        A(lambda: E("v", lambda v: v.tensor_tensor_scan(
            out=scE[:, 1:T + 1], data0=d1t[:, :], data1=scE[:, 1:T + 1],
            initial=0.0, op0=ALU.mult, op1=ALU.add),
          r=[rd1], w=[rsc["scE"]]))
        A(lambda: E("v", lambda v: v.tensor_tensor_scan(
            out=scF[:, 1:T + 1], data0=d1t[:, :], data1=scF[:, 1:T + 1],
            initial=0.0, op0=ALU.mult, op1=ALU.add),
          r=[rd1], w=[rsc["scF"]]))
        A(lambda: E("v", lambda v: v.tensor_scalar(
            out=scA[:, :], in0=scA[:, :], scalar1=eu[:, gm:gm + 1],
            scalar2=None, op0=ALU.mult), r=[rcols], w=[rsc["scA"]]))
        A(lambda: E("v", lambda v: v.tensor_scalar(
            out=scB[:, :], in0=scB[:, :], scalar1=eu[:, gm:gm + 1],
            scalar2=None, op0=ALU.mult), r=[rcols], w=[rsc["scB"]]))
        A(lambda: E("v", lambda v: v.tensor_tensor(
            out=scC[:, 0:T], in0=scC[:, 0:T], in1=scA[:, :], op=ALU.add),
          r=[rsc["scA"]], w=[rsc["scC"]]))
        A(lambda: E("v", lambda v: v.tensor_tensor(
            out=scD[:, 0:T], in0=scD[:, 0:T], in1=scB[:, :], op=ALU.add),
          r=[rsc["scB"]], w=[rsc["scD"]]))
        A(lambda: E("v", lambda v: v.tensor_tensor(
            out=scA[:, :], in0=scA[:, :], in1=scE[:, 0:T][:, ::-1],
            op=ALU.add), r=[rsc["scE"]], w=[rsc["scA"]]))
        A(lambda: E("v", lambda v: v.tensor_tensor(
            out=scB[:, :], in0=scB[:, :], in1=scF[:, 0:T][:, ::-1],
            op=ALU.add), r=[rsc["scF"]], w=[rsc["scB"]]))
        A(lambda: E("v", lambda v: v.tensor_tensor(
            out=scA[:, :], in0=scA[:, :], in1=scD[:, 0:T], op=ALU.mult),
          r=[rsc["scD"]], w=[rsc["scA"]]))
        A(lambda: E("v", lambda v: v.tensor_tensor(
            out=scC[:, 0:T], in0=scC[:, 0:T], in1=scB[:, :], op=ALU.mult),
          r=[rsc["scB"]], w=[rsc["scC"]]))
        A(lambda: E("v", lambda v: v.tensor_tensor(
            out=scC[:, 0:T], in0=scC[:, 0:T], in1=scA[:, :], op=ALU.add),
          r=[rsc["scA"]], w=[rsc["scC"]]))
        A(lambda: E("v", lambda v: v.tensor_tensor(
            out=scD[:, 0:T], in0=scD[:, 0:T], in1=scB[:, :], op=ALU.mult),
          r=[rsc["scB"]], w=[rsc["scD"]]))
        A(lambda: E("s", lambda s: s.activation(
            out=scD[:, 0:T], in_=scD[:, 0:T], func=AF.Ln), w=[rsc["scD"]]))
        A(lambda: E("s", lambda s: s.activation(
            out=scD[:, 0:T], in_=scD[:, 0:T], func=AF.Exp, scale=-1.0),
          w=[rsc["scD"]]))
        A(lambda: E("v", lambda v: v.tensor_tensor(
            out=scC[:, 0:T], in0=scC[:, 0:T], in1=scD[:, 0:T], op=ALU.mult),
          r=[rsc["scD"]], w=[rsc["scC"]]))
        if tap == "l0_wkv":
            def _dump():
                E("v", lambda v: v.tensor_scalar(
                    out=scC[:, 0:T], in0=scC[:, 0:T], scalar1=0.5,
                    scalar2=None, op0=ALU.mult), r=[rsc["scC"]],
                  w=[rsc["scC"]])
                E("q", lambda q: q.dma_start(
                    out=dbg[gm * 128:(gm + 1) * 128, :], in_=scC[:, 0:T]),
                  r=[rsc["scC"]], dma="out")
            A(_dump)
            return th
        def _rw():
            E("v", lambda v: v.scalar_tensor_tensor(
                out=rwt[:, :], in0=sigr[:, :], scalar=1.0, in1=scC[:, 0:T],
                op0=ALU.add, op1=ALU.mult), r=[rsigr, rsc["scC"]], w=[rrwt])
            E("q", lambda q: q.dma_start(
                out=sp_rw[gm * 128:(gm + 1) * 128, :], in_=rwt[:, :]),
              r=[rrwt], w=[rhbm_rw], dma="hrw")
        A(_rw)
        return th

    BBTOT = NG * NTC + NTCB * (NHG + NG)
    pulled = 0
    for gm in range(NG):
        th = rkv_group(gm)
        quota = (gm + 1) * BBTOT // NG - pulled
        n = max(quota, len(th))
        ti = 0
        pi = 0
        for i in range(n):
            while pi * n < quota * (i + 1) and pi < quota:
                bb_pull(1)
                pi += 1
                pulled += 1
            while ti * n <= i * len(th) and ti < len(th):
                th[ti]()
                ti += 1
        while ti < len(th):
            th[ti]()
            ti += 1
    bb_pull(10 ** 9)

    b.barrier()
    p0.close()

    if tap in ("l0_k", "l0_wkv"):
        blk_ctx.__exit__(None, None, None)
        es.close()
        return nc, b

    p1 = contextlib.ExitStack()
    sb1 = lambda name, shape, dtype=F32: p1.enter_context(
        nc.sbuf_tensor(name, shape, dtype))
    x1 = sb1("x1", [128, NG * T])
    xb1 = sb1("xb1", [128, NG * T], BF16)
    rxb1 = [reg(f"xb1.{g}") for g in range(NG)]
    rx1t = [[reg(f"x1.{g}.{t_}") for t_ in range(NTC)] for g in range(NG)]
    rx1 = [reg(f"x1.{g}") for g in range(NG)]

    p1b = contextlib.ExitStack()
    sb1b = lambda name, shape, dtype=F32: p1b.enter_context(
        nc.sbuf_tensor(name, shape, dtype))
    kkb = sb1b("kkb", [128, NFG * TC], BF16)
    rwring = sb1b("rwring", [128, 2 * NG * TC], BF16)
    xmring = rwring
    g1ring = sb1b("g1ring", [128, 2 * TC], BF16)
    xcring = wslot[:, 3072:3072 + NG * TC]

    rkk = [reg(f"kk.{i}") for i in range(NFG)]
    rrwr = [reg("rwring.0"), reg("rwring.1")]
    rxmr = rrwr
    rg1 = [reg("g1ring.0"), reg("g1ring.1")]
    rxcr = reg("xcring")

    for tc in range(NTC):
        ppr = tc % 2
        E("q", lambda q, tc=tc, ppr=ppr: q.dma_start(
            out=rwring[:, ppr * NG * TC:(ppr + 1) * NG * TC].rearrange(
                "p (k c) -> p k c", k=NG),
            in_=sp_rw[:, tc * TC:(tc + 1) * TC].rearrange(
                "(k p) c -> p k c", p=128)), r=[rhbm_rw], w=[rrwr[ppr]], dma=f"rr{ppr}")
        for mo in range(NG):
            si = mo % 2
            stream_w(h["woT"], mo, NG, si)
            bank = 2 + (mo % 2)
            for k in range(NG):
                E("t", lambda t, k=k, si=si, bank=bank, ppr=ppr: t.matmul(
                    out=PS[bank][:, 0:TC], lhsT=slot(si, k),
                    rhs=rwring[:, ppr * NG * TC + k * TC:
                               ppr * NG * TC + (k + 1) * TC],
                    start=(k == 0), stop=(k == NG - 1)),
                  r=[rslot[si], rrwr[ppr]], w=[rps[bank]])
            pp = mo % 2
            E("q", lambda q, mo=mo, tc=tc, pp=pp: q.dma_start(
                out=g1ring[:, pp * TC:(pp + 1) * TC],
                in_=sp_xn[mo * 128:(mo + 1) * 128, tc * TC:(tc + 1) * TC]),
              r=[rhbm_xn], w=[rg1[pp]], dma=f"g1{pp}")
            E("v", lambda v, pp=pp, bank=bank: v.scalar_tensor_tensor(
                out=ev1[:, pp * TC:(pp + 1) * TC], in0=PS[bank][:, 0:TC],
                scalar=0.25, in1=g1ring[:, pp * TC:(pp + 1) * TC],
                op0=ALU.mult, op1=ALU.add),
              r=[rps[bank], rg1[pp]], w=[rev1[pp]])
            E("s", lambda s, pp=pp: s.activation(
                out=x2b[:, pp * TC:(pp + 1) * TC],
                in_=ev1[:, pp * TC:(pp + 1) * TC], func=AF.Copy),
              r=[rev1[pp]], w=[rx2b[pp]])
            E("q", lambda q, mo=mo, tc=tc, pp=pp: q.dma_start(
                out=sp_xm[mo * 128:(mo + 1) * 128, tc * TC:(tc + 1) * TC],
                in_=x2b[:, pp * TC:(pp + 1) * TC]),
              r=[rx2b[pp]], w=[rhbm_xm], dma="hxm")
            E("t", lambda t, mo=mo, pp=pp: t.matmul(
                out=PSS[0:1, 0:TC], lhsT=onescol[:, 0:1],
                rhs=x2b[:, pp * TC:(pp + 1) * TC],
                start=(mo == 0), stop=(mo == NG - 1)),
              r=[rx2b[pp], rones], w=[rpss])
            E("s", lambda s, pp=pp: s.activation(
                out=x2b[:, pp * TC:(pp + 1) * TC],
                in_=ev1[:, pp * TC:(pp + 1) * TC], func=AF.Square),
              r=[rev1[pp], rx2b[pp]], w=[rx2b[pp]])
            E("t", lambda t, mo=mo, pp=pp: t.matmul(
                out=PSB[0:1, 0:TC], lhsT=onescol[:, 0:1],
                rhs=x2b[:, pp * TC:(pp + 1) * TC],
                start=(mo == 0), stop=(mo == NG - 1)),
              r=[rx2b[pp], rones], w=[rpsb])
        E("v", lambda v: v.tensor_copy(out=S1(), in_=PSS[0:1, 0:TC]),
          r=[rpss], w=[rrows])
        E("v", lambda v: v.tensor_copy(out=S2(), in_=PSB[0:1, 0:TC]),
          r=[rpsb], w=[rrows])
        ln_rows_tc(tc, "ln2")

    if tap == "l0_g":
        for g in range(NG):
            for tc in range(NTC):
                pp = (g * NTC + tc) % 2
                E("g", lambda q, g=g, tc=tc, pp=pp: q.dma_start(
                    out=ev1[:, pp * TC:(pp + 1) * TC],
                    in_=sp_g[g * 128:(g + 1) * 128, tc * TC:(tc + 1) * TC]),
                  r=[reg("hbm_g")], w=[rev1[pp]], dma=f"dbr{pp}")
                E("v", lambda v, pp=pp: v.tensor_copy(
                    out=ev2[:, pp * TC:(pp + 1) * TC],
                    in_=ev1[:, pp * TC:(pp + 1) * TC]),
                  r=[rev1[pp]], w=[rev2[pp]])
                E("q", lambda q, g=g, tc=tc, pp=pp: q.dma_start(
                    out=dbg[g * 128:(g + 1) * 128, tc * TC:(tc + 1) * TC],
                    in_=ev2[:, pp * TC:(pp + 1) * TC]),
                  r=[rev2[pp]], dma=f"dbw{pp}")

    if tap == "l0_xmid":
        for g in range(NG):
            for tc in range(NTC):
                pp = (g * NTC + tc) % 2
                E("g", lambda q, g=g, tc=tc, pp=pp: q.dma_start(
                    out=ev1[:, pp * TC:(pp + 1) * TC],
                    in_=sp_xm[g * 128:(g + 1) * 128, tc * TC:(tc + 1) * TC]),
                  r=[rhbm_xm], w=[rev1[pp]], dma=f"dbr{pp}")
                E("v", lambda v, pp=pp: v.tensor_copy(
                    out=ev2[:, pp * TC:(pp + 1) * TC],
                    in_=ev1[:, pp * TC:(pp + 1) * TC]),
                  r=[rev1[pp]], w=[rev2[pp]])
                E("q", lambda q, g=g, tc=tc, pp=pp: q.dma_start(
                    out=dbg[g * 128:(g + 1) * 128, tc * TC:(tc + 1) * TC],
                    in_=ev2[:, pp * TC:(pp + 1) * TC]),
                  r=[rev2[pp]], dma=f"dbw{pp}")

    E("v", lambda v: v.tensor_copy(out=wcolf[:, :], in_=cols["ln2w"][:, :]),
      r=[rcols], w=[rwbcol])
    E("v", lambda v: v.tensor_copy(out=bcolb[:, :], in_=cols["ln2b"][:, :]),
      r=[rcols], w=[rbcolb])

    for tc in range(NTC):
        ppx = tc % 2
        E("q", lambda q, tc=tc, ppx=ppx: q.dma_start(
            out=xmring[:, ppx * NG * TC:(ppx + 1) * NG * TC].rearrange(
                "p (k c) -> p k c", k=NG),
            in_=sp_xm[:, tc * TC:(tc + 1) * TC].rearrange(
                "(k p) c -> p k c", p=128)), r=[rhbm_xm], w=[rxmr[ppx]], dma=f"xm{ppx}")
        def xm(k, ppx=ppx):
            return xmring[:, ppx * NG * TC + k * TC:ppx * NG * TC + (k + 1) * TC]
        def xc(k):
            return xcring[:, k * TC:(k + 1) * TC]
        for k in range(NG):
            E("v", lambda v, a=ev1[:, 0:TC], b_=xm(k),
                   c=mub[:, tc * TC:(tc + 1) * TC]:
                v.scalar_tensor_tensor(out=a, in0=c, scalar=-1.0, in1=b_,
                                       op0=ALU.mult, op1=ALU.add),
              r=[rxmr[ppx], rmub], w=[rev1[0]])
            E("v", lambda v, tc=tc: v.tensor_tensor(
                out=ev1[:, 0:TC], in0=ev1[:, 0:TC],
                in1=s1b[:, tc * TC:(tc + 1) * TC], op=ALU.mult),
              r=[rs1b], w=[rev1[0]])
            E("v", lambda v, a=xc(k), k=k: v.tensor_scalar(
                out=a, in0=ev1[:, 0:TC], scalar1=wcolf[:, k:k + 1],
                scalar2=None, op0=ALU.mult), r=[rev1[0], rwbcol], w=[rxcr])
        for mf in range(NFG):
            si = mf % 2
            stream_w(h["wfkT"], mf, NG, si)
            if tc == 0:
                wb_column(si, NG, 8 + mf, ps=PSS, rp=rpss)
            bank = mf % 2
            for k in range(NG):
                E("t", lambda t, k=k, si=si, bank=bank, a=xc(k): t.matmul(
                    out=PS[bank][:, 0:TC], lhsT=slot(si, k), rhs=a,
                    start=(k == 0), stop=(k == NG - 1)),
                  r=[rslot[si], rxcr], w=[rps[bank]])
            pp = mf % 2
            E("v", lambda v, pp=pp, mf=mf, bank=bank: v.tensor_scalar(
                out=ev1[:, pp * TC:(pp + 1) * TC],
                in0=PS[bank][:, 0:TC],
                scalar1=wbcol[:, 8 + mf:9 + mf],
                scalar2=0.0, op0=ALU.add, op1=ALU.max),
              r=[rps[bank], rwbcol], w=[rev1[pp]])
            E("s", lambda s, mf=mf, pp=pp: s.activation(
                out=kkb[:, mf * TC:(mf + 1) * TC],
                in_=ev1[:, pp * TC:(pp + 1) * TC], func=AF.Square),
              r=[rev1[pp]], w=[rkk[mf]])
        for mo in range(NG):
            si = 2
            stream_w(h["wfrT"], mo, NG, si)
            if tc == 0:
                wb_column(si, NG, 40 + mo, ps=PSS, rp=rpss)
                E("v", lambda v, mo=mo: v.tensor_scalar(
                    out=wbcol[:, 40 + mo:41 + mo],
                    in0=wbcol[:, 40 + mo:41 + mo], scalar1=0.5,
                    scalar2=None, op0=ALU.mult), r=[rwbcol], w=[rwbcol])
            bank = 2
            for k in range(NG):
                E("t", lambda t, k=k, si=si, bank=bank, a=xc(k): t.matmul(
                    out=PS[bank][:, 0:TC], lhsT=slot(si, k), rhs=a,
                    start=(k == 0), stop=(k == NG - 1)),
                  r=[rslot[si], rxcr], w=[rps[bank]])
            pp2 = mo % 2
            E("s", lambda s, pp2=pp2, mo=mo, bank=bank: s.activation(
                out=ev2[:, pp2 * TC:(pp2 + 1) * TC],
                in_=PS[bank][:, 0:TC], func=AF.Tanh,
                scale=0.5, bias=wbcol[:, 40 + mo:41 + mo]),
              r=[rps[bank], rwbcol], w=[rev2[pp2]])
            stream_w5(h["wfvT"], mo, NFG)
            bank2 = 4 + (mo % 2)
            for k in range(NFG):
                E("t", lambda t, k=k, bank2=bank2: t.matmul(
                    out=PS[bank2][:, 0:TC], lhsT=slot(5, k),
                    rhs=kkb[:, k * TC:(k + 1) * TC],
                    start=(k == 0), stop=(k == NFG - 1)),
                  r=[rslot5h[2 * k // NFG], rkk[k]], w=[rps[bank2]])
            pp = mo % 2
            xfs = xf[:, pp * TC:(pp + 1) * TC]
            E("q", lambda q, mo=mo, tc=tc, xfs=xfs: q.dma_start(
                out=xfs, in_=h["xT"][mo * 128:(mo + 1) * 128,
                                     tc * TC:(tc + 1) * TC]),
              w=[rxf[pp]], dma=f"xf{pp}")
            E("q", lambda q, mo=mo, tc=tc, pp=pp: q.dma_start(
                out=g1ring[:, pp * TC:(pp + 1) * TC],
                in_=sp_g[mo * 128:(mo + 1) * 128, tc * TC:(tc + 1) * TC]),
              r=[rhbm_g], w=[rg1[pp]], dma=f"g1{pp}")
            E("v", lambda v, pp2=pp2, bank2=bank2: v.scalar_tensor_tensor(
                out=ev2[:, pp2 * TC:(pp2 + 1) * TC],
                in0=ev2[:, pp2 * TC:(pp2 + 1) * TC], scalar=1.0,
                in1=PS[bank2][:, 0:TC], op0=ALU.add, op1=ALU.mult),
              r=[rps[bank2]], w=[rev2[pp2]])
            E("v", lambda v, pp2=pp2, a=xm(mo): v.scalar_tensor_tensor(
                out=ev2[:, pp2 * TC:(pp2 + 1) * TC],
                in0=ev2[:, pp2 * TC:(pp2 + 1) * TC], scalar=0.5,
                in1=a, op0=ALU.mult, op1=ALU.add),
              r=[rxmr[ppx]], w=[rev2[pp2]])
            E("v", lambda v, pp2=pp2, pp=pp: v.tensor_tensor(
                out=ev2[:, pp2 * TC:(pp2 + 1) * TC],
                in0=ev2[:, pp2 * TC:(pp2 + 1) * TC],
                in1=g1ring[:, pp * TC:(pp + 1) * TC], op=ALU.mult),
              r=[rg1[pp]], w=[rev2[pp2]])
            E("v", lambda v, pp2=pp2, mo=mo, tc=tc, pp=pp:
                v.scalar_tensor_tensor(
                    out=x1[:, mo * T + tc * TC:mo * T + (tc + 1) * TC],
                    in0=ev2[:, pp2 * TC:(pp2 + 1) * TC], scalar=1.0 / 508.0,
                    in1=xf[:, pp * TC:(pp + 1) * TC], op0=ALU.mult,
                    op1=ALU.add), r=[rev2[pp2], rxf[pp]], w=[rx1t[mo][tc]])
            E("s", lambda s, mo=mo, tc=tc, pp=pp: s.activation(
                out=x2b[:, pp * TC:(pp + 1) * TC],
                in_=x1[:, mo * T + tc * TC:mo * T + (tc + 1) * TC],
                func=AF.Copy), r=[rx1t[mo][tc]], w=[rx2b[pp]])
            E("t", lambda t, mo=mo, pp=pp: t.matmul(
                out=PSB[0:1, 0:TC], lhsT=onescol[:, 0:1],
                rhs=x2b[:, pp * TC:(pp + 1) * TC],
                start=(mo == 0), stop=(mo == NG - 1)),
              r=[rx2b[pp], rones], w=[rpsb])
            E("s", lambda s, mo=mo, tc=tc, pp=pp: s.activation(
                out=x2b[:, pp * TC:(pp + 1) * TC],
                in_=x1[:, mo * T + tc * TC:mo * T + (tc + 1) * TC],
                func=AF.Square), r=[rx1t[mo][tc]], w=[rx2b[pp]])
            E("t", lambda t, mo=mo, pp=pp: t.matmul(
                out=PS[3][0:1, 0:TC], lhsT=onescol[:, 0:1],
                rhs=x2b[:, pp * TC:(pp + 1) * TC],
                start=(mo == 0), stop=(mo == NG - 1)),
              r=[rx2b[pp], rones], w=[rps[3]])
        E("v", lambda v: v.tensor_copy(out=S1(), in_=PSB[0:1, 0:TC]),
          r=[rpsb], w=[rrows])
        E("v", lambda v: v.tensor_copy(out=S2(), in_=PS[3][0:1, 0:TC]),
          r=[rps[3]], w=[rrows])
        ln_rows_tc(tc, "l1")
        for mo in range(NG):
            E("v", lambda v, mo=mo, tc=tc: v.tensor_tensor(
                out=xb1[:, mo * T + tc * TC:mo * T + (tc + 1) * TC],
                in0=x1[:, mo * T + tc * TC:mo * T + (tc + 1) * TC],
                in1=mub[:, tc * TC:(tc + 1) * TC], op=ALU.subtract),
              r=[rx1t[mo][tc], rmub], w=[rxb1[mo]])

    if tap == "l0_out":
        for g in range(NG):
            E("q", lambda q, g=g: q.dma_start(
                out=dbg[g * 128:(g + 1) * 128, :],
                in_=x1[:, g * T:(g + 1) * T]), r=list(rx1t[g]), dma="out")

    stream_w(h["q_T"], 0, NG, 0)
    stream_w(h["k_T"], 0, NG, 1)
    stream_w(h["v_T"], 0, NG, 2)

    b.barrier()
    p1b.close()

    p2 = contextlib.ExitStack()
    sb2 = lambda name, shape, dtype=F32: p2.enter_context(
        nc.sbuf_tensor(name, shape, dtype))
    qpair = sb2("qpair", [128, 2 * T], BF16)
    kpair = sb2("kpair", [128, 2 * T], BF16)
    vpair = sb2("vpair", [128, 2 * NT2 * 130], BF16)
    attw = sb2("attw", [128, 8 * 512], BF16)
    p64 = sb2("p64", [65, 512])
    p64b = sb2("p64b", [65, 512], BF16)
    ones64 = sb2("ones64", [65, 64], BF16)
    stg = sb2("stg", [64, 2 * 2 * TC], BF16)
    rtile = sb2("rtile", [64, 512], BF16)
    masks = sb2("masksb", [128, (TC // 128) * TC], BF16)
    E("g", lambda g_: g_.dma_start(
        out=masks[:, :], in_=h["maskc"][:, :]), w=[rmask], dma="prem")

    rqp = [reg("qpair.0"), reg("qpair.1")]
    rkp = [reg("kpair.0"), reg("kpair.1")]
    rvp = [reg("vpair.0"), reg("vpair.1")]
    ratt = [reg(f"attw.{i}") for i in range(8)]
    rp64 = reg("p64"); rp64d = reg("p64d")
    rp64b = reg("p64b"); ro64 = reg("ones64")
    rstg = [reg("stg.0"), reg("stg.1")]
    rrt = reg("rtile")

    E("v", lambda v: v.memset(ones64[:, :], 1.0), w=[ro64])

    def prework():
            for g in range(NG):
                for tc in range(NTC):
                    pp = tc % 2
                    E("v", lambda v, g=g, tc=tc, pp=pp: v.tensor_tensor(
                        out=ev1[:, pp * TC:(pp + 1) * TC],
                        in0=x1[:, g * T + tc * TC:g * T + (tc + 1) * TC],
                        in1=s1b[:, tc * TC:(tc + 1) * TC], op=ALU.mult),
                      r=[rx1t[g][tc], rs1b], w=[rev1[pp]])
                    E("v", lambda v, tc=tc, pp=pp: v.tensor_tensor(
                        out=ev1[:, pp * TC:(pp + 1) * TC],
                        in0=ev1[:, pp * TC:(pp + 1) * TC],
                        in1=m0b[:, tc * TC:(tc + 1) * TC], op=ALU.subtract),
                      r=[rm0b], w=[rev1[pp]])
                    E("v", lambda v, g=g, pp=pp: v.tensor_scalar(
                        out=ev1[:, pp * TC:(pp + 1) * TC],
                        in0=ev1[:, pp * TC:(pp + 1) * TC],
                        scalar1=cols["gamma"][:, g:g + 1],
                        scalar2=cols["beta"][:, g:g + 1], op0=ALU.mult, op1=ALU.add),
                      r=[rcols], w=[rev1[pp]])
                    E("v", lambda v, g=g, tc=tc, pp=pp: v.tensor_reduce(
                        out=mxc[:, g * NTC + tc:g * NTC + tc + 1],
                        in_=ev1[:, pp * TC:(pp + 1) * TC], axis=AXX,
                        op=ALU.max, apply_absolute_value=True),
                      r=[rev1[pp]], w=[rmxc])
                    yield
            E("v", lambda v: v.tensor_reduce(
                out=mxc[:, 33:34], in_=mxc[:, 0:NG * NTC], axis=AXX,
                op=ALU.max, apply_absolute_value=False), r=[rmxc], w=[rmxc])
            E("g", lambda g_: g_.dma_start(out=p64[0:1, 0:128], in_=mxc[:, 33:34], allow_slow_non_contiguous=True),
              r=[rmxc], w=[rp64], dma="p64s")
            E("v", lambda v: v.tensor_reduce(
                out=p64[0:1, 128:129], in_=p64[0:1, 0:128], axis=AXX,
                op=ALU.max, apply_absolute_value=False), r=[rp64], w=[rp64])
            E("g", lambda g_: g_.dma_start(out=cc_in[0:1, 0:1],
                                           in_=p64[0:1, 128:129]),
              r=[rp64], w=[reg("hbm_cc")], dma="p64s")
            E("g", lambda g_: g_.collective_compute(
                "AllReduce", ALU.max, replica_groups=[list(range(cores))],
                ins=[cc_in[0:1, 0:1]], outs=[cc_out[0:1, 0:1]]),
              r=[reg("hbm_cc")], w=[rcc], dma="cc", inc=1)
            E("g", lambda g_: g_.dma_start(out=p64[0:1, 129:130],
                                           in_=cc_out[0:1, 0:1]),
              r=[rcc], w=[rp64], dma="p64s")
            E("v", lambda v: v.tensor_scalar(
                out=p64[0:1, 130:131], in0=p64[0:1, 129:130], scalar1=1.0 / 127.0,
                scalar2=None, op0=ALU.mult), r=[rp64], w=[rp64])
            E("v", lambda v: v.reciprocal(out=p64[0:1, 131:132],
                                          in_=p64[0:1, 130:131]), r=[rp64], w=[rp64])
            E("t", lambda t: t.matmul(
                out=PSS[:, 0:2], lhsT=onescolf[0:1, :], rhs=p64[0:1, 130:132],
                start=True, stop=True), r=[rp64, rones], w=[rpss])
            E("v", lambda v: v.tensor_copy(out=mxc[:, 34:36], in_=PSS[:, 0:2]),
              r=[rpss], w=[rmxc])

            for kq in range(NG):
                for c2 in range(NW):
                    pp = (kq * NW + c2) % 2
                    E("q", lambda q, kq=kq, c2=c2, pp=pp: q.dma_start(
                        out=xf[:, pp * TC:pp * TC + WQC],
                        in_=h["wT"][kq * 128:(kq + 1) * 128, c2 * WQC:(c2 + 1) * WQC]),
                      w=[rxf[pp]], dma=f"xf{pp}")
                    E("v", lambda v, kq=kq, c2=c2, pp=pp: v.tensor_reduce(
                        out=mxc[:, 40 + kq * NW + c2:41 + kq * NW + c2],
                        in_=xf[:, pp * TC:pp * TC + WQC], axis=AXX,
                        op=ALU.add, apply_absolute_value=True),
                      r=[rxf[pp]], w=[rmxc])
                    yield
            E("v", lambda v: v.tensor_reduce(
                out=mxc[:, 38:39], in_=mxc[:, 40:40 + NG * NW], axis=AXX,
                op=ALU.add), r=[rmxc], w=[rmxc])
            E("g", lambda g_: g_.dma_start(out=p64[0:1, 0:128], in_=mxc[:, 38:39], allow_slow_non_contiguous=True),
              r=[rmxc], w=[rp64], dma="p64s")
            E("v", lambda v: v.tensor_reduce(
                out=p64[0:1, 132:133], in_=p64[0:1, 0:128], axis=AXX, op=ALU.add),
              r=[rp64], w=[rp64])
            E("v", lambda v: v.tensor_scalar(
                out=p64[0:1, 133:134], in0=p64[0:1, 132:133],
                scalar1=0.7 / float(D * D), scalar2=None, op0=ALU.mult),
              r=[rp64], w=[rp64])
            E("t", lambda t: t.matmul(
                out=PSS[:, 0:1], lhsT=onescolf[0:1, :], rhs=p64[0:1, 133:134],
                start=True, stop=True), r=[rp64, rones], w=[rpss])
            E("v", lambda v: v.tensor_copy(out=mxc[:, 36:37], in_=PSS[:, 0:1]),
              r=[rpss], w=[rmxc])
            for kq in range(NG):
                for c2 in range(NW):
                    pp = (kq * NW + c2) % 2
                    E("q", lambda q, kq=kq, c2=c2, pp=pp: q.dma_start(
                        out=xf[:, pp * TC:pp * TC + WQC],
                        in_=h["wT"][kq * 128:(kq + 1) * 128, c2 * WQC:(c2 + 1) * WQC]),
                      w=[rxf[pp]], dma=f"xf{pp}")
                    E("s", lambda s, pp=pp: s.activation(
                        out=ev1[:, pp * TC:pp * TC + WQC],
                        in_=xf[:, pp * TC:pp * TC + WQC], func=AF.Abs),
                      r=[rxf[pp]], w=[rev1[pp]])
                    E("v", lambda v, pp=pp: v.tensor_scalar(
                        out=ev2[:, pp * TC:pp * TC + WQC],
                        in0=ev1[:, pp * TC:pp * TC + WQC], scalar1=mxc[:, 36:37],
                        scalar2=None, op0=ALU.is_gt), r=[rev1[pp], rmxc],
                      w=[rev2[pp]])
                    E("v", lambda v, kq=kq, c2=c2, pp=pp: v.tensor_reduce(
                        out=mxc[:, 40 + kq * NW + c2:41 + kq * NW + c2],
                        in_=ev2[:, pp * TC:pp * TC + WQC], axis=AXX, op=ALU.add),
                      r=[rev2[pp]], w=[rmxc])
                    E("v", lambda v, pp=pp: v.tensor_tensor(
                        out=ev1[:, pp * TC:pp * TC + WQC],
                        in0=ev1[:, pp * TC:pp * TC + WQC],
                        in1=ev2[:, pp * TC:pp * TC + WQC], op=ALU.mult),
                      r=[rev2[pp]], w=[rev1[pp]])
                    E("v", lambda v, kq=kq, c2=c2, pp=pp: v.tensor_reduce(
                        out=mxc[:, MXB + kq * NW + c2:MXB + 1 + kq * NW + c2],
                        in_=ev1[:, pp * TC:pp * TC + WQC], axis=AXX, op=ALU.add),
                      r=[rev1[pp]], w=[rmxc])
                    yield
            E("v", lambda v: v.tensor_reduce(
                out=mxc[:, 38:39], in_=mxc[:, 40:40 + NG * NW], axis=AXX,
                op=ALU.add), r=[rmxc], w=[rmxc])
            E("v", lambda v: v.tensor_reduce(
                out=mxc[:, 39:40], in_=mxc[:, MXB:MXB + NG * NW], axis=AXX,
                op=ALU.add), r=[rmxc], w=[rmxc])
            E("g", lambda g_: g_.dma_start(out=p64[0:1, 0:128], in_=mxc[:, 38:39], allow_slow_non_contiguous=True),
              r=[rmxc], w=[rp64], dma="p64s")
            E("g", lambda g_: g_.dma_start(out=p64[0:1, 128:256], in_=mxc[:, 39:40], allow_slow_non_contiguous=True),
              r=[rmxc], w=[rp64], dma="p64s")
            E("v", lambda v: v.tensor_reduce(
                out=p64[0:1, 256:257], in_=p64[0:1, 0:128], axis=AXX, op=ALU.add),
              r=[rp64], w=[rp64])
            E("v", lambda v: v.tensor_reduce(
                out=p64[0:1, 257:258], in_=p64[0:1, 128:256], axis=AXX, op=ALU.add),
              r=[rp64], w=[rp64])
            E("v", lambda v: v.tensor_scalar(
                out=p64[0:1, 256:257], in0=p64[0:1, 256:257], scalar1=1.0,
                scalar2=None, op0=ALU.max), r=[rp64], w=[rp64])
            E("v", lambda v: v.reciprocal(out=p64[0:1, 256:257],
                                          in_=p64[0:1, 256:257]), r=[rp64], w=[rp64])
            E("v", lambda v: v.tensor_tensor(
                out=p64[0:1, 257:258], in0=p64[0:1, 257:258], in1=p64[0:1, 256:257],
                op=ALU.mult), r=[rp64], w=[rp64])
            E("t", lambda t: t.matmul(
                out=PSS[:, 0:1], lhsT=onescolf[0:1, :], rhs=p64[0:1, 257:258],
                start=True, stop=True), r=[rp64, rones], w=[rpss])
            E("v", lambda v: v.tensor_copy(out=mxc[:, 37:38], in_=PSS[:, 0:1]),
              r=[rpss], w=[rmxc])
            yield
    _pw = prework()
    _pw_done = [False]
    def pw_pull(n):
        for _ in range(n):
            if _pw_done[0]:
                return
            try:
                next(_pw)
            except StopIteration:
                _pw_done[0] = True
                return

    def qkv_proj(pair, prefetched=False):
        pb = pair % 2
        th = []
        if not prefetched:
            th.append(lambda pair=pair: (stream_w(h["q_T"], pair, NG, 0),
                                         stream_w(h["k_T"], pair, NG, 1)))
        for tc in range(NTC):
            for si, dst, rdst in ((0, qpair, rqp), (1, kpair, rkp)):
                def chunk(tc=tc, si=si, dst=dst, rdst=rdst, pb=pb):
                    bank = si
                    for k in range(NG):
                        E("t", lambda t, k=k, si=si, tc=tc, bank=bank:
                            t.matmul(
                                out=PS[bank][:, 0:TC], lhsT=slot(si, k),
                                rhs=xb1[:, k * T + tc * TC:
                                        k * T + (tc + 1) * TC],
                                start=(k == 0), stop=(k == NG - 1)),
                          r=[rslot[si], rxb1[k]], w=[rps[bank]])
                    E("v", lambda v, tc=tc, bank=bank, dst=dst, pb=pb:
                        v.tensor_tensor(
                            out=dst[:, pb * T + tc * TC:
                                    pb * T + (tc + 1) * TC],
                            in0=PS[bank][:, 0:TC],
                            in1=s0b[:, tc * TC:(tc + 1) * TC], op=ALU.mult),
                      r=[rps[bank], rs0b], w=[rdst[pb]])
                th.append(chunk)
        if not prefetched:
            th.append(lambda pair=pair: stream_w(h["v_T"], pair, NG, 2))
        for t2 in range(NT2):
            def vchunk(t2=t2, pb=pb):
                vb = pb * NT2 * 130
                for k in range(NG):
                    E("t", lambda t, k=k, t2=t2: t.matmul(
                        out=PSB[:, 0:128],
                        lhsT=xb1[:, k * T + t2 * 128:k * T + (t2 + 1) * 128],
                        rhs=slot(2, k), start=(k == 0), stop=(k == NG - 1)),
                      r=[rslot[2], rxb1[k]], w=[rpsb])
                for half in range(2):
                    E("v", lambda v, t2=t2, half=half, vb=vb: v.tensor_scalar(
                        out=vpair[:, vb + t2 * 130 + half * 65:
                                  vb + t2 * 130 + half * 65 + 64],
                        in0=PSB[:, half * 64:(half + 1) * 64],
                        scalar1=scol[:, t2:t2 + 1], scalar2=None,
                        op0=ALU.mult),
                      r=[rpsb, rscol], w=[rvp[pb]])
                E("v", lambda v, t2=t2, vb=vb: v.memset(
                    vpair[:, vb + t2 * 130 + 64:vb + t2 * 130 + 65], 1.0),
                  w=[rvp[pb]])
                E("v", lambda v, t2=t2, vb=vb: v.memset(
                    vpair[:, vb + t2 * 130 + 129:vb + t2 * 130 + 130], 1.0),
                  w=[rvp[pb]])
            th.append(vchunk)
        return th

    def attention(pair, th_next):
        pb = pair % 2
        vb = pb * NT2 * 130
        steps = sum((tc + 1) * (TC // 128) for tc in range(NTC)) + NTC
        step_i = 0
        ti = 0
        def pull():
            nonlocal ti
            while ti * steps < step_i * len(th_next) and ti < len(th_next):
                th_next[ti]()
                ti += 1
        for tc in range(NTC):
            ppg = tc % 2
            nk2 = (tc + 1) * (TC // 128)
            for k2 in range(nk2):
                for half in range(2):
                    hb = half * 64
                    sbank = (k2 % 2) * 2 + half
                    E("t", lambda t, k2=k2, hb=hb, tc=tc, sbank=sbank, pb=pb:
                        t.matmul(
                            out=PS[sbank][:, 0:TC],
                            lhsT=kpair[hb:hb + 64,
                                       pb * T + k2 * 128:pb * T + (k2 + 1) * 128],
                            rhs=qpair[hb:hb + 64,
                                      pb * T + tc * TC:pb * T + (tc + 1) * TC],
                            start=True, stop=True),
                      r=[rkp[pb], rqp[pb]], w=[rps[sbank]])
                    dj = k2 * 128 - tc * TC
                    if dj >= 0:
                        ji = dj // 128
                        E("v", lambda v, ji=ji, sbank=sbank: v.tensor_tensor(
                            out=PS[sbank][:, 0:TC], in0=PS[sbank][:, 0:TC],
                            in1=masks[:, ji * TC:(ji + 1) * TC], op=ALU.add),
                          r=[rmask], w=[rps[sbank]])
                    slot_a = (k2 % 4) * 2 + half
                    E("s", lambda s, sbank=sbank, slot_a=slot_a: s.activation(
                        out=attw[:, slot_a * 512:slot_a * 512 + TC],
                        in_=PS[sbank][:, 0:TC], func=AF.Exp, scale=0.125),
                      r=[rps[sbank]], w=[ratt[slot_a]])
                    E("t", lambda t, k2=k2, half=half, slot_a=slot_a,
                           nk2=nk2, vb=vb: t.matmul(
                            out=PS[4 + half][0:65, 0:TC],
                            lhsT=vpair[:, vb + k2 * 130 + half * 65:
                                       vb + k2 * 130 + half * 65 + 65],
                            rhs=attw[:, slot_a * 512:slot_a * 512 + TC],
                            start=(k2 == 0), stop=(k2 == nk2 - 1)),
                      r=[rvp[pb], ratt[slot_a]], w=[rps[4 + half]])
                step_i += 1
                pull()
            for half in range(2):
                E("s", lambda s, half=half: s.activation(
                    out=p64[64:65, 0:TC],
                    in_=PS[4 + half][64:65, 0:TC], func=AF.Ln),
                  r=[rps[4 + half]], w=[rp64d])
                E("s", lambda s: s.activation(
                    out=p64b[64:65, 0:TC], in_=p64[64:65, 0:TC],
                    func=AF.Exp, scale=-1.0), r=[rp64d], w=[rp64b])
                E("t", lambda t: t.matmul(
                    out=PSS[0:64, 0:TC], lhsT=ones64[64:65, 0:64],
                    rhs=p64b[64:65, 0:TC], start=True, stop=True),
                  r=[rp64b, ro64], w=[rpss])
                E("v", lambda v: v.tensor_copy(
                    out=rtile[0:64, 0:TC], in_=PSS[0:64, 0:TC]),
                  r=[rpss], w=[rrt])
                E("v", lambda v, half=half, ppg=ppg: v.tensor_tensor(
                    out=stg[0:64, ppg * 2 * TC + half * TC:
                            ppg * 2 * TC + half * TC + TC],
                    in0=PS[4 + half][0:64, 0:TC], in1=rtile[0:64, 0:TC],
                    op=ALU.mult), r=[rps[4 + half], rrt], w=[rstg[ppg]])
            E("q", lambda q, pair=pair, tc=tc, ppg=ppg: q.dma_start(
                out=sp_rw[pair * 128:(pair + 1) * 128,
                          tc * TC:(tc + 1) * TC].rearrange(
                    "(hh p) c -> p hh c", hh=2),
                in_=stg[0:64, ppg * 2 * TC:ppg * 2 * TC + 2 * TC].rearrange(
                    "p (hh q) -> p hh q", hh=2)),
              r=[rstg[ppg]], w=[rhbm_rw], dma="hrw")
            step_i += 1
            pull()
        while ti < len(th_next):
            th_next[ti]()
            ti += 1

    PW_SCHED = [8, 8, 8, 8, 8, 19, 19, 19]
    for t_ in qkv_proj(0, prefetched=True):
        t_()
    for pair in range(NPAIR):
        pw_pull(PW_SCHED[pair])
        th_next = qkv_proj(pair + 1) if pair + 1 < NPAIR else []
        attention(pair, th_next)

    pw_pull(10 ** 9)

    if tap == "l1_att":
        for g in range(NG):
            for tc in range(NTC):
                pp = (g * NTC + tc) % 2
                E("g", lambda q, g=g, tc=tc, pp=pp: q.dma_start(
                    out=ev1[:, pp * TC:(pp + 1) * TC],
                    in_=sp_rw[g * 128:(g + 1) * 128, tc * TC:(tc + 1) * TC]),
                  r=[rhbm_rw], w=[rev1[pp]], dma=f"dbr{pp}")
                E("v", lambda v, pp=pp: v.tensor_copy(
                    out=ev2[:, pp * TC:(pp + 1) * TC],
                    in_=ev1[:, pp * TC:(pp + 1) * TC]),
                  r=[rev1[pp]], w=[rev2[pp]])
                E("q", lambda q, g=g, tc=tc, pp=pp: q.dma_start(
                    out=dbg[g * 128:(g + 1) * 128, tc * TC:(tc + 1) * TC],
                    in_=ev2[:, pp * TC:(pp + 1) * TC]),
                  r=[rev2[pp]], dma=f"dbw{pp}")

    b.barrier()
    p2.close()

    p3 = contextlib.ExitStack()
    sb3 = lambda name, shape, dtype=F32: p3.enter_context(
        nc.sbuf_tensor(name, shape, dtype))
    wqb = sb3("wqb", [128, NG * D], BF16)
    xqr = sb3("xqr", [128, 2 * NG * TC], BF16)
    attr = sb3("attr", [128, 2 * NG * TC], BF16)
    qint = sb3("qint", [128, TC], I32)
    xev = xf

    rwq = reg("wqb")
    rxq = [reg("xqr.0"), reg("xqr.1")]
    rqi = reg("qint")
    rxe = rxf
    ratr = [reg("attr.0"), reg("attr.1")]

    for kq in range(NG):
        for c2 in range(NW):
            pp = (kq * NW + c2) % 2
            E("q", lambda q, kq=kq, c2=c2, pp=pp: q.dma_start(
                out=xf[:, pp * TC:pp * TC + WQC],
                in_=h["wT"][kq * 128:(kq + 1) * 128, c2 * WQC:(c2 + 1) * WQC]),
              w=[rxf[pp]], dma=f"xf{pp}")
            E("s", lambda s, pp=pp: s.activation(
                out=ev1[:, pp * TC:pp * TC + WQC],
                in_=xf[:, pp * TC:pp * TC + WQC], func=AF.Abs),
              r=[rxf[pp]], w=[rev1[pp]])
            E("v", lambda v, pp=pp: v.tensor_scalar(
                out=ev1[:, pp * TC:pp * TC + WQC],
                in0=ev1[:, pp * TC:pp * TC + WQC], scalar1=mxc[:, 36:37],
                scalar2=None, op0=ALU.is_gt), r=[rmxc], w=[rev1[pp]])
            E("s", lambda s, pp=pp: s.sign(
                out=ev2[:, pp * TC:pp * TC + WQC],
                in_=xf[:, pp * TC:pp * TC + WQC]), r=[rxf[pp]], w=[rev2[pp]])
            E("v", lambda v, pp=pp: v.tensor_tensor(
                out=ev1[:, pp * TC:pp * TC + WQC],
                in0=ev1[:, pp * TC:pp * TC + WQC],
                in1=ev2[:, pp * TC:pp * TC + WQC], op=ALU.mult),
              r=[rev2[pp]], w=[rev1[pp]])
            E("v", lambda v, kq=kq, c2=c2, pp=pp: v.tensor_scalar(
                out=wqb[:, kq * D + c2 * WQC:kq * D + (c2 + 1) * WQC],
                in0=ev1[:, pp * TC:pp * TC + WQC], scalar1=mxc[:, 37:38],
                scalar2=None, op0=ALU.mult), r=[rev1[pp], rmxc], w=[rwq])

    def xq_chunk(tc, k):
        ppx = tc % 2
        pp = k % 2
        E("v", lambda v, k=k, tc=tc, pp=pp: v.tensor_tensor(
            out=xev[:, pp * TC:(pp + 1) * TC],
            in0=x1[:, k * T + tc * TC:k * T + (tc + 1) * TC],
            in1=s1b[:, tc * TC:(tc + 1) * TC], op=ALU.mult),
          r=[rx1t[k][tc], rs1b], w=[rxe[pp]])
        E("v", lambda v, tc=tc, pp=pp: v.tensor_tensor(
            out=xev[:, pp * TC:(pp + 1) * TC],
            in0=xev[:, pp * TC:(pp + 1) * TC],
            in1=m0b[:, tc * TC:(tc + 1) * TC], op=ALU.subtract),
          r=[rm0b], w=[rxe[pp]])
        E("v", lambda v, k=k, pp=pp: v.tensor_scalar(
            out=xev[:, pp * TC:(pp + 1) * TC],
            in0=xev[:, pp * TC:(pp + 1) * TC],
            scalar1=cols["gamma"][:, k:k + 1],
            scalar2=cols["beta"][:, k:k + 1], op0=ALU.mult, op1=ALU.add),
          r=[rcols], w=[rxe[pp]])
        E("v", lambda v, pp=pp: v.tensor_scalar(
            out=xev[:, pp * TC:(pp + 1) * TC],
            in0=xev[:, pp * TC:(pp + 1) * TC], scalar1=mxc[:, 35:36],
            scalar2=None, op0=ALU.mult), r=[rmxc], w=[rxe[pp]])
        E("v", lambda v, pp=pp: v.tensor_scalar(
            out=xev[:, pp * TC:(pp + 1) * TC],
            in0=xev[:, pp * TC:(pp + 1) * TC], scalar1=-127.0,
            scalar2=127.0, op0=ALU.max, op1=ALU.min), w=[rxe[pp]])
        E("s", lambda s, pp=pp: s.sign(
            out=qint[:, 0:TC].bitcast(F32) if False else ev2[:, 0:TC],
            in_=xev[:, pp * TC:(pp + 1) * TC]),
          r=[rxe[pp]], w=[rev2[0]])
        E("v", lambda v, pp=pp: v.scalar_tensor_tensor(
            out=xev[:, pp * TC:(pp + 1) * TC],
            in0=ev2[:, 0:TC], scalar=0.5,
            in1=xev[:, pp * TC:(pp + 1) * TC], op0=ALU.mult,
            op1=ALU.add), r=[rev2[0]], w=[rxe[pp]])
        E("v", lambda v, pp=pp: v.tensor_copy(
            out=qint[:, 0:TC], in_=xev[:, pp * TC:(pp + 1) * TC]),
          r=[rxe[pp]], w=[rqi])
        E("v", lambda v, k=k, tc=tc, ppx=ppx: v.tensor_scalar(
            out=xqr[:, ppx * NG * TC + k * TC:
                    ppx * NG * TC + (k + 1) * TC],
            in0=qint[:, 0:TC], scalar1=mxc[:, 34:35],
            scalar2=None, op0=ALU.mult), r=[rqi, rmxc],
          w=[rxq[ppx]])

    def attr_dma(tc):
        ppx = tc % 2
        E("q", lambda q, tc=tc, ppx=ppx: q.dma_start(
            out=attr[:, ppx * NG * TC:(ppx + 1) * NG * TC].rearrange(
                "p (k c) -> p k c", k=NG),
            in_=sp_rw[:, tc * TC:(tc + 1) * TC].rearrange(
                "(k p) c -> p k c", p=128)),
          r=[rhbm_rw], w=[ratr[ppx]], dma=f"at{ppx}")

    attr_dma(0)
    for k in range(NG):
        xq_chunk(0, k)
    for tc in range(NTC):
        ppx = tc % 2
        if tc + 1 < NTC:
            attr_dma(tc + 1)
        for mo in range(NG):
            if tc + 1 < NTC:
                xq_chunk(tc + 1, mo)
            si = mo % 2
            stream_w(h["o_T"], mo, NG, si)
            bank = 2 + (mo % 2)
            for k in range(NG):
                E("t", lambda t, k=k, si=si, bank=bank, ppx=ppx: t.matmul(
                    out=PS[bank][:, 0:TC], lhsT=slot(si, k),
                    rhs=attr[:, ppx * NG * TC + k * TC:
                             ppx * NG * TC + (k + 1) * TC],
                    start=(k == 0), stop=(k == NG - 1)),
                  r=[rslot[si], ratr[ppx]], w=[rps[bank]])
            bank2 = 4 + (mo % 2)
            for k in range(NG):
                E("t", lambda t, k=k, mo=mo, bank2=bank2, ppx=ppx: t.matmul(
                    out=PS[bank2][:, 0:TC],
                    lhsT=wqb[:, k * D + mo * 128:k * D + (mo + 1) * 128],
                    rhs=xqr[:, ppx * NG * TC + k * TC:
                            ppx * NG * TC + (k + 1) * TC],
                    start=(k == 0), stop=(k == NG - 1)),
                  r=[rwq, rxq[ppx]], w=[rps[bank2]])
            pp = mo % 2
            E("s", lambda s, bank2=bank2, pp=pp: s.activation(
                out=x2b[:, pp * TC:(pp + 1) * TC], in_=PS[bank2][:, 0:TC],
                func=AF.Tanh, scale=0.5), r=[rps[bank2]], w=[rx2b[pp]])
            E("v", lambda v, bank2=bank2, pp=pp: v.scalar_tensor_tensor(
                out=ev1[:, pp * TC:(pp + 1) * TC],
                in0=x2b[:, pp * TC:(pp + 1) * TC], scalar=1.0,
                in1=PS[bank2][:, 0:TC], op0=ALU.add, op1=ALU.mult),
              r=[rx2b[pp], rps[bank2]], w=[rev1[pp]])
            E("v", lambda v, bank=bank, pp=pp: v.tensor_tensor(
                out=ev1[:, pp * TC:(pp + 1) * TC],
                in0=ev1[:, pp * TC:(pp + 1) * TC], in1=PS[bank][:, 0:TC],
                op=ALU.mult), r=[rps[bank]], w=[rev1[pp]])
            E("v", lambda v, mo=mo, tc=tc, pp=pp: v.scalar_tensor_tensor(
                out=x1[:, mo * T + tc * TC:mo * T + (tc + 1) * TC],
                in0=ev1[:, pp * TC:(pp + 1) * TC], scalar=0.5,
                in1=x1[:, mo * T + tc * TC:mo * T + (tc + 1) * TC],
                op0=ALU.mult, op1=ALU.add), r=[rev1[pp]], w=[rx1t[mo][tc]])
            E("q", lambda q, mo=mo, tc=tc: q.dma_start(
                out=yT[mo * 128:(mo + 1) * 128, tc * TC:(tc + 1) * TC],
                in_=x1[:, mo * T + tc * TC:mo * T + (tc + 1) * TC]),
              r=[rx1t[mo][tc]], dma="out")

    b.barrier(engs=("q",))
    p3.close()
    p1.close()

    blk_ctx.__exit__(None, None, None)
    es.close()
    return nc, b



def make_in_maps(inputs, T=2048):
    tr = lambda a: np.ascontiguousarray(np.asarray(a).T)
    _cv = lambda a: np.ascontiguousarray(np.asarray(a).reshape(8, 128).T)
    def tl(a):
        a = np.asarray(a)
        no, ni = a.shape
        nm, nk = no // 128, ni // 128
        return np.ascontiguousarray(
            a.T.reshape(nk, 128, nm, 128).transpose(2, 1, 0, 3)
            .reshape(nm * 128, nk * 128))
    TC = min(512, T)
    maskc = np.zeros((TC // 128, 128, TC), np.float32)
    for j in range(TC // 128):
        for p in range(128):
            maskc[j, p, :min(p + j * 128, TC)] = NEG
    maskc = np.ascontiguousarray(maskc.transpose(1, 0, 2).reshape(128, -1))
    common = {
        "wrT": tl(inputs["L0_attn_Wr"]), "wkT": tl(inputs["L0_attn_Wk"]),
        "wvT": tl(inputs["L0_attn_Wv"]), "woT": tl(inputs["L0_attn_Wo"]),
        "wfkT": tl(inputs["L0_attn_Wfk"]), "wfvT": tl(inputs["L0_attn_Wfv"]),
        "wfrT": tl(inputs["L0_attn_Wfr"]),
        "wgT": tl(inputs["L0_ffn_w_gate"]), "wuT": tl(inputs["L0_ffn_w_up"]),
        "wdT": tl(inputs["L0_ffn_w_down"]),
        "q_T": tl(inputs["L1_attn_q_proj_weight"]),
        "k_T": tl(inputs["L1_attn_k_proj_weight"]),
        "v_T": tl(inputs["L1_attn_v_proj_weight"]),
        "o_T": tl(inputs["L1_attn_o_proj_weight"]),
        "wT": tr(inputs["L1_ffn_weight"]),
        "ln1w": _cv(inputs["L0_attn_ln1_w"]),
        "ln1b": _cv(inputs["L0_attn_ln1_b"]),
        "ln2w": _cv(inputs["L0_attn_ln2_w"]),
        "ln2b": _cv(inputs["L0_attn_ln2_b"]),
        "decay": _cv(inputs["L0_attn_decay"]),
        "u": _cv(inputs["L0_attn_u"]),
        "gamma": _cv(inputs["L1_ffn_gamma"]),
        "beta": _cv(inputs["L1_ffn_beta"]),
        "maskc": maskc,
    }
    x = np.asarray(inputs["L0_attn_x"])
    in_maps = []
    for ci in range(x.shape[0]):
        m = dict(common)
        m["xT"] = tr(x[ci])
        in_maps.append(m)
    return in_maps


def kernel(**inputs):
    T = 2048
    nc, b = build(T=T, tap=DEBUG_TAP, cores=8)
    in_maps = make_in_maps(inputs, T=T)
    res = run_bass_kernel_spmd(nc, in_maps, core_ids=list(range(8)))
    out = np.stack([np.asarray(r["yT"]).T for r in res.results], axis=0)
    return out.astype(np.float32)


# version 21
# speedup vs baseline: 2.6543x; 1.2231x over previous
import numpy as np
import contextlib
import concourse.bass as bass
import concourse.mybir as mybir
from concourse.bass_utils import run_bass_kernel_spmd

F32 = mybir.dt.float32
BF16 = mybir.dt.bfloat16
I32 = mybir.dt.int32
AF = mybir.ActivationFunctionType
ALU = mybir.AluOpType
AXX = mybir.AxisListType.X

D = 1024
NG = D // 128
HID = 3072
NHG = HID // 128
FFD = 4096
NFG = FFD // 128
NPAIR = 8
EPS = 1e-5
NEG = -1.0e9

DEBUG_TAP = None
SAFE_SYNC = False


class Reg:
    __slots__ = ("name", "w", "readers")
    def __init__(self, name):
        self.name = name
        self.w = None
        self.readers = []


class B:
    ENG_ATTR = {"v": "vector", "s": "scalar", "t": "tensor",
                "g": "gpsimd", "q": "sync"}
    ENG_SEM = {"v": "sv", "s": "ss", "t": "st", "g": "sg", "q": None}

    def __init__(self, nc, block, sems, mksem):
        self.nc = nc
        self.block = block
        self.sems = sems
        self.mksem = mksem
        self.cnt = {k: 0 for k in sems}
        self.wm = {}
        self.maxwait = {}
        self.n_inst = 0
        self.buf = {e: [] for e in self.ENG_ATTR}

    def flush(self):
        for e in self.ENG_ATTR:
            bodies = self.buf[e]
            if not bodies:
                continue
            self.buf[e] = []
            def section(engobj, bodies=bodies):
                for bd in bodies:
                    bd(engobj)
            getattr(self.block, self.ENG_ATTR[e])(section)

    def dma_sem(self, key):
        key = "D_" + key
        if key not in self.sems:
            self.sems[key] = self.mksem(key)
            self.cnt[key] = 0
        return key

    def emit(self, eng, fn, r=(), w=(), dma=None, inc=None):
        needs = {}
        for reg_ in r:
            if reg_.w is not None:
                s, c = reg_.w
                if needs.get(s, 0) < c:
                    needs[s] = c
        for reg_ in w:
            if reg_.w is not None:
                s, c = reg_.w
                if needs.get(s, 0) < c:
                    needs[s] = c
            for (s, c) in reg_.readers:
                if needs.get(s, 0) < c:
                    needs[s] = c
        if dma is not None:
            sem_name = self.dma_sem(dma)
            inc_v = 16 if inc is None else inc
            mw = self.maxwait.get(sem_name, 0)
            if mw > 0 and needs.get(sem_name, 0) < mw:
                needs[sem_name] = mw
        else:
            sem_name = self.ENG_SEM[eng]
            inc_v = 1
        own = self.ENG_SEM[eng]
        if not SAFE_SYNC and eng in ("t", "s") and own in needs:
            del needs[own]
        for s_, v_ in needs.items():
            if self.maxwait.get(s_, 0) < v_:
                self.maxwait[s_] = v_
        def body(engobj):
            for s in sorted(needs):
                v = needs[s]
                key = (eng, s)
                if self.wm.get(key, -1) < v:
                    engobj.wait_ge(self.sems[s], v)
                    self.wm[key] = v
            instr = fn(engobj)
            instr.then_inc(self.sems[sem_name], inc_v)
        self.buf[eng].append(body)
        self.n_inst += 1
        self.cnt[sem_name] += inc_v
        tag = (sem_name, self.cnt[sem_name])
        for reg_ in w:
            reg_.w = tag
            reg_.readers = []
        for reg_ in r:
            reg_.readers.append(tag)
        return tag

    def barrier(self, engs=("v", "s", "t", "g", "q")):
        snapshot = dict(self.cnt)
        for e in engs:
            def body(engobj, e=e):
                for s, v in sorted(snapshot.items()):
                    if v > 0 and self.wm.get((e, s), -1) < v:
                        engobj.wait_ge(self.sems[s], v)
                        self.wm[(e, s)] = v
            self.buf[e].append(body)
        for s, v in snapshot.items():
            if self.maxwait.get(s, 0) < v:
                self.maxwait[s] = v
        self.flush()


def build(T=2048, tap=None, cores=8):
    TC = min(512, T)
    NTC = T // TC
    NT2 = T // 128
    TCB = min(512, T)
    NTCB = T // TCB
    WQC = TC
    NW = D // WQC
    MXB = 40 + NG * NW

    nc = bass.Bass("TRN2", target_bir_lowering=False, debug=False,
                   detect_race_conditions=SAFE_SYNC)

    h = {}
    def din(name, shape, dtype=F32):
        h[name] = nc.dram_tensor(name, shape, dtype, kind="ExternalInput").ap()
        return h[name]

    din("xT", [D, T])
    din("wrT", [D, D]); din("wkT", [D, D]); din("wvT", [D, D]); din("woT", [D, D])
    din("wfkT", [FFD, D]); din("wfvT", [D, FFD]); din("wfrT", [D, D])
    din("wgT", [HID, D], I32); din("wuT", [HID, D], I32); din("wdT", [D, HID], I32)
    din("q_T", [D, D]); din("k_T", [D, D]); din("v_T", [D, D]); din("o_T", [D, D])
    din("wT", [D, D])
    for nm in ("ln1w", "ln1b", "ln2w", "ln2b", "decay", "u", "gamma", "beta"):
        din(nm, [128, NG])
    din("maskc", [128, (TC // 128) * TC])

    yT = nc.dram_tensor("yT", [D, T], F32, kind="ExternalOutput").ap()
    dbg = None
    if tap is not None:
        dbg = nc.dram_tensor("dbg", [D, T], F32, kind="ExternalOutput").ap()
    sp_g = nc.dram_tensor("sp_g", [D, T], BF16, kind="Internal").ap()
    sp_rw = nc.dram_tensor("sp_rw", [D, T], BF16, kind="Internal").ap()
    sp_xm = nc.dram_tensor("sp_xm", [D, T], BF16, kind="Internal").ap()
    sp_xn = nc.dram_tensor("sp_xn", [D, T], BF16, kind="Internal").ap()
    cc_in = nc.dram_tensor("cc_in", [1, 8], F32, kind="Internal").ap()
    cc_out = nc.dram_tensor("cc_out", [1, 8], F32, kind="Internal").ap()

    es = contextlib.ExitStack()
    sb = lambda name, shape, dtype=F32: es.enter_context(
        nc.sbuf_tensor(name, shape, dtype))
    mkpsum = lambda name, shape, dtype=F32: es.enter_context(
        nc.psum_tensor(name, shape, dtype))
    mksem = lambda name: es.enter_context(nc.semaphore(name))

    SLOT_OFF = [0, 1024, 2048, 3072, 4096, 7168]
    SLOT_W = [1024, 1024, 1024, 1024, 3072, 4096]
    wslot = sb("wslot", [128, 11264], BF16)
    s0b = sb("s0b", [128, T], BF16)
    s1b = sb("s1b", [128, T], BF16)
    m0b = sb("m0b", [128, T], BF16)
    rows = sb("rows", [1, 2 * TC])
    rowx = sb("rowx", [1, TC])
    mub = sb("mub", [128, T], BF16)
    wbcol = sb("wbcol", [128, 48])
    wcolf = sb("wcolf", [128, NG])
    bcolb = sb("bcolb", [128, NG], BF16)
    onescol = sb("onescol", [128, 2], BF16)
    onescolf = sb("onescolf", [1, 128])
    xf = sb("xf", [128, 2 * TC])
    x2b = sb("x2b", [128, 2 * TC], BF16)
    ev1 = sb("ev1", [128, 2 * TC])
    ev2 = sb("ev2", [128, 2 * TC])
    cols = {}
    for nm in ("ln1w", "ln1b", "ln2w", "ln2b", "decay", "u", "gamma", "beta"):
        cols[nm] = sb("c_" + nm, [128, NG])
    ew = sb("c_ew", [128, NG])
    eu = sb("c_eu", [128, NG])
    mxc = sb("mxc", [128, 40 + 2 * NG * NW])
    scol = sb("scol", [128, NT2])

    PS = [mkpsum(f"ps{i}", [128, 512]) for i in range(6)]
    PSS = mkpsum("pss", [128, 512])
    PSB = mkpsum("psb", [128, 512])

    sems = {}
    for s_ in ("sv", "ss", "st", "sg"):
        sems[s_] = mksem(s_)

    blk_ctx = nc.Block()
    block = blk_ctx.__enter__()
    b = B(nc, block, sems, mksem)
    E = b.emit
    R = {}
    def reg(name):
        if name not in R:
            R[name] = Reg(name)
        return R[name]

    rps = [reg(f"ps.{i}") for i in range(6)]
    rpss = reg("pss"); rpsb = reg("psb")
    rrows = reg("rows"); rrowx = reg("rowx")
    rwbcol = reg("wbcol"); rbcolb = reg("bcolb")
    rmub = reg("mub")
    rs0b = reg("s0b"); rs1b = reg("s1b"); rm0b = reg("m0b")
    rcols = reg("cols")
    rxf = [reg("xf.0"), reg("xf.1")]
    rx2b = [reg("x2b.0"), reg("x2b.1")]
    rev1 = [reg("ev1.0"), reg("ev1.1")]
    rev2 = [reg("ev2.0"), reg("ev2.1")]
    rslot = [reg(f"wslot.{i}") for i in range(6)]
    rones = reg("ones")
    rmask = reg("masks")
    rmxc = reg("mxc"); rscol = reg("scol")
    rcc = reg("ccdone")
    rhbm_g = reg("hbm_g"); rhbm_rw = reg("hbm_rw"); rhbm_xm = reg("hbm_xm")

    def slot(i, k=None, kw=128):
        o = SLOT_OFF[i]
        if k is None:
            return wslot[:, o:o + SLOT_W[i]]
        return wslot[:, o + k * kw:o + (k + 1) * kw]

    for nm in ("ln1w", "ln1b", "ln2w", "ln2b", "decay", "u", "gamma", "beta"):
        E("q", lambda q, nm=nm: q.dma_start(
            out=cols[nm][:, :], in_=h[nm][:, :]), w=[rcols], dma="pre")
    E("v", lambda v: v.memset(onescol[:, :], 1.0), w=[rones])
    E("v", lambda v: v.memset(onescolf[:, :], 1.0), w=[rones])
    E("s", lambda s: s.activation(out=ew[:, :], in_=cols["decay"][:, :],
                                  func=AF.Exp), r=[rcols], w=[rcols])
    E("s", lambda s: s.activation(out=ew[:, :], in_=ew[:, :],
                                  func=AF.Exp, scale=-1.0), r=[rcols], w=[rcols])
    E("s", lambda s: s.activation(out=eu[:, :], in_=cols["u"][:, :],
                                  func=AF.Exp), r=[rcols], w=[rcols])

    def S1(): return rows[0:1, 0:TC]
    def S2(): return rows[0:1, TC:2 * TC]
    def RX(): return rowx[0:1, 0:TC]

    def bcast_tc(src, sreg, dst, rdst, tc):
        E("t", lambda t: t.matmul(
            out=PSB[:, 0:TC], lhsT=onescolf[0:1, :], rhs=src,
            start=True, stop=True), r=[sreg, rones], w=[rpsb])
        E("v", lambda v: v.tensor_copy(
            out=dst[:, tc * TC:(tc + 1) * TC], in_=PSB[:, 0:TC]),
          r=[rpsb], w=[rdst])

    def ln_rows_tc(tc, mode):
        C = float(D)
        E("v", lambda v: v.tensor_scalar(
            out=S1(), in0=S1(), scalar1=1.0 / C, scalar2=None,
            op0=ALU.mult), r=[rrows], w=[rrows])
        bcast_tc(S1(), rrows, mub, rmub, tc)
        E("v", lambda v: v.tensor_tensor(
            out=RX(), in0=S1(), in1=S1(), op=ALU.mult),
          r=[rrows], w=[rrowx])
        E("v", lambda v: v.scalar_tensor_tensor(
            out=S2(), in0=S2(), scalar=1.0 / C, in1=RX(),
            op0=ALU.mult, op1=ALU.subtract), r=[rrows, rrowx], w=[rrows])
        E("v", lambda v: v.tensor_scalar(
            out=S2(), in0=S2(), scalar1=EPS, scalar2=None, op0=ALU.add),
          r=[rrows], w=[rrows])
        E("s", lambda s: s.activation(out=RX(), in_=S2(), func=AF.Ln),
          r=[rrows], w=[rrowx])
        E("s", lambda s: s.activation(out=RX(), in_=RX(), func=AF.Exp,
                                      scale=-0.5), r=[rrowx], w=[rrowx])
        if mode == "ln2":
            bcast_tc(RX(), rrowx, s1b, rs1b, tc)
            return
        bcast_tc(RX(), rrowx, s0b, rs0b, tc)
        if mode == "l1":
            nj = TC // 128
            for j in range(nj):
                E("g", lambda g_, j=j: g_.dma_start(
                    out=scol[:, tc * nj + j:tc * nj + j + 1],
                    in_=rowx[0:1, j * 128:(j + 1) * 128],
                    allow_slow_non_contiguous=True),
                  r=[rrowx], w=[rscol], dma="scol")
        if mode == "l0":
            E("v", lambda v: v.tensor_tensor(
                out=S1(), in0=S1(), in1=RX(), op=ALU.mult),
              r=[rrows, rrowx], w=[rrows])
            bcast_tc(S1(), rrows, m0b, rm0b, tc)
        E("v", lambda v: v.tensor_tensor(
            out=S2(), in0=S2(), in1=RX(), op=ALU.mult),
          r=[rrows, rrowx], w=[rrows])
        E("v", lambda v: v.tensor_tensor(
            out=S2(), in0=S2(), in1=RX(), op=ALU.mult),
          r=[rrows, rrowx], w=[rrows])
        E("v", lambda v: v.tensor_scalar(
            out=S2(), in0=S2(), scalar1=EPS, scalar2=None, op0=ALU.add),
          r=[rrows], w=[rrows])
        E("s", lambda s: s.activation(out=S2(), in_=S2(), func=AF.Ln),
          r=[rrows], w=[rrows])
        E("s", lambda s: s.activation(out=S2(), in_=S2(), func=AF.Exp,
                                      scale=-0.5), r=[rrows], w=[rrows])
        E("v", lambda v: v.tensor_tensor(
            out=RX(), in0=RX(), in1=S2(), op=ALU.mult),
          r=[rrows], w=[rrowx])
        bcast_tc(RX(), rrowx, s1b, rs1b, tc)
        if mode == "l1":
            E("v", lambda v: v.tensor_tensor(
                out=S1(), in0=S1(), in1=RX(), op=ALU.mult),
              r=[rrowx], w=[rrows])
            bcast_tc(S1(), rrows, m0b, rm0b, tc)

    def stream_w(wap, mi, nk, si):
        E("g", lambda g_: g_.dma_start(
            out=slot(si)[:, 0:nk * 128],
            in_=wap[mi * 128:(mi + 1) * 128, 0:nk * 128]),
          w=[rslot[si]], dma=f"w{si}")

    rslot5h = [reg("wslot.5a"), reg("wslot.5b")]
    def stream_w5(wap, mi, nk):
        hk = nk // 2
        E("g", lambda g_, hk=hk: g_.dma_start(
            out=slot(5)[:, 0:hk * 128],
            in_=wap[mi * 128:(mi + 1) * 128, 0:hk * 128]),
          w=[rslot5h[0]], dma="w5a")
        E("g", lambda g_, hk=hk: g_.dma_start(
            out=slot(5)[:, hk * 128:nk * 128],
            in_=wap[mi * 128:(mi + 1) * 128, hk * 128:nk * 128]),
          w=[rslot5h[1]], dma="w5b")


    def wb_column(si, nk, ci, ps=None, rp=None):
        ps = PSB if ps is None else ps
        rp = rpsb if rp is None else rp
        for k in range(nk):
            E("t", lambda t, k=k, ps=ps: t.matmul(
                out=ps[:, 0:1], lhsT=slot(si, k), rhs=bcolb[:, k:k + 1],
                start=(k == 0), stop=(k == nk - 1)),
              r=[rslot[si], rbcolb], w=[rp])
        E("v", lambda v, ps=ps: v.tensor_copy(
            out=wbcol[:, ci:ci + 1], in_=ps[:, 0:1]), r=[rp], w=[rwbcol])

    def scale_slot(si, nk):
        for k in range(nk):
            E("v", lambda v, k=k: v.tensor_scalar(
                out=slot(si, k), in0=slot(si, k), scalar1=wcolf[:, k:k + 1],
                scalar2=None, op0=ALU.mult),
              r=[rwbcol, rslot[si]], w=[rslot[si]])

    p0 = contextlib.ExitStack()
    sb0 = lambda name, shape, dtype=F32: p0.enter_context(
        nc.sbuf_tensor(name, shape, dtype))
    xbuf = sb0("xbuf", [128, NG * T], BF16)
    scA = sb0("scA", [128, T])
    scB = sb0("scB", [128, T])
    scC = sb0("scC", [128, T + 8])
    scD = sb0("scD", [128, T + 8])
    scE = sb0("scE", [128, T + 8])
    scF = sb0("scF", [128, T + 8])
    sigr = sb0("sigr", [128, T])
    d1t = sb0("d1t", [128, T])
    hbuf = sb0("hbuf", [128, NHG * TCB], BF16)
    gtile = sb0("gtile", [128, 2 * TCB], BF16)
    rwt = sb0("rwt", [128, T], BF16)

    rxb = [reg(f"xb.{g}") for g in range(NG)]
    rsc = {nm: reg(nm) for nm in ("scA", "scB", "scC", "scD", "scE", "scF")}
    rsigr = reg("sigr"); rd1 = reg("d1t")
    rhb = reg("hbuf")
    rgt = [reg("gtile.0"), reg("gtile.1")]
    rrwt = reg("rwt")

    for tc in range(NTC):
        for g in range(NG):
            pp = g % 2
            xfs = xf[:, pp * TC:(pp + 1) * TC]
            E("q", lambda q, g=g, tc=tc, xfs=xfs: q.dma_start(
                out=xfs, in_=h["xT"][g * 128:(g + 1) * 128,
                                     tc * TC:(tc + 1) * TC]),
              w=[rxf[pp]], dma=f"xf{pp}")
            E("s", lambda s, g=g, tc=tc, xfs=xfs: s.activation(
                out=xbuf[:, g * T + tc * TC:g * T + (tc + 1) * TC],
                in_=xfs, func=AF.Copy), r=[rxf[pp]], w=[rxb[g]])
            E("s", lambda s, xfs=xfs, pp=pp: s.activation(
                out=x2b[:, pp * TC:(pp + 1) * TC], in_=xfs, func=AF.Square),
              r=[rxf[pp]], w=[rx2b[pp]])
            E("t", lambda t, g=g, tc=tc: t.matmul(
                out=PSS[0:1, 0:TC], lhsT=onescol[:, 0:1],
                rhs=xbuf[:, g * T + tc * TC:g * T + (tc + 1) * TC],
                start=(g == 0), stop=(g == NG - 1)),
              r=[rxb[g], rones], w=[rpss])
            E("t", lambda t, g=g, pp=pp: t.matmul(
                out=PSB[0:1, 0:TC], lhsT=onescol[:, 0:1],
                rhs=x2b[:, pp * TC:(pp + 1) * TC],
                start=(g == 0), stop=(g == NG - 1)),
              r=[rx2b[pp], rones], w=[rpsb])
        E("v", lambda v: v.tensor_copy(out=S1(), in_=PSS[0:1, 0:TC]),
          r=[rpss], w=[rrows])
        E("v", lambda v: v.tensor_copy(out=S2(), in_=PSB[0:1, 0:TC]),
          r=[rpsb], w=[rrows])
        ln_rows_tc(tc, "l0")

    for g in range(NG):
        E("v", lambda v, g=g: v.tensor_tensor(
            out=xbuf[:, g * T:(g + 1) * T], in0=xbuf[:, g * T:(g + 1) * T],
            in1=mub[:, 0:T], op=ALU.subtract), r=[rmub], w=[rxb[g]])
    E("v", lambda v: v.tensor_copy(out=wcolf[:, :], in_=cols["ln1w"][:, :]),
      r=[rcols], w=[rwbcol])
    E("v", lambda v: v.tensor_copy(out=bcolb[:, :], in_=cols["ln1b"][:, :]),
      r=[rcols], w=[rbcolb])
    rhbm_xn = reg("hbm_xn")
    def xn_chunks():
        for g in range(NG):
            for tc2 in range(NTC):
                def xs(g=g, tc2=tc2):
                    pp = (g * NTC + tc2) % 2
                    E("v", lambda v, g=g, tc2=tc2, pp=pp: v.tensor_tensor(
                        out=x2b[:, pp * TC:(pp + 1) * TC],
                        in0=xbuf[:, g * T + tc2 * TC:g * T + (tc2 + 1) * TC],
                        in1=s0b[:, tc2 * TC:(tc2 + 1) * TC], op=ALU.mult),
                      r=[rxb[g], rs0b], w=[rx2b[pp]])
                    E("q", lambda q, g=g, tc2=tc2, pp=pp: q.dma_start(
                        out=sp_xn[g * 128:(g + 1) * 128,
                                  tc2 * TC:(tc2 + 1) * TC],
                        in_=x2b[:, pp * TC:(pp + 1) * TC]),
                      r=[rx2b[pp]], w=[rhbm_xn], dma="hxn")
                yield xs

    def bb_chunks():
        for tcb in range(NTCB):
            sl = slice(tcb * TCB, (tcb + 1) * TCB)
            for m in range(NHG):
                def mb(tcb=tcb, m=m, sl=sl):
                    for wi, (wap, si, bank) in enumerate(
                            ((h["wgT"], 3, 2), (h["wuT"], 4, 3))):
                        stream_w(wap, m, NG, si)
                        for k in range(NG):
                            E("t", lambda t, k=k, si=si, bank=bank, sl=sl:
                                t.matmul(
                                    out=PS[bank][:, 0:TCB], lhsT=slot(si, k),
                                    rhs=xbuf[:, k * T + sl.start:
                                             k * T + sl.stop],
                                    start=(k == 0), stop=(k == NG - 1)),
                              r=[rslot[si], rxb[k]], w=[rps[bank]])
                    pp = m % 2
                    E("v", lambda v, sl=sl, pp=pp: v.scalar_tensor_tensor(
                        out=ev1[:, pp * TC:pp * TC + TCB],
                        in0=PS[2][:, 0:TCB], scalar=1.0 / 127.0,
                        in1=s0b[:, sl], op0=ALU.mult, op1=ALU.mult),
                      r=[rps[2], rs0b], w=[rev1[pp]])
                    E("v", lambda v, sl=sl, pp=pp: v.scalar_tensor_tensor(
                        out=ev2[:, pp * TC:pp * TC + TCB],
                        in0=PS[3][:, 0:TCB], scalar=1.0 / 127.0,
                        in1=s0b[:, sl], op0=ALU.mult, op1=ALU.mult),
                      r=[rps[3], rs0b], w=[rev2[pp]])
                    E("s", lambda s, pp=pp: s.activation(
                        out=x2b[:, pp * TC:pp * TC + TCB],
                        in_=ev1[:, pp * TC:pp * TC + TCB],
                        func=AF.Tanh, scale=0.5), r=[rev1[pp]], w=[rx2b[pp]])
                    E("v", lambda v, pp=pp: v.scalar_tensor_tensor(
                        out=ev1[:, pp * TC:pp * TC + TCB],
                        in0=x2b[:, pp * TC:pp * TC + TCB], scalar=1.0,
                        in1=ev1[:, pp * TC:pp * TC + TCB],
                        op0=ALU.add, op1=ALU.mult),
                      r=[rx2b[pp], rev1[pp]], w=[rev1[pp]])
                    E("v", lambda v, m=m, pp=pp: v.tensor_tensor(
                        out=hbuf[:, m * TCB:(m + 1) * TCB],
                        in0=ev1[:, pp * TC:pp * TC + TCB],
                        in1=ev2[:, pp * TC:pp * TC + TCB], op=ALU.mult),
                      r=[rev1[pp], rev2[pp]], w=[rhb])
                yield mb
            for mo in range(NG):
                def dn(tcb=tcb, mo=mo, sl=sl):
                    stream_w5(h["wdT"], mo, NHG)
                    bank = 4 + (mo % 2)
                    for k in range(NHG):
                        E("t", lambda t, k=k, bank=bank: t.matmul(
                            out=PS[bank][:, 0:TCB], lhsT=slot(5, k),
                            rhs=hbuf[:, k * TCB:(k + 1) * TCB],
                            start=(k == 0), stop=(k == NHG - 1)),
                          r=[rslot5h[2 * k // NHG], rhb], w=[rps[bank]])
                    pp = mo % 2
                    E("s", lambda s, bank=bank, pp=pp: s.activation(
                        out=x2b[:, pp * TC:pp * TC + TCB],
                        in_=PS[bank][:, 0:TCB], func=AF.Tanh,
                        scale=0.5 / 254.0), r=[rps[bank]], w=[rx2b[pp]])
                    E("v", lambda v, bank=bank, pp=pp: v.scalar_tensor_tensor(
                        out=gtile[:, pp * TCB:(pp + 1) * TCB],
                        in0=x2b[:, pp * TC:pp * TC + TCB], scalar=1.0,
                        in1=PS[bank][:, 0:TCB], op0=ALU.add, op1=ALU.mult),
                      r=[rx2b[pp], rps[bank]], w=[rgt[pp]])
                    E("q", lambda q, mo=mo, sl=sl, pp=pp: q.dma_start(
                        out=sp_g[mo * 128:(mo + 1) * 128, sl],
                        in_=gtile[:, pp * TCB:(pp + 1) * TCB]),
                      r=[rgt[pp]], w=[rhbm_g], dma="hg")
                yield dn

    import itertools
    bbgen = itertools.chain(xn_chunks(), bb_chunks())
    bb_left = [NG * NTC + NTCB * (NHG + NG)]
    def bb_pull(n):
        for _ in range(n):
            if bb_left[0] <= 0:
                return
            next(bbgen)()
            bb_left[0] -= 1

    def rkv_group(gm):
        stream_w(h["wkT"], gm, NG, 0)
        stream_w(h["wvT"], gm, NG, 1)
        stream_w(h["wrT"], gm, NG, 2)
        for si, ci in ((0, 0), (1, 1), (2, 2)):
            wb_column(si, NG, ci)
            scale_slot(si, NG)

        def proj(si, ci, dst, rdst, bias_add=True):
            for tc in range(NTC):
                bank = tc % 2
                for k in range(NG):
                    E("t", lambda t, k=k, si=si, tc=tc, bank=bank: t.matmul(
                        out=PS[bank][:, 0:TC], lhsT=slot(si, k),
                        rhs=xbuf[:, k * T + tc * TC:k * T + (tc + 1) * TC],
                        start=(k == 0), stop=(k == NG - 1)),
                      r=[rslot[si], rxb[k]], w=[rps[bank]])
                E("v", lambda v, tc=tc, bank=bank: v.tensor_tensor(
                    out=dst[:, tc * TC:(tc + 1) * TC], in0=PS[bank][:, 0:TC],
                    in1=s1b[:, tc * TC:(tc + 1) * TC], op=ALU.mult),
                  r=[rps[bank], rs1b], w=[rdst])
                if bias_add:
                    E("v", lambda v, tc=tc, ci=ci: v.tensor_scalar(
                        out=dst[:, tc * TC:(tc + 1) * TC],
                        in0=dst[:, tc * TC:(tc + 1) * TC],
                        scalar1=wbcol[:, ci:ci + 1], scalar2=None,
                        op0=ALU.add), r=[rwbcol], w=[rdst])

        proj(0, 0, scB, rsc["scB"], bias_add=False)
        if tap == "l0_k":
            E("v", lambda v: v.tensor_scalar(
                out=scB[:, :], in0=scB[:, :], scalar1=wbcol[:, 0:1],
                scalar2=None, op0=ALU.add), r=[rwbcol], w=[rsc["scB"]])
            E("q", lambda q: q.dma_start(
                out=dbg[gm * 128:(gm + 1) * 128, :], in_=scB[:, :]),
              r=[rsc["scB"]], dma="out")
            E("s", lambda s: s.activation(out=scB[:, :], in_=scB[:, :],
                                          func=AF.Exp),
              r=[rsc["scB"]], w=[rsc["scB"]])
        else:
            E("s", lambda s: s.activation(out=scB[:, :], in_=scB[:, :],
                                          func=AF.Exp, bias=wbcol[:, 0:1]),
              r=[rsc["scB"], rwbcol], w=[rsc["scB"]])
        proj(1, 1, scA, rsc["scA"], bias_add=False)
        E("v", lambda v: v.scalar_tensor_tensor(
            out=scA[:, :], in0=scA[:, :], scalar=wbcol[:, 1:2],
            in1=scB[:, :], op0=ALU.add, op1=ALU.mult),
          r=[rsc["scB"], rwbcol], w=[rsc["scA"]])
        proj(2, 2, sigr, rsigr, bias_add=False)
        E("v", lambda v: v.tensor_scalar(
            out=wbcol[:, 3:4], in0=wbcol[:, 2:3], scalar1=0.5, scalar2=None,
            op0=ALU.mult), r=[rwbcol], w=[rwbcol])
        E("s", lambda s: s.activation(out=sigr[:, :], in_=sigr[:, :],
                                      func=AF.Tanh, scale=0.5,
                                      bias=wbcol[:, 3:4]),
          r=[rsigr, rwbcol], w=[rsigr])

        th = []
        A = th.append
        A(lambda: (E("v", lambda v: v.memset(d1t[:, :], 1.0), w=[rd1]),
                   E("v", lambda v: v.tensor_scalar(
                       out=d1t[:, :], in0=d1t[:, :], scalar1=ew[:, gm:gm + 1],
                       scalar2=None, op0=ALU.mult), r=[rcols], w=[rd1])))
        A(lambda: E("v", lambda v: v.tensor_copy(out=scE[:, 1:T + 1],
                                                 in_=scA[:, ::-1]),
                    r=[rsc["scA"]], w=[rsc["scE"]]))
        A(lambda: E("v", lambda v: v.tensor_copy(out=scF[:, 1:T + 1],
                                                 in_=scB[:, ::-1]),
                    r=[rsc["scB"]], w=[rsc["scF"]]))
        def _zc():
            for nm, tt in (("scC", scC), ("scD", scD), ("scE", scE),
                           ("scF", scF)):
                E("v", lambda v, tt=tt: v.memset(tt[:, 0:1], 0.0),
                  w=[rsc[nm]])
        A(_zc)
        A(lambda: E("v", lambda v: v.tensor_tensor_scan(
            out=scC[:, 1:T + 1], data0=d1t[:, :], data1=scA[:, :],
            initial=0.0, op0=ALU.mult, op1=ALU.add),
          r=[rd1, rsc["scA"]], w=[rsc["scC"]]))
        A(lambda: E("v", lambda v: v.tensor_tensor_scan(
            out=scD[:, 1:T + 1], data0=d1t[:, :], data1=scB[:, :],
            initial=0.0, op0=ALU.mult, op1=ALU.add),
          r=[rd1, rsc["scB"]], w=[rsc["scD"]]))
        A(lambda: E("v", lambda v: v.tensor_tensor_scan(
            out=scE[:, 1:T + 1], data0=d1t[:, :], data1=scE[:, 1:T + 1],
            initial=0.0, op0=ALU.mult, op1=ALU.add),
          r=[rd1], w=[rsc["scE"]]))
        A(lambda: E("v", lambda v: v.tensor_tensor_scan(
            out=scF[:, 1:T + 1], data0=d1t[:, :], data1=scF[:, 1:T + 1],
            initial=0.0, op0=ALU.mult, op1=ALU.add),
          r=[rd1], w=[rsc["scF"]]))
        A(lambda: E("v", lambda v: v.tensor_scalar(
            out=scA[:, :], in0=scA[:, :], scalar1=eu[:, gm:gm + 1],
            scalar2=None, op0=ALU.mult), r=[rcols], w=[rsc["scA"]]))
        A(lambda: E("v", lambda v: v.tensor_scalar(
            out=scB[:, :], in0=scB[:, :], scalar1=eu[:, gm:gm + 1],
            scalar2=None, op0=ALU.mult), r=[rcols], w=[rsc["scB"]]))
        A(lambda: E("v", lambda v: v.tensor_tensor(
            out=scC[:, 0:T], in0=scC[:, 0:T], in1=scA[:, :], op=ALU.add),
          r=[rsc["scA"]], w=[rsc["scC"]]))
        A(lambda: E("v", lambda v: v.tensor_tensor(
            out=scD[:, 0:T], in0=scD[:, 0:T], in1=scB[:, :], op=ALU.add),
          r=[rsc["scB"]], w=[rsc["scD"]]))
        A(lambda: E("v", lambda v: v.tensor_tensor(
            out=scA[:, :], in0=scA[:, :], in1=scE[:, 0:T][:, ::-1],
            op=ALU.add), r=[rsc["scE"]], w=[rsc["scA"]]))
        A(lambda: E("v", lambda v: v.tensor_tensor(
            out=scB[:, :], in0=scB[:, :], in1=scF[:, 0:T][:, ::-1],
            op=ALU.add), r=[rsc["scF"]], w=[rsc["scB"]]))
        A(lambda: E("v", lambda v: v.tensor_tensor(
            out=scA[:, :], in0=scA[:, :], in1=scD[:, 0:T], op=ALU.mult),
          r=[rsc["scD"]], w=[rsc["scA"]]))
        A(lambda: E("v", lambda v: v.tensor_tensor(
            out=scC[:, 0:T], in0=scC[:, 0:T], in1=scB[:, :], op=ALU.mult),
          r=[rsc["scB"]], w=[rsc["scC"]]))
        A(lambda: E("v", lambda v: v.tensor_tensor(
            out=scC[:, 0:T], in0=scC[:, 0:T], in1=scA[:, :], op=ALU.add),
          r=[rsc["scA"]], w=[rsc["scC"]]))
        A(lambda: E("v", lambda v: v.tensor_tensor(
            out=scD[:, 0:T], in0=scD[:, 0:T], in1=scB[:, :], op=ALU.mult),
          r=[rsc["scB"]], w=[rsc["scD"]]))
        A(lambda: E("s", lambda s: s.activation(
            out=scD[:, 0:T], in_=scD[:, 0:T], func=AF.Ln), w=[rsc["scD"]]))
        A(lambda: E("s", lambda s: s.activation(
            out=scD[:, 0:T], in_=scD[:, 0:T], func=AF.Exp, scale=-1.0),
          w=[rsc["scD"]]))
        A(lambda: E("v", lambda v: v.tensor_tensor(
            out=scC[:, 0:T], in0=scC[:, 0:T], in1=scD[:, 0:T], op=ALU.mult),
          r=[rsc["scD"]], w=[rsc["scC"]]))
        if tap == "l0_wkv":
            def _dump():
                E("v", lambda v: v.tensor_scalar(
                    out=scC[:, 0:T], in0=scC[:, 0:T], scalar1=0.5,
                    scalar2=None, op0=ALU.mult), r=[rsc["scC"]],
                  w=[rsc["scC"]])
                E("q", lambda q: q.dma_start(
                    out=dbg[gm * 128:(gm + 1) * 128, :], in_=scC[:, 0:T]),
                  r=[rsc["scC"]], dma="out")
            A(_dump)
            return th
        def _rw():
            E("v", lambda v: v.scalar_tensor_tensor(
                out=rwt[:, :], in0=sigr[:, :], scalar=1.0, in1=scC[:, 0:T],
                op0=ALU.add, op1=ALU.mult), r=[rsigr, rsc["scC"]], w=[rrwt])
            E("q", lambda q: q.dma_start(
                out=sp_rw[gm * 128:(gm + 1) * 128, :], in_=rwt[:, :]),
              r=[rrwt], w=[rhbm_rw], dma="hrw")
        A(_rw)
        return th

    BBTOT = NG * NTC + NTCB * (NHG + NG)
    pulled = 0
    for gm in range(NG):
        th = rkv_group(gm)
        quota = (gm + 1) * BBTOT // NG - pulled
        n = max(quota, len(th))
        ti = 0
        pi = 0
        for i in range(n):
            while pi * n < quota * (i + 1) and pi < quota:
                bb_pull(1)
                pi += 1
                pulled += 1
            while ti * n <= i * len(th) and ti < len(th):
                th[ti]()
                ti += 1
        while ti < len(th):
            th[ti]()
            ti += 1
    bb_pull(10 ** 9)

    b.barrier()
    p0.close()

    if tap in ("l0_k", "l0_wkv"):
        blk_ctx.__exit__(None, None, None)
        es.close()
        return nc, b

    p1 = contextlib.ExitStack()
    sb1 = lambda name, shape, dtype=F32: p1.enter_context(
        nc.sbuf_tensor(name, shape, dtype))
    x1 = sb1("x1", [128, NG * T])
    xb1 = sb1("xb1", [128, NG * T], BF16)
    rxb1 = [reg(f"xb1.{g}") for g in range(NG)]
    rx1t = [[reg(f"x1.{g}.{t_}") for t_ in range(NTC)] for g in range(NG)]
    rx1 = [reg(f"x1.{g}") for g in range(NG)]

    p1b = contextlib.ExitStack()
    sb1b = lambda name, shape, dtype=F32: p1b.enter_context(
        nc.sbuf_tensor(name, shape, dtype))
    kkb = sb1b("kkb", [128, NFG * TC], BF16)
    rwring = sb1b("rwring", [128, 2 * NG * TC], BF16)
    xmring = rwring
    g1ring = sb1b("g1ring", [128, 2 * TC], BF16)
    xcring = wslot[:, 3072:3072 + NG * TC]

    rkk = [reg(f"kk.{i}") for i in range(NFG)]
    rrwr = [reg("rwring.0"), reg("rwring.1")]
    rxmr = rrwr
    rg1 = [reg("g1ring.0"), reg("g1ring.1")]
    rxcr = reg("xcring")

    for tc in range(NTC):
        ppr = tc % 2
        E("q", lambda q, tc=tc, ppr=ppr: q.dma_start(
            out=rwring[:, ppr * NG * TC:(ppr + 1) * NG * TC].rearrange(
                "p (k c) -> p k c", k=NG),
            in_=sp_rw[:, tc * TC:(tc + 1) * TC].rearrange(
                "(k p) c -> p k c", p=128)), r=[rhbm_rw], w=[rrwr[ppr]], dma=f"rr{ppr}")
        for mo in range(NG):
            si = mo % 2
            stream_w(h["woT"], mo, NG, si)
            bank = 2 + (mo % 2)
            for k in range(NG):
                E("t", lambda t, k=k, si=si, bank=bank, ppr=ppr: t.matmul(
                    out=PS[bank][:, 0:TC], lhsT=slot(si, k),
                    rhs=rwring[:, ppr * NG * TC + k * TC:
                               ppr * NG * TC + (k + 1) * TC],
                    start=(k == 0), stop=(k == NG - 1)),
                  r=[rslot[si], rrwr[ppr]], w=[rps[bank]])
            pp = mo % 2
            E("q", lambda q, mo=mo, tc=tc, pp=pp: q.dma_start(
                out=g1ring[:, pp * TC:(pp + 1) * TC],
                in_=sp_xn[mo * 128:(mo + 1) * 128, tc * TC:(tc + 1) * TC]),
              r=[rhbm_xn], w=[rg1[pp]], dma=f"g1{pp}")
            E("v", lambda v, pp=pp, bank=bank: v.scalar_tensor_tensor(
                out=ev1[:, pp * TC:(pp + 1) * TC], in0=PS[bank][:, 0:TC],
                scalar=0.25, in1=g1ring[:, pp * TC:(pp + 1) * TC],
                op0=ALU.mult, op1=ALU.add),
              r=[rps[bank], rg1[pp]], w=[rev1[pp]])
            E("s", lambda s, pp=pp: s.activation(
                out=x2b[:, pp * TC:(pp + 1) * TC],
                in_=ev1[:, pp * TC:(pp + 1) * TC], func=AF.Copy),
              r=[rev1[pp]], w=[rx2b[pp]])
            E("q", lambda q, mo=mo, tc=tc, pp=pp: q.dma_start(
                out=sp_xm[mo * 128:(mo + 1) * 128, tc * TC:(tc + 1) * TC],
                in_=x2b[:, pp * TC:(pp + 1) * TC]),
              r=[rx2b[pp]], w=[rhbm_xm], dma="hxm")
            E("t", lambda t, mo=mo, pp=pp: t.matmul(
                out=PSS[0:1, 0:TC], lhsT=onescol[:, 0:1],
                rhs=x2b[:, pp * TC:(pp + 1) * TC],
                start=(mo == 0), stop=(mo == NG - 1)),
              r=[rx2b[pp], rones], w=[rpss])
            E("s", lambda s, pp=pp: s.activation(
                out=x2b[:, pp * TC:(pp + 1) * TC],
                in_=ev1[:, pp * TC:(pp + 1) * TC], func=AF.Square),
              r=[rev1[pp], rx2b[pp]], w=[rx2b[pp]])
            E("t", lambda t, mo=mo, pp=pp: t.matmul(
                out=PSB[0:1, 0:TC], lhsT=onescol[:, 0:1],
                rhs=x2b[:, pp * TC:(pp + 1) * TC],
                start=(mo == 0), stop=(mo == NG - 1)),
              r=[rx2b[pp], rones], w=[rpsb])
        E("v", lambda v: v.tensor_copy(out=S1(), in_=PSS[0:1, 0:TC]),
          r=[rpss], w=[rrows])
        E("v", lambda v: v.tensor_copy(out=S2(), in_=PSB[0:1, 0:TC]),
          r=[rpsb], w=[rrows])
        ln_rows_tc(tc, "ln2")

    if tap == "l0_g":
        for g in range(NG):
            for tc in range(NTC):
                pp = (g * NTC + tc) % 2
                E("g", lambda q, g=g, tc=tc, pp=pp: q.dma_start(
                    out=ev1[:, pp * TC:(pp + 1) * TC],
                    in_=sp_g[g * 128:(g + 1) * 128, tc * TC:(tc + 1) * TC]),
                  r=[reg("hbm_g")], w=[rev1[pp]], dma=f"dbr{pp}")
                E("v", lambda v, pp=pp: v.tensor_copy(
                    out=ev2[:, pp * TC:(pp + 1) * TC],
                    in_=ev1[:, pp * TC:(pp + 1) * TC]),
                  r=[rev1[pp]], w=[rev2[pp]])
                E("q", lambda q, g=g, tc=tc, pp=pp: q.dma_start(
                    out=dbg[g * 128:(g + 1) * 128, tc * TC:(tc + 1) * TC],
                    in_=ev2[:, pp * TC:(pp + 1) * TC]),
                  r=[rev2[pp]], dma=f"dbw{pp}")

    if tap == "l0_xmid":
        for g in range(NG):
            for tc in range(NTC):
                pp = (g * NTC + tc) % 2
                E("g", lambda q, g=g, tc=tc, pp=pp: q.dma_start(
                    out=ev1[:, pp * TC:(pp + 1) * TC],
                    in_=sp_xm[g * 128:(g + 1) * 128, tc * TC:(tc + 1) * TC]),
                  r=[rhbm_xm], w=[rev1[pp]], dma=f"dbr{pp}")
                E("v", lambda v, pp=pp: v.tensor_copy(
                    out=ev2[:, pp * TC:(pp + 1) * TC],
                    in_=ev1[:, pp * TC:(pp + 1) * TC]),
                  r=[rev1[pp]], w=[rev2[pp]])
                E("q", lambda q, g=g, tc=tc, pp=pp: q.dma_start(
                    out=dbg[g * 128:(g + 1) * 128, tc * TC:(tc + 1) * TC],
                    in_=ev2[:, pp * TC:(pp + 1) * TC]),
                  r=[rev2[pp]], dma=f"dbw{pp}")

    E("v", lambda v: v.tensor_copy(out=wcolf[:, :], in_=cols["ln2w"][:, :]),
      r=[rcols], w=[rwbcol])
    E("v", lambda v: v.tensor_copy(out=bcolb[:, :], in_=cols["ln2b"][:, :]),
      r=[rcols], w=[rbcolb])

    for tc in range(NTC):
        ppx = tc % 2
        E("q", lambda q, tc=tc, ppx=ppx: q.dma_start(
            out=xmring[:, ppx * NG * TC:(ppx + 1) * NG * TC].rearrange(
                "p (k c) -> p k c", k=NG),
            in_=sp_xm[:, tc * TC:(tc + 1) * TC].rearrange(
                "(k p) c -> p k c", p=128)), r=[rhbm_xm], w=[rxmr[ppx]], dma=f"xm{ppx}")
        def xm(k, ppx=ppx):
            return xmring[:, ppx * NG * TC + k * TC:ppx * NG * TC + (k + 1) * TC]
        def xc(k):
            return xcring[:, k * TC:(k + 1) * TC]
        for k in range(NG):
            E("v", lambda v, a=ev1[:, 0:TC], b_=xm(k),
                   c=mub[:, tc * TC:(tc + 1) * TC]:
                v.scalar_tensor_tensor(out=a, in0=c, scalar=-1.0, in1=b_,
                                       op0=ALU.mult, op1=ALU.add),
              r=[rxmr[ppx], rmub], w=[rev1[0]])
            E("v", lambda v, tc=tc: v.tensor_tensor(
                out=ev1[:, 0:TC], in0=ev1[:, 0:TC],
                in1=s1b[:, tc * TC:(tc + 1) * TC], op=ALU.mult),
              r=[rs1b], w=[rev1[0]])
            E("v", lambda v, a=xc(k), k=k: v.tensor_scalar(
                out=a, in0=ev1[:, 0:TC], scalar1=wcolf[:, k:k + 1],
                scalar2=None, op0=ALU.mult), r=[rev1[0], rwbcol], w=[rxcr])
        for mf in range(NFG):
            si = mf % 2
            stream_w(h["wfkT"], mf, NG, si)
            if tc == 0:
                wb_column(si, NG, 8 + mf, ps=PSS, rp=rpss)
            bank = mf % 2
            for k in range(NG):
                E("t", lambda t, k=k, si=si, bank=bank, a=xc(k): t.matmul(
                    out=PS[bank][:, 0:TC], lhsT=slot(si, k), rhs=a,
                    start=(k == 0), stop=(k == NG - 1)),
                  r=[rslot[si], rxcr], w=[rps[bank]])
            pp = mf % 2
            E("v", lambda v, pp=pp, mf=mf, bank=bank: v.tensor_scalar(
                out=ev1[:, pp * TC:(pp + 1) * TC],
                in0=PS[bank][:, 0:TC],
                scalar1=wbcol[:, 8 + mf:9 + mf],
                scalar2=0.0, op0=ALU.add, op1=ALU.max),
              r=[rps[bank], rwbcol], w=[rev1[pp]])
            E("s", lambda s, mf=mf, pp=pp: s.activation(
                out=kkb[:, mf * TC:(mf + 1) * TC],
                in_=ev1[:, pp * TC:(pp + 1) * TC], func=AF.Square),
              r=[rev1[pp]], w=[rkk[mf]])
        for mo in range(NG):
            si = 2
            stream_w(h["wfrT"], mo, NG, si)
            if tc == 0:
                wb_column(si, NG, 40 + mo, ps=PSS, rp=rpss)
                E("v", lambda v, mo=mo: v.tensor_scalar(
                    out=wbcol[:, 40 + mo:41 + mo],
                    in0=wbcol[:, 40 + mo:41 + mo], scalar1=0.5,
                    scalar2=None, op0=ALU.mult), r=[rwbcol], w=[rwbcol])
            bank = 2
            for k in range(NG):
                E("t", lambda t, k=k, si=si, bank=bank, a=xc(k): t.matmul(
                    out=PS[bank][:, 0:TC], lhsT=slot(si, k), rhs=a,
                    start=(k == 0), stop=(k == NG - 1)),
                  r=[rslot[si], rxcr], w=[rps[bank]])
            pp2 = mo % 2
            E("s", lambda s, pp2=pp2, mo=mo, bank=bank: s.activation(
                out=ev2[:, pp2 * TC:(pp2 + 1) * TC],
                in_=PS[bank][:, 0:TC], func=AF.Tanh,
                scale=0.5, bias=wbcol[:, 40 + mo:41 + mo]),
              r=[rps[bank], rwbcol], w=[rev2[pp2]])
            stream_w5(h["wfvT"], mo, NFG)
            bank2 = 4 + (mo % 2)
            for k in range(NFG):
                E("t", lambda t, k=k, bank2=bank2: t.matmul(
                    out=PS[bank2][:, 0:TC], lhsT=slot(5, k),
                    rhs=kkb[:, k * TC:(k + 1) * TC],
                    start=(k == 0), stop=(k == NFG - 1)),
                  r=[rslot5h[2 * k // NFG], rkk[k]], w=[rps[bank2]])
            pp = mo % 2
            xfs = xf[:, pp * TC:(pp + 1) * TC]
            E("q", lambda q, mo=mo, tc=tc, xfs=xfs: q.dma_start(
                out=xfs, in_=h["xT"][mo * 128:(mo + 1) * 128,
                                     tc * TC:(tc + 1) * TC]),
              w=[rxf[pp]], dma=f"xf{pp}")
            E("q", lambda q, mo=mo, tc=tc, pp=pp: q.dma_start(
                out=g1ring[:, pp * TC:(pp + 1) * TC],
                in_=sp_g[mo * 128:(mo + 1) * 128, tc * TC:(tc + 1) * TC]),
              r=[rhbm_g], w=[rg1[pp]], dma=f"g1{pp}")
            E("v", lambda v, pp2=pp2, bank2=bank2: v.scalar_tensor_tensor(
                out=ev2[:, pp2 * TC:(pp2 + 1) * TC],
                in0=ev2[:, pp2 * TC:(pp2 + 1) * TC], scalar=1.0,
                in1=PS[bank2][:, 0:TC], op0=ALU.add, op1=ALU.mult),
              r=[rps[bank2]], w=[rev2[pp2]])
            E("v", lambda v, pp2=pp2, a=xm(mo): v.scalar_tensor_tensor(
                out=ev2[:, pp2 * TC:(pp2 + 1) * TC],
                in0=ev2[:, pp2 * TC:(pp2 + 1) * TC], scalar=0.5,
                in1=a, op0=ALU.mult, op1=ALU.add),
              r=[rxmr[ppx]], w=[rev2[pp2]])
            E("v", lambda v, pp2=pp2, pp=pp: v.tensor_tensor(
                out=ev2[:, pp2 * TC:(pp2 + 1) * TC],
                in0=ev2[:, pp2 * TC:(pp2 + 1) * TC],
                in1=g1ring[:, pp * TC:(pp + 1) * TC], op=ALU.mult),
              r=[rg1[pp]], w=[rev2[pp2]])
            E("v", lambda v, pp2=pp2, mo=mo, tc=tc, pp=pp:
                v.scalar_tensor_tensor(
                    out=x1[:, mo * T + tc * TC:mo * T + (tc + 1) * TC],
                    in0=ev2[:, pp2 * TC:(pp2 + 1) * TC], scalar=1.0 / 508.0,
                    in1=xf[:, pp * TC:(pp + 1) * TC], op0=ALU.mult,
                    op1=ALU.add), r=[rev2[pp2], rxf[pp]], w=[rx1t[mo][tc]])
            E("s", lambda s, mo=mo, tc=tc, pp=pp: s.activation(
                out=x2b[:, pp * TC:(pp + 1) * TC],
                in_=x1[:, mo * T + tc * TC:mo * T + (tc + 1) * TC],
                func=AF.Copy), r=[rx1t[mo][tc]], w=[rx2b[pp]])
            E("t", lambda t, mo=mo, pp=pp: t.matmul(
                out=PSB[0:1, 0:TC], lhsT=onescol[:, 0:1],
                rhs=x2b[:, pp * TC:(pp + 1) * TC],
                start=(mo == 0), stop=(mo == NG - 1)),
              r=[rx2b[pp], rones], w=[rpsb])
            E("s", lambda s, mo=mo, tc=tc, pp=pp: s.activation(
                out=x2b[:, pp * TC:(pp + 1) * TC],
                in_=x1[:, mo * T + tc * TC:mo * T + (tc + 1) * TC],
                func=AF.Square), r=[rx1t[mo][tc]], w=[rx2b[pp]])
            E("t", lambda t, mo=mo, pp=pp: t.matmul(
                out=PS[3][0:1, 0:TC], lhsT=onescol[:, 0:1],
                rhs=x2b[:, pp * TC:(pp + 1) * TC],
                start=(mo == 0), stop=(mo == NG - 1)),
              r=[rx2b[pp], rones], w=[rps[3]])
        E("v", lambda v: v.tensor_copy(out=S1(), in_=PSB[0:1, 0:TC]),
          r=[rpsb], w=[rrows])
        E("v", lambda v: v.tensor_copy(out=S2(), in_=PS[3][0:1, 0:TC]),
          r=[rps[3]], w=[rrows])
        ln_rows_tc(tc, "l1")
        for mo in range(NG):
            E("v", lambda v, mo=mo, tc=tc: v.tensor_tensor(
                out=xb1[:, mo * T + tc * TC:mo * T + (tc + 1) * TC],
                in0=x1[:, mo * T + tc * TC:mo * T + (tc + 1) * TC],
                in1=mub[:, tc * TC:(tc + 1) * TC], op=ALU.subtract),
              r=[rx1t[mo][tc], rmub], w=[rxb1[mo]])

    if tap == "l0_out":
        for g in range(NG):
            E("q", lambda q, g=g: q.dma_start(
                out=dbg[g * 128:(g + 1) * 128, :],
                in_=x1[:, g * T:(g + 1) * T]), r=list(rx1t[g]), dma="out")

    stream_w(h["q_T"], 0, NG, 0)
    stream_w(h["k_T"], 0, NG, 1)
    stream_w(h["v_T"], 0, NG, 2)

    b.barrier()
    p1b.close()

    p2 = contextlib.ExitStack()
    sb2 = lambda name, shape, dtype=F32: p2.enter_context(
        nc.sbuf_tensor(name, shape, dtype))
    qpair = sb2("qpair", [128, 2 * T], BF16)
    kpair = sb2("kpair", [128, 2 * T], BF16)
    vpair = sb2("vpair", [128, 2 * NT2 * 130], BF16)
    attw = sb2("attw", [128, 8 * 512], BF16)
    p64 = sb2("p64", [65, 512])
    p64b = sb2("p64b", [65, 512], BF16)
    ones64 = sb2("ones64", [65, 64], BF16)
    stg = sb2("stg", [64, 2 * 2 * TC], BF16)
    rtile = sb2("rtile", [64, 512], BF16)
    masks = sb2("masksb", [128, (TC // 128) * TC], BF16)
    E("g", lambda g_: g_.dma_start(
        out=masks[:, :], in_=h["maskc"][:, :]), w=[rmask], dma="prem")

    rqp = [reg("qpair.0"), reg("qpair.1")]
    rkp = [reg("kpair.0"), reg("kpair.1")]
    rvp = [reg("vpair.0"), reg("vpair.1")]
    ratt = [reg(f"attw.{i}") for i in range(8)]
    rp64 = reg("p64"); rp64d = reg("p64d")
    rp64b = reg("p64b"); ro64 = reg("ones64")
    rstg = [reg("stg.0"), reg("stg.1")]
    rrt = reg("rtile")

    E("v", lambda v: v.memset(ones64[:, :], 1.0), w=[ro64])

    def prework():
            for g in range(NG):
                for tc in range(NTC):
                    pp = tc % 2
                    E("v", lambda v, g=g, tc=tc, pp=pp: v.tensor_tensor(
                        out=ev1[:, pp * TC:(pp + 1) * TC],
                        in0=x1[:, g * T + tc * TC:g * T + (tc + 1) * TC],
                        in1=s1b[:, tc * TC:(tc + 1) * TC], op=ALU.mult),
                      r=[rx1t[g][tc], rs1b], w=[rev1[pp]])
                    E("v", lambda v, tc=tc, pp=pp: v.tensor_tensor(
                        out=ev1[:, pp * TC:(pp + 1) * TC],
                        in0=ev1[:, pp * TC:(pp + 1) * TC],
                        in1=m0b[:, tc * TC:(tc + 1) * TC], op=ALU.subtract),
                      r=[rm0b], w=[rev1[pp]])
                    E("v", lambda v, g=g, pp=pp: v.tensor_scalar(
                        out=ev1[:, pp * TC:(pp + 1) * TC],
                        in0=ev1[:, pp * TC:(pp + 1) * TC],
                        scalar1=cols["gamma"][:, g:g + 1],
                        scalar2=cols["beta"][:, g:g + 1], op0=ALU.mult, op1=ALU.add),
                      r=[rcols], w=[rev1[pp]])
                    E("v", lambda v, g=g, tc=tc, pp=pp: v.tensor_reduce(
                        out=mxc[:, g * NTC + tc:g * NTC + tc + 1],
                        in_=ev1[:, pp * TC:(pp + 1) * TC], axis=AXX,
                        op=ALU.max, apply_absolute_value=True),
                      r=[rev1[pp]], w=[rmxc])
                    yield
            E("v", lambda v: v.tensor_reduce(
                out=mxc[:, 33:34], in_=mxc[:, 0:NG * NTC], axis=AXX,
                op=ALU.max, apply_absolute_value=False), r=[rmxc], w=[rmxc])
            E("g", lambda g_: g_.dma_start(out=p64[0:1, 0:128], in_=mxc[:, 33:34], allow_slow_non_contiguous=True),
              r=[rmxc], w=[rp64], dma="p64s")
            E("v", lambda v: v.tensor_reduce(
                out=p64[0:1, 128:129], in_=p64[0:1, 0:128], axis=AXX,
                op=ALU.max, apply_absolute_value=False), r=[rp64], w=[rp64])
            E("g", lambda g_: g_.dma_start(out=cc_in[0:1, 0:1],
                                           in_=p64[0:1, 128:129]),
              r=[rp64], w=[reg("hbm_cc")], dma="p64s")
            E("g", lambda g_: g_.collective_compute(
                "AllReduce", ALU.max, replica_groups=[list(range(cores))],
                ins=[cc_in[0:1, 0:1]], outs=[cc_out[0:1, 0:1]]),
              r=[reg("hbm_cc")], w=[rcc], dma="cc", inc=1)
            E("g", lambda g_: g_.dma_start(out=p64[0:1, 129:130],
                                           in_=cc_out[0:1, 0:1]),
              r=[rcc], w=[rp64], dma="p64s")
            E("v", lambda v: v.tensor_scalar(
                out=p64[0:1, 130:131], in0=p64[0:1, 129:130], scalar1=1.0 / 127.0,
                scalar2=None, op0=ALU.mult), r=[rp64], w=[rp64])
            E("v", lambda v: v.reciprocal(out=p64[0:1, 131:132],
                                          in_=p64[0:1, 130:131]), r=[rp64], w=[rp64])
            E("t", lambda t: t.matmul(
                out=PSS[:, 0:2], lhsT=onescolf[0:1, :], rhs=p64[0:1, 130:132],
                start=True, stop=True), r=[rp64, rones], w=[rpss])
            E("v", lambda v: v.tensor_copy(out=mxc[:, 34:36], in_=PSS[:, 0:2]),
              r=[rpss], w=[rmxc])

            for kq in range(NG):
                for c2 in range(NW):
                    pp = (kq * NW + c2) % 2
                    E("q", lambda q, kq=kq, c2=c2, pp=pp: q.dma_start(
                        out=xf[:, pp * TC:pp * TC + WQC],
                        in_=h["wT"][kq * 128:(kq + 1) * 128, c2 * WQC:(c2 + 1) * WQC]),
                      w=[rxf[pp]], dma=f"xf{pp}")
                    E("v", lambda v, kq=kq, c2=c2, pp=pp: v.tensor_reduce(
                        out=mxc[:, 40 + kq * NW + c2:41 + kq * NW + c2],
                        in_=xf[:, pp * TC:pp * TC + WQC], axis=AXX,
                        op=ALU.add, apply_absolute_value=True),
                      r=[rxf[pp]], w=[rmxc])
                    yield
            E("v", lambda v: v.tensor_reduce(
                out=mxc[:, 38:39], in_=mxc[:, 40:40 + NG * NW], axis=AXX,
                op=ALU.add), r=[rmxc], w=[rmxc])
            E("g", lambda g_: g_.dma_start(out=p64[0:1, 0:128], in_=mxc[:, 38:39], allow_slow_non_contiguous=True),
              r=[rmxc], w=[rp64], dma="p64s")
            E("v", lambda v: v.tensor_reduce(
                out=p64[0:1, 132:133], in_=p64[0:1, 0:128], axis=AXX, op=ALU.add),
              r=[rp64], w=[rp64])
            E("v", lambda v: v.tensor_scalar(
                out=p64[0:1, 133:134], in0=p64[0:1, 132:133],
                scalar1=0.7 / float(D * D), scalar2=None, op0=ALU.mult),
              r=[rp64], w=[rp64])
            E("t", lambda t: t.matmul(
                out=PSS[:, 0:1], lhsT=onescolf[0:1, :], rhs=p64[0:1, 133:134],
                start=True, stop=True), r=[rp64, rones], w=[rpss])
            E("v", lambda v: v.tensor_copy(out=mxc[:, 36:37], in_=PSS[:, 0:1]),
              r=[rpss], w=[rmxc])
            for kq in range(NG):
                for c2 in range(NW):
                    pp = (kq * NW + c2) % 2
                    E("q", lambda q, kq=kq, c2=c2, pp=pp: q.dma_start(
                        out=xf[:, pp * TC:pp * TC + WQC],
                        in_=h["wT"][kq * 128:(kq + 1) * 128, c2 * WQC:(c2 + 1) * WQC]),
                      w=[rxf[pp]], dma=f"xf{pp}")
                    E("s", lambda s, pp=pp: s.activation(
                        out=ev1[:, pp * TC:pp * TC + WQC],
                        in_=xf[:, pp * TC:pp * TC + WQC], func=AF.Abs),
                      r=[rxf[pp]], w=[rev1[pp]])
                    E("v", lambda v, pp=pp: v.tensor_scalar(
                        out=ev2[:, pp * TC:pp * TC + WQC],
                        in0=ev1[:, pp * TC:pp * TC + WQC], scalar1=mxc[:, 36:37],
                        scalar2=None, op0=ALU.is_gt), r=[rev1[pp], rmxc],
                      w=[rev2[pp]])
                    E("v", lambda v, kq=kq, c2=c2, pp=pp: v.tensor_reduce(
                        out=mxc[:, 40 + kq * NW + c2:41 + kq * NW + c2],
                        in_=ev2[:, pp * TC:pp * TC + WQC], axis=AXX, op=ALU.add),
                      r=[rev2[pp]], w=[rmxc])
                    E("v", lambda v, pp=pp: v.tensor_tensor(
                        out=ev1[:, pp * TC:pp * TC + WQC],
                        in0=ev1[:, pp * TC:pp * TC + WQC],
                        in1=ev2[:, pp * TC:pp * TC + WQC], op=ALU.mult),
                      r=[rev2[pp]], w=[rev1[pp]])
                    E("v", lambda v, kq=kq, c2=c2, pp=pp: v.tensor_reduce(
                        out=mxc[:, MXB + kq * NW + c2:MXB + 1 + kq * NW + c2],
                        in_=ev1[:, pp * TC:pp * TC + WQC], axis=AXX, op=ALU.add),
                      r=[rev1[pp]], w=[rmxc])
                    yield
            E("v", lambda v: v.tensor_reduce(
                out=mxc[:, 38:39], in_=mxc[:, 40:40 + NG * NW], axis=AXX,
                op=ALU.add), r=[rmxc], w=[rmxc])
            E("v", lambda v: v.tensor_reduce(
                out=mxc[:, 39:40], in_=mxc[:, MXB:MXB + NG * NW], axis=AXX,
                op=ALU.add), r=[rmxc], w=[rmxc])
            E("g", lambda g_: g_.dma_start(out=p64[0:1, 0:128], in_=mxc[:, 38:39], allow_slow_non_contiguous=True),
              r=[rmxc], w=[rp64], dma="p64s")
            E("g", lambda g_: g_.dma_start(out=p64[0:1, 128:256], in_=mxc[:, 39:40], allow_slow_non_contiguous=True),
              r=[rmxc], w=[rp64], dma="p64s")
            E("v", lambda v: v.tensor_reduce(
                out=p64[0:1, 256:257], in_=p64[0:1, 0:128], axis=AXX, op=ALU.add),
              r=[rp64], w=[rp64])
            E("v", lambda v: v.tensor_reduce(
                out=p64[0:1, 257:258], in_=p64[0:1, 128:256], axis=AXX, op=ALU.add),
              r=[rp64], w=[rp64])
            E("v", lambda v: v.tensor_scalar(
                out=p64[0:1, 256:257], in0=p64[0:1, 256:257], scalar1=1.0,
                scalar2=None, op0=ALU.max), r=[rp64], w=[rp64])
            E("v", lambda v: v.reciprocal(out=p64[0:1, 256:257],
                                          in_=p64[0:1, 256:257]), r=[rp64], w=[rp64])
            E("v", lambda v: v.tensor_tensor(
                out=p64[0:1, 257:258], in0=p64[0:1, 257:258], in1=p64[0:1, 256:257],
                op=ALU.mult), r=[rp64], w=[rp64])
            E("t", lambda t: t.matmul(
                out=PSS[:, 0:1], lhsT=onescolf[0:1, :], rhs=p64[0:1, 257:258],
                start=True, stop=True), r=[rp64, rones], w=[rpss])
            E("v", lambda v: v.tensor_copy(out=mxc[:, 37:38], in_=PSS[:, 0:1]),
              r=[rpss], w=[rmxc])
            yield
    _pw = prework()
    _pw_done = [False]
    def pw_pull(n):
        for _ in range(n):
            if _pw_done[0]:
                return
            try:
                next(_pw)
            except StopIteration:
                _pw_done[0] = True
                return

    def qkv_proj(pair, prefetched=False):
        pb = pair % 2
        th = []
        if not prefetched:
            th.append(lambda pair=pair: (stream_w(h["q_T"], pair, NG, 0),
                                         stream_w(h["k_T"], pair, NG, 1)))
        for tc in range(NTC):
            for si, dst, rdst in ((0, qpair, rqp), (1, kpair, rkp)):
                def chunk(tc=tc, si=si, dst=dst, rdst=rdst, pb=pb):
                    bank = si
                    for k in range(NG):
                        E("t", lambda t, k=k, si=si, tc=tc, bank=bank:
                            t.matmul(
                                out=PS[bank][:, 0:TC], lhsT=slot(si, k),
                                rhs=xb1[:, k * T + tc * TC:
                                        k * T + (tc + 1) * TC],
                                start=(k == 0), stop=(k == NG - 1)),
                          r=[rslot[si], rxb1[k]], w=[rps[bank]])
                    E("v", lambda v, tc=tc, bank=bank, dst=dst, pb=pb:
                        v.tensor_tensor(
                            out=dst[:, pb * T + tc * TC:
                                    pb * T + (tc + 1) * TC],
                            in0=PS[bank][:, 0:TC],
                            in1=s0b[:, tc * TC:(tc + 1) * TC], op=ALU.mult),
                      r=[rps[bank], rs0b], w=[rdst[pb]])
                th.append(chunk)
        if not prefetched:
            th.append(lambda pair=pair: stream_w(h["v_T"], pair, NG, 2))
        for t2 in range(NT2):
            def vchunk(t2=t2, pb=pb):
                vb = pb * NT2 * 130
                for k in range(NG):
                    E("t", lambda t, k=k, t2=t2: t.matmul(
                        out=PSB[:, 0:128],
                        lhsT=xb1[:, k * T + t2 * 128:k * T + (t2 + 1) * 128],
                        rhs=slot(2, k), start=(k == 0), stop=(k == NG - 1)),
                      r=[rslot[2], rxb1[k]], w=[rpsb])
                for half in range(2):
                    E("v", lambda v, t2=t2, half=half, vb=vb: v.tensor_scalar(
                        out=vpair[:, vb + t2 * 130 + half * 65:
                                  vb + t2 * 130 + half * 65 + 64],
                        in0=PSB[:, half * 64:(half + 1) * 64],
                        scalar1=scol[:, t2:t2 + 1], scalar2=None,
                        op0=ALU.mult),
                      r=[rpsb, rscol], w=[rvp[pb]])
                E("v", lambda v, t2=t2, vb=vb: v.memset(
                    vpair[:, vb + t2 * 130 + 64:vb + t2 * 130 + 65], 1.0),
                  w=[rvp[pb]])
                E("v", lambda v, t2=t2, vb=vb: v.memset(
                    vpair[:, vb + t2 * 130 + 129:vb + t2 * 130 + 130], 1.0),
                  w=[rvp[pb]])
            th.append(vchunk)
        return th

    def attention(pair, th_next):
        pb = pair % 2
        vb = pb * NT2 * 130
        steps = sum((tc + 1) * (TC // 128) for tc in range(NTC)) + NTC
        step_i = 0
        ti = 0
        def pull():
            nonlocal ti
            while ti * steps < step_i * len(th_next) and ti < len(th_next):
                th_next[ti]()
                ti += 1
        for tc in range(NTC):
            ppg = tc % 2
            nk2 = (tc + 1) * (TC // 128)
            def sc_tile(k2, tc=tc):
                for half in range(2):
                    hb = half * 64
                    sbank = (k2 % 2) * 2 + half
                    E("t", lambda t, k2=k2, hb=hb, tc=tc, sbank=sbank, pb=pb:
                        t.matmul(
                            out=PS[sbank][:, 0:TC],
                            lhsT=kpair[hb:hb + 64,
                                       pb * T + k2 * 128:pb * T + (k2 + 1) * 128],
                            rhs=qpair[hb:hb + 64,
                                      pb * T + tc * TC:pb * T + (tc + 1) * TC],
                            start=True, stop=True),
                      r=[rkp[pb], rqp[pb]], w=[rps[sbank]])
                    dj = k2 * 128 - tc * TC
                    if dj >= 0:
                        ji = dj // 128
                        E("v", lambda v, ji=ji, sbank=sbank: v.tensor_tensor(
                            out=PS[sbank][:, 0:TC], in0=PS[sbank][:, 0:TC],
                            in1=masks[:, ji * TC:(ji + 1) * TC], op=ALU.add),
                          r=[rmask], w=[rps[sbank]])
                    slot_a = (k2 % 4) * 2 + half
                    E("s", lambda s, sbank=sbank, slot_a=slot_a: s.activation(
                        out=attw[:, slot_a * 512:slot_a * 512 + TC],
                        in_=PS[sbank][:, 0:TC], func=AF.Exp, scale=0.125),
                      r=[rps[sbank]], w=[ratt[slot_a]])
            def av_tile(k2, nk2=nk2):
                for half in range(2):
                    slot_a = (k2 % 4) * 2 + half
                    E("t", lambda t, k2=k2, half=half, slot_a=slot_a,
                           nk2=nk2, vb=vb: t.matmul(
                            out=PS[4 + half][0:65, 0:TC],
                            lhsT=vpair[:, vb + k2 * 130 + half * 65:
                                       vb + k2 * 130 + half * 65 + 65],
                            rhs=attw[:, slot_a * 512:slot_a * 512 + TC],
                            start=(k2 == 0), stop=(k2 == nk2 - 1)),
                      r=[rvp[pb], ratt[slot_a]], w=[rps[4 + half]])
            sc_tile(0)
            for k2 in range(nk2):
                if k2 + 1 < nk2:
                    sc_tile(k2 + 1)
                av_tile(k2)
                step_i += 1
                pull()
            for half in range(2):
                E("s", lambda s, half=half: s.activation(
                    out=p64[64:65, 0:TC],
                    in_=PS[4 + half][64:65, 0:TC], func=AF.Ln),
                  r=[rps[4 + half]], w=[rp64d])
                E("s", lambda s: s.activation(
                    out=p64b[64:65, 0:TC], in_=p64[64:65, 0:TC],
                    func=AF.Exp, scale=-1.0), r=[rp64d], w=[rp64b])
                E("t", lambda t: t.matmul(
                    out=PSS[0:64, 0:TC], lhsT=ones64[64:65, 0:64],
                    rhs=p64b[64:65, 0:TC], start=True, stop=True),
                  r=[rp64b, ro64], w=[rpss])
                E("v", lambda v: v.tensor_copy(
                    out=rtile[0:64, 0:TC], in_=PSS[0:64, 0:TC]),
                  r=[rpss], w=[rrt])
                E("v", lambda v, half=half, ppg=ppg: v.tensor_tensor(
                    out=stg[0:64, ppg * 2 * TC + half * TC:
                            ppg * 2 * TC + half * TC + TC],
                    in0=PS[4 + half][0:64, 0:TC], in1=rtile[0:64, 0:TC],
                    op=ALU.mult), r=[rps[4 + half], rrt], w=[rstg[ppg]])
            E("q", lambda q, pair=pair, tc=tc, ppg=ppg: q.dma_start(
                out=sp_rw[pair * 128:(pair + 1) * 128,
                          tc * TC:(tc + 1) * TC].rearrange(
                    "(hh p) c -> p hh c", hh=2),
                in_=stg[0:64, ppg * 2 * TC:ppg * 2 * TC + 2 * TC].rearrange(
                    "p (hh q) -> p hh q", hh=2)),
              r=[rstg[ppg]], w=[rhbm_rw], dma="hrw")
            step_i += 1
            pull()
        while ti < len(th_next):
            th_next[ti]()
            ti += 1

    PW_SCHED = [8, 8, 8, 8, 8, 19, 19, 19]
    for t_ in qkv_proj(0, prefetched=True):
        t_()
    for pair in range(NPAIR):
        pw_pull(PW_SCHED[pair])
        th_next = qkv_proj(pair + 1) if pair + 1 < NPAIR else []
        attention(pair, th_next)

    pw_pull(10 ** 9)

    if tap == "l1_att":
        for g in range(NG):
            for tc in range(NTC):
                pp = (g * NTC + tc) % 2
                E("g", lambda q, g=g, tc=tc, pp=pp: q.dma_start(
                    out=ev1[:, pp * TC:(pp + 1) * TC],
                    in_=sp_rw[g * 128:(g + 1) * 128, tc * TC:(tc + 1) * TC]),
                  r=[rhbm_rw], w=[rev1[pp]], dma=f"dbr{pp}")
                E("v", lambda v, pp=pp: v.tensor_copy(
                    out=ev2[:, pp * TC:(pp + 1) * TC],
                    in_=ev1[:, pp * TC:(pp + 1) * TC]),
                  r=[rev1[pp]], w=[rev2[pp]])
                E("q", lambda q, g=g, tc=tc, pp=pp: q.dma_start(
                    out=dbg[g * 128:(g + 1) * 128, tc * TC:(tc + 1) * TC],
                    in_=ev2[:, pp * TC:(pp + 1) * TC]),
                  r=[rev2[pp]], dma=f"dbw{pp}")

    b.barrier()
    p2.close()

    p3 = contextlib.ExitStack()
    sb3 = lambda name, shape, dtype=F32: p3.enter_context(
        nc.sbuf_tensor(name, shape, dtype))
    wqb = sb3("wqb", [128, NG * D], BF16)
    xqr = sb3("xqr", [128, 2 * NG * TC], BF16)
    attr = sb3("attr", [128, 2 * NG * TC], BF16)
    qint = sb3("qint", [128, TC], I32)
    xev = xf

    rwq = reg("wqb")
    rxq = [reg("xqr.0"), reg("xqr.1")]
    rqi = reg("qint")
    rxe = rxf
    ratr = [reg("attr.0"), reg("attr.1")]

    for kq in range(NG):
        for c2 in range(NW):
            pp = (kq * NW + c2) % 2
            E("q", lambda q, kq=kq, c2=c2, pp=pp: q.dma_start(
                out=xf[:, pp * TC:pp * TC + WQC],
                in_=h["wT"][kq * 128:(kq + 1) * 128, c2 * WQC:(c2 + 1) * WQC]),
              w=[rxf[pp]], dma=f"xf{pp}")
            E("s", lambda s, pp=pp: s.activation(
                out=ev1[:, pp * TC:pp * TC + WQC],
                in_=xf[:, pp * TC:pp * TC + WQC], func=AF.Abs),
              r=[rxf[pp]], w=[rev1[pp]])
            E("v", lambda v, pp=pp: v.tensor_scalar(
                out=ev1[:, pp * TC:pp * TC + WQC],
                in0=ev1[:, pp * TC:pp * TC + WQC], scalar1=mxc[:, 36:37],
                scalar2=None, op0=ALU.is_gt), r=[rmxc], w=[rev1[pp]])
            E("s", lambda s, pp=pp: s.sign(
                out=ev2[:, pp * TC:pp * TC + WQC],
                in_=xf[:, pp * TC:pp * TC + WQC]), r=[rxf[pp]], w=[rev2[pp]])
            E("v", lambda v, pp=pp: v.tensor_tensor(
                out=ev1[:, pp * TC:pp * TC + WQC],
                in0=ev1[:, pp * TC:pp * TC + WQC],
                in1=ev2[:, pp * TC:pp * TC + WQC], op=ALU.mult),
              r=[rev2[pp]], w=[rev1[pp]])
            E("v", lambda v, kq=kq, c2=c2, pp=pp: v.tensor_scalar(
                out=wqb[:, kq * D + c2 * WQC:kq * D + (c2 + 1) * WQC],
                in0=ev1[:, pp * TC:pp * TC + WQC], scalar1=mxc[:, 37:38],
                scalar2=None, op0=ALU.mult), r=[rev1[pp], rmxc], w=[rwq])

    def xq_chunk(tc, k):
        ppx = tc % 2
        pp = k % 2
        E("v", lambda v, k=k, tc=tc, pp=pp: v.tensor_tensor(
            out=xev[:, pp * TC:(pp + 1) * TC],
            in0=x1[:, k * T + tc * TC:k * T + (tc + 1) * TC],
            in1=s1b[:, tc * TC:(tc + 1) * TC], op=ALU.mult),
          r=[rx1t[k][tc], rs1b], w=[rxe[pp]])
        E("v", lambda v, tc=tc, pp=pp: v.tensor_tensor(
            out=xev[:, pp * TC:(pp + 1) * TC],
            in0=xev[:, pp * TC:(pp + 1) * TC],
            in1=m0b[:, tc * TC:(tc + 1) * TC], op=ALU.subtract),
          r=[rm0b], w=[rxe[pp]])
        E("v", lambda v, k=k, pp=pp: v.tensor_scalar(
            out=xev[:, pp * TC:(pp + 1) * TC],
            in0=xev[:, pp * TC:(pp + 1) * TC],
            scalar1=cols["gamma"][:, k:k + 1],
            scalar2=cols["beta"][:, k:k + 1], op0=ALU.mult, op1=ALU.add),
          r=[rcols], w=[rxe[pp]])
        E("v", lambda v, pp=pp: v.tensor_scalar(
            out=xev[:, pp * TC:(pp + 1) * TC],
            in0=xev[:, pp * TC:(pp + 1) * TC], scalar1=mxc[:, 35:36],
            scalar2=None, op0=ALU.mult), r=[rmxc], w=[rxe[pp]])
        E("v", lambda v, pp=pp: v.tensor_scalar(
            out=xev[:, pp * TC:(pp + 1) * TC],
            in0=xev[:, pp * TC:(pp + 1) * TC], scalar1=-127.0,
            scalar2=127.0, op0=ALU.max, op1=ALU.min), w=[rxe[pp]])
        E("s", lambda s, pp=pp: s.sign(
            out=qint[:, 0:TC].bitcast(F32) if False else ev2[:, 0:TC],
            in_=xev[:, pp * TC:(pp + 1) * TC]),
          r=[rxe[pp]], w=[rev2[0]])
        E("v", lambda v, pp=pp: v.scalar_tensor_tensor(
            out=xev[:, pp * TC:(pp + 1) * TC],
            in0=ev2[:, 0:TC], scalar=0.5,
            in1=xev[:, pp * TC:(pp + 1) * TC], op0=ALU.mult,
            op1=ALU.add), r=[rev2[0]], w=[rxe[pp]])
        E("v", lambda v, pp=pp: v.tensor_copy(
            out=qint[:, 0:TC], in_=xev[:, pp * TC:(pp + 1) * TC]),
          r=[rxe[pp]], w=[rqi])
        E("v", lambda v, k=k, tc=tc, ppx=ppx: v.tensor_scalar(
            out=xqr[:, ppx * NG * TC + k * TC:
                    ppx * NG * TC + (k + 1) * TC],
            in0=qint[:, 0:TC], scalar1=mxc[:, 34:35],
            scalar2=None, op0=ALU.mult), r=[rqi, rmxc],
          w=[rxq[ppx]])

    def attr_dma(tc):
        ppx = tc % 2
        E("q", lambda q, tc=tc, ppx=ppx: q.dma_start(
            out=attr[:, ppx * NG * TC:(ppx + 1) * NG * TC].rearrange(
                "p (k c) -> p k c", k=NG),
            in_=sp_rw[:, tc * TC:(tc + 1) * TC].rearrange(
                "(k p) c -> p k c", p=128)),
          r=[rhbm_rw], w=[ratr[ppx]], dma=f"at{ppx}")

    attr_dma(0)
    for k in range(NG):
        xq_chunk(0, k)
    for tc in range(NTC):
        ppx = tc % 2
        if tc + 1 < NTC:
            attr_dma(tc + 1)
        for mo in range(NG):
            if tc + 1 < NTC:
                xq_chunk(tc + 1, mo)
            si = mo % 2
            stream_w(h["o_T"], mo, NG, si)
            bank = 2 + (mo % 2)
            for k in range(NG):
                E("t", lambda t, k=k, si=si, bank=bank, ppx=ppx: t.matmul(
                    out=PS[bank][:, 0:TC], lhsT=slot(si, k),
                    rhs=attr[:, ppx * NG * TC + k * TC:
                             ppx * NG * TC + (k + 1) * TC],
                    start=(k == 0), stop=(k == NG - 1)),
                  r=[rslot[si], ratr[ppx]], w=[rps[bank]])
            bank2 = 4 + (mo % 2)
            for k in range(NG):
                E("t", lambda t, k=k, mo=mo, bank2=bank2, ppx=ppx: t.matmul(
                    out=PS[bank2][:, 0:TC],
                    lhsT=wqb[:, k * D + mo * 128:k * D + (mo + 1) * 128],
                    rhs=xqr[:, ppx * NG * TC + k * TC:
                            ppx * NG * TC + (k + 1) * TC],
                    start=(k == 0), stop=(k == NG - 1)),
                  r=[rwq, rxq[ppx]], w=[rps[bank2]])
            pp = mo % 2
            E("s", lambda s, bank2=bank2, pp=pp: s.activation(
                out=x2b[:, pp * TC:(pp + 1) * TC], in_=PS[bank2][:, 0:TC],
                func=AF.Tanh, scale=0.5), r=[rps[bank2]], w=[rx2b[pp]])
            E("v", lambda v, bank2=bank2, pp=pp: v.scalar_tensor_tensor(
                out=ev1[:, pp * TC:(pp + 1) * TC],
                in0=x2b[:, pp * TC:(pp + 1) * TC], scalar=1.0,
                in1=PS[bank2][:, 0:TC], op0=ALU.add, op1=ALU.mult),
              r=[rx2b[pp], rps[bank2]], w=[rev1[pp]])
            E("v", lambda v, bank=bank, pp=pp: v.tensor_tensor(
                out=ev1[:, pp * TC:(pp + 1) * TC],
                in0=ev1[:, pp * TC:(pp + 1) * TC], in1=PS[bank][:, 0:TC],
                op=ALU.mult), r=[rps[bank]], w=[rev1[pp]])
            E("v", lambda v, mo=mo, tc=tc, pp=pp: v.scalar_tensor_tensor(
                out=x1[:, mo * T + tc * TC:mo * T + (tc + 1) * TC],
                in0=ev1[:, pp * TC:(pp + 1) * TC], scalar=0.5,
                in1=x1[:, mo * T + tc * TC:mo * T + (tc + 1) * TC],
                op0=ALU.mult, op1=ALU.add), r=[rev1[pp]], w=[rx1t[mo][tc]])
            E("q", lambda q, mo=mo, tc=tc: q.dma_start(
                out=yT[mo * 128:(mo + 1) * 128, tc * TC:(tc + 1) * TC],
                in_=x1[:, mo * T + tc * TC:mo * T + (tc + 1) * TC]),
              r=[rx1t[mo][tc]], dma="out")

    b.barrier(engs=("q",))
    p3.close()
    p1.close()

    blk_ctx.__exit__(None, None, None)
    es.close()
    return nc, b



def make_in_maps(inputs, T=2048):
    tr = lambda a: np.ascontiguousarray(np.asarray(a).T)
    _cv = lambda a: np.ascontiguousarray(np.asarray(a).reshape(8, 128).T)
    def tl(a):
        a = np.asarray(a)
        no, ni = a.shape
        nm, nk = no // 128, ni // 128
        return np.ascontiguousarray(
            a.T.reshape(nk, 128, nm, 128).transpose(2, 1, 0, 3)
            .reshape(nm * 128, nk * 128))
    TC = min(512, T)
    maskc = np.zeros((TC // 128, 128, TC), np.float32)
    for j in range(TC // 128):
        for p in range(128):
            maskc[j, p, :min(p + j * 128, TC)] = NEG
    maskc = np.ascontiguousarray(maskc.transpose(1, 0, 2).reshape(128, -1))
    common = {
        "wrT": tl(inputs["L0_attn_Wr"]), "wkT": tl(inputs["L0_attn_Wk"]),
        "wvT": tl(inputs["L0_attn_Wv"]), "woT": tl(inputs["L0_attn_Wo"]),
        "wfkT": tl(inputs["L0_attn_Wfk"]), "wfvT": tl(inputs["L0_attn_Wfv"]),
        "wfrT": tl(inputs["L0_attn_Wfr"]),
        "wgT": tl(inputs["L0_ffn_w_gate"]), "wuT": tl(inputs["L0_ffn_w_up"]),
        "wdT": tl(inputs["L0_ffn_w_down"]),
        "q_T": tl(inputs["L1_attn_q_proj_weight"]),
        "k_T": tl(inputs["L1_attn_k_proj_weight"]),
        "v_T": tl(inputs["L1_attn_v_proj_weight"]),
        "o_T": tl(inputs["L1_attn_o_proj_weight"]),
        "wT": tr(inputs["L1_ffn_weight"]),
        "ln1w": _cv(inputs["L0_attn_ln1_w"]),
        "ln1b": _cv(inputs["L0_attn_ln1_b"]),
        "ln2w": _cv(inputs["L0_attn_ln2_w"]),
        "ln2b": _cv(inputs["L0_attn_ln2_b"]),
        "decay": _cv(inputs["L0_attn_decay"]),
        "u": _cv(inputs["L0_attn_u"]),
        "gamma": _cv(inputs["L1_ffn_gamma"]),
        "beta": _cv(inputs["L1_ffn_beta"]),
        "maskc": maskc,
    }
    x = np.asarray(inputs["L0_attn_x"])
    in_maps = []
    for ci in range(x.shape[0]):
        m = dict(common)
        m["xT"] = tr(x[ci])
        in_maps.append(m)
    return in_maps


def kernel(**inputs):
    T = 2048
    nc, b = build(T=T, tap=DEBUG_TAP, cores=8)
    in_maps = make_in_maps(inputs, T=T)
    res = run_bass_kernel_spmd(nc, in_maps, core_ids=list(range(8)))
    out = np.stack([np.asarray(r["yT"]).T for r in res.results], axis=0)
    return out.astype(np.float32)
